# version 01
# baseline (speedup 1.0000x reference)
import contextlib
import numpy as np
import ml_dtypes
import concourse.bass as bass
import concourse.mybir as mybir
from concourse.bass_utils import run_bass_kernel_spmd

F32 = mybir.dt.float32
BF16 = mybir.dt.bfloat16
AF = mybir.ActivationFunctionType
ALU = mybir.AluOpType
AX = mybir.AxisListType
AP = bass.AP

NCORES = 8
TLOC = 1024
D = 2048
NCH = D // 128
NEG = -1e30
BF = ml_dtypes.bfloat16


def _cc(a):
    return np.ascontiguousarray(a)



def prep_host(inputs):
    g = {}
    g["w_attn"] = _cc(inputs["L0_attn_w_attn"])
    g["b_attn"] = _cc(inputs["L0_attn_b_attn"].reshape(48, 128).T)
    g["w_proj"] = _cc(inputs["L0_attn_w_proj"])
    g["b_proj"] = _cc(inputs["L0_attn_b_proj"].reshape(16, 128).T)
    g["f0_bwT"] = _cc(inputs["L0_ffn_base_w"].T)
    g["f0_bb"] = _cc(inputs["L0_ffn_base_b"].reshape(16, 128).T)
    g["f0_rwT"] = _cc(inputs["L0_ffn_router_w"].T)
    g["f0_rb"] = _cc(inputs["L0_ffn_router_b"].reshape(8, 1))
    g["f0_Acat"] = _cc(inputs["L0_ffn_lora_A"].transpose(1, 0, 2).reshape(2048, 128))
    g["f0_Bcat"] = _cc(inputs["L0_ffn_lora_B"].reshape(128, 2048))
    g["Wq_c"] = _cc(inputs["L1_attn_Wq_c"])
    g["Wq_up"] = _cc(inputs["L1_attn_Wq_up"])
    g["Wk"] = _cc(inputs["L1_attn_Wk"])
    g["Wv"] = _cc(inputs["L1_attn_Wv"])
    g["W_comp"] = _cc(inputs["L1_attn_W_comp"])
    g["qnw"] = _cc(inputs["L1_attn_q_norm_w"].reshape(1, 128))
    g["knw"] = _cc(inputs["L1_attn_k_norm_w"].reshape(1, 128))
    g["sink"] = _cc(inputs["L1_attn_sink"].reshape(1, 16))
    g["Wg"] = _cc(inputs["L1_attn_Wg"].reshape(4 * 512, 2048))
    g["Wo"] = _cc(inputs["L1_attn_Wo"])
    g["wealth"] = _cc(inputs["L1_ffn_expert_wealth"].reshape(8, 1))
    g["conf_WT"] = _cc(inputs["L1_ffn_conf_W"].T)
    g["conf_b"] = _cc(inputs["L1_ffn_conf_b"].reshape(8, 1))
    g["base_gate"] = _cc(inputs["L1_ffn_base_gate"])
    g["base_up"] = _cc(inputs["L1_ffn_base_up"])
    g["base_down"] = _cc(inputs["L1_ffn_base_down"])
    g["gateAcat"] = _cc(inputs["L1_ffn_gate_A"].transpose(1, 0, 2).reshape(2048, 512))
    g["upAcat"] = _cc(inputs["L1_ffn_up_A"].transpose(1, 0, 2).reshape(2048, 512))
    g["gate_B"] = _cc(inputs["L1_ffn_gate_B"])
    g["up_B"] = _cc(inputs["L1_ffn_up_B"])
    g["down_A"] = _cc(inputs["L1_ffn_down_A"])
    g["down_B"] = _cc(inputs["L1_ffn_down_B"])

    g["ident_bf"] = np.eye(128, dtype=BF)
    g["ident_f32"] = np.eye(128, dtype=np.float32)
    g["ones_col"] = np.ones((128, 1), np.float32)
    g["ones_row"] = np.ones((1, 128), np.float32)
    g["ones_col_bf"] = np.ones((128, 1), BF)
    emat = np.zeros((8, 128), BF)
    for e in range(8):
        emat[e, 16 * e : 16 * e + 16] = 1
    g["emat"] = emat
    emat64 = np.zeros((8, 512), BF)
    for e in range(8):
        emat64[e, 64 * e : 64 * e + 64] = 1
    g["emat64"] = emat64
    a0m = np.zeros((128, 4, 512), np.float32)
    for i, Doff in enumerate([0, -128, -256, -384]):
        ss = np.arange(128)[:, None]
        qq = np.arange(512)[None, :]
        a0m[:, i, :] = np.where(qq + Doff - ss >= 0, 0.0, NEG)
    g["attn0_masks"] = _cc(a0m.reshape(128, 4 * 512))
    qq = np.arange(128)[:, None]
    ss = np.arange(128)[None, :]
    g["sw_mask_diag"] = _cc(np.where(ss <= qq, 0.0, -1e28).astype(np.float32))
    g["sw_mask_old"] = _cc(np.where(ss > qq, 0.0, -1e28).astype(np.float32))
    er = np.zeros((8, 8 * 128), np.float32)
    for e in range(8):
        er[e, e * 128 : (e + 1) * 128] = 1.0
    g["erow"] = er
    cv = np.zeros((128, 8), np.float32)
    cv[:, 0] = 1e-5
    cv[:, 1] = 1e-6
    cv[:, 2] = 1e-30
    g["cvals"] = cv

    rope = 64
    theta = 10000.0
    inv = 1.0 / theta ** (np.arange(0, rope, 2, dtype=np.float32) / rope)
    ang = np.arange(4096, dtype=np.float32)[:, None] * inv[None, :]
    cos_all = np.cos(ang).astype(np.float32)
    sin_all = np.sin(ang).astype(np.float32)

    in_maps = []
    for c in range(NCORES):
        bidx = c // 4
        t0 = 1024 * (c % 4)
        m = dict(g)
        x = inputs["L0_attn_x"][bidx, t0 : t0 + TLOC, :]
        m["x"] = _cc(np.asarray(x, np.float32).T)
        ct = cos_all[t0 : t0 + TLOC].T
        st_ = sin_all[t0 : t0 + TLOC].T
        m["rcos"] = _cc(np.concatenate([ct, ct], axis=0))
        m["rsin"] = _cc(np.concatenate([st_, st_], axis=0))
        cm = np.zeros((8, 128, 256), np.float32)
        for qt in range(8):
            q_global = t0 + qt * 128 + np.arange(128)
            blk_end = np.arange(256) * 16 + 15
            cm[qt] = np.where(blk_end[None, :] < q_global[:, None], 0.0, NEG)
        m["c_masks"] = _cc(cm.reshape(8 * 128, 256))
        r = c % 4
        oth = [(r + 1 + j) % 4 for j in range(3)]
        bo = np.zeros((1, 8), np.int32)
        bo[0, 0] = (4096 * r) * 1024
        bo[0, 1] = (4096 * r + 2048) * 1024
        for j in range(3):
            bo[0, 2 + 2 * j] = (4096 * oth[j]) * 1024
            bo[0, 3 + 2 * j] = (4096 * oth[j] + 2048) * 1024
        m["broff"] = bo
        mc = np.zeros((128, 24), np.float32)
        for j in range(3):
            val = 0.0 if oth[j] < r else NEG
            mc[:, j * 8 : (j + 1) * 8] = val
        m["mcol"] = _cc(mc)
        prev = (r - 1) % 4
        a2 = np.zeros((1, 4), np.int32)
        a2[0, 0] = prev * 1056 * 256
        a2[0, 1] = prev * 1056 * 256 + 131072
        a2[0, 2] = r * 1056 * 256
        a2[0, 3] = r * 1056 * 256 + 131072
        m["a2off"] = a2
        swm = np.zeros((8, 640), np.float32)
        if r == 0:
            for qt in range(8):
                swm[qt, 0 : 128 * (4 - qt) if qt < 4 else 0] = -1e6
        m["swm_row"] = _cc(swm.reshape(1, 8 * 640))
        in_maps.append(m)
    return in_maps


ALL_INPUTS = [
    ("x", (2048, 1024), F32),
    ("w_attn", (2048, 6144), F32), ("b_attn", (128, 48), F32),
    ("w_proj", (2048, 2048), F32), ("b_proj", (128, 16), F32),
    ("f0_bwT", (2048, 2048), F32), ("f0_bb", (128, 16), F32),
    ("f0_rwT", (2048, 8), F32), ("f0_rb", (8, 1), F32),
    ("f0_Acat", (2048, 128), F32), ("f0_Bcat", (128, 2048), F32),
    ("Wq_c", (2048, 512), F32), ("Wq_up", (512, 2048), F32),
    ("Wk", (2048, 128), F32), ("Wv", (2048, 128), F32),
    ("W_comp", (32768, 128), F32),
    ("qnw", (1, 128), F32), ("knw", (1, 128), F32), ("sink", (1, 16), F32),
    ("Wg", (2048, 2048), F32), ("Wo", (8192, 2048), F32),
    ("wealth", (8, 1), F32), ("conf_WT", (2048, 8), F32), ("conf_b", (8, 1), F32),
    ("base_gate", (2048, 7168), F32), ("base_up", (2048, 7168), F32),
    ("base_down", (7168, 2048), F32),
    ("gateAcat", (2048, 512), F32), ("upAcat", (2048, 512), F32),
    ("gate_B", (8, 64, 7168), F32), ("up_B", (8, 64, 7168), F32),
    ("down_A", (8, 7168, 64), F32), ("down_B", (8, 64, 2048), F32),
    ("ident_bf", (128, 128), BF16), ("ident_f32", (128, 128), F32),
    ("ones_col", (128, 1), F32), ("ones_row", (1, 128), F32),
    ("ones_col_bf", (128, 1), BF16),
    ("emat", (8, 128), BF16), ("emat64", (8, 512), BF16),
    ("attn0_masks", (128, 4 * 512), F32),
    ("sw_mask_diag", (128, 128), F32), ("sw_mask_old", (128, 128), F32),
    ("rcos", (64, 1024), F32), ("rsin", (64, 1024), F32),
    ("c_masks", (8 * 128, 256), F32),
    ("cvals", (128, 8), F32),
    ("broff", (1, 8), mybir.dt.int32),
    ("erow", (8, 8 * 128), F32),
    ("a2off", (1, 4), mybir.dt.int32),
    ("swm_row", (1, 8 * 640), F32),
    ("mcol", (128, 24), F32),
]



class B:
    def __init__(self, nc, stack):
        self.nc = nc
        self.stack = stack
        self.scopes = [stack]
        self.sem = {}
        self.cnt = {}
        for name in ["ld", "lg", "st", "te", "ve", "se", "po", "cc",
                     "wa", "wb", "wc", "wd", "sa", "sb", "wx", "wy", "wz",
                     "ha", "hb", "hc", "hd"]:
            self.sem[name] = stack.enter_context(nc.semaphore(name))
            self.cnt[name] = 0
        self.tn = {}
        self.uid = 0

    def push(self):
        s = contextlib.ExitStack()
        self.scopes.append(s)
        return s

    def pop(self):
        s = self.scopes.pop()
        s.close()

    def name(self, base):
        self.uid += 1
        return f"{base}_{self.uid}"

    def inc(self, inst, ctr, n=1):
        inst.then_inc(self.sem[ctr], n)
        self.cnt[ctr] += n
        return self.cnt[ctr]

    def dma(self, eng, out, in_, ctr="ld"):
        if eng is self.nc.gpsimd and ctr == "ld":
            ctr = "lg"
        return self.inc(eng.dma_start(out=out, in_=in_), ctr, 16)

    def wait(self, eng, ctr, val=None):
        v = self.cnt[ctr] if val is None else val
        if v > 0:
            eng.wait_ge(self.sem[ctr], v)

    def sb(self, base, shape, dtype, persistent=False):
        s = self.stack if persistent else self.scopes[-1]
        t = s.enter_context(self.nc.sbuf_tensor(self.name(base), shape, dtype))
        if persistent:
            self.tn[base] = t
        return t

    def ps(self, base, shape, dtype):
        return self.scopes[-1].enter_context(
            self.nc.psum_tensor(self.name(base), shape, dtype))

    def fence(self):
        nc = self.nc
        for eng in [nc.tensor, nc.vector, nc.scalar, nc.gpsimd, nc.sync]:
            for ctr in ["te", "ve", "se", "po", "ld", "lg", "st", "cc",
                        "wa", "wb", "wc", "wd", "sa", "sb", "wx", "wy", "wz",
                     "ha", "hb", "hc", "hd"]:
                self.wait(eng, ctr)


def ap2(t, col, pat):
    rowlen = int(np.prod(t.shape[1:])) if hasattr(t, "shape") else None
    return AP(t, col, [[rowlen, t.shape[0]]] + pat)



def emit_load(b, io):
    nc = b.nc
    x = b.sb("x_res", [128, NCH * TLOC], F32, persistent=True)
    b.sb("x_ln", [128, NCH * TLOC], BF16, persistent=True)
    for c in range(NCH):
        b.dma(nc.sync, x[:, c * TLOC : (c + 1) * TLOC],
              AP(io["x"], c * 128 * TLOC, [[TLOC, 128], [1, TLOC]]))
    for nm, shape, dt in [("ident_bf", [128, 128], BF16), ("ident_f32", [128, 128], F32),
                          ("ones_col", [128, 1], F32), ("ones_row", [1, 128], F32),
                          ("ones_col_bf", [128, 1], BF16),
                          ("emat", [8, 128], BF16),
                          ("cvals", [128, 8], F32)]:
        t = b.sb(nm, shape, dt, persistent=True)
        b.dma(nc.sync, t[:, :], io[nm][:, :])
    b.load_done = b.cnt["ld"]



def emit_ln(b):
    nc = b.nc
    x = b.tn["x_res"]
    xln = b.tn["x_ln"]
    b.push()
    ps_sx = b.ps("ps_sx", [1, 1024], F32)
    ps_sxx = b.ps("ps_sxx", [1, 1024], F32)
    ps_bc = b.ps("ps_bc", [128, 2048], F32)
    sq = b.sb("sq", [128, 2 * 1024], F32)
    sm = b.sb("lnsm", [1, 8 * 1024], F32)
    mean = sm[:, 0:1024]; ex2 = sm[:, 1024:2048]; m2 = sm[:, 2048:3072]
    var = sm[:, 3072:4096]; std = sm[:, 4096:5120]; rstd = sm[:, 5120:6144]
    bmr = sm[:, 6144:7168]
    ones = b.tn["ones_col"]
    onesr = b.tn["ones_row"]

    b.fence()
    sq_guard = {}
    for c in range(NCH):
        buf = sq[:, (c % 2) * 1024 : ((c % 2) + 1) * 1024]
        if c >= 2:
            b.wait(nc.scalar, "te", sq_guard[c - 2])
        b.inc(nc.scalar.activation(buf, x[:, c * TLOC : (c + 1) * TLOC], AF.Square), "se")
        b.wait(nc.tensor, "se")
        for hf in range(2):
            xs = x[:, c * TLOC + hf * 512 : c * TLOC + (hf + 1) * 512]
            b.inc(nc.tensor.matmul(ps_sx[:, hf * 512 : (hf + 1) * 512], ones[:, :], xs,
                                   start=(c == 0), stop=(c == NCH - 1)), "te")
            qs = buf[:, hf * 512 : (hf + 1) * 512]
            b.inc(nc.tensor.matmul(ps_sxx[:, hf * 512 : (hf + 1) * 512], ones[:, :], qs,
                                   start=(c == 0), stop=(c == NCH - 1)), "te")
        sq_guard[c] = b.cnt["te"]

    b.wait(nc.vector, "te")
    b.inc(nc.vector.tensor_scalar_mul(mean, ps_sx[:, :], 1.0 / D), "ve")
    b.inc(nc.vector.tensor_scalar_mul(ex2, ps_sxx[:, :], 1.0 / D), "ve")
    b.wait(nc.vector, "ve")
    b.inc(nc.vector.tensor_tensor(m2, mean, mean, ALU.mult), "ve")
    b.wait(nc.vector, "ve")
    b.inc(nc.vector.tensor_tensor(var, ex2, m2, ALU.subtract), "ve")
    b.wait(nc.scalar, "ve")
    b.inc(nc.scalar.activation(std, var, AF.Sqrt, bias=b.tn["cvals"][0:1, 0:1]), "se")
    b.wait(nc.vector, "se")
    b.inc(nc.vector.reciprocal(rstd, std), "ve")
    b.wait(nc.vector, "ve")
    b.inc(nc.vector.scalar_tensor_tensor(bmr, mean, -1.0, rstd, ALU.mult, ALU.mult), "ve")

    b.wait(nc.tensor, "ve")
    for hf in range(2):
        b.inc(nc.tensor.matmul(ps_bc[:, hf * 512 : (hf + 1) * 512], onesr[:, :],
                               rstd[:, hf * 512 : (hf + 1) * 512], start=True, stop=True), "te")
        b.inc(nc.tensor.matmul(ps_bc[:, 1024 + hf * 512 : 1024 + (hf + 1) * 512], onesr[:, :],
                               bmr[:, hf * 512 : (hf + 1) * 512], start=True, stop=True), "te")
    te_bc = b.cnt["te"]

    b.wait(nc.vector, "te", te_bc)
    for c in range(NCH):
        buf = sq[:, (c % 2) * 1024 : ((c % 2) + 1) * 1024]
        b.wait(nc.vector, "ve")
        b.inc(nc.vector.tensor_tensor(buf, x[:, c * TLOC : (c + 1) * TLOC],
                                      ps_bc[:, 0:1024], ALU.mult), "ve")
        b.wait(nc.vector, "ve")
        b.inc(nc.vector.tensor_tensor(xln[:, c * TLOC : (c + 1) * TLOC], buf,
                                      ps_bc[:, 1024:2048], ALU.add), "ve")
    b.ln_done = b.cnt["ve"]
    b.pop()



def emit_ffn0(b, io):
    nc = b.nc
    x = b.tn["x_res"]
    xln = b.tn["x_ln"]
    ln_done = b.ln_done
    b.push()
    b.fence()

    rw = b.sb("f0_rw", [128, NCH * 8], BF16)
    acat = b.sb("f0_acat", [128, NCH * 128], BF16)
    bcat = b.sb("f0_bcat", [128, 2048], BF16)
    bb = b.sb("f0_bb", [128, 16], F32)
    bbh = b.sb("f0_bbh", [128, 16], F32)
    rb = b.sb("f0_rb", [8, 1], F32)
    b.dma(nc.gpsimd, rw[:, :], AP(io["f0_rwT"], 0, [[8, 128], [128 * 8, NCH], [1, 8]]))
    b.dma(nc.gpsimd, acat[:, :], AP(io["f0_Acat"], 0, [[128, 128], [128 * 128, NCH], [1, 128]]))
    b.dma(nc.gpsimd, bcat[:, :], io["f0_Bcat"][:, :])
    b.dma(nc.sync, bb[:, :], io["f0_bb"][:, :])
    b.dma(nc.sync, rb[:, :], io["f0_rb"][:, :])
    ld_small = b.cnt["ld"]
    lg_small = b.cnt["lg"]

    ps_r = b.ps("f0_psr", [8, 1024], F32)
    ps_mid = b.ps("f0_psmid", [128, 1024], F32)
    ps_small = b.ps("f0_pssmall", [128, 512], F32)
    ps_res = b.ps("f0_psres", [128, 2 * 512], F32)

    logits = b.sb("f0_logits", [8, 1024], F32)
    mid_sb = b.sb("f0_mid", [128, 1024], BF16)
    ltm = b.sb("f0_ltm", [128, 64], F32)
    sm = b.sb("f0_sm", [128, 6 * 64], F32)
    m1 = sm[:, 0:8]; m2 = sm[:, 8:16]; dd = sm[:, 16:24]
    w1 = sm[:, 24:32]; w2 = sm[:, 32:40]
    mask1 = sm[:, 64:128]; l2 = sm[:, 128:192]; mask2 = sm[:, 192:256]
    g1 = sm[:, 256:320]; gate_tm = sm[:, 320:384]
    gfm_sb = b.sb("f0_gfm", [8, 1024], BF16)
    gmid = b.sb("f0_gmid", [128, 1024], BF16)
    wbuf = b.sb("f0_wbuf", [128, 2 * 2048], BF16)
    tmp = b.sb("f0_tmp", [128, 2 * 512], F32)

    identf = b.tn["ident_f32"]
    emat = b.tn["emat"]

    b.wait(nc.vector, "ld", ld_small)
    b.inc(nc.vector.tensor_scalar_mul(bbh[:, :], bb[:, :], 0.5), "ve")
    bbh_done = b.cnt["ve"]

    b.wait(nc.tensor, "ld", ld_small)
    b.wait(nc.tensor, "lg", lg_small)
    b.wait(nc.tensor, "ve", ln_done)
    for c in range(NCH):
        for hf in range(2):
            xs = xln[:, c * TLOC + hf * 512 : c * TLOC + (hf + 1) * 512]
            b.inc(nc.tensor.matmul(ps_r[:, hf * 512 : (hf + 1) * 512],
                                   rw[:, c * 8 : (c + 1) * 8], xs,
                                   start=(c == 0), stop=(c == NCH - 1)), "te")
    te_router = b.cnt["te"]
    for c in range(NCH):
        for hf in range(2):
            xs = xln[:, c * TLOC + hf * 512 : c * TLOC + (hf + 1) * 512]
            b.inc(nc.tensor.matmul(ps_mid[:, hf * 512 : (hf + 1) * 512],
                                   acat[:, c * 128 : (c + 1) * 128], xs,
                                   start=(c == 0), stop=(c == NCH - 1)), "te")
    te_mid = b.cnt["te"]

    b.wait(nc.scalar, "te", te_router)
    b.inc(nc.scalar.activation(logits[:, :], ps_r[:, :], AF.Identity, bias=rb[:, 0:1]), "se")
    se_logits = b.cnt["se"]
    b.wait(nc.scalar, "te", te_mid)
    b.inc(nc.scalar.activation(mid_sb[:, :], ps_mid[:, :], AF.Copy), "se")
    se_mid = b.cnt["se"]

    b.wait(nc.tensor, "se", se_logits)
    for tt in range(8):
        b.inc(nc.tensor.transpose(ps_small[:, 8 * tt : 8 * tt + 8],
                                  logits[:, tt * 128 : (tt + 1) * 128], identf[0:8, 0:8]), "te")
    te_tm = b.cnt["te"]
    b.wait(nc.vector, "te", te_tm)
    b.inc(nc.vector.tensor_copy(ltm[:, :], ps_small[:, 0:64]), "ve")
    b.wait(nc.vector, "ve")

    ltm_v = AP(ltm, 0, [[64, 128], [8, 8], [1, 8]])
    SMW = 6 * 64
    m1rep = AP(sm, 0, [[SMW, 128], [1, 8], [0, 8]])
    m2rep = AP(sm, 8, [[SMW, 128], [1, 8], [0, 8]])
    w1rep = AP(sm, 24, [[SMW, 128], [1, 8], [0, 8]])
    w2rep = AP(sm, 32, [[SMW, 128], [1, 8], [0, 8]])
    l2_v = AP(sm, 128, [[SMW, 128], [8, 8], [1, 8]])

    b.inc(nc.vector.tensor_reduce(m1, ltm_v, AX.X, ALU.max), "ve")
    b.wait(nc.vector, "ve")
    b.inc(nc.vector.tensor_tensor(mask1, ltm[:, :], m1rep, ALU.is_equal), "ve")
    b.wait(nc.vector, "ve")
    b.inc(nc.vector.scalar_tensor_tensor(l2, mask1, NEG, ltm[:, :], ALU.mult, ALU.add), "ve")
    b.wait(nc.vector, "ve")
    b.inc(nc.vector.tensor_reduce(m2, l2_v, AX.X, ALU.max), "ve")
    b.wait(nc.vector, "ve")
    b.inc(nc.vector.tensor_tensor(mask2, l2, m2rep, ALU.is_equal), "ve")
    b.inc(nc.vector.tensor_tensor(dd, m2, m1, ALU.subtract), "ve")
    b.wait(nc.scalar, "ve")
    b.inc(nc.scalar.activation(w1, dd, AF.Sigmoid, scale=-1.0), "se")
    b.wait(nc.vector, "se")
    b.inc(nc.vector.tensor_scalar(w2, w1, -2.0, 2.0, ALU.mult, ALU.add), "ve")
    b.wait(nc.vector, "ve")
    b.inc(nc.vector.tensor_scalar_mul(w1, w1, 2.0), "ve")
    b.wait(nc.vector, "ve")
    b.inc(nc.vector.tensor_tensor(g1, mask1, w1rep, ALU.mult), "ve")
    b.wait(nc.vector, "ve")
    b.inc(nc.vector.tensor_tensor(gate_tm, mask2, w2rep, ALU.mult), "ve")
    b.wait(nc.vector, "ve")
    b.inc(nc.vector.tensor_tensor(gate_tm, gate_tm, g1, ALU.add), "ve")
    ve_gate = b.cnt["ve"]

    b.wait(nc.tensor, "ve", ve_gate)
    for tt in range(8):
        b.inc(nc.tensor.transpose(ps_r[:, tt * 128 : (tt + 1) * 128],
                                  gate_tm[:, 8 * tt : 8 * tt + 8], identf[:, :]), "te")
    te_gfm = b.cnt["te"]
    b.wait(nc.scalar, "te", te_gfm)
    b.inc(nc.scalar.activation(gfm_sb[:, :], ps_r[:, :], AF.Copy), "se")
    se_gfm = b.cnt["se"]
    b.wait(nc.tensor, "se", se_gfm)
    b.wait(nc.tensor, "se", se_mid)
    for hf in range(2):
        b.inc(nc.tensor.matmul(ps_mid[:, hf * 512 : (hf + 1) * 512], emat[:, :],
                               gfm_sb[:, hf * 512 : (hf + 1) * 512], start=True, stop=True), "te")
    te_gexp = b.cnt["te"]
    b.wait(nc.vector, "te", te_gexp)
    b.inc(nc.vector.tensor_tensor(gmid[:, :], mid_sb[:, :], ps_mid[:, :], ALU.mult), "ve")
    ve_gmid = b.cnt["ve"]

    ld_w = {}
    w_guard = {}
    ep_guard_se = {}
    ep_guard_ve = {}

    def issue_wload(oc):
        par = "wa" if oc % 2 == 0 else "wb"
        if oc >= 2:
            b.wait(nc.gpsimd, "te", w_guard[oc - 2])
        ld_w[oc] = (par, b.dma(nc.gpsimd,
                         AP(wbuf, (oc % 2) * 2048,
                            [[2 * 2048, 128], [128, NCH], [1, 128]]),
                         AP(io["f0_bwT"], oc * 128, [[2048, 128], [128 * 2048, NCH], [1, 128]]),
                         ctr=par))

    issue_wload(0)
    issue_wload(1)
    for oc in range(NCH):
        wb = wbuf[:, (oc % 2) * 2048 : ((oc % 2) + 1) * 2048]
        b.wait(nc.tensor, ld_w[oc][0], ld_w[oc][1])
        if oc == 0:
            b.wait(nc.tensor, "ve", ve_gmid)
            b.wait(nc.scalar, "ve", bbh_done)
        for hf in range(2):
            pr = ps_res[:, hf * 512 : (hf + 1) * 512]
            if oc >= 1:
                b.wait(nc.tensor, "se", ep_guard_se[(oc - 1, hf)])
            for c in range(NCH):
                xs = xln[:, c * TLOC + hf * 512 : c * TLOC + (hf + 1) * 512]
                b.inc(nc.tensor.matmul(pr, wb[:, c * 128 : (c + 1) * 128], xs,
                                       start=(c == 0), stop=False), "te")
            b.inc(nc.tensor.matmul(pr, bcat[:, oc * 128 : (oc + 1) * 128],
                                   gmid[:, hf * 512 : (hf + 1) * 512],
                                   start=False, stop=True), "te")
        w_guard[oc] = b.cnt["te"]
        if oc + 2 < NCH:
            issue_wload(oc + 2)
        for hf in range(2):
            pr = ps_res[:, hf * 512 : (hf + 1) * 512]
            tb = tmp[:, hf * 512 : (hf + 1) * 512]
            b.wait(nc.scalar, "te", w_guard[oc])
            if oc >= 1:
                b.wait(nc.scalar, "ve", ep_guard_ve[(oc - 1, hf)])
            ep_guard_se[(oc, hf)] = b.inc(
                nc.scalar.activation(tb, pr, AF.Identity, bias=bbh[:, oc : oc + 1], scale=0.5), "se")
            b.wait(nc.vector, "se")
            xs_out = x[:, oc * TLOC + hf * 512 : oc * TLOC + (hf + 1) * 512]
            ep_guard_ve[(oc, hf)] = b.inc(
                nc.vector.tensor_tensor(xs_out, xs_out, tb, ALU.add), "ve")
    b.pop()




def emit_attn0(b, io):
    nc = b.nc
    x = b.tn["x_res"]
    xln = b.tn["x_ln"]
    ln_done = b.ln_done
    b.push()
    b.fence()

    SCALE = 1.0 / float(np.sqrt(128.0))
    ag_in = nc.dram_tensor(b.name("ag1_in"), [4096, 1024], BF16)
    ag_out = nc.dram_tensor(b.name("ag1_out"), [16384, 1024], BF16)

    q_sb = b.sb("a0_q", [128, 16 * 1024], BF16)
    ba = b.sb("a0_ba", [128, 48], F32)
    bp = b.sb("a0_bp", [128, 16], F32)
    mcol = b.sb("a0_mcol", [128, 24], F32)
    dm = b.sb("a0_dm", [128, 4 * 512], F32)
    bro = b.sb("a0_bro", [1, 8], mybir.dt.int32)
    b.dma(nc.sync, ba[:, :], io["b_attn"][:, :])
    b.dma(nc.sync, bp[:, :], io["b_proj"][:, :])
    b.dma(nc.sync, mcol[:, :], io["mcol"][:, :])
    b.dma(nc.sync, dm[:, :], io["attn0_masks"][:, :])
    b.dma(nc.sync, bro[:, :], io["broff"][:, :])
    ld_small = b.cnt["ld"]

    b.push()
    wbuf = b.sb("a0_wbuf", [128, 2 * 2048], BF16)
    kvst = b.sb("a0_kvst", [128, 2 * 1024], BF16)
    ps_qkv = b.ps("a0_psqkv", [128, 2 * 1024], F32)

    ld_w = {}
    w_guard = {}
    ev_guard = {}
    st_guard = {}

    def wload_a0(oc):
        par = "wa" if oc % 2 == 0 else "wb"
        if oc >= 2:
            b.wait(nc.gpsimd, "te", w_guard[oc - 2])
        ld_w[oc] = (par, b.dma(nc.gpsimd,
                    AP(wbuf, (oc % 2) * 2048, [[2 * 2048, 128], [128, NCH], [1, 128]]),
                    AP(io["w_attn"], oc * 128, [[6144, 128], [128 * 6144, NCH], [1, 128]]),
                    ctr=par))

    wload_a0(0)
    wload_a0(1)
    b.wait(nc.tensor, "ve", ln_done)
    b.wait(nc.scalar, "ld", ld_small)
    for oc in range(48):
        wb = wbuf[:, (oc % 2) * 2048 : ((oc % 2) + 1) * 2048]
        pq = ps_qkv[:, (oc % 2) * 1024 : ((oc % 2) + 1) * 1024]
        b.wait(nc.tensor, ld_w[oc][0], ld_w[oc][1])
        if oc >= 2:
            b.wait(nc.tensor, "se", ev_guard[oc - 2])
        for c in range(NCH):
            for hf in range(2):
                xs = xln[:, c * TLOC + hf * 512 : c * TLOC + (hf + 1) * 512]
                b.inc(nc.tensor.matmul(pq[:, hf * 512 : (hf + 1) * 512],
                                       wb[:, c * 128 : (c + 1) * 128], xs,
                                       start=(c == 0), stop=(c == NCH - 1)), "te")
        w_guard[oc] = b.cnt["te"]
        if oc + 2 < 48:
            wload_a0(oc + 2)
        b.wait(nc.scalar, "te", w_guard[oc])
        if oc < 16:
            ev_guard[oc] = b.inc(nc.scalar.activation(
                q_sb[:, oc * 1024 : (oc + 1) * 1024], pq, AF.Identity,
                bias=ba[:, oc : oc + 1]), "se")
        else:
            kv = kvst[:, (oc % 2) * 1024 : ((oc % 2) + 1) * 1024]
            spar = "hc" if oc % 2 == 0 else "hd"
            if oc >= 18:
                b.wait(nc.scalar, spar, st_guard[oc - 2])
            ev_guard[oc] = b.inc(nc.scalar.activation(
                kv, pq, AF.Identity, bias=ba[:, oc : oc + 1]), "se")
            b.wait(nc.sync, "se", ev_guard[oc])
            row = (oc - 16) * 128 if oc < 32 else 2048 + (oc - 32) * 128
            st_guard[oc] = b.dma(nc.sync, AP(ag_in, row * 1024, [[1024, 128], [1, 1024]]),
                                 kv, ctr=spar)
    b.wait(nc.gpsimd, "hc")
    b.wait(nc.gpsimd, "hd")
    b.inc(nc.gpsimd.collective_compute(
        "AllGather", ALU.bypass, replica_groups=[[0, 1, 2, 3], [4, 5, 6, 7]],
        ins=[ag_in.ap().opt()], outs=[ag_out.ap().opt()]), "cc")
    cc_done = b.cnt["cc"]
    b.pop()

    b.push()
    kbuf = b.sb("a0_kbuf", [128, 2 * 3 * 1024], BF16)
    vbuf = b.sb("a0_vbuf", [128, 2 * 3 * 1024], BF16)
    kobuf = b.sb("a0_kobuf", [128, 2 * 1024], BF16)
    vobuf = b.sb("a0_vobuf", [128, 2 * 1024], BF16)
    vt_sb = b.sb("a0_vt", [128, 4 * 1024], BF16)
    pt_sb = b.sb("a0_pt", [128, 2 * 512], BF16)
    negm_sb = b.sb("a0_negm", [1, 2 * 1024], F32)
    negmc = b.sb("a0_negmc", [128, 8], F32)
    zr = b.sb("a0_zr", [1, 4 * 512], F32)
    zbc_sb = b.sb("a0_zbc", [128, 512], F32)

    ps_sA = b.ps("a0_pssA", [128, 512], F32)
    ps_sB = b.ps("a0_pssB", [128, 512], F32)
    ps_o = b.ps("a0_pso", [128, 512], F32)
    ps_z = b.ps("a0_psz", [1, 512], F32)
    ps_m = b.ps("a0_psm", [1, 1024], F32)
    ps_vt = b.ps("a0_psvt", [128, 1024], BF16)
    ps_zbc = b.ps("a0_pszbc", [128, 512], F32)

    onesr = b.tn["ones_row"]
    onescb = b.tn["ones_col_bf"]
    identf = b.tn["ident_f32"]
    identb = b.tn["ident_bf"]
    zcol = b.tn["cvals"][:, 3:4]

    kvn = nc.dram_tensor(b.name("a0_kvn"), [16384, 1024], BF16)
    regs = [b.scopes[-1].enter_context(nc.gpsimd.register(f"a0r{i}")) for i in range(8)]
    b.wait(nc.gpsimd, "ld", ld_small)
    for i in range(8):
        nc.gpsimd.reg_load(regs[i], bro[0:1, i : i + 1])
    b.wait(nc.gpsimd, "cc", cc_done)
    dst_rows = [0, 2048, 4096, 10240, 6144, 12288, 8192, 14336]
    dst_for = [0, 2048, 4096, 10240, 6144, 12288, 8192, 14336]
    for i in range(8):
        b.dma(nc.gpsimd, AP(kvn, dst_for[i] * 1024, [[1024, 2048], [1, 1024]]),
              AP(ag_out, regs[i], [[1024, 2048], [1, 1024]]), ctr="lg")
    kvn_done = b.cnt["lg"]

    kv_ld = {}

    def kvload(h):
        par = h % 2
        ctr = "ha" if par == 0 else "hb"
        if h >= 2:
            b.wait(nc.sync, "te", b.a0_head_guard[h - 2])
        else:
            b.wait(nc.sync, "lg", kvn_done)
        b.dma(nc.sync, kobuf[:, par * 1024 : (par + 1) * 1024],
              AP(kvn, (0 + 128 * h) * 1024, [[1024, 128], [1, 1024]]), ctr=ctr)
        b.dma(nc.sync, vobuf[:, par * 1024 : (par + 1) * 1024],
              AP(kvn, (2048 + 128 * h) * 1024, [[1024, 128], [1, 1024]]), ctr=ctr)
        for j in range(3):
            b.dma(nc.sync, kbuf[:, (par * 3 + j) * 1024 : (par * 3 + j + 1) * 1024],
                  AP(kvn, (4096 + 2048 * j + 128 * h) * 1024, [[1024, 128], [1, 1024]]), ctr=ctr)
            b.dma(nc.sync, vbuf[:, (par * 3 + j) * 1024 : (par * 3 + j + 1) * 1024],
                  AP(kvn, (10240 + 2048 * j + 128 * h) * 1024, [[1024, 128], [1, 1024]]), ctr=ctr)
        kv_ld[h] = (ctr, b.cnt[ctr])

    b.a0_head_guard = {}
    kvload(0)
    kvload(1)
    for h in range(16):
        par = h % 2
        ctr, ldv = kv_ld[h]
        b.wait(nc.tensor, ctr, ldv)
        vt_guard = b.cnt["ve"]
        for grp in range(4):
            if grp < 3:
                vsrc = vbuf[:, (par * 3 + grp) * 1024 : (par * 3 + grp + 1) * 1024]
            else:
                vsrc = vobuf[:, par * 1024 : (par + 1) * 1024]
            b.wait(nc.tensor, "ve", vt_guard)
            for tt in range(8):
                b.inc(nc.tensor.transpose(ps_vt[:, tt * 128 : (tt + 1) * 128],
                                          vsrc[:, tt * 128 : (tt + 1) * 128],
                                          identb[:, :]), "te")
            b.wait(nc.vector, "te")
            vt_guard = b.inc(nc.vector.tensor_copy(
                vt_sb[:, grp * 1024 : (grp + 1) * 1024], ps_vt[:, :]), "ve")
        msub_guard = b.cnt["ve"]
        for qt in range(8):
            ns = max(4, 4 * qt)
            b.wait(nc.tensor, "ve", msub_guard if qt == 0 else b.cnt["ve"])
            b.inc(nc.tensor.matmul(
                ps_sA[:, 0:ns],
                q_sb[:, h * 1024 + qt * 128 : h * 1024 + (qt + 1) * 128],
                AP(kobuf, par * 1024, [[2 * 1024, 128], [32, ns]]),
                start=True, stop=True), "te")
            b.wait(nc.vector, "te")
            b.inc(nc.vector.tensor_reduce(
                negmc[:, qt : qt + 1], ps_sA[:, 0:ns],
                AX.X, ALU.max, negate=True), "ve")
        b.wait(nc.tensor, "ve")
        for qt in range(8):
            b.inc(nc.tensor.transpose(ps_m[0:1, qt * 128 : (qt + 1) * 128],
                                      negmc[:, qt : qt + 1], identf[:, :]), "te")
        b.wait(nc.vector, "te")
        b.inc(nc.vector.tensor_copy(negm_sb[:, par * 1024 : (par + 1) * 1024],
                                    ps_m[:, :]), "ve")
        negm_done = b.cnt["ve"]

        for qb in range(2):
            visits = [("oth", st) for st in range(24)] +                      [("own", j) for j in range(4 * qb + 4)]
            nv = len(visits)
            exp_guard = {}
            zav_guard = {}
            for vi, (kind, idx) in enumerate(visits):
                pp = vi % 2
                ps_s = ps_sA if pp == 0 else ps_sB
                if vi >= 2:
                    b.wait(nc.tensor, "se", exp_guard[vi - 2])
                if vi == 0:
                    b.wait(nc.tensor, "ve")
                b.inc(nc.tensor.matmul(ps_s[:, :], onesr[:, :],
                                       negm_sb[0:1, par * 1024 + qb * 512 : par * 1024 + (qb + 1) * 512],
                                       start=True, stop=False, skip_group_check=True), "te")
                if kind == "oth":
                    kt = kbuf[:, (par * 3) * 1024 + idx * 128 : (par * 3) * 1024 + (idx + 1) * 128]
                    vt = vt_sb[:, idx * 128 : (idx + 1) * 128]
                    biascol = mcol[:, idx : idx + 1]
                    diag = None
                else:
                    kt = kobuf[:, par * 1024 + idx * 128 : par * 1024 + (idx + 1) * 128]
                    vt = vt_sb[:, 3 * 1024 + idx * 128 : 3 * 1024 + (idx + 1) * 128]
                    biascol = zcol
                    di = idx - 4 * qb
                    diag = dm[:, di * 512 : (di + 1) * 512] if 0 <= di < 4 else None
                b.inc(nc.tensor.matmul(ps_s[:, :], kt,
                                       q_sb[:, h * 1024 + qb * 512 : h * 1024 + (qb + 1) * 512],
                                       start=False, stop=True, skip_group_check=True), "te")
                te_s = b.cnt["te"]
                if diag is not None:
                    b.wait(nc.vector, "te", te_s)
                    b.inc(nc.vector.tensor_tensor(ps_s[:, :], ps_s[:, :], diag, ALU.add), "ve")
                    b.wait(nc.scalar, "ve")
                else:
                    b.wait(nc.scalar, "te", te_s)
                if vi >= 2:
                    b.wait(nc.scalar, "te", zav_guard[vi - 2])
                exp_guard[vi] = b.inc(nc.scalar.activation(
                    pt_sb[:, pp * 512 : (pp + 1) * 512], ps_s[:, :], AF.Exp,
                    bias=biascol, scale=SCALE), "se")
                b.wait(nc.tensor, "se", exp_guard[vi])
                b.inc(nc.tensor.matmul(ps_z[:, :], onescb[:, :],
                                       pt_sb[:, pp * 512 : (pp + 1) * 512],
                                       start=(vi == 0), stop=(vi == nv - 1),
                                       skip_group_check=True), "te")
                b.inc(nc.tensor.matmul(ps_o[:, :], vt,
                                       pt_sb[:, pp * 512 : (pp + 1) * 512],
                                       start=(vi == 0), stop=(vi == nv - 1),
                                       skip_group_check=True), "te")
                zav_guard[vi] = b.cnt["te"]
            te_fin = b.cnt["te"]
            b.wait(nc.vector, "te", te_fin)
            b.inc(nc.vector.tensor_copy(zr[:, 0:512], ps_z[:, :]), "ve")
            b.wait(nc.vector, "ve")
            b.inc(nc.vector.tensor_scalar_add(zr[:, 512:1024], zr[:, 0:512], 1e-30), "ve")
            b.wait(nc.vector, "ve")
            b.inc(nc.vector.reciprocal(zr[:, 1024:1536], zr[:, 512:1024]), "ve")
            ve_zi = b.cnt["ve"]
            b.wait(nc.tensor, "ve", ve_zi)
            b.inc(nc.tensor.matmul(ps_zbc[:, :], onesr[:, :], zr[0:1, 1024:1536],
                                   start=True, stop=True, skip_group_check=True), "te")
            b.wait(nc.scalar, "te")
            se_zbc = b.inc(nc.scalar.activation(zbc_sb[:, :], ps_zbc[:, :], AF.Identity,
                                                bias=zcol), "se")
            b.wait(nc.vector, "se", se_zbc)
            b.inc(nc.vector.tensor_tensor(
                xln[:, h * 1024 + qb * 512 : h * 1024 + (qb + 1) * 512],
                ps_o[:, :], zbc_sb[:, :], ALU.mult), "ve")
        b.a0_head_guard[h] = b.cnt["te"]
        if h + 2 < 16:
            kvload(h + 2)
    ao_done = b.cnt["ve"]
    b.pop()

    b.push()
    b.fence()
    wbufp = b.sb("a0_wbufp", [128, 2 * 2048], BF16)
    tmp = b.sb("a0_ptmp", [128, 2 * 512], F32)
    ps_res = b.ps("a0_psres", [128, 2 * 512], F32)
    ld_wp = {}
    wp_guard = {}
    epp_se = {}
    epp_ve = {}

    def wloadp(oc):
        par = "wa" if oc % 2 == 0 else "wb"
        if oc >= 2:
            b.wait(nc.gpsimd, "te", wp_guard[oc - 2])
        ld_wp[oc] = (par, b.dma(nc.gpsimd,
                     AP(wbufp, (oc % 2) * 2048, [[2 * 2048, 128], [128, NCH], [1, 128]]),
                     AP(io["w_proj"], oc * 128, [[2048, 128], [128 * 2048, NCH], [1, 128]]),
                     ctr=par))

    wloadp(0)
    wloadp(1)
    b.wait(nc.tensor, "ve", ao_done)
    for oc in range(NCH):
        wb = wbufp[:, (oc % 2) * 2048 : ((oc % 2) + 1) * 2048]
        b.wait(nc.tensor, ld_wp[oc][0], ld_wp[oc][1])
        for hf in range(2):
            pr = ps_res[:, hf * 512 : (hf + 1) * 512]
            if oc >= 1:
                b.wait(nc.tensor, "se", epp_se[(oc - 1, hf)])
            for c in range(NCH):
                aos = xln[:, c * TLOC + hf * 512 : c * TLOC + (hf + 1) * 512]
                b.inc(nc.tensor.matmul(pr, wb[:, c * 128 : (c + 1) * 128], aos,
                                       start=(c == 0), stop=(c == NCH - 1)), "te")
        wp_guard[oc] = b.cnt["te"]
        if oc + 2 < NCH:
            wloadp(oc + 2)
        for hf in range(2):
            pr = ps_res[:, hf * 512 : (hf + 1) * 512]
            tb = tmp[:, hf * 512 : (hf + 1) * 512]
            b.wait(nc.scalar, "te", wp_guard[oc])
            if oc >= 1:
                b.wait(nc.scalar, "ve", epp_ve[(oc - 1, hf)])
            epp_se[(oc, hf)] = b.inc(nc.scalar.activation(
                tb, pr, AF.Identity, bias=bp[:, oc : oc + 1]), "se")
            b.wait(nc.vector, "se")
            xs_out = x[:, oc * TLOC + hf * 512 : oc * TLOC + (hf + 1) * 512]
            epp_ve[(oc, hf)] = b.inc(
                nc.vector.tensor_tensor(xs_out, xs_out, tb, ALU.add), "ve")
    b.pop()
    b.pop()



def emit_ffn1(b, io):
    nc = b.nc
    x = b.tn["x_res"]
    xln = b.tn["x_ln"]
    ln_done = b.ln_done
    b.push()
    b.fence()

    IC = 256
    NIC = 7168 // IC
    NIC_RUN = FFN1_NIC
    identf = b.tn["ident_f32"]

    confw = b.sb("f1_confw", [128, NCH * 8], BF16)
    cb = b.sb("f1_cb", [8, 1], F32)
    wl = b.sb("f1_wl", [8, 1], F32)
    webc = b.sb("f1_webc", [128, 8 * 1024], BF16)
    mid_g = b.sb("f1_midg", [128, 4 * 1024], BF16)
    mid_u = b.sb("f1_midu", [128, 4 * 1024], BF16)
    md_sb = b.sb("f1_md", [128, 4 * 1024], BF16)
    ps_base = b.ps("f1_psbase", [128, 2 * 512], F32)
    ps_md = b.ps("f1_psmd", [64, 512], F32)
    ps_util = b.ps("f1_psutil", [128, 2 * 512], F32)
    ps_tm = b.ps("f1_pstm", [128, 128], F32)

    b.push()
    erow = b.sb("f1_erow", [8, 8 * 128], F32)
    we8 = b.sb("f1_we8", [8, 1024], F32)
    conf_sb = b.sb("f1_conf", [8, 1024], F32)
    bids = b.sb("f1_bids", [8, 1024], F32)
    ltm = b.sb("f1_ltm", [128, 64], F32)
    sm = b.sb("f1_sm", [128, 6 * 64], F32)
    m1 = sm[:, 0:8]; m2 = sm[:, 8:16]; dd = sm[:, 16:24]
    w1 = sm[:, 24:32]; w2 = sm[:, 32:40]
    mask1 = sm[:, 64:128]; l2 = sm[:, 128:192]; mask2 = sm[:, 192:256]
    g1 = sm[:, 256:320]; gate_tm = sm[:, 320:384]

    b.dma(nc.gpsimd, confw[:, :], AP(io["conf_WT"], 0, [[8, 128], [128 * 8, NCH], [1, 8]]))
    b.dma(nc.sync, cb[:, :], io["conf_b"][:, :])
    b.dma(nc.sync, wl[:, :], io["wealth"][:, :])
    b.dma(nc.sync, erow[:, :], io["erow"][:, :])
    ld_small = b.cnt["ld"]
    lg_small = b.cnt["lg"]

    b.wait(nc.tensor, "lg", lg_small)
    b.wait(nc.tensor, "ve", ln_done)
    for hf in range(2):
        for c in range(NCH):
            xs = xln[:, c * TLOC + hf * 512 : c * TLOC + (hf + 1) * 512]
            b.inc(nc.tensor.matmul(ps_util[0:8, hf * 512 : (hf + 1) * 512],
                                   confw[:, c * 8 : (c + 1) * 8], xs,
                                   start=(c == 0), stop=(c == NCH - 1)), "te")
    te_conf = b.cnt["te"]
    b.wait(nc.scalar, "te", te_conf)
    b.wait(nc.scalar, "ld", ld_small)
    b.inc(nc.scalar.activation(conf_sb[:, :], ps_util[0:8, 0:1024], AF.Sigmoid,
                               bias=cb[:, 0:1]), "se")
    b.wait(nc.vector, "se")
    b.inc(nc.vector.tensor_scalar(bids[:, :], conf_sb[:, :], wl[:, 0:1], None, ALU.mult), "ve")
    ve_bids = b.cnt["ve"]
    b.wait(nc.tensor, "ve", ve_bids)
    for tt in range(8):
        b.inc(nc.tensor.transpose(AP(ps_tm, 8 * tt, [[128, 128], [1, 8]]),
                                  bids[:, tt * 128 : (tt + 1) * 128], identf[0:8, 0:8]), "te")
    te_tm = b.cnt["te"]
    b.wait(nc.vector, "te", te_tm)
    b.inc(nc.vector.tensor_copy(ltm[:, :], AP(ps_tm, 0, [[128, 128], [1, 64]])), "ve")
    b.wait(nc.vector, "ve")
    SMW = 6 * 64
    ltm_v = AP(ltm, 0, [[64, 128], [8, 8], [1, 8]])
    m1rep = AP(sm, 0, [[SMW, 128], [1, 8], [0, 8]])
    m2rep = AP(sm, 8, [[SMW, 128], [1, 8], [0, 8]])
    w1rep = AP(sm, 24, [[SMW, 128], [1, 8], [0, 8]])
    w2rep = AP(sm, 32, [[SMW, 128], [1, 8], [0, 8]])
    l2_v = AP(sm, 128, [[SMW, 128], [8, 8], [1, 8]])
    b.inc(nc.vector.tensor_reduce(m1, ltm_v, AX.X, ALU.max), "ve")
    b.wait(nc.vector, "ve")
    b.inc(nc.vector.tensor_tensor(mask1, ltm[:, :], m1rep, ALU.is_equal), "ve")
    b.wait(nc.vector, "ve")
    b.inc(nc.vector.scalar_tensor_tensor(l2, mask1, NEG, ltm[:, :], ALU.mult, ALU.add), "ve")
    b.wait(nc.vector, "ve")
    b.inc(nc.vector.tensor_reduce(m2, l2_v, AX.X, ALU.max), "ve")
    b.wait(nc.vector, "ve")
    b.inc(nc.vector.tensor_tensor(mask2, l2, m2rep, ALU.is_equal), "ve")
    b.inc(nc.vector.tensor_tensor(dd, m2, m1, ALU.subtract), "ve")
    b.wait(nc.scalar, "ve")
    b.inc(nc.scalar.activation(w1, dd, AF.Sigmoid, scale=-1.0), "se")
    b.wait(nc.vector, "se")
    b.inc(nc.vector.tensor_scalar(w2, w1, -0.5, 0.5, ALU.mult, ALU.add), "ve")
    b.wait(nc.vector, "ve")
    b.inc(nc.vector.tensor_scalar_mul(w1, w1, 0.5), "ve")
    b.wait(nc.vector, "ve")
    b.inc(nc.vector.tensor_tensor(g1, mask1, w1rep, ALU.mult), "ve")
    b.wait(nc.vector, "ve")
    b.inc(nc.vector.tensor_tensor(gate_tm, mask2, w2rep, ALU.mult), "ve")
    b.wait(nc.vector, "ve")
    b.inc(nc.vector.tensor_tensor(gate_tm, gate_tm, g1, ALU.add), "ve")
    ve_gate = b.cnt["ve"]
    b.wait(nc.tensor, "ve", ve_gate)
    b.wait(nc.tensor, "se")
    for tt in range(8):
        b.inc(nc.tensor.transpose(ps_util[0:8, tt * 128 : (tt + 1) * 128],
                                  gate_tm[:, 8 * tt : 8 * tt + 8], identf[:, :]), "te")
    te_we = b.cnt["te"]
    b.wait(nc.vector, "te", te_we)
    b.inc(nc.vector.tensor_copy(we8[:, :], ps_util[0:8, 0:1024]), "ve")
    ve_we8 = b.cnt["ve"]
    b.wait(nc.tensor, "ve", ve_we8)
    for e in range(8):
        if e > 0:
            b.wait(nc.tensor, "se")
        for hf in range(2):
            b.inc(nc.tensor.matmul(ps_util[:, hf * 512 : (hf + 1) * 512],
                                   erow[:, e * 128 : (e + 1) * 128],
                                   we8[:, hf * 512 : (hf + 1) * 512],
                                   start=True, stop=True), "te")
        b.wait(nc.scalar, "te")
        b.inc(nc.scalar.activation(webc[:, e * 1024 : (e + 1) * 1024],
                                   ps_util[:, :], AF.Copy), "se")
    se_webc = b.cnt["se"]
    b.fence()
    b.pop()

    b.push()
    wg_b = b.sb("f1_wgb", [128, 4096], BF16)
    wu_b = b.sb("f1_wub", [128, 4096], BF16)
    wd_b = b.sb("f1_wdb", [128, 2 * 4096], BF16)
    gB_sb = b.sb("f1_gB2", [128, 2 * 1024], BF16)
    uB_sb = b.sb("f1_uB2", [128, 2 * 1024], BF16)
    dA_sb = b.sb("f1_dA", [128, 1024], BF16)
    G_sb = b.sb("f1_G", [128, 1024], F32)
    U_sb = b.sb("f1_U", [128, 1024], F32)
    gscr = b.sb("f1_gscr", [128, 1024], F32)
    uscr = b.sb("f1_uscr", [128, 1024], F32)
    whbf = b.sb("f1_whbf", [128, 1024], BF16)
    hbarbf = b.sb("f1_hbarbf", [128, 4 * 1024], BF16)

    b.wait(nc.tensor, "se", se_webc)
    for (mid, wsrc, ioA, par) in [(mid_g, wg_b, "gateAcat", "wa"),
                                  (mid_u, wu_b, "upAcat", "wb")]:
        for piece in range(2):
            if piece == 1:
                b.wait(nc.gpsimd, "te")
            ldv = b.dma(nc.gpsimd, AP(wsrc, 0, [[4096, 128], [256, NCH], [1, 256]]),
                        AP(io[ioA], piece * 256, [[512, 128], [512 * 128, NCH], [1, 256]]),
                        ctr=par)
            b.wait(nc.tensor, par, ldv)
            for mc2 in range(2):
                mc = piece * 2 + mc2
                c0 = mc2 * 128
                for hf in range(2):
                    pu = ps_util[:, hf * 512 : (hf + 1) * 512]
                    for c in range(NCH):
                        xs = xln[:, c * TLOC + hf * 512 : c * TLOC + (hf + 1) * 512]
                        b.inc(nc.tensor.matmul(pu, wsrc[:, c * 256 + c0 : c * 256 + c0 + 128],
                                               xs, start=(c == 0), stop=(c == NCH - 1)), "te")
                    b.wait(nc.scalar, "te")
                    b.inc(nc.scalar.activation(mid[:, mc * 1024 + hf * 512 : mc * 1024 + (hf + 1) * 512],
                                               pu, AF.Copy, scale=0.25), "se")
                    b.wait(nc.tensor, "se")
    se_mid = b.cnt["se"]
    b.wait(nc.vector, "se", se_mid)
    b.inc(nc.vector.memset(md_sb[:, :], 0.0), "ve")

    wload = {}

    def loadw(tag, ic):
        par = {"g": ("wa", "wa"), "u": ("wb", "wb"), "d": ("sa", "sb"),
               "gB": ("wd", "wx"), "uB": ("wy", "wz"), "dA": ("wc", "wc")}[tag][ic % 2]
        back = 1 if tag in ("g", "u", "dA") else 2
        if (tag, ic - back) in wload and wload[(tag, ic - back)][2] is not None:
            b.wait(nc.gpsimd, "te", wload[(tag, ic - back)][2])
        sl = ic % 2
        if tag == "g":
            v = b.dma(nc.gpsimd, AP(wg_b, 0, [[4096, 128], [256, NCH], [1, 256]]),
                      AP(io["base_gate"], ic * 256, [[7168, 128], [128 * 7168, NCH], [1, 256]]),
                      ctr=par)
        elif tag == "u":
            v = b.dma(nc.gpsimd, AP(wu_b, 0, [[4096, 128], [256, NCH], [1, 256]]),
                      AP(io["base_up"], ic * 256, [[7168, 128], [128 * 7168, NCH], [1, 256]]),
                      ctr=par)
        elif tag == "d":
            v = b.dma(nc.gpsimd, AP(wd_b, sl * 4096, [[2 * 4096, 128], [2048, 2], [1, 2048]]),
                      AP(io["base_down"], ic * 256 * 2048, [[2048, 128], [128 * 2048, 2], [1, 2048]]),
                      ctr=par)
        elif tag in ("gB", "uB"):
            buf = gB_sb if tag == "gB" else uB_sb
            ionm = "gate_B" if tag == "gB" else "up_B"
            for parity in range(2):
                dst = buf[64 * parity : 64 * parity + 64, sl * 1024 : (sl + 1) * 1024]
                v = b.dma(nc.gpsimd,
                          AP(dst.tensor, dst.offset, [[2 * 1024, 64], [256, 4], [1, 256]]),
                          AP(io[ionm], parity * 64 * 7168 + ic * 256,
                             [[7168, 64], [2 * 64 * 7168, 4], [1, 256]]),
                          ctr=par)
        elif tag == "dA":
            for kc in range(2):
                v = b.dma(nc.gpsimd,
                          AP(dA_sb, kc * 64, [[1024, 128], [128, 8], [1, 64]]),
                          AP(io["down_A"], (ic * 256 + kc * 128) * 64,
                             [[64, 128], [7168 * 64, 8], [1, 64]]),
                          ctr=par)
        wload[(tag, ic)] = [par, v, None]

    def setg(tag, ic):
        wload[(tag, ic)][2] = b.cnt["te"]

    b.wait(nc.gpsimd, "te")
    loadw("g", 0)
    loadw("u", 0)
    loadw("dA", 0)
    for ic in [0, 1]:
        for tag in ["d", "gB", "uB"]:
            loadw(tag, ic)

    for ic in range(min(NIC, NIC_RUN) if FFN1_SECT >= 2 else 0):
        for tag in ["g", "u", "d", "gB", "uB", "dA"]:
            b.wait(nc.tensor, wload[(tag, ic)][0], wload[(tag, ic)][1])
        sl = (ic % 2)
        for th in range(2):
            if ic > 0 or th > 0:
                b.wait(nc.tensor, "ve")
            for oc in range(2):
                pb = ps_base[:, oc * 512 : (oc + 1) * 512]
                for c in range(NCH):
                    xs = xln[:, c * TLOC + th * 512 : c * TLOC + (th + 1) * 512]
                    b.inc(nc.tensor.matmul(pb, wg_b[:, c * 256 + oc * 128 :
                                                    c * 256 + (oc + 1) * 128],
                                           xs, start=(c == 0), stop=(c == NCH - 1)), "te")
            te_G = b.cnt["te"]
            b.wait(nc.scalar, "te", te_G)
            b.inc(nc.scalar.activation(G_sb[:, :], AP(ps_base, 0, [[1024, 128], [1, 1024]]),
                                       AF.Copy), "se")
            se_G = b.cnt["se"]
            b.wait(nc.tensor, "se", se_G)
            for oc in range(2):
                pb = ps_base[:, oc * 512 : (oc + 1) * 512]
                for c in range(NCH):
                    xs = xln[:, c * TLOC + th * 512 : c * TLOC + (th + 1) * 512]
                    b.inc(nc.tensor.matmul(pb, wu_b[:, c * 256 + oc * 128 :
                                                    c * 256 + (oc + 1) * 128],
                                           xs, start=(c == 0), stop=(c == NCH - 1)), "te")
            te_U = b.cnt["te"]
            b.wait(nc.scalar, "te", te_U)
            b.inc(nc.scalar.activation(U_sb[:, :], AP(ps_base, 0, [[1024, 128], [1, 1024]]),
                                       AF.Copy), "se")
            se_U = b.cnt["se"]
            for e in range(8):
                ebase = 64 * (e % 2)
                eg = (e // 2) * 1024
                if e == 0:
                    b.wait(nc.tensor, "se", se_U)
                else:
                    b.wait(nc.tensor, "ve")
                for oc in range(2):
                    pb = ps_base[:, oc * 512 : (oc + 1) * 512]
                    b.inc(nc.tensor.matmul(
                        pb, gB_sb[ebase : ebase + 64, sl * 1024 + (e // 2) * 256 + oc * 128 :
                                  sl * 1024 + (e // 2) * 256 + (oc + 1) * 128],
                        mid_g[ebase : ebase + 64, eg + th * 512 : eg + (th + 1) * 512],
                        start=True, stop=True), "te")
                te_dg = b.cnt["te"]
                b.wait(nc.vector, "te", te_dg)
                if e > 0:
                    b.wait(nc.vector, "po")
                b.inc(nc.vector.tensor_tensor(gscr[:, :], G_sb[:, :],
                                              AP(ps_base, 0, [[1024, 128], [1, 1024]]),
                                              ALU.add), "ve")
                ve_g = b.cnt["ve"]
                b.wait(nc.tensor, "ve", ve_g)
                for oc in range(2):
                    pb = ps_base[:, oc * 512 : (oc + 1) * 512]
                    b.inc(nc.tensor.matmul(
                        pb, uB_sb[ebase : ebase + 64, sl * 1024 + (e // 2) * 256 + oc * 128 :
                                  sl * 1024 + (e // 2) * 256 + (oc + 1) * 128],
                        mid_u[ebase : ebase + 64, eg + th * 512 : eg + (th + 1) * 512],
                        start=True, stop=True), "te")
                te_du = b.cnt["te"]
                b.wait(nc.scalar, "ve", ve_g)
                se_sil = b.inc(nc.scalar.activation(gscr[:, :], gscr[:, :], AF.Silu), "se")
                b.wait(nc.vector, "te", te_du)
                b.inc(nc.vector.tensor_tensor(uscr[:, :], U_sb[:, :],
                                              AP(ps_base, 0, [[1024, 128], [1, 1024]]),
                                              ALU.add), "ve")
                ve_u = b.cnt["ve"]
                b.wait(nc.gpsimd, "se", se_sil)
                b.wait(nc.gpsimd, "ve", ve_u)
                b.inc(nc.gpsimd.tensor_tensor(gscr[:, :], gscr[:, :], uscr[:, :], ALU.mult), "po")
                b.wait(nc.gpsimd, "po")
                wev = AP(webc, e * 1024 + th * 512, [[8 * 1024, 128], [0, 2], [1, 512]])
                b.inc(nc.gpsimd.tensor_tensor(whbf[:, :], gscr[:, :], wev, ALU.mult), "po")
                po_wh = b.cnt["po"]
                b.wait(nc.vector, "po", po_wh)
                hbs = hbarbf[:, (sl * 2 + th) * 1024 : (sl * 2 + th + 1) * 1024]
                if e == 0:
                    b.inc(nc.vector.tensor_copy(hbs, whbf[:, :]), "ve")
                else:
                    b.inc(nc.vector.tensor_tensor(hbs, hbs, whbf[:, :], ALU.add), "ve")
                b.wait(nc.tensor, "po", po_wh)
                b.wait(nc.tensor, "ve")
                for kc in range(2):
                    b.inc(nc.tensor.matmul(
                        ps_md[:, :],
                        dA_sb[:, e * 128 + kc * 64 : e * 128 + (kc + 1) * 64],
                        whbf[:, kc * 512 : (kc + 1) * 512],
                        start=(kc == 0), stop=(kc == 1)), "te")
                te_md = b.cnt["te"]
                b.wait(nc.vector, "te", te_md)
                mslice = md_sb[ebase : ebase + 64, (e // 2) * 1024 + th * 512 :
                               (e // 2) * 1024 + (th + 1) * 512]
                b.inc(nc.vector.tensor_tensor(mslice, mslice, ps_md[:, :], ALU.add), "ve")
            ve_hb = b.cnt["ve"]
            if ic % 2 == 1:
                b.wait(nc.tensor, "ve", ve_hb)
                dp_ve = {}
                for oc in range(NCH):
                    pu = ps_util[:, (oc % 2) * 512 : ((oc % 2) + 1) * 512]
                    if oc >= 2:
                        b.wait(nc.tensor, "ve", dp_ve[oc - 2])
                    for pr in range(2):
                        psl = (ic - 1 + pr) % 2
                        for kc in range(2):
                            b.inc(nc.tensor.matmul(
                                pu, wd_b[:, psl * 4096 + kc * 2048 + oc * 128 :
                                         psl * 4096 + kc * 2048 + (oc + 1) * 128],
                                hbarbf[:, (psl * 2 + th) * 1024 + kc * 512 :
                                       (psl * 2 + th) * 1024 + (kc + 1) * 512],
                                start=(pr == 0 and kc == 0), stop=(pr == 1 and kc == 1)), "te")
                    b.wait(nc.vector, "te")
                    xs_out = x[:, oc * TLOC + th * 512 : oc * TLOC + (th + 1) * 512]
                    dp_ve[oc] = b.inc(nc.vector.tensor_tensor(xs_out, xs_out, pu, ALU.add), "ve")
        for tag in ["g", "u", "gB", "uB", "dA"]:
            setg(tag, ic)
        if ic % 2 == 1:
            setg("d", ic - 1)
            setg("d", ic)
        if ic + 1 < NIC:
            loadw("g", ic + 1)
            loadw("u", ic + 1)
            loadw("dA", ic + 1)
        if ic % 2 == 1:
            if ic + 1 < NIC:
                loadw("d", ic + 1)
            if ic + 2 < NIC:
                loadw("d", ic + 2)
        if ic + 2 < NIC:
            for tag in ["gB", "uB"]:
                loadw(tag, ic + 2)
    b.pop()

    b.push()
    b.fence()
    mdb = b.sb("f1_mdbf", [128, 4 * 1024], BF16)
    dB_sb = b.sb("f1_dB", [128, 4 * 2048], BF16)
    if FFN1_SECT < 3:
        b.pop()
        b.pop()
        return
    b.inc(nc.vector.tensor_scalar_mul(mdb[:, :], md_sb[:, :], 0.25), "ve")
    ve_mdb = b.cnt["ve"]
    for e in range(8):
        dst = dB_sb[64 * (e % 2) : 64 * (e % 2) + 64,
                    (e // 2) * 2048 : (e // 2 + 1) * 2048]
        b.dma(nc.gpsimd, dst, AP(io["down_B"], e * 64 * 2048, [[2048, 64], [1, 2048]]),
              ctr="lg")
    lg_dB = b.cnt["lg"]
    b.wait(nc.tensor, "lg", lg_dB)
    b.wait(nc.tensor, "ve", ve_mdb)
    ol_ve = {}
    ps_ol = b.ps("f1_psol", [128, 2 * 512], F32)
    for oc in range(NCH):
        for th in range(2):
            puA = ps_ol[:, 0:512]
            puB = ps_ol[:, 512:1024]
            if oc >= 1 or th == 1:
                b.wait(nc.tensor, "ve")
            for parity, pu in [(0, puA), (1, puB)]:
                for i_e, e in enumerate([parity, 2 + parity, 4 + parity, 6 + parity]):
                    ebase = 64 * parity
                    b.inc(nc.tensor.matmul(
                        pu, dB_sb[ebase : ebase + 64, (e // 2) * 2048 + oc * 128 :
                                  (e // 2) * 2048 + (oc + 1) * 128],
                        mdb[ebase : ebase + 64, (e // 2) * 1024 + th * 512 :
                            (e // 2) * 1024 + (th + 1) * 512],
                        start=(i_e == 0), stop=(i_e == 3)), "te")
            b.wait(nc.vector, "te")
            xs_out = x[:, oc * TLOC + th * 512 : oc * TLOC + (th + 1) * 512]
            b.inc(nc.vector.tensor_tensor(xs_out, xs_out, puA, ALU.add), "ve")
            b.wait(nc.vector, "ve")
            ol_ve[(oc, th)] = b.inc(nc.vector.tensor_tensor(xs_out, xs_out, puB, ALU.add), "ve")
    b.pop()
    b.pop()



def emit_attn1(b, io):
    nc = b.nc
    x = b.tn["x_res"]
    xln = b.tn["x_ln"]
    ln_done = b.ln_done
    b.push()
    b.fence()

    SC = 1.0 / float(np.sqrt(128.0))
    identf = b.tn["ident_f32"]
    identb = b.tn["ident_bf"]
    onesr = b.tn["ones_row"]

    ag_in = nc.dram_tensor(b.name("ag2_in"), [1056, 256], BF16)
    ag_out = nc.dram_tensor(b.name("ag2_out"), [4224, 256], BF16)

    qnw = b.sb("a1_qnw", [1, 128], F32)
    knw = b.sb("a1_knw", [1, 128], F32)
    sinkr = b.sb("a1_sink", [1, 16], F32)
    rcos = b.sb("a1_rcos", [64, 1024], F32)
    rsin = b.sb("a1_rsin", [64, 1024], F32)
    a2o = b.sb("a1_a2o", [1, 4], mybir.dt.int32)
    b.dma(nc.sync, qnw[:, :], io["qnw"][:, :])
    b.dma(nc.sync, knw[:, :], io["knw"][:, :])
    b.dma(nc.sync, sinkr[:, :], io["sink"][:, :])
    b.dma(nc.sync, rcos[:, :], io["rcos"][:, :])
    b.dma(nc.sync, rsin[:, :], io["rsin"][:, :])
    b.dma(nc.sync, a2o[:, :], io["a2off"][:, :])
    ld_small = b.cnt["ld"]

    b.push()
    q_sb = b.sb("a1_q", [128, 16 * 1024], BF16)

    b.push()
    hq_sb = b.sb("a1_hq", [128, 4 * 1024], BF16)
    ksw_f = b.sb("a1_kswf", [128, 1024], F32)
    kn_sb = b.sb("a1_kn", [128, 1024], BF16)
    vsw_sb = b.sb("a1_vsw", [128, 1024], BF16)
    vswtm_sb = b.sb("a1_vswtm", [128, 1024], BF16)
    ck_f = b.sb("a1_ckf", [128, 64], F32)
    ckn_sb = b.sb("a1_ckn", [128, 64], BF16)
    wbuf = b.sb("a1_wbuf", [128, 2 * 2048], BF16)
    rsc = b.sb("a1_rsc", [32, 3 * 2048], F32)
    sqs = b.sb("a1_sqs", [128, 1024], F32)
    srow = b.sb("a1_srow", [1, 1024], F32)

    ps_a = b.ps("a1_psa", [128, 2 * 1024], F32)
    ps_st = b.ps("a1_psst", [1, 512], F32)
    ps_bc = b.ps("a1_psbc", [128, 1024], F32)
    ps_vt = b.ps("a1_psvt", [128, 1024], BF16)

    ld_w = {}
    wg_guard = {}

    def wload(tag, ioname, col0, idx, nkc, par):
        if (tag, idx - 2) in ld_w and ld_w[(tag, idx - 2)][2] is not None:
            b.wait(nc.gpsimd, "te", ld_w[(tag, idx - 2)][2])
        ncols = io[ioname].shape[1]
        v = b.dma(nc.gpsimd,
                  AP(wbuf, (idx % 2) * 2048, [[2 * 2048, 128], [128, nkc], [1, 128]]),
                  AP(io[ioname], col0, [[ncols, 128], [128 * ncols, nkc], [1, 128]]),
                  ctr=par)
        ld_w[(tag, idx)] = [par, v, None]

    b.wait(nc.tensor, "ve", ln_done)
    b.wait(nc.tensor, "ld", ld_small)
    wload("qc", "Wq_c", 0, 0, NCH, "wa")
    wload("qc", "Wq_c", 128, 1, NCH, "wb")
    for oc in range(4):
        pq = ps_a[:, (oc % 2) * 1024 : ((oc % 2) + 1) * 1024]
        b.wait(nc.tensor, ld_w[("qc", oc)][0], ld_w[("qc", oc)][1])
        if oc >= 2:
            b.wait(nc.tensor, "se")
        for hf in range(2):
            for c in range(NCH):
                xs = xln[:, c * TLOC + hf * 512 : c * TLOC + (hf + 1) * 512]
                b.inc(nc.tensor.matmul(pq[:, hf * 512 : (hf + 1) * 512],
                                       wbuf[:, (oc % 2) * 2048 + c * 128 : (oc % 2) * 2048 + (c + 1) * 128],
                                       xs, start=(c == 0), stop=(c == NCH - 1)), "te")
        ld_w[("qc", oc)][2] = b.cnt["te"]
        if oc + 2 < 4:
            wload("qc", "Wq_c", (oc + 2) * 128, oc + 2, NCH, "wa" if oc % 2 == 0 else "wb")
        b.wait(nc.scalar, "te")
        b.inc(nc.scalar.activation(hq_sb[:, oc * 1024 : (oc + 1) * 1024], pq, AF.Copy), "se")
    se_hq = b.cnt["se"]
    for oc in range(16):
        if oc < 2:
            b.wait(nc.gpsimd, "te")
        wload("qu", "Wq_up", oc * 128, 4 + oc, 4, "wa" if oc % 2 == 0 else "wb")
        pq = ps_a[:, (oc % 2) * 1024 : ((oc % 2) + 1) * 1024]
        b.wait(nc.tensor, ld_w[("qu", 4 + oc)][0], ld_w[("qu", 4 + oc)][1])
        if oc == 0:
            b.wait(nc.tensor, "se", se_hq)
        else:
            b.wait(nc.tensor, "se")
        for hf in range(2):
            for c in range(4):
                hs = hq_sb[:, c * 1024 + hf * 512 : c * 1024 + (hf + 1) * 512]
                b.inc(nc.tensor.matmul(pq[:, hf * 512 : (hf + 1) * 512],
                                       wbuf[:, (oc % 2) * 2048 + c * 128 : (oc % 2) * 2048 + (c + 1) * 128],
                                       hs, start=(c == 0), stop=(c == 3)), "te")
        ld_w[("qu", 4 + oc)][2] = b.cnt["te"]
        b.wait(nc.scalar, "te")
        b.inc(nc.scalar.activation(q_sb[:, oc * 1024 : (oc + 1) * 1024], pq, AF.Copy), "se")
    se_qpre = b.cnt["se"]
    b.wait(nc.gpsimd, "te")
    wload("kv", "Wk", 0, 20, NCH, "wa")
    wload("kv", "Wv", 0, 21, NCH, "wb")
    b.wait(nc.tensor, ld_w[("kv", 20)][0], ld_w[("kv", 20)][1])
    b.wait(nc.tensor, "se")
    for hf in range(2):
        for c in range(NCH):
            xs = xln[:, c * TLOC + hf * 512 : c * TLOC + (hf + 1) * 512]
            b.inc(nc.tensor.matmul(ps_a[:, hf * 512 : (hf + 1) * 512],
                                   wbuf[:, c * 128 : (c + 1) * 128], xs,
                                   start=(c == 0), stop=(c == NCH - 1)), "te")
    te_k = b.cnt["te"]
    b.wait(nc.tensor, ld_w[("kv", 21)][0], ld_w[("kv", 21)][1])
    for hf in range(2):
        for c in range(NCH):
            xs = xln[:, c * TLOC + hf * 512 : c * TLOC + (hf + 1) * 512]
            b.inc(nc.tensor.matmul(ps_a[:, 1024 + hf * 512 : 1024 + (hf + 1) * 512],
                                   wbuf[:, 2048 + c * 128 : 2048 + (c + 1) * 128], xs,
                                   start=(c == 0), stop=(c == NCH - 1)), "te")
    te_v = b.cnt["te"]
    b.wait(nc.scalar, "te", te_k)
    b.inc(nc.scalar.activation(ksw_f[:, :], ps_a[:, 0:1024], AF.Copy), "se")
    b.wait(nc.scalar, "te", te_v)
    b.inc(nc.scalar.activation(vsw_sb[:, :], ps_a[:, 1024:2048], AF.Copy), "se")
    se_kv = b.cnt["se"]

    wld_c = {}
    for j in range(16):
        par = "wc" if j % 2 == 0 else "wd"
        if j < 2:
            b.wait(nc.gpsimd, "te")
        else:
            b.wait(nc.gpsimd, "te", wld_c[j - 2])
        b.dma(nc.gpsimd, AP(wbuf, (j % 2) * 2048, [[2 * 2048, 128], [128, NCH], [1, 128]]),
              AP(io["W_comp"], j * 2048 * 128, [[128, 128], [128 * 128, NCH], [1, 128]]),
              ctr=par)
        ldv = b.cnt[par]
        b.wait(nc.tensor, par, ldv)
        for dc in range(NCH):
            rhs = AP(xln, dc * TLOC + j, [[NCH * TLOC, 128], [16, 64]])
            b.inc(nc.tensor.matmul(ps_bc[:, 0:64],
                                   wbuf[:, (j % 2) * 2048 + dc * 128 : (j % 2) * 2048 + (dc + 1) * 128],
                                   rhs, start=(j == 0 and dc == 0), stop=(j == 15 and dc == NCH - 1)),
                  "te")
        wld_c[j] = b.cnt["te"]
    te_ck = b.cnt["te"]
    b.wait(nc.scalar, "te", te_ck)
    b.inc(nc.scalar.activation(ck_f[:, :], ps_bc[:, 0:64], AF.Copy), "se")
    se_ck = b.cnt["se"]

    def rope(buf, nblk, blkstride, cosap, sinap, base_ve):
        x1 = AP(buf, 0, [[blkstride * nblk if False else buf.shape[1], 32], [blkstride, nblk], [1, 1024]])
        return None

    b.wait(nc.vector, "se", se_qpre)
    QW = 16 * 1024
    tA = rsc[:, 0:2048]; tB = rsc[:, 2048:4096]; tC = rsc[:, 4096:6144]
    for hp in range(8):
        off = hp * 2048
        x1 = AP(q_sb, off, [[QW, 32], [1024, 2], [1, 1024]])
        x2 = AP(q_sb, 32 * QW + off, [[QW, 32], [1024, 2], [1, 1024]])
        cosr = AP(rcos, 0, [[1024, 32], [0, 2], [1, 1024]])
        sinr = AP(rsin, 0, [[1024, 32], [0, 2], [1, 1024]])
        cosr32 = AP(rcos, 32 * 1024, [[1024, 32], [0, 2], [1, 1024]])
        sinr32 = AP(rsin, 32 * 1024, [[1024, 32], [0, 2], [1, 1024]])
        b.wait(nc.vector, "ve")
        b.inc(nc.vector.tensor_tensor(tA, x1, cosr, ALU.mult), "ve")
        b.inc(nc.vector.tensor_tensor(tB, x1, sinr, ALU.mult), "ve")
        b.inc(nc.vector.tensor_tensor(tC, x2, sinr32, ALU.mult), "ve")
        b.wait(nc.vector, "ve")
        b.inc(nc.vector.tensor_tensor(x1, tA, tC, ALU.subtract), "ve")
        b.wait(nc.vector, "ve")
        b.inc(nc.vector.tensor_tensor(tC, x2, cosr32, ALU.mult), "ve")
        b.wait(nc.vector, "ve")
        b.inc(nc.vector.tensor_tensor(x2, tB, tC, ALU.add), "ve")
    ve_qrope = b.cnt["ve"]
    b.wait(nc.vector, "se", se_kv)
    b.wait(nc.vector, "ve")
    x1 = AP(ksw_f, 0, [[1024, 32], [1, 1024]])
    x2 = AP(ksw_f, 32 * 1024, [[1024, 32], [1, 1024]])
    cos1 = rcos[0:32, :]
    sin1 = rsin[0:32, :]
    cos32 = rcos[32:64, :]
    sin32 = rsin[32:64, :]
    b.inc(nc.vector.tensor_tensor(tA[:, 0:1024], x1, cos1, ALU.mult), "ve")
    b.inc(nc.vector.tensor_tensor(tB[:, 0:1024], x1, sin1, ALU.mult), "ve")
    b.inc(nc.vector.tensor_tensor(tC[:, 0:1024], x2, sin32, ALU.mult), "ve")
    b.wait(nc.vector, "ve")
    b.inc(nc.vector.tensor_tensor(x1, tA[:, 0:1024], tC[:, 0:1024], ALU.subtract), "ve")
    b.wait(nc.vector, "ve")
    b.inc(nc.vector.tensor_tensor(tC[:, 0:1024], x2, cos32, ALU.mult), "ve")
    b.wait(nc.vector, "ve")
    b.inc(nc.vector.tensor_tensor(x2, tB[:, 0:1024], tC[:, 0:1024], ALU.add), "ve")
    ve_krope = b.cnt["ve"]

    def rmsnorm_fm(src_t, out_t, src_col, w_row, width, ve_ready):
        b.wait(nc.scalar, "ve", ve_ready)
        for hf in range((width + 511) // 512):
            w0 = src_col + hf * 512
            w1 = min(src_col + width, w0 + 512)
            n = w1 - w0
            b.wait(nc.scalar, "te")
            b.inc(nc.scalar.activation(sqs[:, 0:n], src_t[:, w0:w1], AF.Square), "se")
            b.wait(nc.tensor, "se")
            b.inc(nc.tensor.matmul(ps_st[:, 0:n], b.tn["ones_col"][:, :],
                                   sqs[:, 0:n], start=True, stop=True), "te")
            te_ss = b.cnt["te"]
            b.wait(nc.vector, "te", te_ss)
            b.inc(nc.vector.tensor_scalar_mul(srow[:, 0:n], ps_st[:, 0:n], 1.0 / 128.0), "ve")
            b.wait(nc.scalar, "ve")
            b.inc(nc.scalar.activation(srow[:, 0:n], srow[:, 0:n], AF.Sqrt,
                                       bias=b.tn["cvals"][0:1, 1:2]), "se")
            b.wait(nc.vector, "se")
            b.inc(nc.vector.reciprocal(srow[:, 0:n], srow[:, 0:n]), "ve")
            ve_r = b.cnt["ve"]
            b.wait(nc.tensor, "ve", ve_r)
            b.inc(nc.tensor.matmul(ps_bc[:, 0:n], w_row, srow[:, 0:n],
                                   start=True, stop=True), "te")
            te_m = b.cnt["te"]
            b.wait(nc.vector, "te", te_m)
            b.inc(nc.vector.tensor_tensor(out_t[:, w0:w1], src_t[:, w0:w1],
                                          ps_bc[:, 0:n], ALU.mult), "ve")
        return b.cnt["ve"]

    last = ve_qrope
    for h in range(16):
        last = rmsnorm_fm(q_sb, q_sb, h * 1024, qnw[:, :], 1024, last)
    ve_qn = last
    ve_kn = rmsnorm_fm(ksw_f, kn_sb, 0, knw[:, :], 1024, ve_krope)
    b.wait(nc.vector, "se", se_ck)
    ve_ckn = rmsnorm_fm(ck_f, ckn_sb, 0, knw[:, :], 64, b.cnt["ve"])

    b.wait(nc.tensor, "se", se_kv)
    for tt in range(8):
        b.inc(nc.tensor.transpose(ps_vt[:, tt * 128 : (tt + 1) * 128],
                                  vsw_sb[:, tt * 128 : (tt + 1) * 128], identb[:, :]), "te")
    te_vt = b.cnt["te"]
    b.wait(nc.vector, "te", te_vt)
    b.inc(nc.vector.tensor_copy(vswtm_sb[:, :], ps_vt[:, :]), "ve")
    ve_vt = b.cnt["ve"]

    b.wait(nc.sync, "ve", ve_kn)
    b.dma(nc.sync, AP(ag_in, 0, [[1024, 128], [1, 1024]]), kn_sb[:, :], ctr="st")
    b.wait(nc.sync, "ve", ve_vt)
    b.dma(nc.sync, AP(ag_in, 131072, [[1024, 128], [1, 1024]]), vswtm_sb[:, :], ctr="st")
    b.wait(nc.sync, "ve", ve_ckn)
    b.dma(nc.sync, AP(ag_in, 262144, [[64, 128], [1, 64]]), ckn_sb[:, :], ctr="st")
    b.wait(nc.gpsimd, "st")
    b.inc(nc.gpsimd.collective_compute(
        "AllGather", ALU.bypass, replica_groups=[[0, 1, 2, 3], [4, 5, 6, 7]],
        ins=[ag_in.ap().opt()], outs=[ag_out.ap().opt()]), "cc")
    cc2 = b.cnt["cc"]
    b.wait(nc.vector, "ve", ve_qn)
    b.pop()

    b.push()
    b.fence()
    kswp = b.sb("a1_kswp", [128, 2 * 1024], BF16)
    vswp = b.sb("a1_vswp", [128, 2 * 1024], BF16)
    cka = b.sb("a1_cka", [128, 256], BF16)
    cktm = b.sb("a1_cktm", [128, 2 * 128], BF16)
    cmask = b.sb("a1_cmask", [128, 8 * 256], F32)
    swmr = b.sb("a1_swmr", [1, 8 * 640], F32)
    swmd = b.sb("a1_swmd", [128, 128], F32)
    swmo = b.sb("a1_swmo", [128, 128], F32)
    sinkpack = b.sb("a1_sinkpack", [128, 64], F32)
    spr = b.sb("a1_spr", [1, 64], F32)
    tsw = b.sb("a1_tsw", [128, 640], F32)
    ptile = b.sb("a1_ptile", [128, 640 + 256], BF16)
    pts = b.sb("a1_pts", [128, 7 * 128], BF16)
    zcol = b.sb("a1_zcol", [128, 8], F32)

    ps_sw = b.ps("a1_pssw", [128, 640], F32)
    ps_c = b.ps("a1_psc", [128, 256], F32)
    ps_o = b.ps("a1_pso", [128, 128], F32)
    ps_pt = b.ps("a1_pspt", [128, 7 * 128], BF16)
    ps_sk = b.ps("a1_pssk", [128, 64], F32)

    b.dma(nc.sync, cmask[:, :], AP(io["c_masks"], 0, [[256, 128], [128 * 256, 8], [1, 256]]))
    b.dma(nc.sync, swmr[:, :], io["swm_row"][:, :])
    b.dma(nc.sync, swmd[:, :], io["sw_mask_diag"][:, :])
    b.dma(nc.sync, swmo[:, :], io["sw_mask_old"][:, :])
    ld_m = b.cnt["ld"]

    regs = [b.scopes[-1].enter_context(nc.gpsimd.register(f"a1r{i}")) for i in range(4)]
    b.wait(nc.gpsimd, "ld", ld_small)
    for i in range(4):
        nc.gpsimd.reg_load(regs[i], a2o[0:1, i : i + 1])
    b.wait(nc.gpsimd, "cc", cc2)
    dst = [kswp[:, 0:1024], vswp[:, 0:1024], kswp[:, 1024:2048], vswp[:, 1024:2048]]
    for i in range(4):
        b.dma(nc.gpsimd, dst[i], AP(ag_out, regs[i], [[1024, 128], [1, 1024]]), ctr="lg")
    for r2 in range(4):
        b.dma(nc.sync, cka[:, r2 * 64 : (r2 + 1) * 64],
              AP(ag_out, r2 * 1056 * 256 + 262144, [[64, 128], [1, 64]]), ctr="ha")
    lg_kv = b.cnt["lg"]
    wc_ck = b.cnt["ha"]

    b.wait(nc.tensor, "ha", wc_ck)
    for tt in range(2):
        b.inc(nc.tensor.transpose(ps_pt[:, tt * 128 : (tt + 1) * 128],
                                  cka[:, tt * 128 : (tt + 1) * 128], identb[:, :]), "te")
    te_cktm = b.cnt["te"]
    b.wait(nc.vector, "te", te_cktm)
    b.inc(nc.vector.tensor_copy(cktm[:, :], AP(ps_pt, 0, [[896, 128], [1, 256]])), "ve")
    ve_cktm = b.cnt["ve"]

    b.wait(nc.vector, "ld", ld_small)
    b.inc(nc.vector.tensor_scalar_max(spr[:, 48:64], sinkr[:, :], 0.0), "ve")
    b.wait(nc.vector, "ve")
    b.inc(nc.vector.tensor_tensor(spr[:, 16:32], sinkr[:, :], spr[:, 48:64], ALU.subtract), "ve")
    b.inc(nc.vector.tensor_scalar_mul(spr[:, 0:16], spr[:, 48:64], -1.0), "ve")
    b.wait(nc.scalar, "ve")
    b.inc(nc.scalar.activation(spr[:, 16:32], spr[:, 16:32], AF.Exp), "se")
    b.wait(nc.vector, "ve")
    b.inc(nc.vector.tensor_scalar_mul(spr[:, 32:48], sinkr[:, :], -1.0), "ve")
    b.wait(nc.vector, "ve")
    b.inc(nc.vector.tensor_copy(spr[:, 48:64], sinkr[:, :]), "ve")
    ve_spr = b.cnt["ve"]
    b.wait(nc.tensor, "ve", ve_spr)
    b.wait(nc.tensor, "se")
    b.inc(nc.tensor.matmul(ps_sk[:, :], onesr[:, :], spr[:, :], start=True, stop=True), "te")
    b.wait(nc.scalar, "te")
    b.inc(nc.scalar.activation(sinkpack[:, :], ps_sk[:, :], AF.Copy), "se")
    se_sink = b.cnt["se"]

    b.wait(nc.tensor, "lg", lg_kv)
    b.wait(nc.tensor, "ld", ld_m)
    b.wait(nc.tensor, "se", se_sink)
    b.wait(nc.tensor, "ve", ve_cktm)
    for h in range(16):
        for qt in range(8):
            qtile = q_sb[:, h * 1024 + qt * 128 : h * 1024 + (qt + 1) * 128]
            w0 = 512 + 128 * qt
            b.wait(nc.tensor, "se")
            b.inc(nc.tensor.matmul(ps_sw[:, 0:512], onesr[:, :],
                                   swmr[0:1, qt * 640 : qt * 640 + 512],
                                   start=True, stop=False, skip_group_check=True), "te")
            b.inc(nc.tensor.matmul(ps_sw[:, 512:640], onesr[:, :],
                                   swmr[0:1, qt * 640 + 512 : (qt + 1) * 640],
                                   start=True, stop=False, skip_group_check=True), "te")
            b.inc(nc.tensor.matmul(ps_sw[:, 0:512], qtile,
                                   AP(kswp, w0, [[2048, 128], [1, 512]]),
                                   start=False, stop=True, skip_group_check=True), "te")
            b.inc(nc.tensor.matmul(ps_sw[:, 512:640], qtile,
                                   AP(kswp, w0 + 512, [[2048, 128], [1, 128]]),
                                   start=False, stop=True, skip_group_check=True), "te")
            te_sws = b.cnt["te"]
            b.wait(nc.scalar, "te", te_sws)
            b.wait(nc.scalar, "ve")
            b.inc(nc.scalar.activation(tsw[:, :], ps_sw[:, :], AF.Tanh,
                                       scale=SC / 50.0), "se")
            te_tanh = b.cnt["se"]
            b.wait(nc.vector, "se", te_tanh)
            b.inc(nc.vector.tensor_tensor(tsw[:, 0:128], tsw[:, 0:128], swmo[:, :], ALU.add), "ve")
            b.inc(nc.vector.tensor_tensor(tsw[:, 512:640], tsw[:, 512:640], swmd[:, :], ALU.add), "ve")
            ve_msk = b.cnt["ve"]
            b.wait(nc.scalar, "ve", ve_msk)
            b.inc(nc.scalar.activation(ptile[:, 0:640], tsw[:, :], AF.Exp, scale=50.0,
                                       bias=sinkpack[:, h : h + 1],
                                       accum_out=zcol[:, 0:1]), "se")
            se_swexp = b.cnt["se"]
            b.wait(nc.tensor, "ve")
            b.inc(nc.tensor.matmul(ps_c[:, :], qtile, cka[:, :],
                                   start=True, stop=True, skip_group_check=True), "te")
            te_cs = b.cnt["te"]
            b.wait(nc.vector, "te", te_cs)
            b.inc(nc.vector.tensor_tensor(ps_c[:, :], ps_c[:, :],
                                          cmask[:, qt * 256 : (qt + 1) * 256], ALU.add), "ve")
            b.wait(nc.vector, "ve")
            b.inc(nc.vector.tensor_reduce(zcol[:, 2:3], ps_c[:, :], AX.X, ALU.max,
                                          negate=True), "ve")
            b.wait(nc.vector, "ve")
            b.inc(nc.vector.tensor_scalar(zcol[:, 3:4], zcol[:, 2:3], SC,
                                          sinkpack[:, 32 + h : 33 + h], ALU.mult, ALU.min), "ve")
            ve_cbias = b.cnt["ve"]
            b.wait(nc.scalar, "ve", ve_cbias)
            b.inc(nc.scalar.activation(ptile[:, 640:896], ps_c[:, :], AF.Exp, scale=SC,
                                       bias=zcol[:, 3:4], accum_out=zcol[:, 4:5]), "se")
            se_cexp = b.cnt["se"]
            b.wait(nc.scalar, "se", se_cexp)
            b.inc(nc.scalar.activation(zcol[:, 5:6], zcol[:, 3:4], AF.Exp,
                                       bias=sinkpack[:, 48 + h : 49 + h]), "se")
            se_csink = b.cnt["se"]
            b.wait(nc.vector, "se", se_csink)
            b.inc(nc.vector.tensor_tensor(zcol[:, 4:5], zcol[:, 4:5], zcol[:, 5:6], ALU.add), "ve")
            b.inc(nc.vector.tensor_scalar(zcol[:, 1:2], zcol[:, 0:1],
                                          sinkpack[:, 16 + h : 17 + h], None, ALU.add), "ve")
            b.wait(nc.vector, "ve")
            b.inc(nc.vector.reciprocal(zcol[:, 6:7], zcol[:, 1:2]), "ve")
            b.inc(nc.vector.reciprocal(zcol[:, 7:8], zcol[:, 4:5]), "ve")
            b.wait(nc.vector, "ve")
            b.inc(nc.vector.tensor_scalar(ptile[:, 0:640], ptile[:, 0:640],
                                          zcol[:, 6:7], None, ALU.mult), "ve")
            b.inc(nc.vector.tensor_scalar(ptile[:, 640:896], ptile[:, 640:896],
                                          zcol[:, 7:8], None, ALU.mult), "ve")
            ve_pn = b.cnt["ve"]
            b.wait(nc.tensor, "ve", ve_pn)
            for tt in range(7):
                b.inc(nc.tensor.transpose(ps_pt[:, tt * 128 : (tt + 1) * 128],
                                          ptile[:, tt * 128 : (tt + 1) * 128], identb[:, :]),
                      "te")
            te_pt = b.cnt["te"]
            b.wait(nc.vector, "te", te_pt)
            b.inc(nc.vector.tensor_copy(pts[:, :], AP(ps_pt, 0, [[896, 128], [1, 896]])), "ve")
            ve_pts = b.cnt["ve"]
            b.wait(nc.tensor, "ve", ve_pts)
            for k in range(5):
                vt = AP(vswp, (4 + qt + k) * 128, [[2048, 128], [1, 128]])
                b.inc(nc.tensor.matmul(ps_o[:, :], vt, pts[:, k * 128 : (k + 1) * 128],
                                       start=(k == 0), stop=False, skip_group_check=True), "te")
            for tt in range(2):
                b.inc(nc.tensor.matmul(ps_o[:, :], cktm[:, tt * 128 : (tt + 1) * 128],
                                       pts[:, (5 + tt) * 128 : (6 + tt) * 128],
                                       start=False, stop=(tt == 1), skip_group_check=True), "te")
            te_av = b.cnt["te"]
            b.wait(nc.vector, "te", te_av)
            b.inc(nc.vector.tensor_copy(
                xln[:, h * 1024 + qt * 128 : h * 1024 + (qt + 1) * 128], ps_o[:, :]), "ve")
    ao_done = b.cnt["ve"]
    b.pop()
    b.fence()
    b.pop()

    b.push()
    b.fence()
    inter = b.sb("a1_inter", [128, 64 * 512], BF16)
    wbufc = b.sb("a1_wbufc", [128, 2 * 512], BF16)
    wbufo = b.sb("a1_wbufo", [128, 2 * 4096], BF16)
    ps_i = b.ps("a1_psi", [128, 2 * 512], F32)
    ps_r = b.ps("a1_psr", [128, 2 * 512], F32)

    ldc = {}
    ldo = {}
    for th in range(2):
        for ic in range(64):
            g = ic // 16
            par = "wa" if ic % 2 == 0 else "wb"
            key = th * 64 + ic
            if th == 0 and ic < 2:
                pass
            if key >= 2:
                b.wait(nc.gpsimd, "te", ldc[key - 2][1])
            v = b.dma(nc.gpsimd,
                      AP(wbufc, (ic % 2) * 512, [[2 * 512, 128], [128, 4], [1, 128]]),
                      AP(io["Wg"], (g * 512) * 2048 + (ic % 16) * 128,
                         [[2048, 128], [128 * 2048, 4], [1, 128]]),
                      ctr=par)
            ldc[key] = [(par, v), None]
            b.wait(nc.tensor, par, v)
            pi = ps_i[:, (ic % 2) * 512 : ((ic % 2) + 1) * 512]
            if key >= 2:
                b.wait(nc.tensor, "se", ldc[key - 2][2] if len(ldc[key - 2]) > 2 else 0)
            for c in range(4):
                aos = xln[:, (g * 4 + c) * TLOC + th * 512 : (g * 4 + c) * TLOC + (th + 1) * 512]
                b.inc(nc.tensor.matmul(pi, wbufc[:, (ic % 2) * 512 + c * 128 :
                                                 (ic % 2) * 512 + (c + 1) * 128],
                                       aos, start=(c == 0), stop=(c == 3)), "te")
            ldc[key][1] = b.cnt["te"]
            b.wait(nc.scalar, "te", ldc[key][1])
            se_i = b.inc(nc.scalar.activation(inter[:, ic * 512 : (ic + 1) * 512], pi,
                                              AF.Copy), "se")
            ldc[key].append(se_i)
        se_inter = b.cnt["se"]
        b.wait(nc.tensor, "se", se_inter)
        for oc in range(16):
            for half in range(2):
                key = (th * 16 + oc) * 2 + half
                par = "wc" if key % 2 == 0 else "wd"
                if th == 0 and oc == 0:
                    prevk = None
                if key >= 2 and (key - 2) in ldo:
                    b.wait(nc.gpsimd, "te", ldo[key - 2][1])
                v = b.dma(nc.gpsimd,
                          AP(wbufo, (key % 2) * 4096, [[2 * 4096, 128], [128, 32], [1, 128]]),
                          AP(io["Wo"], half * 4096 * 2048 + oc * 128,
                             [[2048, 128], [128 * 2048, 32], [1, 128]]),
                          ctr=par)
                ldo[key] = [(par, v), None]
            pr = ps_r[:, (oc % 2) * 512 : ((oc % 2) + 1) * 512]
            if oc >= 2 or th == 1:
                b.wait(nc.tensor, "ve")
            for half in range(2):
                key = (th * 16 + oc) * 2 + half
                b.wait(nc.tensor, ldo[key][0][0], ldo[key][0][1])
                for c in range(32):
                    iis = inter[:, (half * 32 + c) * 512 : (half * 32 + c + 1) * 512]
                    b.inc(nc.tensor.matmul(pr, wbufo[:, (key % 2) * 4096 + c * 128 :
                                                     (key % 2) * 4096 + (c + 1) * 128],
                                           iis, start=(half == 0 and c == 0),
                                           stop=(half == 1 and c == 31)), "te")
                ldo[key][1] = b.cnt["te"]
            b.wait(nc.vector, "te")
            xs_out = x[:, oc * TLOC + th * 512 : oc * TLOC + (th + 1) * 512]
            b.inc(nc.vector.tensor_tensor(xs_out, xs_out, pr, ALU.add), "ve")
    b.pop()
    b.pop()



def emit_store(b, io_out):
    nc = b.nc
    x = b.tn["x_res"]
    b.fence()
    for c in range(NCH):
        b.dma(nc.sync, AP(io_out, c * 128 * TLOC, [[TLOC, 128], [1, TLOC]]),
              x[:, c * TLOC : (c + 1) * TLOC], ctr="st")
    b.wait(nc.sync, "st")



PHASE_ORDER = ["ffn0", "attn0", "ffn0", "ffn1", "attn1", "ffn1"]
FFN1_NIC = 28
FFN1_SECT = 9
OL_EXPERTS = list(range(8))


def build_program(phases):
    nc = bass.Bass("TRN2", target_bir_lowering=False)
    io = {}
    for name, shape, dt in ALL_INPUTS:
        io[name] = nc.dram_tensor(name, list(shape), dt, kind="ExternalInput")
    y = nc.dram_tensor("y", [2048, 1024], F32, kind="ExternalOutput")
    with contextlib.ExitStack() as stack:
        b = B(nc, stack)
        with nc.Block() as blk:
            @blk.gpsimd
            def _(g):
                emit_load(b, io)
                for ph in phases:
                    emit_ln(b)
                    if ph == "ffn0":
                        emit_ffn0(b, io)
                    elif ph == "attn0":
                        emit_attn0(b, io)
                    elif ph == "ffn1":
                        emit_ffn1(b, io)
                    elif ph == "attn1":
                        emit_attn1(b, io)
                    else:
                        raise ValueError(ph)
                emit_store(b, y)
    return nc


def run_phases(in_maps, phase_groups, trace=False):
    cur = [m["x"] for m in in_maps]
    res = None
    for grp in phase_groups:
        nc = build_program(grp)
        maps = []
        for c in range(NCORES):
            m = dict(in_maps[c])
            m["x"] = cur[c]
            maps.append(m)
        res = run_bass_kernel_spmd(nc, maps, core_ids=list(range(NCORES)), trace=trace)
        cur = [np.asarray(res.results[c]["y"]) for c in range(NCORES)]
    return cur, res


def assemble(cur):
    out = np.zeros((2, 4096, 2048), np.float32)
    for c in range(NCORES):
        bidx = c // 4
        t0 = 1024 * (c % 4)
        out[bidx, t0 : t0 + TLOC, :] = cur[c].T
    return out


def kernel(**inputs):
    in_maps = prep_host(inputs)
    cur, _ = run_phases(in_maps, [PHASE_ORDER])
    return assemble(cur)


# version 02
# speedup vs baseline: 1.0087x; 1.0087x over previous
import contextlib
import numpy as np
import ml_dtypes
import concourse.bass as bass
import concourse.mybir as mybir
from concourse.bass_utils import run_bass_kernel_spmd

F32 = mybir.dt.float32
BF16 = mybir.dt.bfloat16
AF = mybir.ActivationFunctionType
ALU = mybir.AluOpType
AX = mybir.AxisListType
AP = bass.AP

NCORES = 8
TLOC = 1024
D = 2048
NCH = D // 128
NEG = -1e30
BF = ml_dtypes.bfloat16


def _cc(a):
    return np.ascontiguousarray(a)



def prep_host(inputs):
    g = {}
    def colblocks(W, bs=128):
        K_, N_ = W.shape
        nk, nb = K_ // 128, N_ // bs
        return _cc(W.reshape(nk, 128, nb, bs).transpose(2, 1, 0, 3).reshape(nb, 128, nk * bs))

    def rowblocks(W, rs):
        K_, N_ = W.shape
        ns, nk = K_ // rs, rs // 128
        return _cc(W.reshape(ns, nk, 128, N_).transpose(0, 2, 1, 3).reshape(ns, 128, nk * N_))

    g["w_attn"] = colblocks(inputs["L0_attn_w_attn"])
    g["b_attn"] = _cc(inputs["L0_attn_b_attn"].reshape(48, 128).T)
    g["w_proj"] = colblocks(inputs["L0_attn_w_proj"])
    g["b_proj"] = _cc(inputs["L0_attn_b_proj"].reshape(16, 128).T)
    g["f0_bwT"] = colblocks(_cc(inputs["L0_ffn_base_w"].T))
    g["f0_bb"] = _cc(inputs["L0_ffn_base_b"].reshape(16, 128).T)
    g["f0_rwT"] = colblocks(_cc(inputs["L0_ffn_router_w"].T), bs=8)[0]
    g["f0_rb"] = _cc(inputs["L0_ffn_router_b"].reshape(8, 1))
    g["f0_Acat"] = colblocks(inputs["L0_ffn_lora_A"].transpose(1, 0, 2).reshape(2048, 128))[0]
    g["f0_Bcat"] = _cc(inputs["L0_ffn_lora_B"].reshape(128, 2048))
    g["Wq_c"] = colblocks(inputs["L1_attn_Wq_c"])
    g["Wq_up"] = colblocks(inputs["L1_attn_Wq_up"])
    g["Wk"] = colblocks(inputs["L1_attn_Wk"])[0]
    g["Wv"] = colblocks(inputs["L1_attn_Wv"])[0]
    g["W_comp"] = rowblocks(inputs["L1_attn_W_comp"], 2048)
    g["qnw"] = _cc(inputs["L1_attn_q_norm_w"].reshape(1, 128))
    g["knw"] = _cc(inputs["L1_attn_k_norm_w"].reshape(1, 128))
    g["sink"] = _cc(inputs["L1_attn_sink"].reshape(1, 16))
    Wgf = inputs["L1_attn_Wg"].reshape(4 * 512, 2048)
    wg = np.zeros((64, 128, 512), np.float32)
    for g_ in range(4):
        blk = colblocks(_cc(Wgf[g_ * 512 : (g_ + 1) * 512, :]))
        wg[g_ * 16 : (g_ + 1) * 16] = blk
    g["Wg"] = _cc(wg)
    Wo = inputs["L1_attn_Wo"]
    wo = np.zeros((32, 128, 4096), np.float32)
    for oc in range(16):
        for half in range(2):
            sl = _cc(Wo[half * 4096 : (half + 1) * 4096, oc * 128 : (oc + 1) * 128])
            wo[oc * 2 + half] = rowblocks(sl, 4096)[0]
    g["Wo"] = _cc(wo)
    g["wealth"] = _cc(inputs["L1_ffn_expert_wealth"].reshape(8, 1))
    g["conf_WT"] = colblocks(_cc(inputs["L1_ffn_conf_W"].T), bs=8)[0]
    g["conf_b"] = _cc(inputs["L1_ffn_conf_b"].reshape(8, 1))
    g["base_gate"] = colblocks(inputs["L1_ffn_base_gate"], bs=256)
    g["base_up"] = colblocks(inputs["L1_ffn_base_up"], bs=256)
    g["base_down"] = rowblocks(inputs["L1_ffn_base_down"], 256)
    g["gateAcat"] = colblocks(inputs["L1_ffn_gate_A"].transpose(1, 0, 2).reshape(2048, 512), bs=256)
    g["upAcat"] = colblocks(inputs["L1_ffn_up_A"].transpose(1, 0, 2).reshape(2048, 512), bs=256)
    def bpack(Bm):
        a = Bm.reshape(4, 2, 64, 28, 256).transpose(3, 1, 2, 0, 4).reshape(28, 2, 64, 1024)
        return _cc(a)
    g["gate_B"] = bpack(inputs["L1_ffn_gate_B"].reshape(4, 2, 64, 7168))
    g["up_B"] = bpack(inputs["L1_ffn_up_B"].reshape(4, 2, 64, 7168))
    dA = inputs["L1_ffn_down_A"]
    da = dA.reshape(8, 28, 2, 128, 64).transpose(1, 3, 0, 2, 4).reshape(28, 128, 8 * 128)
    g["down_A"] = _cc(da)
    g["down_B"] = _cc(inputs["L1_ffn_down_B"])

    g["ident_bf"] = np.eye(128, dtype=BF)
    g["ident_f32"] = np.eye(128, dtype=np.float32)
    g["ones_col"] = np.ones((128, 1), np.float32)
    g["ones_row"] = np.ones((1, 128), np.float32)
    g["ones_col_bf"] = np.ones((128, 1), BF)
    emat = np.zeros((8, 128), BF)
    for e in range(8):
        emat[e, 16 * e : 16 * e + 16] = 1
    g["emat"] = emat
    emat64 = np.zeros((8, 512), BF)
    for e in range(8):
        emat64[e, 64 * e : 64 * e + 64] = 1
    g["emat64"] = emat64
    a0m = np.zeros((128, 4, 512), np.float32)
    for i, Doff in enumerate([0, -128, -256, -384]):
        ss = np.arange(128)[:, None]
        qq = np.arange(512)[None, :]
        a0m[:, i, :] = np.where(qq + Doff - ss >= 0, 0.0, NEG)
    g["attn0_masks"] = _cc(a0m.reshape(128, 4 * 512))
    qq = np.arange(128)[:, None]
    ss = np.arange(128)[None, :]
    g["sw_mask_diag"] = _cc(np.where(ss <= qq, 0.0, -1e28).astype(np.float32))
    g["sw_mask_old"] = _cc(np.where(ss > qq, 0.0, -1e28).astype(np.float32))
    er = np.zeros((8, 8 * 128), np.float32)
    for e in range(8):
        er[e, e * 128 : (e + 1) * 128] = 1.0
    g["erow"] = er
    cv = np.zeros((128, 8), np.float32)
    cv[:, 0] = 1e-5
    cv[:, 1] = 1e-6
    cv[:, 2] = 1e-30
    g["cvals"] = cv

    rope = 64
    theta = 10000.0
    inv = 1.0 / theta ** (np.arange(0, rope, 2, dtype=np.float32) / rope)
    ang = np.arange(4096, dtype=np.float32)[:, None] * inv[None, :]
    cos_all = np.cos(ang).astype(np.float32)
    sin_all = np.sin(ang).astype(np.float32)

    in_maps = []
    for c in range(NCORES):
        bidx = c // 4
        t0 = 1024 * (c % 4)
        m = dict(g)
        x = inputs["L0_attn_x"][bidx, t0 : t0 + TLOC, :]
        m["x"] = _cc(np.asarray(x, np.float32).T)
        ct = cos_all[t0 : t0 + TLOC].T
        st_ = sin_all[t0 : t0 + TLOC].T
        m["rcos"] = _cc(np.concatenate([ct, ct], axis=0))
        m["rsin"] = _cc(np.concatenate([st_, st_], axis=0))
        cm = np.zeros((8, 128, 256), np.float32)
        for qt in range(8):
            q_global = t0 + qt * 128 + np.arange(128)
            blk_end = np.arange(256) * 16 + 15
            cm[qt] = np.where(blk_end[None, :] < q_global[:, None], 0.0, NEG)
        m["c_masks"] = _cc(cm.reshape(8 * 128, 256))
        r = c % 4
        oth = [(r + 1 + j) % 4 for j in range(3)]
        bo = np.zeros((1, 8), np.int32)
        bo[0, 0] = (4096 * r) * 1024
        bo[0, 1] = (4096 * r + 2048) * 1024
        for j in range(3):
            bo[0, 2 + 2 * j] = (4096 * oth[j]) * 1024
            bo[0, 3 + 2 * j] = (4096 * oth[j] + 2048) * 1024
        m["broff"] = bo
        mc = np.zeros((128, 24), np.float32)
        for j in range(3):
            val = 0.0 if oth[j] < r else NEG
            mc[:, j * 8 : (j + 1) * 8] = val
        m["mcol"] = _cc(mc)
        prev = (r - 1) % 4
        a2 = np.zeros((1, 4), np.int32)
        a2[0, 0] = prev * 1056 * 256
        a2[0, 1] = prev * 1056 * 256 + 131072
        a2[0, 2] = r * 1056 * 256
        a2[0, 3] = r * 1056 * 256 + 131072
        m["a2off"] = a2
        swm = np.zeros((8, 640), np.float32)
        if r == 0:
            for qt in range(8):
                swm[qt, 0 : 128 * (4 - qt) if qt < 4 else 0] = -1e6
        m["swm_row"] = _cc(swm.reshape(1, 8 * 640))
        in_maps.append(m)
    return in_maps


ALL_INPUTS = [
    ("x", (2048, 1024), F32),
    ("w_attn", (48, 128, 2048), F32), ("b_attn", (128, 48), F32),
    ("w_proj", (16, 128, 2048), F32), ("b_proj", (128, 16), F32),
    ("f0_bwT", (16, 128, 2048), F32), ("f0_bb", (128, 16), F32),
    ("f0_rwT", (128, 128), F32), ("f0_rb", (8, 1), F32),
    ("f0_Acat", (128, 2048), F32), ("f0_Bcat", (128, 2048), F32),
    ("Wq_c", (4, 128, 2048), F32), ("Wq_up", (16, 128, 512), F32),
    ("Wk", (128, 2048), F32), ("Wv", (128, 2048), F32),
    ("W_comp", (16, 128, 2048), F32),
    ("qnw", (1, 128), F32), ("knw", (1, 128), F32), ("sink", (1, 16), F32),
    ("Wg", (64, 128, 512), F32), ("Wo", (32, 128, 4096), F32),
    ("wealth", (8, 1), F32), ("conf_WT", (128, 128), F32), ("conf_b", (8, 1), F32),
    ("base_gate", (28, 128, 4096), F32), ("base_up", (28, 128, 4096), F32),
    ("base_down", (28, 128, 4096), F32),
    ("gateAcat", (2, 128, 4096), F32), ("upAcat", (2, 128, 4096), F32),
    ("gate_B", (28, 2, 64, 1024), F32), ("up_B", (28, 2, 64, 1024), F32),
    ("down_A", (28, 128, 1024), F32), ("down_B", (8, 64, 2048), F32),
    ("ident_bf", (128, 128), BF16), ("ident_f32", (128, 128), F32),
    ("ones_col", (128, 1), F32), ("ones_row", (1, 128), F32),
    ("ones_col_bf", (128, 1), BF16),
    ("emat", (8, 128), BF16), ("emat64", (8, 512), BF16),
    ("attn0_masks", (128, 4 * 512), F32),
    ("sw_mask_diag", (128, 128), F32), ("sw_mask_old", (128, 128), F32),
    ("rcos", (64, 1024), F32), ("rsin", (64, 1024), F32),
    ("c_masks", (8 * 128, 256), F32),
    ("cvals", (128, 8), F32),
    ("broff", (1, 8), mybir.dt.int32),
    ("erow", (8, 8 * 128), F32),
    ("a2off", (1, 4), mybir.dt.int32),
    ("swm_row", (1, 8 * 640), F32),
    ("mcol", (128, 24), F32),
]



class B:
    def __init__(self, nc, stack):
        self.nc = nc
        self.stack = stack
        self.scopes = [stack]
        self.sem = {}
        self.cnt = {}
        for name in ["ld", "lg", "st", "te", "ve", "se", "po", "cc",
                     "wa", "wb", "wc", "wd", "sa", "sb", "wx", "wy", "wz",
                     "ha", "hb", "hc", "hd"]:
            self.sem[name] = stack.enter_context(nc.semaphore(name))
            self.cnt[name] = 0
        self.tn = {}
        self.uid = 0

    def push(self):
        s = contextlib.ExitStack()
        self.scopes.append(s)
        return s

    def pop(self):
        s = self.scopes.pop()
        s.close()

    def name(self, base):
        self.uid += 1
        return f"{base}_{self.uid}"

    def inc(self, inst, ctr, n=1):
        inst.then_inc(self.sem[ctr], n)
        self.cnt[ctr] += n
        return self.cnt[ctr]

    def dma(self, eng, out, in_, ctr="ld"):
        if eng is self.nc.gpsimd and ctr == "ld":
            ctr = "lg"
        return self.inc(eng.dma_start(out=out, in_=in_), ctr, 16)

    def wait(self, eng, ctr, val=None):
        v = self.cnt[ctr] if val is None else val
        if v > 0:
            eng.wait_ge(self.sem[ctr], v)

    def sb(self, base, shape, dtype, persistent=False):
        s = self.stack if persistent else self.scopes[-1]
        t = s.enter_context(self.nc.sbuf_tensor(self.name(base), shape, dtype))
        if persistent:
            self.tn[base] = t
        return t

    def ps(self, base, shape, dtype):
        return self.scopes[-1].enter_context(
            self.nc.psum_tensor(self.name(base), shape, dtype))

    def fence(self):
        nc = self.nc
        for eng in [nc.tensor, nc.vector, nc.scalar, nc.gpsimd, nc.sync]:
            for ctr in ["te", "ve", "se", "po", "ld", "lg", "st", "cc",
                        "wa", "wb", "wc", "wd", "sa", "sb", "wx", "wy", "wz",
                     "ha", "hb", "hc", "hd"]:
                self.wait(eng, ctr)


def ap2(t, col, pat):
    rowlen = int(np.prod(t.shape[1:])) if hasattr(t, "shape") else None
    return AP(t, col, [[rowlen, t.shape[0]]] + pat)



def emit_load(b, io):
    nc = b.nc
    x = b.sb("x_res", [128, NCH * TLOC], F32, persistent=True)
    b.sb("x_ln", [128, NCH * TLOC], BF16, persistent=True)
    for c in range(NCH):
        b.dma(nc.sync, x[:, c * TLOC : (c + 1) * TLOC],
              AP(io["x"], c * 128 * TLOC, [[TLOC, 128], [1, TLOC]]))
    for nm, shape, dt in [("ident_bf", [128, 128], BF16), ("ident_f32", [128, 128], F32),
                          ("ones_col", [128, 1], F32), ("ones_row", [1, 128], F32),
                          ("ones_col_bf", [128, 1], BF16),
                          ("emat", [8, 128], BF16),
                          ("cvals", [128, 8], F32)]:
        t = b.sb(nm, shape, dt, persistent=True)
        b.dma(nc.sync, t[:, :], io[nm][:, :])
    b.load_done = b.cnt["ld"]



def emit_ln(b):
    nc = b.nc
    x = b.tn["x_res"]
    xln = b.tn["x_ln"]
    b.push()
    ps_sx = b.ps("ps_sx", [1, 1024], F32)
    ps_sxx = b.ps("ps_sxx", [1, 1024], F32)
    ps_bc = b.ps("ps_bc", [128, 2048], F32)
    sq = b.sb("sq", [128, 2 * 1024], F32)
    sm = b.sb("lnsm", [1, 8 * 1024], F32)
    mean = sm[:, 0:1024]; ex2 = sm[:, 1024:2048]; m2 = sm[:, 2048:3072]
    var = sm[:, 3072:4096]; std = sm[:, 4096:5120]; rstd = sm[:, 5120:6144]
    bmr = sm[:, 6144:7168]
    ones = b.tn["ones_col"]
    onesr = b.tn["ones_row"]

    b.fence()
    sq_guard = {}
    for c in range(NCH):
        buf = sq[:, (c % 2) * 1024 : ((c % 2) + 1) * 1024]
        if c >= 2:
            b.wait(nc.scalar, "te", sq_guard[c - 2])
        b.inc(nc.scalar.activation(buf, x[:, c * TLOC : (c + 1) * TLOC], AF.Square), "se")
        b.wait(nc.tensor, "se")
        for hf in range(2):
            xs = x[:, c * TLOC + hf * 512 : c * TLOC + (hf + 1) * 512]
            b.inc(nc.tensor.matmul(ps_sx[:, hf * 512 : (hf + 1) * 512], ones[:, :], xs,
                                   start=(c == 0), stop=(c == NCH - 1)), "te")
            qs = buf[:, hf * 512 : (hf + 1) * 512]
            b.inc(nc.tensor.matmul(ps_sxx[:, hf * 512 : (hf + 1) * 512], ones[:, :], qs,
                                   start=(c == 0), stop=(c == NCH - 1)), "te")
        sq_guard[c] = b.cnt["te"]

    b.wait(nc.vector, "te")
    b.inc(nc.vector.tensor_scalar_mul(mean, ps_sx[:, :], 1.0 / D), "ve")
    b.inc(nc.vector.tensor_scalar_mul(ex2, ps_sxx[:, :], 1.0 / D), "ve")
    b.wait(nc.vector, "ve")
    b.inc(nc.vector.tensor_tensor(m2, mean, mean, ALU.mult), "ve")
    b.wait(nc.vector, "ve")
    b.inc(nc.vector.tensor_tensor(var, ex2, m2, ALU.subtract), "ve")
    b.wait(nc.scalar, "ve")
    b.inc(nc.scalar.activation(std, var, AF.Sqrt, bias=b.tn["cvals"][0:1, 0:1]), "se")
    b.wait(nc.vector, "se")
    b.inc(nc.vector.reciprocal(rstd, std), "ve")
    b.wait(nc.vector, "ve")
    b.inc(nc.vector.scalar_tensor_tensor(bmr, mean, -1.0, rstd, ALU.mult, ALU.mult), "ve")

    b.wait(nc.tensor, "ve")
    for hf in range(2):
        b.inc(nc.tensor.matmul(ps_bc[:, hf * 512 : (hf + 1) * 512], onesr[:, :],
                               rstd[:, hf * 512 : (hf + 1) * 512], start=True, stop=True), "te")
        b.inc(nc.tensor.matmul(ps_bc[:, 1024 + hf * 512 : 1024 + (hf + 1) * 512], onesr[:, :],
                               bmr[:, hf * 512 : (hf + 1) * 512], start=True, stop=True), "te")
    te_bc = b.cnt["te"]

    b.wait(nc.vector, "te", te_bc)
    for c in range(NCH):
        buf = sq[:, (c % 2) * 1024 : ((c % 2) + 1) * 1024]
        b.wait(nc.vector, "ve")
        b.inc(nc.vector.tensor_tensor(buf, x[:, c * TLOC : (c + 1) * TLOC],
                                      ps_bc[:, 0:1024], ALU.mult), "ve")
        b.wait(nc.vector, "ve")
        b.inc(nc.vector.tensor_tensor(xln[:, c * TLOC : (c + 1) * TLOC], buf,
                                      ps_bc[:, 1024:2048], ALU.add), "ve")
    b.ln_done = b.cnt["ve"]
    b.pop()



def emit_ffn0(b, io):
    nc = b.nc
    x = b.tn["x_res"]
    xln = b.tn["x_ln"]
    ln_done = b.ln_done
    b.push()
    b.fence()

    rw = b.sb("f0_rw", [128, NCH * 8], BF16)
    acat = b.sb("f0_acat", [128, NCH * 128], BF16)
    bcat = b.sb("f0_bcat", [128, 2048], BF16)
    bb = b.sb("f0_bb", [128, 16], F32)
    bbh = b.sb("f0_bbh", [128, 16], F32)
    rb = b.sb("f0_rb", [8, 1], F32)
    b.dma(nc.gpsimd, rw[:, :], io["f0_rwT"][:, :])
    b.dma(nc.gpsimd, acat[:, :], io["f0_Acat"][:, :])
    b.dma(nc.gpsimd, bcat[:, :], io["f0_Bcat"][:, :])
    b.dma(nc.sync, bb[:, :], io["f0_bb"][:, :])
    b.dma(nc.sync, rb[:, :], io["f0_rb"][:, :])
    ld_small = b.cnt["ld"]
    lg_small = b.cnt["lg"]

    ps_r = b.ps("f0_psr", [8, 1024], F32)
    ps_mid = b.ps("f0_psmid", [128, 1024], F32)
    ps_small = b.ps("f0_pssmall", [128, 512], F32)
    ps_res = b.ps("f0_psres", [128, 2 * 512], F32)

    logits = b.sb("f0_logits", [8, 1024], F32)
    mid_sb = b.sb("f0_mid", [128, 1024], BF16)
    ltm = b.sb("f0_ltm", [128, 64], F32)
    sm = b.sb("f0_sm", [128, 6 * 64], F32)
    m1 = sm[:, 0:8]; m2 = sm[:, 8:16]; dd = sm[:, 16:24]
    w1 = sm[:, 24:32]; w2 = sm[:, 32:40]
    mask1 = sm[:, 64:128]; l2 = sm[:, 128:192]; mask2 = sm[:, 192:256]
    g1 = sm[:, 256:320]; gate_tm = sm[:, 320:384]
    gfm_sb = b.sb("f0_gfm", [8, 1024], BF16)
    gmid = b.sb("f0_gmid", [128, 1024], BF16)
    wbuf = b.sb("f0_wbuf", [128, 2 * 2048], BF16)
    tmp = b.sb("f0_tmp", [128, 2 * 512], F32)

    identf = b.tn["ident_f32"]
    emat = b.tn["emat"]

    b.wait(nc.vector, "ld", ld_small)
    b.inc(nc.vector.tensor_scalar_mul(bbh[:, :], bb[:, :], 0.5), "ve")
    bbh_done = b.cnt["ve"]

    b.wait(nc.tensor, "ld", ld_small)
    b.wait(nc.tensor, "lg", lg_small)
    b.wait(nc.tensor, "ve", ln_done)
    for c in range(NCH):
        for hf in range(2):
            xs = xln[:, c * TLOC + hf * 512 : c * TLOC + (hf + 1) * 512]
            b.inc(nc.tensor.matmul(ps_r[:, hf * 512 : (hf + 1) * 512],
                                   rw[:, c * 8 : (c + 1) * 8], xs,
                                   start=(c == 0), stop=(c == NCH - 1)), "te")
    te_router = b.cnt["te"]
    for c in range(NCH):
        for hf in range(2):
            xs = xln[:, c * TLOC + hf * 512 : c * TLOC + (hf + 1) * 512]
            b.inc(nc.tensor.matmul(ps_mid[:, hf * 512 : (hf + 1) * 512],
                                   acat[:, c * 128 : (c + 1) * 128], xs,
                                   start=(c == 0), stop=(c == NCH - 1)), "te")
    te_mid = b.cnt["te"]

    b.wait(nc.scalar, "te", te_router)
    b.inc(nc.scalar.activation(logits[:, :], ps_r[:, :], AF.Identity, bias=rb[:, 0:1]), "se")
    se_logits = b.cnt["se"]
    b.wait(nc.scalar, "te", te_mid)
    b.inc(nc.scalar.activation(mid_sb[:, :], ps_mid[:, :], AF.Copy), "se")
    se_mid = b.cnt["se"]

    b.wait(nc.tensor, "se", se_logits)
    for tt in range(8):
        b.inc(nc.tensor.transpose(ps_small[:, 8 * tt : 8 * tt + 8],
                                  logits[:, tt * 128 : (tt + 1) * 128], identf[0:8, 0:8]), "te")
    te_tm = b.cnt["te"]
    b.wait(nc.vector, "te", te_tm)
    b.inc(nc.vector.tensor_copy(ltm[:, :], ps_small[:, 0:64]), "ve")
    b.wait(nc.vector, "ve")

    ltm_v = AP(ltm, 0, [[64, 128], [8, 8], [1, 8]])
    SMW = 6 * 64
    m1rep = AP(sm, 0, [[SMW, 128], [1, 8], [0, 8]])
    m2rep = AP(sm, 8, [[SMW, 128], [1, 8], [0, 8]])
    w1rep = AP(sm, 24, [[SMW, 128], [1, 8], [0, 8]])
    w2rep = AP(sm, 32, [[SMW, 128], [1, 8], [0, 8]])
    l2_v = AP(sm, 128, [[SMW, 128], [8, 8], [1, 8]])

    b.inc(nc.vector.tensor_reduce(m1, ltm_v, AX.X, ALU.max), "ve")
    b.wait(nc.vector, "ve")
    b.inc(nc.vector.tensor_tensor(mask1, ltm[:, :], m1rep, ALU.is_equal), "ve")
    b.wait(nc.vector, "ve")
    b.inc(nc.vector.scalar_tensor_tensor(l2, mask1, NEG, ltm[:, :], ALU.mult, ALU.add), "ve")
    b.wait(nc.vector, "ve")
    b.inc(nc.vector.tensor_reduce(m2, l2_v, AX.X, ALU.max), "ve")
    b.wait(nc.vector, "ve")
    b.inc(nc.vector.tensor_tensor(mask2, l2, m2rep, ALU.is_equal), "ve")
    b.inc(nc.vector.tensor_tensor(dd, m2, m1, ALU.subtract), "ve")
    b.wait(nc.scalar, "ve")
    b.inc(nc.scalar.activation(w1, dd, AF.Sigmoid, scale=-1.0), "se")
    b.wait(nc.vector, "se")
    b.inc(nc.vector.tensor_scalar(w2, w1, -2.0, 2.0, ALU.mult, ALU.add), "ve")
    b.wait(nc.vector, "ve")
    b.inc(nc.vector.tensor_scalar_mul(w1, w1, 2.0), "ve")
    b.wait(nc.vector, "ve")
    b.inc(nc.vector.tensor_tensor(g1, mask1, w1rep, ALU.mult), "ve")
    b.wait(nc.vector, "ve")
    b.inc(nc.vector.tensor_tensor(gate_tm, mask2, w2rep, ALU.mult), "ve")
    b.wait(nc.vector, "ve")
    b.inc(nc.vector.tensor_tensor(gate_tm, gate_tm, g1, ALU.add), "ve")
    ve_gate = b.cnt["ve"]

    b.wait(nc.tensor, "ve", ve_gate)
    for tt in range(8):
        b.inc(nc.tensor.transpose(ps_r[:, tt * 128 : (tt + 1) * 128],
                                  gate_tm[:, 8 * tt : 8 * tt + 8], identf[:, :]), "te")
    te_gfm = b.cnt["te"]
    b.wait(nc.scalar, "te", te_gfm)
    b.inc(nc.scalar.activation(gfm_sb[:, :], ps_r[:, :], AF.Copy), "se")
    se_gfm = b.cnt["se"]
    b.wait(nc.tensor, "se", se_gfm)
    b.wait(nc.tensor, "se", se_mid)
    for hf in range(2):
        b.inc(nc.tensor.matmul(ps_mid[:, hf * 512 : (hf + 1) * 512], emat[:, :],
                               gfm_sb[:, hf * 512 : (hf + 1) * 512], start=True, stop=True), "te")
    te_gexp = b.cnt["te"]
    b.wait(nc.vector, "te", te_gexp)
    b.inc(nc.vector.tensor_tensor(gmid[:, :], mid_sb[:, :], ps_mid[:, :], ALU.mult), "ve")
    ve_gmid = b.cnt["ve"]

    ld_w = {}
    w_guard = {}
    ep_guard_se = {}
    ep_guard_ve = {}

    def issue_wload(oc):
        par = "wa" if oc % 2 == 0 else "wb"
        if oc >= 2:
            b.wait(nc.gpsimd, "te", w_guard[oc - 2])
        ld_w[oc] = (par, b.dma(nc.gpsimd,
                         AP(wbuf, (oc % 2) * 2048, [[2 * 2048, 128], [1, 2048]]),
                         AP(io["f0_bwT"], oc * 128 * 2048, [[2048, 128], [1, 2048]]),
                         ctr=par))

    issue_wload(0)
    issue_wload(1)
    for oc in range(NCH):
        wb = wbuf[:, (oc % 2) * 2048 : ((oc % 2) + 1) * 2048]
        b.wait(nc.tensor, ld_w[oc][0], ld_w[oc][1])
        if oc == 0:
            b.wait(nc.tensor, "ve", ve_gmid)
            b.wait(nc.scalar, "ve", bbh_done)
        for hf in range(2):
            pr = ps_res[:, hf * 512 : (hf + 1) * 512]
            if oc >= 1:
                b.wait(nc.tensor, "se", ep_guard_se[(oc - 1, hf)])
            for c in range(NCH):
                xs = xln[:, c * TLOC + hf * 512 : c * TLOC + (hf + 1) * 512]
                b.inc(nc.tensor.matmul(pr, wb[:, c * 128 : (c + 1) * 128], xs,
                                       start=(c == 0), stop=False), "te")
            b.inc(nc.tensor.matmul(pr, bcat[:, oc * 128 : (oc + 1) * 128],
                                   gmid[:, hf * 512 : (hf + 1) * 512],
                                   start=False, stop=True), "te")
        w_guard[oc] = b.cnt["te"]
        if oc + 2 < NCH:
            issue_wload(oc + 2)
        for hf in range(2):
            pr = ps_res[:, hf * 512 : (hf + 1) * 512]
            tb = tmp[:, hf * 512 : (hf + 1) * 512]
            b.wait(nc.scalar, "te", w_guard[oc])
            if oc >= 1:
                b.wait(nc.scalar, "ve", ep_guard_ve[(oc - 1, hf)])
            ep_guard_se[(oc, hf)] = b.inc(
                nc.scalar.activation(tb, pr, AF.Identity, bias=bbh[:, oc : oc + 1], scale=0.5), "se")
            b.wait(nc.vector, "se")
            xs_out = x[:, oc * TLOC + hf * 512 : oc * TLOC + (hf + 1) * 512]
            ep_guard_ve[(oc, hf)] = b.inc(
                nc.vector.tensor_tensor(xs_out, xs_out, tb, ALU.add), "ve")
    b.pop()




def emit_attn0(b, io):
    nc = b.nc
    x = b.tn["x_res"]
    xln = b.tn["x_ln"]
    ln_done = b.ln_done
    b.push()
    b.fence()

    SCALE = 1.0 / float(np.sqrt(128.0))
    ag_in = nc.dram_tensor(b.name("ag1_in"), [4096, 1024], BF16)
    ag_out = nc.dram_tensor(b.name("ag1_out"), [16384, 1024], BF16)

    q_sb = b.sb("a0_q", [128, 16 * 1024], BF16)
    ba = b.sb("a0_ba", [128, 48], F32)
    bp = b.sb("a0_bp", [128, 16], F32)
    mcol = b.sb("a0_mcol", [128, 24], F32)
    dm = b.sb("a0_dm", [128, 4 * 512], F32)
    bro = b.sb("a0_bro", [1, 8], mybir.dt.int32)
    b.dma(nc.sync, ba[:, :], io["b_attn"][:, :])
    b.dma(nc.sync, bp[:, :], io["b_proj"][:, :])
    b.dma(nc.sync, mcol[:, :], io["mcol"][:, :])
    b.dma(nc.sync, dm[:, :], io["attn0_masks"][:, :])
    b.dma(nc.sync, bro[:, :], io["broff"][:, :])
    ld_small = b.cnt["ld"]

    b.push()
    wbuf = b.sb("a0_wbuf", [128, 2 * 2048], BF16)
    kvst = b.sb("a0_kvst", [128, 2 * 1024], BF16)
    ps_qkv = b.ps("a0_psqkv", [128, 2 * 1024], F32)

    ld_w = {}
    w_guard = {}
    ev_guard = {}
    st_guard = {}

    def wload_a0(oc):
        par = "wa" if oc % 2 == 0 else "wb"
        if oc >= 2:
            b.wait(nc.gpsimd, "te", w_guard[oc - 2])
        ld_w[oc] = (par, b.dma(nc.gpsimd,
                    AP(wbuf, (oc % 2) * 2048, [[2 * 2048, 128], [1, 2048]]),
                    AP(io["w_attn"], oc * 128 * 2048, [[2048, 128], [1, 2048]]),
                    ctr=par))

    wload_a0(0)
    wload_a0(1)
    b.wait(nc.tensor, "ve", ln_done)
    b.wait(nc.scalar, "ld", ld_small)
    for oc in range(48):
        wb = wbuf[:, (oc % 2) * 2048 : ((oc % 2) + 1) * 2048]
        pq = ps_qkv[:, (oc % 2) * 1024 : ((oc % 2) + 1) * 1024]
        b.wait(nc.tensor, ld_w[oc][0], ld_w[oc][1])
        if oc >= 2:
            b.wait(nc.tensor, "se", ev_guard[oc - 2])
        for c in range(NCH):
            for hf in range(2):
                xs = xln[:, c * TLOC + hf * 512 : c * TLOC + (hf + 1) * 512]
                b.inc(nc.tensor.matmul(pq[:, hf * 512 : (hf + 1) * 512],
                                       wb[:, c * 128 : (c + 1) * 128], xs,
                                       start=(c == 0), stop=(c == NCH - 1)), "te")
        w_guard[oc] = b.cnt["te"]
        if oc + 2 < 48:
            wload_a0(oc + 2)
        b.wait(nc.scalar, "te", w_guard[oc])
        if oc < 16:
            ev_guard[oc] = b.inc(nc.scalar.activation(
                q_sb[:, oc * 1024 : (oc + 1) * 1024], pq, AF.Identity,
                bias=ba[:, oc : oc + 1]), "se")
        else:
            kv = kvst[:, (oc % 2) * 1024 : ((oc % 2) + 1) * 1024]
            spar = "hc" if oc % 2 == 0 else "hd"
            if oc >= 18:
                b.wait(nc.scalar, spar, st_guard[oc - 2])
            ev_guard[oc] = b.inc(nc.scalar.activation(
                kv, pq, AF.Identity, bias=ba[:, oc : oc + 1]), "se")
            b.wait(nc.sync, "se", ev_guard[oc])
            row = (oc - 16) * 128 if oc < 32 else 2048 + (oc - 32) * 128
            st_guard[oc] = b.dma(nc.sync, AP(ag_in, row * 1024, [[1024, 128], [1, 1024]]),
                                 kv, ctr=spar)
    b.wait(nc.gpsimd, "hc")
    b.wait(nc.gpsimd, "hd")
    b.inc(nc.gpsimd.collective_compute(
        "AllGather", ALU.bypass, replica_groups=[[0, 1, 2, 3], [4, 5, 6, 7]],
        ins=[ag_in.ap().opt()], outs=[ag_out.ap().opt()]), "cc")
    cc_done = b.cnt["cc"]
    b.pop()

    b.push()
    kbuf = b.sb("a0_kbuf", [128, 2 * 3 * 1024], BF16)
    vbuf = b.sb("a0_vbuf", [128, 2 * 3 * 1024], BF16)
    kobuf = b.sb("a0_kobuf", [128, 2 * 1024], BF16)
    vobuf = b.sb("a0_vobuf", [128, 2 * 1024], BF16)
    vt_sb = b.sb("a0_vt", [128, 4 * 1024], BF16)
    pt_sb = b.sb("a0_pt", [128, 2 * 512], BF16)
    negm_sb = b.sb("a0_negm", [1, 2 * 1024], F32)
    negmc = b.sb("a0_negmc", [128, 8], F32)
    zr = b.sb("a0_zr", [1, 4 * 512], F32)
    zbc_sb = b.sb("a0_zbc", [128, 512], F32)

    ps_sA = b.ps("a0_pssA", [128, 512], F32)
    ps_sB = b.ps("a0_pssB", [128, 512], F32)
    ps_o = b.ps("a0_pso", [128, 512], F32)
    ps_z = b.ps("a0_psz", [1, 512], F32)
    ps_m = b.ps("a0_psm", [1, 1024], F32)
    ps_vt = b.ps("a0_psvt", [128, 1024], BF16)
    ps_zbc = b.ps("a0_pszbc", [128, 512], F32)

    onesr = b.tn["ones_row"]
    onescb = b.tn["ones_col_bf"]
    identf = b.tn["ident_f32"]
    identb = b.tn["ident_bf"]
    zcol = b.tn["cvals"][:, 3:4]

    kvn = nc.dram_tensor(b.name("a0_kvn"), [16384, 1024], BF16)
    regs = [b.scopes[-1].enter_context(nc.gpsimd.register(f"a0r{i}")) for i in range(8)]
    b.wait(nc.gpsimd, "ld", ld_small)
    for i in range(8):
        nc.gpsimd.reg_load(regs[i], bro[0:1, i : i + 1])
    b.wait(nc.gpsimd, "cc", cc_done)
    dst_rows = [0, 2048, 4096, 10240, 6144, 12288, 8192, 14336]
    dst_for = [0, 2048, 4096, 10240, 6144, 12288, 8192, 14336]
    for i in range(8):
        b.dma(nc.gpsimd, AP(kvn, dst_for[i] * 1024, [[1024, 2048], [1, 1024]]),
              AP(ag_out, regs[i], [[1024, 2048], [1, 1024]]), ctr="lg")
    kvn_done = b.cnt["lg"]

    kv_ld = {}

    def kvload(h):
        par = h % 2
        ctr = "ha" if par == 0 else "hb"
        if h >= 2:
            b.wait(nc.sync, "te", b.a0_head_guard[h - 2])
        else:
            b.wait(nc.sync, "lg", kvn_done)
        b.dma(nc.sync, kobuf[:, par * 1024 : (par + 1) * 1024],
              AP(kvn, (0 + 128 * h) * 1024, [[1024, 128], [1, 1024]]), ctr=ctr)
        b.dma(nc.sync, vobuf[:, par * 1024 : (par + 1) * 1024],
              AP(kvn, (2048 + 128 * h) * 1024, [[1024, 128], [1, 1024]]), ctr=ctr)
        for j in range(3):
            b.dma(nc.sync, kbuf[:, (par * 3 + j) * 1024 : (par * 3 + j + 1) * 1024],
                  AP(kvn, (4096 + 2048 * j + 128 * h) * 1024, [[1024, 128], [1, 1024]]), ctr=ctr)
            b.dma(nc.sync, vbuf[:, (par * 3 + j) * 1024 : (par * 3 + j + 1) * 1024],
                  AP(kvn, (10240 + 2048 * j + 128 * h) * 1024, [[1024, 128], [1, 1024]]), ctr=ctr)
        kv_ld[h] = (ctr, b.cnt[ctr])

    b.a0_head_guard = {}
    kvload(0)
    kvload(1)
    for h in range(16):
        par = h % 2
        ctr, ldv = kv_ld[h]
        b.wait(nc.tensor, ctr, ldv)
        vt_guard = b.cnt["ve"]
        for grp in range(4):
            if grp < 3:
                vsrc = vbuf[:, (par * 3 + grp) * 1024 : (par * 3 + grp + 1) * 1024]
            else:
                vsrc = vobuf[:, par * 1024 : (par + 1) * 1024]
            b.wait(nc.tensor, "ve", vt_guard)
            for tt in range(8):
                b.inc(nc.tensor.transpose(ps_vt[:, tt * 128 : (tt + 1) * 128],
                                          vsrc[:, tt * 128 : (tt + 1) * 128],
                                          identb[:, :]), "te")
            b.wait(nc.vector, "te")
            vt_guard = b.inc(nc.vector.tensor_copy(
                vt_sb[:, grp * 1024 : (grp + 1) * 1024], ps_vt[:, :]), "ve")
        msub_guard = b.cnt["ve"]
        for qt in range(8):
            ns = max(4, 4 * qt)
            b.wait(nc.tensor, "ve", msub_guard if qt == 0 else b.cnt["ve"])
            b.inc(nc.tensor.matmul(
                ps_sA[:, 0:ns],
                q_sb[:, h * 1024 + qt * 128 : h * 1024 + (qt + 1) * 128],
                AP(kobuf, par * 1024, [[2 * 1024, 128], [32, ns]]),
                start=True, stop=True), "te")
            b.wait(nc.vector, "te")
            b.inc(nc.vector.tensor_reduce(
                negmc[:, qt : qt + 1], ps_sA[:, 0:ns],
                AX.X, ALU.max, negate=True), "ve")
        b.wait(nc.tensor, "ve")
        for qt in range(8):
            b.inc(nc.tensor.transpose(ps_m[0:1, qt * 128 : (qt + 1) * 128],
                                      negmc[:, qt : qt + 1], identf[:, :]), "te")
        b.wait(nc.vector, "te")
        b.inc(nc.vector.tensor_copy(negm_sb[:, par * 1024 : (par + 1) * 1024],
                                    ps_m[:, :]), "ve")
        negm_done = b.cnt["ve"]

        for qb in range(2):
            visits = [("oth", st) for st in range(24)] +                      [("own", j) for j in range(4 * qb + 4)]
            nv = len(visits)
            exp_guard = {}
            zav_guard = {}
            for vi, (kind, idx) in enumerate(visits):
                pp = vi % 2
                ps_s = ps_sA if pp == 0 else ps_sB
                if vi >= 2:
                    b.wait(nc.tensor, "se", exp_guard[vi - 2])
                if vi == 0:
                    b.wait(nc.tensor, "ve")
                b.inc(nc.tensor.matmul(ps_s[:, :], onesr[:, :],
                                       negm_sb[0:1, par * 1024 + qb * 512 : par * 1024 + (qb + 1) * 512],
                                       start=True, stop=False, skip_group_check=True), "te")
                if kind == "oth":
                    kt = kbuf[:, (par * 3) * 1024 + idx * 128 : (par * 3) * 1024 + (idx + 1) * 128]
                    vt = vt_sb[:, idx * 128 : (idx + 1) * 128]
                    biascol = mcol[:, idx : idx + 1]
                    diag = None
                else:
                    kt = kobuf[:, par * 1024 + idx * 128 : par * 1024 + (idx + 1) * 128]
                    vt = vt_sb[:, 3 * 1024 + idx * 128 : 3 * 1024 + (idx + 1) * 128]
                    biascol = zcol
                    di = idx - 4 * qb
                    diag = dm[:, di * 512 : (di + 1) * 512] if 0 <= di < 4 else None
                b.inc(nc.tensor.matmul(ps_s[:, :], kt,
                                       q_sb[:, h * 1024 + qb * 512 : h * 1024 + (qb + 1) * 512],
                                       start=False, stop=True, skip_group_check=True), "te")
                te_s = b.cnt["te"]
                if diag is not None:
                    b.wait(nc.vector, "te", te_s)
                    b.inc(nc.vector.tensor_tensor(ps_s[:, :], ps_s[:, :], diag, ALU.add), "ve")
                    b.wait(nc.scalar, "ve")
                else:
                    b.wait(nc.scalar, "te", te_s)
                if vi >= 2:
                    b.wait(nc.scalar, "te", zav_guard[vi - 2])
                exp_guard[vi] = b.inc(nc.scalar.activation(
                    pt_sb[:, pp * 512 : (pp + 1) * 512], ps_s[:, :], AF.Exp,
                    bias=biascol, scale=SCALE), "se")
                b.wait(nc.tensor, "se", exp_guard[vi])
                b.inc(nc.tensor.matmul(ps_z[:, :], onescb[:, :],
                                       pt_sb[:, pp * 512 : (pp + 1) * 512],
                                       start=(vi == 0), stop=(vi == nv - 1),
                                       skip_group_check=True), "te")
                b.inc(nc.tensor.matmul(ps_o[:, :], vt,
                                       pt_sb[:, pp * 512 : (pp + 1) * 512],
                                       start=(vi == 0), stop=(vi == nv - 1),
                                       skip_group_check=True), "te")
                zav_guard[vi] = b.cnt["te"]
            te_fin = b.cnt["te"]
            b.wait(nc.vector, "te", te_fin)
            b.inc(nc.vector.tensor_copy(zr[:, 0:512], ps_z[:, :]), "ve")
            b.wait(nc.vector, "ve")
            b.inc(nc.vector.tensor_scalar_add(zr[:, 512:1024], zr[:, 0:512], 1e-30), "ve")
            b.wait(nc.vector, "ve")
            b.inc(nc.vector.reciprocal(zr[:, 1024:1536], zr[:, 512:1024]), "ve")
            ve_zi = b.cnt["ve"]
            b.wait(nc.tensor, "ve", ve_zi)
            b.inc(nc.tensor.matmul(ps_zbc[:, :], onesr[:, :], zr[0:1, 1024:1536],
                                   start=True, stop=True, skip_group_check=True), "te")
            b.wait(nc.scalar, "te")
            se_zbc = b.inc(nc.scalar.activation(zbc_sb[:, :], ps_zbc[:, :], AF.Identity,
                                                bias=zcol), "se")
            b.wait(nc.vector, "se", se_zbc)
            b.inc(nc.vector.tensor_tensor(
                xln[:, h * 1024 + qb * 512 : h * 1024 + (qb + 1) * 512],
                ps_o[:, :], zbc_sb[:, :], ALU.mult), "ve")
        b.a0_head_guard[h] = b.cnt["te"]
        if h + 2 < 16:
            kvload(h + 2)
    ao_done = b.cnt["ve"]
    b.pop()

    b.push()
    b.fence()
    wbufp = b.sb("a0_wbufp", [128, 2 * 2048], BF16)
    tmp = b.sb("a0_ptmp", [128, 2 * 512], F32)
    ps_res = b.ps("a0_psres", [128, 2 * 512], F32)
    ld_wp = {}
    wp_guard = {}
    epp_se = {}
    epp_ve = {}

    def wloadp(oc):
        par = "wa" if oc % 2 == 0 else "wb"
        if oc >= 2:
            b.wait(nc.gpsimd, "te", wp_guard[oc - 2])
        ld_wp[oc] = (par, b.dma(nc.gpsimd,
                     AP(wbufp, (oc % 2) * 2048, [[2 * 2048, 128], [1, 2048]]),
                     AP(io["w_proj"], oc * 128 * 2048, [[2048, 128], [1, 2048]]),
                     ctr=par))

    wloadp(0)
    wloadp(1)
    b.wait(nc.tensor, "ve", ao_done)
    for oc in range(NCH):
        wb = wbufp[:, (oc % 2) * 2048 : ((oc % 2) + 1) * 2048]
        b.wait(nc.tensor, ld_wp[oc][0], ld_wp[oc][1])
        for hf in range(2):
            pr = ps_res[:, hf * 512 : (hf + 1) * 512]
            if oc >= 1:
                b.wait(nc.tensor, "se", epp_se[(oc - 1, hf)])
            for c in range(NCH):
                aos = xln[:, c * TLOC + hf * 512 : c * TLOC + (hf + 1) * 512]
                b.inc(nc.tensor.matmul(pr, wb[:, c * 128 : (c + 1) * 128], aos,
                                       start=(c == 0), stop=(c == NCH - 1)), "te")
        wp_guard[oc] = b.cnt["te"]
        if oc + 2 < NCH:
            wloadp(oc + 2)
        for hf in range(2):
            pr = ps_res[:, hf * 512 : (hf + 1) * 512]
            tb = tmp[:, hf * 512 : (hf + 1) * 512]
            b.wait(nc.scalar, "te", wp_guard[oc])
            if oc >= 1:
                b.wait(nc.scalar, "ve", epp_ve[(oc - 1, hf)])
            epp_se[(oc, hf)] = b.inc(nc.scalar.activation(
                tb, pr, AF.Identity, bias=bp[:, oc : oc + 1]), "se")
            b.wait(nc.vector, "se")
            xs_out = x[:, oc * TLOC + hf * 512 : oc * TLOC + (hf + 1) * 512]
            epp_ve[(oc, hf)] = b.inc(
                nc.vector.tensor_tensor(xs_out, xs_out, tb, ALU.add), "ve")
    b.pop()
    b.pop()



def emit_ffn1(b, io):
    nc = b.nc
    x = b.tn["x_res"]
    xln = b.tn["x_ln"]
    ln_done = b.ln_done
    b.push()
    b.fence()

    IC = 256
    NIC = 7168 // IC
    NIC_RUN = FFN1_NIC
    identf = b.tn["ident_f32"]

    confw = b.sb("f1_confw", [128, NCH * 8], BF16)
    cb = b.sb("f1_cb", [8, 1], F32)
    wl = b.sb("f1_wl", [8, 1], F32)
    webc = b.sb("f1_webc", [128, 8 * 1024], BF16)
    mid_g = b.sb("f1_midg", [128, 4 * 1024], BF16)
    mid_u = b.sb("f1_midu", [128, 4 * 1024], BF16)
    md_sb = b.sb("f1_md", [128, 4 * 1024], BF16)
    ps_base = b.ps("f1_psbase", [128, 2 * 512], F32)
    ps_md = b.ps("f1_psmd", [64, 512], F32)
    ps_util = b.ps("f1_psutil", [128, 2 * 512], F32)
    ps_tm = b.ps("f1_pstm", [128, 128], F32)

    b.push()
    erow = b.sb("f1_erow", [8, 8 * 128], F32)
    we8 = b.sb("f1_we8", [8, 1024], F32)
    conf_sb = b.sb("f1_conf", [8, 1024], F32)
    bids = b.sb("f1_bids", [8, 1024], F32)
    ltm = b.sb("f1_ltm", [128, 64], F32)
    sm = b.sb("f1_sm", [128, 6 * 64], F32)
    m1 = sm[:, 0:8]; m2 = sm[:, 8:16]; dd = sm[:, 16:24]
    w1 = sm[:, 24:32]; w2 = sm[:, 32:40]
    mask1 = sm[:, 64:128]; l2 = sm[:, 128:192]; mask2 = sm[:, 192:256]
    g1 = sm[:, 256:320]; gate_tm = sm[:, 320:384]

    b.dma(nc.gpsimd, confw[:, :], io["conf_WT"][:, :])
    b.dma(nc.sync, cb[:, :], io["conf_b"][:, :])
    b.dma(nc.sync, wl[:, :], io["wealth"][:, :])
    b.dma(nc.sync, erow[:, :], io["erow"][:, :])
    ld_small = b.cnt["ld"]
    lg_small = b.cnt["lg"]

    b.wait(nc.tensor, "lg", lg_small)
    b.wait(nc.tensor, "ve", ln_done)
    for hf in range(2):
        for c in range(NCH):
            xs = xln[:, c * TLOC + hf * 512 : c * TLOC + (hf + 1) * 512]
            b.inc(nc.tensor.matmul(ps_util[0:8, hf * 512 : (hf + 1) * 512],
                                   confw[:, c * 8 : (c + 1) * 8], xs,
                                   start=(c == 0), stop=(c == NCH - 1)), "te")
    te_conf = b.cnt["te"]
    b.wait(nc.scalar, "te", te_conf)
    b.wait(nc.scalar, "ld", ld_small)
    b.inc(nc.scalar.activation(conf_sb[:, :], ps_util[0:8, 0:1024], AF.Sigmoid,
                               bias=cb[:, 0:1]), "se")
    b.wait(nc.vector, "se")
    b.inc(nc.vector.tensor_scalar(bids[:, :], conf_sb[:, :], wl[:, 0:1], None, ALU.mult), "ve")
    ve_bids = b.cnt["ve"]
    b.wait(nc.tensor, "ve", ve_bids)
    for tt in range(8):
        b.inc(nc.tensor.transpose(AP(ps_tm, 8 * tt, [[128, 128], [1, 8]]),
                                  bids[:, tt * 128 : (tt + 1) * 128], identf[0:8, 0:8]), "te")
    te_tm = b.cnt["te"]
    b.wait(nc.vector, "te", te_tm)
    b.inc(nc.vector.tensor_copy(ltm[:, :], AP(ps_tm, 0, [[128, 128], [1, 64]])), "ve")
    b.wait(nc.vector, "ve")
    SMW = 6 * 64
    ltm_v = AP(ltm, 0, [[64, 128], [8, 8], [1, 8]])
    m1rep = AP(sm, 0, [[SMW, 128], [1, 8], [0, 8]])
    m2rep = AP(sm, 8, [[SMW, 128], [1, 8], [0, 8]])
    w1rep = AP(sm, 24, [[SMW, 128], [1, 8], [0, 8]])
    w2rep = AP(sm, 32, [[SMW, 128], [1, 8], [0, 8]])
    l2_v = AP(sm, 128, [[SMW, 128], [8, 8], [1, 8]])
    b.inc(nc.vector.tensor_reduce(m1, ltm_v, AX.X, ALU.max), "ve")
    b.wait(nc.vector, "ve")
    b.inc(nc.vector.tensor_tensor(mask1, ltm[:, :], m1rep, ALU.is_equal), "ve")
    b.wait(nc.vector, "ve")
    b.inc(nc.vector.scalar_tensor_tensor(l2, mask1, NEG, ltm[:, :], ALU.mult, ALU.add), "ve")
    b.wait(nc.vector, "ve")
    b.inc(nc.vector.tensor_reduce(m2, l2_v, AX.X, ALU.max), "ve")
    b.wait(nc.vector, "ve")
    b.inc(nc.vector.tensor_tensor(mask2, l2, m2rep, ALU.is_equal), "ve")
    b.inc(nc.vector.tensor_tensor(dd, m2, m1, ALU.subtract), "ve")
    b.wait(nc.scalar, "ve")
    b.inc(nc.scalar.activation(w1, dd, AF.Sigmoid, scale=-1.0), "se")
    b.wait(nc.vector, "se")
    b.inc(nc.vector.tensor_scalar(w2, w1, -0.5, 0.5, ALU.mult, ALU.add), "ve")
    b.wait(nc.vector, "ve")
    b.inc(nc.vector.tensor_scalar_mul(w1, w1, 0.5), "ve")
    b.wait(nc.vector, "ve")
    b.inc(nc.vector.tensor_tensor(g1, mask1, w1rep, ALU.mult), "ve")
    b.wait(nc.vector, "ve")
    b.inc(nc.vector.tensor_tensor(gate_tm, mask2, w2rep, ALU.mult), "ve")
    b.wait(nc.vector, "ve")
    b.inc(nc.vector.tensor_tensor(gate_tm, gate_tm, g1, ALU.add), "ve")
    ve_gate = b.cnt["ve"]
    b.wait(nc.tensor, "ve", ve_gate)
    b.wait(nc.tensor, "se")
    for tt in range(8):
        b.inc(nc.tensor.transpose(ps_util[0:8, tt * 128 : (tt + 1) * 128],
                                  gate_tm[:, 8 * tt : 8 * tt + 8], identf[:, :]), "te")
    te_we = b.cnt["te"]
    b.wait(nc.vector, "te", te_we)
    b.inc(nc.vector.tensor_copy(we8[:, :], ps_util[0:8, 0:1024]), "ve")
    ve_we8 = b.cnt["ve"]
    b.wait(nc.tensor, "ve", ve_we8)
    for e in range(8):
        if e > 0:
            b.wait(nc.tensor, "se")
        for hf in range(2):
            b.inc(nc.tensor.matmul(ps_util[:, hf * 512 : (hf + 1) * 512],
                                   erow[:, e * 128 : (e + 1) * 128],
                                   we8[:, hf * 512 : (hf + 1) * 512],
                                   start=True, stop=True), "te")
        b.wait(nc.scalar, "te")
        b.inc(nc.scalar.activation(webc[:, e * 1024 : (e + 1) * 1024],
                                   ps_util[:, :], AF.Copy), "se")
    se_webc = b.cnt["se"]
    b.fence()
    b.pop()

    b.push()
    wg_b = b.sb("f1_wgb", [128, 4096], BF16)
    wu_b = b.sb("f1_wub", [128, 4096], BF16)
    wd_b = b.sb("f1_wdb", [128, 2 * 4096], BF16)
    gB_sb = b.sb("f1_gB2", [128, 2 * 1024], BF16)
    uB_sb = b.sb("f1_uB2", [128, 2 * 1024], BF16)
    dA_sb = b.sb("f1_dA", [128, 1024], BF16)
    G_sb = b.sb("f1_G", [128, 1024], F32)
    U_sb = b.sb("f1_U", [128, 1024], F32)
    gscr = b.sb("f1_gscr", [128, 1024], F32)
    uscr = b.sb("f1_uscr", [128, 1024], F32)
    whbf = b.sb("f1_whbf", [128, 1024], BF16)
    hbarbf = b.sb("f1_hbarbf", [128, 4 * 1024], BF16)

    b.wait(nc.tensor, "se", se_webc)
    for (mid, wsrc, ioA, par) in [(mid_g, wg_b, "gateAcat", "wa"),
                                  (mid_u, wu_b, "upAcat", "wb")]:
        for piece in range(2):
            if piece == 1:
                b.wait(nc.gpsimd, "te")
            ldv = b.dma(nc.gpsimd, AP(wsrc, 0, [[4096, 128], [1, 4096]]),
                        AP(io[ioA], piece * 128 * 4096, [[4096, 128], [1, 4096]]),
                        ctr=par)
            b.wait(nc.tensor, par, ldv)
            for mc2 in range(2):
                mc = piece * 2 + mc2
                c0 = mc2 * 128
                for hf in range(2):
                    pu = ps_util[:, hf * 512 : (hf + 1) * 512]
                    for c in range(NCH):
                        xs = xln[:, c * TLOC + hf * 512 : c * TLOC + (hf + 1) * 512]
                        b.inc(nc.tensor.matmul(pu, wsrc[:, c * 256 + c0 : c * 256 + c0 + 128],
                                               xs, start=(c == 0), stop=(c == NCH - 1)), "te")
                    b.wait(nc.scalar, "te")
                    b.inc(nc.scalar.activation(mid[:, mc * 1024 + hf * 512 : mc * 1024 + (hf + 1) * 512],
                                               pu, AF.Copy, scale=0.25), "se")
                    b.wait(nc.tensor, "se")
    se_mid = b.cnt["se"]
    b.wait(nc.vector, "se", se_mid)
    b.inc(nc.vector.memset(md_sb[:, :], 0.0), "ve")

    wload = {}

    def loadw(tag, ic):
        par = {"g": ("wa", "wa"), "u": ("wb", "wb"), "d": ("sa", "sb"),
               "gB": ("wd", "wx"), "uB": ("wy", "wz"), "dA": ("wc", "wc")}[tag][ic % 2]
        back = 1 if tag in ("g", "u", "dA") else 2
        if (tag, ic - back) in wload and wload[(tag, ic - back)][2] is not None:
            b.wait(nc.gpsimd, "te", wload[(tag, ic - back)][2])
        sl = ic % 2
        if tag == "g":
            v = b.dma(nc.gpsimd, AP(wg_b, 0, [[4096, 128], [1, 4096]]),
                      AP(io["base_gate"], ic * 128 * 4096, [[4096, 128], [1, 4096]]),
                      ctr=par)
        elif tag == "u":
            v = b.dma(nc.gpsimd, AP(wu_b, 0, [[4096, 128], [1, 4096]]),
                      AP(io["base_up"], ic * 128 * 4096, [[4096, 128], [1, 4096]]),
                      ctr=par)
        elif tag == "d":
            v = b.dma(nc.gpsimd, AP(wd_b, sl * 4096, [[2 * 4096, 128], [1, 4096]]),
                      AP(io["base_down"], ic * 128 * 4096, [[4096, 128], [1, 4096]]),
                      ctr=par)
        elif tag in ("gB", "uB"):
            buf = gB_sb if tag == "gB" else uB_sb
            ionm = "gate_B" if tag == "gB" else "up_B"
            for parity in range(2):
                dst = buf[64 * parity : 64 * parity + 64, sl * 1024 : (sl + 1) * 1024]
                v = b.dma(nc.gpsimd,
                          AP(dst.tensor, dst.offset, [[2 * 1024, 64], [1, 1024]]),
                          AP(io[ionm], (ic * 2 + parity) * 64 * 1024, [[1024, 64], [1, 1024]]),
                          ctr=par)
        elif tag == "dA":
            v = b.dma(nc.gpsimd, AP(dA_sb, 0, [[1024, 128], [1, 1024]]),
                      AP(io["down_A"], ic * 128 * 1024, [[1024, 128], [1, 1024]]),
                      ctr=par)
        wload[(tag, ic)] = [par, v, None]

    def setg(tag, ic):
        wload[(tag, ic)][2] = b.cnt["te"]

    b.wait(nc.gpsimd, "te")
    loadw("g", 0)
    loadw("u", 0)
    loadw("dA", 0)
    for ic in [0, 1]:
        for tag in ["d", "gB", "uB"]:
            loadw(tag, ic)

    for ic in range(min(NIC, NIC_RUN) if FFN1_SECT >= 2 else 0):
        for tag in ["g", "u", "d", "gB", "uB", "dA"]:
            b.wait(nc.tensor, wload[(tag, ic)][0], wload[(tag, ic)][1])
        sl = (ic % 2)
        for th in range(2):
            if ic > 0 or th > 0:
                b.wait(nc.tensor, "ve")
            for oc in range(2):
                pb = ps_base[:, oc * 512 : (oc + 1) * 512]
                for c in range(NCH):
                    xs = xln[:, c * TLOC + th * 512 : c * TLOC + (th + 1) * 512]
                    b.inc(nc.tensor.matmul(pb, wg_b[:, c * 256 + oc * 128 :
                                                    c * 256 + (oc + 1) * 128],
                                           xs, start=(c == 0), stop=(c == NCH - 1)), "te")
            te_G = b.cnt["te"]
            b.wait(nc.scalar, "te", te_G)
            b.inc(nc.scalar.activation(G_sb[:, :], AP(ps_base, 0, [[1024, 128], [1, 1024]]),
                                       AF.Copy), "se")
            se_G = b.cnt["se"]
            b.wait(nc.tensor, "se", se_G)
            for oc in range(2):
                pb = ps_base[:, oc * 512 : (oc + 1) * 512]
                for c in range(NCH):
                    xs = xln[:, c * TLOC + th * 512 : c * TLOC + (th + 1) * 512]
                    b.inc(nc.tensor.matmul(pb, wu_b[:, c * 256 + oc * 128 :
                                                    c * 256 + (oc + 1) * 128],
                                           xs, start=(c == 0), stop=(c == NCH - 1)), "te")
            te_U = b.cnt["te"]
            b.wait(nc.scalar, "te", te_U)
            b.inc(nc.scalar.activation(U_sb[:, :], AP(ps_base, 0, [[1024, 128], [1, 1024]]),
                                       AF.Copy), "se")
            se_U = b.cnt["se"]
            for e in range(8):
                ebase = 64 * (e % 2)
                eg = (e // 2) * 1024
                if e == 0:
                    b.wait(nc.tensor, "se", se_U)
                else:
                    b.wait(nc.tensor, "ve")
                for oc in range(2):
                    pb = ps_base[:, oc * 512 : (oc + 1) * 512]
                    b.inc(nc.tensor.matmul(
                        pb, gB_sb[ebase : ebase + 64, sl * 1024 + (e // 2) * 256 + oc * 128 :
                                  sl * 1024 + (e // 2) * 256 + (oc + 1) * 128],
                        mid_g[ebase : ebase + 64, eg + th * 512 : eg + (th + 1) * 512],
                        start=True, stop=True), "te")
                te_dg = b.cnt["te"]
                b.wait(nc.vector, "te", te_dg)
                if e > 0:
                    b.wait(nc.vector, "po")
                b.inc(nc.vector.tensor_tensor(gscr[:, :], G_sb[:, :],
                                              AP(ps_base, 0, [[1024, 128], [1, 1024]]),
                                              ALU.add), "ve")
                ve_g = b.cnt["ve"]
                b.wait(nc.tensor, "ve", ve_g)
                for oc in range(2):
                    pb = ps_base[:, oc * 512 : (oc + 1) * 512]
                    b.inc(nc.tensor.matmul(
                        pb, uB_sb[ebase : ebase + 64, sl * 1024 + (e // 2) * 256 + oc * 128 :
                                  sl * 1024 + (e // 2) * 256 + (oc + 1) * 128],
                        mid_u[ebase : ebase + 64, eg + th * 512 : eg + (th + 1) * 512],
                        start=True, stop=True), "te")
                te_du = b.cnt["te"]
                b.wait(nc.scalar, "ve", ve_g)
                se_sil = b.inc(nc.scalar.activation(gscr[:, :], gscr[:, :], AF.Silu), "se")
                b.wait(nc.vector, "te", te_du)
                b.inc(nc.vector.tensor_tensor(uscr[:, :], U_sb[:, :],
                                              AP(ps_base, 0, [[1024, 128], [1, 1024]]),
                                              ALU.add), "ve")
                ve_u = b.cnt["ve"]
                b.wait(nc.gpsimd, "se", se_sil)
                b.wait(nc.gpsimd, "ve", ve_u)
                b.inc(nc.gpsimd.tensor_tensor(gscr[:, :], gscr[:, :], uscr[:, :], ALU.mult), "po")
                b.wait(nc.gpsimd, "po")
                wev = AP(webc, e * 1024 + th * 512, [[8 * 1024, 128], [0, 2], [1, 512]])
                b.inc(nc.gpsimd.tensor_tensor(whbf[:, :], gscr[:, :], wev, ALU.mult), "po")
                po_wh = b.cnt["po"]
                b.wait(nc.vector, "po", po_wh)
                hbs = hbarbf[:, (sl * 2 + th) * 1024 : (sl * 2 + th + 1) * 1024]
                if e == 0:
                    b.inc(nc.vector.tensor_copy(hbs, whbf[:, :]), "ve")
                else:
                    b.inc(nc.vector.tensor_tensor(hbs, hbs, whbf[:, :], ALU.add), "ve")
                b.wait(nc.tensor, "po", po_wh)
                b.wait(nc.tensor, "ve")
                for kc in range(2):
                    b.inc(nc.tensor.matmul(
                        ps_md[:, :],
                        dA_sb[:, e * 128 + kc * 64 : e * 128 + (kc + 1) * 64],
                        whbf[:, kc * 512 : (kc + 1) * 512],
                        start=(kc == 0), stop=(kc == 1)), "te")
                te_md = b.cnt["te"]
                b.wait(nc.vector, "te", te_md)
                mslice = md_sb[ebase : ebase + 64, (e // 2) * 1024 + th * 512 :
                               (e // 2) * 1024 + (th + 1) * 512]
                b.inc(nc.vector.tensor_tensor(mslice, mslice, ps_md[:, :], ALU.add), "ve")
            ve_hb = b.cnt["ve"]
            if ic % 2 == 1:
                b.wait(nc.tensor, "ve", ve_hb)
                dp_ve = {}
                for oc in range(NCH):
                    pu = ps_util[:, (oc % 2) * 512 : ((oc % 2) + 1) * 512]
                    if oc >= 2:
                        b.wait(nc.tensor, "ve", dp_ve[oc - 2])
                    for pr in range(2):
                        psl = (ic - 1 + pr) % 2
                        for kc in range(2):
                            b.inc(nc.tensor.matmul(
                                pu, wd_b[:, psl * 4096 + kc * 2048 + oc * 128 :
                                         psl * 4096 + kc * 2048 + (oc + 1) * 128],
                                hbarbf[:, (psl * 2 + th) * 1024 + kc * 512 :
                                       (psl * 2 + th) * 1024 + (kc + 1) * 512],
                                start=(pr == 0 and kc == 0), stop=(pr == 1 and kc == 1)), "te")
                    b.wait(nc.vector, "te")
                    xs_out = x[:, oc * TLOC + th * 512 : oc * TLOC + (th + 1) * 512]
                    dp_ve[oc] = b.inc(nc.vector.tensor_tensor(xs_out, xs_out, pu, ALU.add), "ve")
        for tag in ["g", "u", "gB", "uB", "dA"]:
            setg(tag, ic)
        if ic % 2 == 1:
            setg("d", ic - 1)
            setg("d", ic)
        if ic + 1 < NIC:
            loadw("g", ic + 1)
            loadw("u", ic + 1)
            loadw("dA", ic + 1)
        if ic % 2 == 1:
            if ic + 1 < NIC:
                loadw("d", ic + 1)
            if ic + 2 < NIC:
                loadw("d", ic + 2)
        if ic + 2 < NIC:
            for tag in ["gB", "uB"]:
                loadw(tag, ic + 2)
    b.pop()

    b.push()
    b.fence()
    mdb = b.sb("f1_mdbf", [128, 4 * 1024], BF16)
    dB_sb = b.sb("f1_dB", [128, 4 * 2048], BF16)
    if FFN1_SECT < 3:
        b.pop()
        b.pop()
        return
    b.inc(nc.vector.tensor_scalar_mul(mdb[:, :], md_sb[:, :], 0.25), "ve")
    ve_mdb = b.cnt["ve"]
    for e in range(8):
        dst = dB_sb[64 * (e % 2) : 64 * (e % 2) + 64,
                    (e // 2) * 2048 : (e // 2 + 1) * 2048]
        b.dma(nc.gpsimd, dst, AP(io["down_B"], e * 64 * 2048, [[2048, 64], [1, 2048]]),
              ctr="lg")
    lg_dB = b.cnt["lg"]
    b.wait(nc.tensor, "lg", lg_dB)
    b.wait(nc.tensor, "ve", ve_mdb)
    ol_ve = {}
    ps_ol = b.ps("f1_psol", [128, 2 * 512], F32)
    for oc in range(NCH):
        for th in range(2):
            puA = ps_ol[:, 0:512]
            puB = ps_ol[:, 512:1024]
            if oc >= 1 or th == 1:
                b.wait(nc.tensor, "ve")
            for parity, pu in [(0, puA), (1, puB)]:
                for i_e, e in enumerate([parity, 2 + parity, 4 + parity, 6 + parity]):
                    ebase = 64 * parity
                    b.inc(nc.tensor.matmul(
                        pu, dB_sb[ebase : ebase + 64, (e // 2) * 2048 + oc * 128 :
                                  (e // 2) * 2048 + (oc + 1) * 128],
                        mdb[ebase : ebase + 64, (e // 2) * 1024 + th * 512 :
                            (e // 2) * 1024 + (th + 1) * 512],
                        start=(i_e == 0), stop=(i_e == 3)), "te")
            b.wait(nc.vector, "te")
            xs_out = x[:, oc * TLOC + th * 512 : oc * TLOC + (th + 1) * 512]
            b.inc(nc.vector.tensor_tensor(xs_out, xs_out, puA, ALU.add), "ve")
            b.wait(nc.vector, "ve")
            ol_ve[(oc, th)] = b.inc(nc.vector.tensor_tensor(xs_out, xs_out, puB, ALU.add), "ve")
    b.pop()
    b.pop()



def emit_attn1(b, io):
    nc = b.nc
    x = b.tn["x_res"]
    xln = b.tn["x_ln"]
    ln_done = b.ln_done
    b.push()
    b.fence()

    SC = 1.0 / float(np.sqrt(128.0))
    identf = b.tn["ident_f32"]
    identb = b.tn["ident_bf"]
    onesr = b.tn["ones_row"]

    ag_in = nc.dram_tensor(b.name("ag2_in"), [1056, 256], BF16)
    ag_out = nc.dram_tensor(b.name("ag2_out"), [4224, 256], BF16)

    qnw = b.sb("a1_qnw", [1, 128], F32)
    knw = b.sb("a1_knw", [1, 128], F32)
    sinkr = b.sb("a1_sink", [1, 16], F32)
    rcos = b.sb("a1_rcos", [64, 1024], F32)
    rsin = b.sb("a1_rsin", [64, 1024], F32)
    a2o = b.sb("a1_a2o", [1, 4], mybir.dt.int32)
    b.dma(nc.sync, qnw[:, :], io["qnw"][:, :])
    b.dma(nc.sync, knw[:, :], io["knw"][:, :])
    b.dma(nc.sync, sinkr[:, :], io["sink"][:, :])
    b.dma(nc.sync, rcos[:, :], io["rcos"][:, :])
    b.dma(nc.sync, rsin[:, :], io["rsin"][:, :])
    b.dma(nc.sync, a2o[:, :], io["a2off"][:, :])
    ld_small = b.cnt["ld"]

    b.push()
    q_sb = b.sb("a1_q", [128, 16 * 1024], BF16)

    b.push()
    hq_sb = b.sb("a1_hq", [128, 4 * 1024], BF16)
    ksw_f = b.sb("a1_kswf", [128, 1024], F32)
    kn_sb = b.sb("a1_kn", [128, 1024], BF16)
    vsw_sb = b.sb("a1_vsw", [128, 1024], BF16)
    vswtm_sb = b.sb("a1_vswtm", [128, 1024], BF16)
    ck_f = b.sb("a1_ckf", [128, 64], F32)
    ckn_sb = b.sb("a1_ckn", [128, 64], BF16)
    wbuf = b.sb("a1_wbuf", [128, 2 * 2048], BF16)
    rsc = b.sb("a1_rsc", [32, 3 * 2048], F32)
    sqs = b.sb("a1_sqs", [128, 1024], F32)
    srow = b.sb("a1_srow", [1, 1024], F32)

    ps_a = b.ps("a1_psa", [128, 2 * 1024], F32)
    ps_st = b.ps("a1_psst", [1, 512], F32)
    ps_bc = b.ps("a1_psbc", [128, 1024], F32)
    ps_vt = b.ps("a1_psvt", [128, 1024], BF16)

    ld_w = {}
    wg_guard = {}

    def wload(tag, ioname, blk, idx, nkc, par):
        if (tag, idx - 2) in ld_w and ld_w[(tag, idx - 2)][2] is not None:
            b.wait(nc.gpsimd, "te", ld_w[(tag, idx - 2)][2])
        w = nkc * 128
        v = b.dma(nc.gpsimd,
                  AP(wbuf, (idx % 2) * 2048, [[2 * 2048, 128], [1, w]]),
                  AP(io[ioname], blk * 128 * w, [[w, 128], [1, w]]),
                  ctr=par)
        ld_w[(tag, idx)] = [par, v, None]

    b.wait(nc.tensor, "ve", ln_done)
    b.wait(nc.tensor, "ld", ld_small)
    wload("qc", "Wq_c", 0, 0, NCH, "wa")
    wload("qc", "Wq_c", 1, 1, NCH, "wb")
    for oc in range(4):
        pq = ps_a[:, (oc % 2) * 1024 : ((oc % 2) + 1) * 1024]
        b.wait(nc.tensor, ld_w[("qc", oc)][0], ld_w[("qc", oc)][1])
        if oc >= 2:
            b.wait(nc.tensor, "se")
        for hf in range(2):
            for c in range(NCH):
                xs = xln[:, c * TLOC + hf * 512 : c * TLOC + (hf + 1) * 512]
                b.inc(nc.tensor.matmul(pq[:, hf * 512 : (hf + 1) * 512],
                                       wbuf[:, (oc % 2) * 2048 + c * 128 : (oc % 2) * 2048 + (c + 1) * 128],
                                       xs, start=(c == 0), stop=(c == NCH - 1)), "te")
        ld_w[("qc", oc)][2] = b.cnt["te"]
        if oc + 2 < 4:
            wload("qc", "Wq_c", oc + 2, oc + 2, NCH, "wa" if oc % 2 == 0 else "wb")
        b.wait(nc.scalar, "te")
        b.inc(nc.scalar.activation(hq_sb[:, oc * 1024 : (oc + 1) * 1024], pq, AF.Copy), "se")
    se_hq = b.cnt["se"]
    for oc in range(16):
        if oc < 2:
            b.wait(nc.gpsimd, "te")
        wload("qu", "Wq_up", oc, 4 + oc, 4, "wa" if oc % 2 == 0 else "wb")
        pq = ps_a[:, (oc % 2) * 1024 : ((oc % 2) + 1) * 1024]
        b.wait(nc.tensor, ld_w[("qu", 4 + oc)][0], ld_w[("qu", 4 + oc)][1])
        if oc == 0:
            b.wait(nc.tensor, "se", se_hq)
        else:
            b.wait(nc.tensor, "se")
        for hf in range(2):
            for c in range(4):
                hs = hq_sb[:, c * 1024 + hf * 512 : c * 1024 + (hf + 1) * 512]
                b.inc(nc.tensor.matmul(pq[:, hf * 512 : (hf + 1) * 512],
                                       wbuf[:, (oc % 2) * 2048 + c * 128 : (oc % 2) * 2048 + (c + 1) * 128],
                                       hs, start=(c == 0), stop=(c == 3)), "te")
        ld_w[("qu", 4 + oc)][2] = b.cnt["te"]
        b.wait(nc.scalar, "te")
        b.inc(nc.scalar.activation(q_sb[:, oc * 1024 : (oc + 1) * 1024], pq, AF.Copy), "se")
    se_qpre = b.cnt["se"]
    b.wait(nc.gpsimd, "te")
    wload("kv", "Wk", 0, 20, NCH, "wa")
    wload("kv", "Wv", 0, 21, NCH, "wb")
    b.wait(nc.tensor, ld_w[("kv", 20)][0], ld_w[("kv", 20)][1])
    b.wait(nc.tensor, "se")
    for hf in range(2):
        for c in range(NCH):
            xs = xln[:, c * TLOC + hf * 512 : c * TLOC + (hf + 1) * 512]
            b.inc(nc.tensor.matmul(ps_a[:, hf * 512 : (hf + 1) * 512],
                                   wbuf[:, c * 128 : (c + 1) * 128], xs,
                                   start=(c == 0), stop=(c == NCH - 1)), "te")
    te_k = b.cnt["te"]
    b.wait(nc.tensor, ld_w[("kv", 21)][0], ld_w[("kv", 21)][1])
    for hf in range(2):
        for c in range(NCH):
            xs = xln[:, c * TLOC + hf * 512 : c * TLOC + (hf + 1) * 512]
            b.inc(nc.tensor.matmul(ps_a[:, 1024 + hf * 512 : 1024 + (hf + 1) * 512],
                                   wbuf[:, 2048 + c * 128 : 2048 + (c + 1) * 128], xs,
                                   start=(c == 0), stop=(c == NCH - 1)), "te")
    te_v = b.cnt["te"]
    b.wait(nc.scalar, "te", te_k)
    b.inc(nc.scalar.activation(ksw_f[:, :], ps_a[:, 0:1024], AF.Copy), "se")
    b.wait(nc.scalar, "te", te_v)
    b.inc(nc.scalar.activation(vsw_sb[:, :], ps_a[:, 1024:2048], AF.Copy), "se")
    se_kv = b.cnt["se"]

    wld_c = {}
    for j in range(16):
        par = "wc" if j % 2 == 0 else "wd"
        if j < 2:
            b.wait(nc.gpsimd, "te")
        else:
            b.wait(nc.gpsimd, "te", wld_c[j - 2])
        b.dma(nc.gpsimd, AP(wbuf, (j % 2) * 2048, [[2 * 2048, 128], [1, 2048]]),
              AP(io["W_comp"], j * 128 * 2048, [[2048, 128], [1, 2048]]),
              ctr=par)
        ldv = b.cnt[par]
        b.wait(nc.tensor, par, ldv)
        for dc in range(NCH):
            rhs = AP(xln, dc * TLOC + j, [[NCH * TLOC, 128], [16, 64]])
            b.inc(nc.tensor.matmul(ps_bc[:, 0:64],
                                   wbuf[:, (j % 2) * 2048 + dc * 128 : (j % 2) * 2048 + (dc + 1) * 128],
                                   rhs, start=(j == 0 and dc == 0), stop=(j == 15 and dc == NCH - 1)),
                  "te")
        wld_c[j] = b.cnt["te"]
    te_ck = b.cnt["te"]
    b.wait(nc.scalar, "te", te_ck)
    b.inc(nc.scalar.activation(ck_f[:, :], ps_bc[:, 0:64], AF.Copy), "se")
    se_ck = b.cnt["se"]

    def rope(buf, nblk, blkstride, cosap, sinap, base_ve):
        x1 = AP(buf, 0, [[blkstride * nblk if False else buf.shape[1], 32], [blkstride, nblk], [1, 1024]])
        return None

    b.wait(nc.vector, "se", se_qpre)
    QW = 16 * 1024
    tA = rsc[:, 0:2048]; tB = rsc[:, 2048:4096]; tC = rsc[:, 4096:6144]
    for hp in range(8):
        off = hp * 2048
        x1 = AP(q_sb, off, [[QW, 32], [1024, 2], [1, 1024]])
        x2 = AP(q_sb, 32 * QW + off, [[QW, 32], [1024, 2], [1, 1024]])
        cosr = AP(rcos, 0, [[1024, 32], [0, 2], [1, 1024]])
        sinr = AP(rsin, 0, [[1024, 32], [0, 2], [1, 1024]])
        cosr32 = AP(rcos, 32 * 1024, [[1024, 32], [0, 2], [1, 1024]])
        sinr32 = AP(rsin, 32 * 1024, [[1024, 32], [0, 2], [1, 1024]])
        b.wait(nc.vector, "ve")
        b.inc(nc.vector.tensor_tensor(tA, x1, cosr, ALU.mult), "ve")
        b.inc(nc.vector.tensor_tensor(tB, x1, sinr, ALU.mult), "ve")
        b.inc(nc.vector.tensor_tensor(tC, x2, sinr32, ALU.mult), "ve")
        b.wait(nc.vector, "ve")
        b.inc(nc.vector.tensor_tensor(x1, tA, tC, ALU.subtract), "ve")
        b.wait(nc.vector, "ve")
        b.inc(nc.vector.tensor_tensor(tC, x2, cosr32, ALU.mult), "ve")
        b.wait(nc.vector, "ve")
        b.inc(nc.vector.tensor_tensor(x2, tB, tC, ALU.add), "ve")
    ve_qrope = b.cnt["ve"]
    b.wait(nc.vector, "se", se_kv)
    b.wait(nc.vector, "ve")
    x1 = AP(ksw_f, 0, [[1024, 32], [1, 1024]])
    x2 = AP(ksw_f, 32 * 1024, [[1024, 32], [1, 1024]])
    cos1 = rcos[0:32, :]
    sin1 = rsin[0:32, :]
    cos32 = rcos[32:64, :]
    sin32 = rsin[32:64, :]
    b.inc(nc.vector.tensor_tensor(tA[:, 0:1024], x1, cos1, ALU.mult), "ve")
    b.inc(nc.vector.tensor_tensor(tB[:, 0:1024], x1, sin1, ALU.mult), "ve")
    b.inc(nc.vector.tensor_tensor(tC[:, 0:1024], x2, sin32, ALU.mult), "ve")
    b.wait(nc.vector, "ve")
    b.inc(nc.vector.tensor_tensor(x1, tA[:, 0:1024], tC[:, 0:1024], ALU.subtract), "ve")
    b.wait(nc.vector, "ve")
    b.inc(nc.vector.tensor_tensor(tC[:, 0:1024], x2, cos32, ALU.mult), "ve")
    b.wait(nc.vector, "ve")
    b.inc(nc.vector.tensor_tensor(x2, tB[:, 0:1024], tC[:, 0:1024], ALU.add), "ve")
    ve_krope = b.cnt["ve"]

    def rmsnorm_fm(src_t, out_t, src_col, w_row, width, ve_ready):
        b.wait(nc.scalar, "ve", ve_ready)
        for hf in range((width + 511) // 512):
            w0 = src_col + hf * 512
            w1 = min(src_col + width, w0 + 512)
            n = w1 - w0
            b.wait(nc.scalar, "te")
            b.inc(nc.scalar.activation(sqs[:, 0:n], src_t[:, w0:w1], AF.Square), "se")
            b.wait(nc.tensor, "se")
            b.inc(nc.tensor.matmul(ps_st[:, 0:n], b.tn["ones_col"][:, :],
                                   sqs[:, 0:n], start=True, stop=True), "te")
            te_ss = b.cnt["te"]
            b.wait(nc.vector, "te", te_ss)
            b.inc(nc.vector.tensor_scalar_mul(srow[:, 0:n], ps_st[:, 0:n], 1.0 / 128.0), "ve")
            b.wait(nc.scalar, "ve")
            b.inc(nc.scalar.activation(srow[:, 0:n], srow[:, 0:n], AF.Sqrt,
                                       bias=b.tn["cvals"][0:1, 1:2]), "se")
            b.wait(nc.vector, "se")
            b.inc(nc.vector.reciprocal(srow[:, 0:n], srow[:, 0:n]), "ve")
            ve_r = b.cnt["ve"]
            b.wait(nc.tensor, "ve", ve_r)
            b.inc(nc.tensor.matmul(ps_bc[:, 0:n], w_row, srow[:, 0:n],
                                   start=True, stop=True), "te")
            te_m = b.cnt["te"]
            b.wait(nc.vector, "te", te_m)
            b.inc(nc.vector.tensor_tensor(out_t[:, w0:w1], src_t[:, w0:w1],
                                          ps_bc[:, 0:n], ALU.mult), "ve")
        return b.cnt["ve"]

    last = ve_qrope
    for h in range(16):
        last = rmsnorm_fm(q_sb, q_sb, h * 1024, qnw[:, :], 1024, last)
    ve_qn = last
    ve_kn = rmsnorm_fm(ksw_f, kn_sb, 0, knw[:, :], 1024, ve_krope)
    b.wait(nc.vector, "se", se_ck)
    ve_ckn = rmsnorm_fm(ck_f, ckn_sb, 0, knw[:, :], 64, b.cnt["ve"])

    b.wait(nc.tensor, "se", se_kv)
    for tt in range(8):
        b.inc(nc.tensor.transpose(ps_vt[:, tt * 128 : (tt + 1) * 128],
                                  vsw_sb[:, tt * 128 : (tt + 1) * 128], identb[:, :]), "te")
    te_vt = b.cnt["te"]
    b.wait(nc.vector, "te", te_vt)
    b.inc(nc.vector.tensor_copy(vswtm_sb[:, :], ps_vt[:, :]), "ve")
    ve_vt = b.cnt["ve"]

    b.wait(nc.sync, "ve", ve_kn)
    b.dma(nc.sync, AP(ag_in, 0, [[1024, 128], [1, 1024]]), kn_sb[:, :], ctr="st")
    b.wait(nc.sync, "ve", ve_vt)
    b.dma(nc.sync, AP(ag_in, 131072, [[1024, 128], [1, 1024]]), vswtm_sb[:, :], ctr="st")
    b.wait(nc.sync, "ve", ve_ckn)
    b.dma(nc.sync, AP(ag_in, 262144, [[64, 128], [1, 64]]), ckn_sb[:, :], ctr="st")
    b.wait(nc.gpsimd, "st")
    b.inc(nc.gpsimd.collective_compute(
        "AllGather", ALU.bypass, replica_groups=[[0, 1, 2, 3], [4, 5, 6, 7]],
        ins=[ag_in.ap().opt()], outs=[ag_out.ap().opt()]), "cc")
    cc2 = b.cnt["cc"]
    b.wait(nc.vector, "ve", ve_qn)
    b.pop()

    b.push()
    b.fence()
    kswp = b.sb("a1_kswp", [128, 2 * 1024], BF16)
    vswp = b.sb("a1_vswp", [128, 2 * 1024], BF16)
    cka = b.sb("a1_cka", [128, 256], BF16)
    cktm = b.sb("a1_cktm", [128, 2 * 128], BF16)
    cmask = b.sb("a1_cmask", [128, 8 * 256], F32)
    swmr = b.sb("a1_swmr", [1, 8 * 640], F32)
    swmd = b.sb("a1_swmd", [128, 128], F32)
    swmo = b.sb("a1_swmo", [128, 128], F32)
    sinkpack = b.sb("a1_sinkpack", [128, 64], F32)
    spr = b.sb("a1_spr", [1, 64], F32)
    tsw = b.sb("a1_tsw", [128, 640], F32)
    ptile = b.sb("a1_ptile", [128, 640 + 256], BF16)
    pts = b.sb("a1_pts", [128, 7 * 128], BF16)
    zcol = b.sb("a1_zcol", [128, 8], F32)

    ps_sw = b.ps("a1_pssw", [128, 640], F32)
    ps_c = b.ps("a1_psc", [128, 256], F32)
    ps_o = b.ps("a1_pso", [128, 128], F32)
    ps_pt = b.ps("a1_pspt", [128, 7 * 128], BF16)
    ps_sk = b.ps("a1_pssk", [128, 64], F32)

    b.dma(nc.sync, cmask[:, :], AP(io["c_masks"], 0, [[256, 128], [128 * 256, 8], [1, 256]]))
    b.dma(nc.sync, swmr[:, :], io["swm_row"][:, :])
    b.dma(nc.sync, swmd[:, :], io["sw_mask_diag"][:, :])
    b.dma(nc.sync, swmo[:, :], io["sw_mask_old"][:, :])
    ld_m = b.cnt["ld"]

    regs = [b.scopes[-1].enter_context(nc.gpsimd.register(f"a1r{i}")) for i in range(4)]
    b.wait(nc.gpsimd, "ld", ld_small)
    for i in range(4):
        nc.gpsimd.reg_load(regs[i], a2o[0:1, i : i + 1])
    b.wait(nc.gpsimd, "cc", cc2)
    dst = [kswp[:, 0:1024], vswp[:, 0:1024], kswp[:, 1024:2048], vswp[:, 1024:2048]]
    for i in range(4):
        b.dma(nc.gpsimd, dst[i], AP(ag_out, regs[i], [[1024, 128], [1, 1024]]), ctr="lg")
    for r2 in range(4):
        b.dma(nc.sync, cka[:, r2 * 64 : (r2 + 1) * 64],
              AP(ag_out, r2 * 1056 * 256 + 262144, [[64, 128], [1, 64]]), ctr="ha")
    lg_kv = b.cnt["lg"]
    wc_ck = b.cnt["ha"]

    b.wait(nc.tensor, "ha", wc_ck)
    for tt in range(2):
        b.inc(nc.tensor.transpose(ps_pt[:, tt * 128 : (tt + 1) * 128],
                                  cka[:, tt * 128 : (tt + 1) * 128], identb[:, :]), "te")
    te_cktm = b.cnt["te"]
    b.wait(nc.vector, "te", te_cktm)
    b.inc(nc.vector.tensor_copy(cktm[:, :], AP(ps_pt, 0, [[896, 128], [1, 256]])), "ve")
    ve_cktm = b.cnt["ve"]

    b.wait(nc.vector, "ld", ld_small)
    b.inc(nc.vector.tensor_scalar_max(spr[:, 48:64], sinkr[:, :], 0.0), "ve")
    b.wait(nc.vector, "ve")
    b.inc(nc.vector.tensor_tensor(spr[:, 16:32], sinkr[:, :], spr[:, 48:64], ALU.subtract), "ve")
    b.inc(nc.vector.tensor_scalar_mul(spr[:, 0:16], spr[:, 48:64], -1.0), "ve")
    b.wait(nc.scalar, "ve")
    b.inc(nc.scalar.activation(spr[:, 16:32], spr[:, 16:32], AF.Exp), "se")
    b.wait(nc.vector, "ve")
    b.inc(nc.vector.tensor_scalar_mul(spr[:, 32:48], sinkr[:, :], -1.0), "ve")
    b.wait(nc.vector, "ve")
    b.inc(nc.vector.tensor_copy(spr[:, 48:64], sinkr[:, :]), "ve")
    ve_spr = b.cnt["ve"]
    b.wait(nc.tensor, "ve", ve_spr)
    b.wait(nc.tensor, "se")
    b.inc(nc.tensor.matmul(ps_sk[:, :], onesr[:, :], spr[:, :], start=True, stop=True), "te")
    b.wait(nc.scalar, "te")
    b.inc(nc.scalar.activation(sinkpack[:, :], ps_sk[:, :], AF.Copy), "se")
    se_sink = b.cnt["se"]

    b.wait(nc.tensor, "lg", lg_kv)
    b.wait(nc.tensor, "ld", ld_m)
    b.wait(nc.tensor, "se", se_sink)
    b.wait(nc.tensor, "ve", ve_cktm)
    for h in range(16):
        for qt in range(8):
            qtile = q_sb[:, h * 1024 + qt * 128 : h * 1024 + (qt + 1) * 128]
            w0 = 512 + 128 * qt
            b.wait(nc.tensor, "se")
            b.inc(nc.tensor.matmul(ps_sw[:, 0:512], onesr[:, :],
                                   swmr[0:1, qt * 640 : qt * 640 + 512],
                                   start=True, stop=False, skip_group_check=True), "te")
            b.inc(nc.tensor.matmul(ps_sw[:, 512:640], onesr[:, :],
                                   swmr[0:1, qt * 640 + 512 : (qt + 1) * 640],
                                   start=True, stop=False, skip_group_check=True), "te")
            b.inc(nc.tensor.matmul(ps_sw[:, 0:512], qtile,
                                   AP(kswp, w0, [[2048, 128], [1, 512]]),
                                   start=False, stop=True, skip_group_check=True), "te")
            b.inc(nc.tensor.matmul(ps_sw[:, 512:640], qtile,
                                   AP(kswp, w0 + 512, [[2048, 128], [1, 128]]),
                                   start=False, stop=True, skip_group_check=True), "te")
            te_sws = b.cnt["te"]
            b.wait(nc.scalar, "te", te_sws)
            b.wait(nc.scalar, "ve")
            b.inc(nc.scalar.activation(tsw[:, :], ps_sw[:, :], AF.Tanh,
                                       scale=SC / 50.0), "se")
            te_tanh = b.cnt["se"]
            b.wait(nc.vector, "se", te_tanh)
            b.inc(nc.vector.tensor_tensor(tsw[:, 0:128], tsw[:, 0:128], swmo[:, :], ALU.add), "ve")
            b.inc(nc.vector.tensor_tensor(tsw[:, 512:640], tsw[:, 512:640], swmd[:, :], ALU.add), "ve")
            ve_msk = b.cnt["ve"]
            b.wait(nc.scalar, "ve", ve_msk)
            b.inc(nc.scalar.activation(ptile[:, 0:640], tsw[:, :], AF.Exp, scale=50.0,
                                       bias=sinkpack[:, h : h + 1],
                                       accum_out=zcol[:, 0:1]), "se")
            se_swexp = b.cnt["se"]
            b.wait(nc.tensor, "ve")
            b.inc(nc.tensor.matmul(ps_c[:, :], qtile, cka[:, :],
                                   start=True, stop=True, skip_group_check=True), "te")
            te_cs = b.cnt["te"]
            b.wait(nc.vector, "te", te_cs)
            b.inc(nc.vector.tensor_tensor(ps_c[:, :], ps_c[:, :],
                                          cmask[:, qt * 256 : (qt + 1) * 256], ALU.add), "ve")
            b.wait(nc.vector, "ve")
            b.inc(nc.vector.tensor_reduce(zcol[:, 2:3], ps_c[:, :], AX.X, ALU.max,
                                          negate=True), "ve")
            b.wait(nc.vector, "ve")
            b.inc(nc.vector.tensor_scalar(zcol[:, 3:4], zcol[:, 2:3], SC,
                                          sinkpack[:, 32 + h : 33 + h], ALU.mult, ALU.min), "ve")
            ve_cbias = b.cnt["ve"]
            b.wait(nc.scalar, "ve", ve_cbias)
            b.inc(nc.scalar.activation(ptile[:, 640:896], ps_c[:, :], AF.Exp, scale=SC,
                                       bias=zcol[:, 3:4], accum_out=zcol[:, 4:5]), "se")
            se_cexp = b.cnt["se"]
            b.wait(nc.scalar, "se", se_cexp)
            b.inc(nc.scalar.activation(zcol[:, 5:6], zcol[:, 3:4], AF.Exp,
                                       bias=sinkpack[:, 48 + h : 49 + h]), "se")
            se_csink = b.cnt["se"]
            b.wait(nc.vector, "se", se_csink)
            b.inc(nc.vector.tensor_tensor(zcol[:, 4:5], zcol[:, 4:5], zcol[:, 5:6], ALU.add), "ve")
            b.inc(nc.vector.tensor_scalar(zcol[:, 1:2], zcol[:, 0:1],
                                          sinkpack[:, 16 + h : 17 + h], None, ALU.add), "ve")
            b.wait(nc.vector, "ve")
            b.inc(nc.vector.reciprocal(zcol[:, 6:7], zcol[:, 1:2]), "ve")
            b.inc(nc.vector.reciprocal(zcol[:, 7:8], zcol[:, 4:5]), "ve")
            b.wait(nc.vector, "ve")
            b.inc(nc.vector.tensor_scalar(ptile[:, 0:640], ptile[:, 0:640],
                                          zcol[:, 6:7], None, ALU.mult), "ve")
            b.inc(nc.vector.tensor_scalar(ptile[:, 640:896], ptile[:, 640:896],
                                          zcol[:, 7:8], None, ALU.mult), "ve")
            ve_pn = b.cnt["ve"]
            b.wait(nc.tensor, "ve", ve_pn)
            for tt in range(7):
                b.inc(nc.tensor.transpose(ps_pt[:, tt * 128 : (tt + 1) * 128],
                                          ptile[:, tt * 128 : (tt + 1) * 128], identb[:, :]),
                      "te")
            te_pt = b.cnt["te"]
            b.wait(nc.vector, "te", te_pt)
            b.inc(nc.vector.tensor_copy(pts[:, :], AP(ps_pt, 0, [[896, 128], [1, 896]])), "ve")
            ve_pts = b.cnt["ve"]
            b.wait(nc.tensor, "ve", ve_pts)
            for k in range(5):
                vt = AP(vswp, (4 + qt + k) * 128, [[2048, 128], [1, 128]])
                b.inc(nc.tensor.matmul(ps_o[:, :], vt, pts[:, k * 128 : (k + 1) * 128],
                                       start=(k == 0), stop=False, skip_group_check=True), "te")
            for tt in range(2):
                b.inc(nc.tensor.matmul(ps_o[:, :], cktm[:, tt * 128 : (tt + 1) * 128],
                                       pts[:, (5 + tt) * 128 : (6 + tt) * 128],
                                       start=False, stop=(tt == 1), skip_group_check=True), "te")
            te_av = b.cnt["te"]
            b.wait(nc.vector, "te", te_av)
            b.inc(nc.vector.tensor_copy(
                xln[:, h * 1024 + qt * 128 : h * 1024 + (qt + 1) * 128], ps_o[:, :]), "ve")
    ao_done = b.cnt["ve"]
    b.pop()
    b.fence()
    b.pop()

    b.push()
    b.fence()
    inter = b.sb("a1_inter", [128, 64 * 512], BF16)
    wbufc = b.sb("a1_wbufc", [128, 2 * 512], BF16)
    wbufo = b.sb("a1_wbufo", [128, 2 * 4096], BF16)
    ps_i = b.ps("a1_psi", [128, 2 * 512], F32)
    ps_r = b.ps("a1_psr", [128, 2 * 512], F32)

    ldc = {}
    ldo = {}
    for th in range(2):
        for ic in range(64):
            g = ic // 16
            par = "wa" if ic % 2 == 0 else "wb"
            key = th * 64 + ic
            if th == 0 and ic < 2:
                pass
            if key >= 2:
                b.wait(nc.gpsimd, "te", ldc[key - 2][1])
            v = b.dma(nc.gpsimd,
                      AP(wbufc, (ic % 2) * 512, [[2 * 512, 128], [1, 512]]),
                      AP(io["Wg"], ic * 128 * 512, [[512, 128], [1, 512]]),
                      ctr=par)
            ldc[key] = [(par, v), None]
            b.wait(nc.tensor, par, v)
            pi = ps_i[:, (ic % 2) * 512 : ((ic % 2) + 1) * 512]
            if key >= 2:
                b.wait(nc.tensor, "se", ldc[key - 2][2] if len(ldc[key - 2]) > 2 else 0)
            for c in range(4):
                aos = xln[:, (g * 4 + c) * TLOC + th * 512 : (g * 4 + c) * TLOC + (th + 1) * 512]
                b.inc(nc.tensor.matmul(pi, wbufc[:, (ic % 2) * 512 + c * 128 :
                                                 (ic % 2) * 512 + (c + 1) * 128],
                                       aos, start=(c == 0), stop=(c == 3)), "te")
            ldc[key][1] = b.cnt["te"]
            b.wait(nc.scalar, "te", ldc[key][1])
            se_i = b.inc(nc.scalar.activation(inter[:, ic * 512 : (ic + 1) * 512], pi,
                                              AF.Copy), "se")
            ldc[key].append(se_i)
        se_inter = b.cnt["se"]
        b.wait(nc.tensor, "se", se_inter)
        for oc in range(16):
            for half in range(2):
                key = (th * 16 + oc) * 2 + half
                par = "wc" if key % 2 == 0 else "wd"
                if th == 0 and oc == 0:
                    prevk = None
                if key >= 2 and (key - 2) in ldo:
                    b.wait(nc.gpsimd, "te", ldo[key - 2][1])
                v = b.dma(nc.gpsimd,
                          AP(wbufo, (key % 2) * 4096, [[2 * 4096, 128], [1, 4096]]),
                          AP(io["Wo"], (oc * 2 + half) * 128 * 4096, [[4096, 128], [1, 4096]]),
                          ctr=par)
                ldo[key] = [(par, v), None]
            pr = ps_r[:, (oc % 2) * 512 : ((oc % 2) + 1) * 512]
            if oc >= 2 or th == 1:
                b.wait(nc.tensor, "ve")
            for half in range(2):
                key = (th * 16 + oc) * 2 + half
                b.wait(nc.tensor, ldo[key][0][0], ldo[key][0][1])
                for c in range(32):
                    iis = inter[:, (half * 32 + c) * 512 : (half * 32 + c + 1) * 512]
                    b.inc(nc.tensor.matmul(pr, wbufo[:, (key % 2) * 4096 + c * 128 :
                                                     (key % 2) * 4096 + (c + 1) * 128],
                                           iis, start=(half == 0 and c == 0),
                                           stop=(half == 1 and c == 31)), "te")
                ldo[key][1] = b.cnt["te"]
            b.wait(nc.vector, "te")
            xs_out = x[:, oc * TLOC + th * 512 : oc * TLOC + (th + 1) * 512]
            b.inc(nc.vector.tensor_tensor(xs_out, xs_out, pr, ALU.add), "ve")
    b.pop()
    b.pop()



def emit_store(b, io_out):
    nc = b.nc
    x = b.tn["x_res"]
    b.fence()
    for c in range(NCH):
        b.dma(nc.sync, AP(io_out, c * 128 * TLOC, [[TLOC, 128], [1, TLOC]]),
              x[:, c * TLOC : (c + 1) * TLOC], ctr="st")
    b.wait(nc.sync, "st")



PHASE_ORDER = ["ffn0", "attn0", "ffn0", "ffn1", "attn1", "ffn1"]
FFN1_NIC = 28
FFN1_SECT = 9
OL_EXPERTS = list(range(8))


def build_program(phases):
    nc = bass.Bass("TRN2", target_bir_lowering=False)
    io = {}
    for name, shape, dt in ALL_INPUTS:
        io[name] = nc.dram_tensor(name, list(shape), dt, kind="ExternalInput")
    y = nc.dram_tensor("y", [2048, 1024], F32, kind="ExternalOutput")
    with contextlib.ExitStack() as stack:
        b = B(nc, stack)
        with nc.Block() as blk:
            @blk.gpsimd
            def _(g):
                emit_load(b, io)
                for ph in phases:
                    emit_ln(b)
                    if ph == "ffn0":
                        emit_ffn0(b, io)
                    elif ph == "attn0":
                        emit_attn0(b, io)
                    elif ph == "ffn1":
                        emit_ffn1(b, io)
                    elif ph == "attn1":
                        emit_attn1(b, io)
                    else:
                        raise ValueError(ph)
                emit_store(b, y)
    return nc


def run_phases(in_maps, phase_groups, trace=False):
    cur = [m["x"] for m in in_maps]
    res = None
    for grp in phase_groups:
        nc = build_program(grp)
        maps = []
        for c in range(NCORES):
            m = dict(in_maps[c])
            m["x"] = cur[c]
            maps.append(m)
        res = run_bass_kernel_spmd(nc, maps, core_ids=list(range(NCORES)), trace=trace)
        cur = [np.asarray(res.results[c]["y"]) for c in range(NCORES)]
    return cur, res


def assemble(cur):
    out = np.zeros((2, 4096, 2048), np.float32)
    for c in range(NCORES):
        bidx = c // 4
        t0 = 1024 * (c % 4)
        out[bidx, t0 : t0 + TLOC, :] = cur[c].T
    return out


def kernel(**inputs):
    in_maps = prep_host(inputs)
    cur, _ = run_phases(in_maps, [PHASE_ORDER])
    return assemble(cur)


# version 03
# speedup vs baseline: 1.0792x; 1.0699x over previous
import contextlib
import numpy as np
import ml_dtypes
import concourse.bass as bass
import concourse.mybir as mybir
from concourse.bass_utils import run_bass_kernel_spmd

F32 = mybir.dt.float32
BF16 = mybir.dt.bfloat16
AF = mybir.ActivationFunctionType
ALU = mybir.AluOpType
AX = mybir.AxisListType
AP = bass.AP

NCORES = 8
TLOC = 1024
D = 2048
NCH = D // 128
NEG = -1e30
BF = ml_dtypes.bfloat16


def _cc(a):
    return np.ascontiguousarray(a)



def prep_host(inputs):
    g = {}
    def colblocks(W, bs=128):
        K_, N_ = W.shape
        nk, nb = K_ // 128, N_ // bs
        return _cc(W.reshape(nk, 128, nb, bs).transpose(2, 1, 0, 3).reshape(nb, 128, nk * bs))

    def rowblocks(W, rs):
        K_, N_ = W.shape
        ns, nk = K_ // rs, rs // 128
        return _cc(W.reshape(ns, nk, 128, N_).transpose(0, 2, 1, 3).reshape(ns, 128, nk * N_))

    g["w_attn"] = colblocks(inputs["L0_attn_w_attn"])
    g["b_attn"] = _cc(inputs["L0_attn_b_attn"].reshape(48, 128).T)
    g["w_proj"] = colblocks(inputs["L0_attn_w_proj"])
    g["b_proj"] = _cc(inputs["L0_attn_b_proj"].reshape(16, 128).T)
    g["f0_bwT"] = colblocks(_cc(inputs["L0_ffn_base_w"].T))
    g["f0_bb"] = _cc(inputs["L0_ffn_base_b"].reshape(16, 128).T)
    g["f0_rwT"] = colblocks(_cc(inputs["L0_ffn_router_w"].T), bs=8)[0]
    g["f0_rb"] = _cc(inputs["L0_ffn_router_b"].reshape(8, 1))
    g["f0_Acat"] = colblocks(inputs["L0_ffn_lora_A"].transpose(1, 0, 2).reshape(2048, 128))[0]
    g["f0_Bcat"] = _cc(inputs["L0_ffn_lora_B"].reshape(128, 2048))
    g["Wq_c"] = colblocks(inputs["L1_attn_Wq_c"])
    g["Wq_up"] = colblocks(inputs["L1_attn_Wq_up"])
    g["Wk"] = colblocks(inputs["L1_attn_Wk"])[0]
    g["Wv"] = colblocks(inputs["L1_attn_Wv"])[0]
    g["W_comp"] = rowblocks(inputs["L1_attn_W_comp"], 2048)
    g["qnw"] = _cc(inputs["L1_attn_q_norm_w"].reshape(1, 128))
    g["knw"] = _cc(inputs["L1_attn_k_norm_w"].reshape(1, 128))
    g["sink"] = _cc(inputs["L1_attn_sink"].reshape(1, 16))
    Wgf = inputs["L1_attn_Wg"].reshape(4 * 512, 2048)
    wg = np.zeros((64, 128, 512), np.float32)
    for g_ in range(4):
        blk = colblocks(_cc(Wgf[g_ * 512 : (g_ + 1) * 512, :]))
        wg[g_ * 16 : (g_ + 1) * 16] = blk
    g["Wg"] = _cc(wg)
    Wo = inputs["L1_attn_Wo"]
    wo = np.zeros((32, 128, 4096), np.float32)
    for oc in range(16):
        for half in range(2):
            sl = _cc(Wo[half * 4096 : (half + 1) * 4096, oc * 128 : (oc + 1) * 128])
            wo[oc * 2 + half] = rowblocks(sl, 4096)[0]
    g["Wo"] = _cc(wo)
    g["wealth"] = _cc(inputs["L1_ffn_expert_wealth"].reshape(8, 1))
    g["conf_WT"] = colblocks(_cc(inputs["L1_ffn_conf_W"].T), bs=8)[0]
    g["conf_b"] = _cc(inputs["L1_ffn_conf_b"].reshape(8, 1))
    g["base_gate"] = colblocks(inputs["L1_ffn_base_gate"], bs=256)
    g["base_up"] = colblocks(inputs["L1_ffn_base_up"], bs=256)
    g["base_down"] = rowblocks(inputs["L1_ffn_base_down"], 256)
    g["gateAcat"] = colblocks(inputs["L1_ffn_gate_A"].transpose(1, 0, 2).reshape(2048, 512), bs=256)
    g["upAcat"] = colblocks(inputs["L1_ffn_up_A"].transpose(1, 0, 2).reshape(2048, 512), bs=256)
    def bpack(Bm):
        a = Bm.reshape(4, 2, 64, 28, 256).transpose(3, 1, 2, 0, 4).reshape(28, 2, 64, 1024)
        return _cc(a)
    g["gate_B"] = bpack(inputs["L1_ffn_gate_B"].reshape(4, 2, 64, 7168))
    g["up_B"] = bpack(inputs["L1_ffn_up_B"].reshape(4, 2, 64, 7168))
    dA = inputs["L1_ffn_down_A"]
    da = dA.reshape(8, 28, 2, 128, 64).transpose(1, 3, 0, 2, 4).reshape(28, 128, 8 * 128)
    g["down_A"] = _cc(da)
    g["down_B"] = _cc(inputs["L1_ffn_down_B"])

    g["ident_bf"] = np.eye(128, dtype=BF)
    g["ident_f32"] = np.eye(128, dtype=np.float32)
    g["ones_col"] = np.ones((128, 1), np.float32)
    g["ones_row"] = np.ones((1, 128), np.float32)
    g["ones_col_bf"] = np.ones((128, 1), BF)
    emat = np.zeros((8, 128), BF)
    for e in range(8):
        emat[e, 16 * e : 16 * e + 16] = 1
    g["emat"] = emat
    emat64 = np.zeros((8, 512), BF)
    for e in range(8):
        emat64[e, 64 * e : 64 * e + 64] = 1
    g["emat64"] = emat64
    a0m = np.zeros((128, 4, 512), np.float32)
    for i, Doff in enumerate([0, -128, -256, -384]):
        ss = np.arange(128)[:, None]
        qq = np.arange(512)[None, :]
        a0m[:, i, :] = np.where(qq + Doff - ss >= 0, 0.0, NEG)
    g["attn0_masks"] = _cc(a0m.reshape(128, 4 * 512))
    qq = np.arange(128)[:, None]
    ss = np.arange(128)[None, :]
    g["sw_mask_diag"] = _cc(np.where(ss <= qq, 0.0, -1e28).astype(np.float32))
    g["sw_mask_old"] = _cc(np.where(ss > qq, 0.0, -1e28).astype(np.float32))
    er = np.zeros((8, 8 * 128), np.float32)
    for e in range(8):
        er[e, e * 128 : (e + 1) * 128] = 1.0
    g["erow"] = er
    cv = np.zeros((128, 8), np.float32)
    cv[:, 0] = 1e-5
    cv[:, 1] = 1e-6
    cv[:, 2] = 1e-30
    g["cvals"] = cv

    rope = 64
    theta = 10000.0
    inv = 1.0 / theta ** (np.arange(0, rope, 2, dtype=np.float32) / rope)
    ang = np.arange(4096, dtype=np.float32)[:, None] * inv[None, :]
    cos_all = np.cos(ang).astype(np.float32)
    sin_all = np.sin(ang).astype(np.float32)

    in_maps = []
    for c in range(NCORES):
        bidx = c // 4
        t0 = 1024 * (c % 4)
        m = dict(g)
        x = inputs["L0_attn_x"][bidx, t0 : t0 + TLOC, :]
        m["x"] = _cc(np.asarray(x, np.float32).T)
        ct = cos_all[t0 : t0 + TLOC].T
        st_ = sin_all[t0 : t0 + TLOC].T
        m["rcos"] = _cc(np.concatenate([ct, ct], axis=0))
        m["rsin"] = _cc(np.concatenate([st_, st_], axis=0))
        cm = np.zeros((8, 128, 256), np.float32)
        for qt in range(8):
            q_global = t0 + qt * 128 + np.arange(128)
            blk_end = np.arange(256) * 16 + 15
            cm[qt] = np.where(blk_end[None, :] < q_global[:, None], 0.0, NEG)
        m["c_masks"] = _cc(cm.reshape(8 * 128, 256))
        r = c % 4
        oth = [(r + 1 + j) % 4 for j in range(3)]
        bo = np.zeros((1, 8), np.int32)
        bo[0, 0] = (4096 * r) * 1024
        bo[0, 1] = (4096 * r + 2048) * 1024
        for j in range(3):
            bo[0, 2 + 2 * j] = (4096 * oth[j]) * 1024
            bo[0, 3 + 2 * j] = (4096 * oth[j] + 2048) * 1024
        m["broff"] = bo
        mc = np.zeros((128, 24), np.float32)
        for j in range(3):
            val = 0.0 if oth[j] < r else NEG
            mc[:, j * 8 : (j + 1) * 8] = val
        m["mcol"] = _cc(mc)
        prev = (r - 1) % 4
        a2 = np.zeros((1, 4), np.int32)
        a2[0, 0] = prev * 1056 * 256
        a2[0, 1] = prev * 1056 * 256 + 131072
        a2[0, 2] = r * 1056 * 256
        a2[0, 3] = r * 1056 * 256 + 131072
        m["a2off"] = a2
        swm = np.zeros((8, 640), np.float32)
        if r == 0:
            for qt in range(8):
                swm[qt, 0 : 128 * (4 - qt) if qt < 4 else 0] = -1e6
        m["swm_row"] = _cc(swm.reshape(1, 8 * 640))
        in_maps.append(m)
    return in_maps


ALL_INPUTS = [
    ("x", (2048, 1024), F32),
    ("w_attn", (48, 128, 2048), F32), ("b_attn", (128, 48), F32),
    ("w_proj", (16, 128, 2048), F32), ("b_proj", (128, 16), F32),
    ("f0_bwT", (16, 128, 2048), F32), ("f0_bb", (128, 16), F32),
    ("f0_rwT", (128, 128), F32), ("f0_rb", (8, 1), F32),
    ("f0_Acat", (128, 2048), F32), ("f0_Bcat", (128, 2048), F32),
    ("Wq_c", (4, 128, 2048), F32), ("Wq_up", (16, 128, 512), F32),
    ("Wk", (128, 2048), F32), ("Wv", (128, 2048), F32),
    ("W_comp", (16, 128, 2048), F32),
    ("qnw", (1, 128), F32), ("knw", (1, 128), F32), ("sink", (1, 16), F32),
    ("Wg", (64, 128, 512), F32), ("Wo", (32, 128, 4096), F32),
    ("wealth", (8, 1), F32), ("conf_WT", (128, 128), F32), ("conf_b", (8, 1), F32),
    ("base_gate", (28, 128, 4096), F32), ("base_up", (28, 128, 4096), F32),
    ("base_down", (28, 128, 4096), F32),
    ("gateAcat", (2, 128, 4096), F32), ("upAcat", (2, 128, 4096), F32),
    ("gate_B", (28, 2, 64, 1024), F32), ("up_B", (28, 2, 64, 1024), F32),
    ("down_A", (28, 128, 1024), F32), ("down_B", (8, 64, 2048), F32),
    ("ident_bf", (128, 128), BF16), ("ident_f32", (128, 128), F32),
    ("ones_col", (128, 1), F32), ("ones_row", (1, 128), F32),
    ("ones_col_bf", (128, 1), BF16),
    ("emat", (8, 128), BF16), ("emat64", (8, 512), BF16),
    ("attn0_masks", (128, 4 * 512), F32),
    ("sw_mask_diag", (128, 128), F32), ("sw_mask_old", (128, 128), F32),
    ("rcos", (64, 1024), F32), ("rsin", (64, 1024), F32),
    ("c_masks", (8 * 128, 256), F32),
    ("cvals", (128, 8), F32),
    ("broff", (1, 8), mybir.dt.int32),
    ("erow", (8, 8 * 128), F32),
    ("a2off", (1, 4), mybir.dt.int32),
    ("swm_row", (1, 8 * 640), F32),
    ("mcol", (128, 24), F32),
]



class B:
    def __init__(self, nc, stack):
        self.nc = nc
        self.stack = stack
        self.scopes = [stack]
        self.sem = {}
        self.cnt = {}
        for name in ["ld", "lg", "st", "te", "ve", "se", "po", "cc",
                     "wa", "wb", "wc", "wd", "sa", "sb", "wx", "wy", "wz",
                     "ha", "hb", "hc", "hd"]:
            self.sem[name] = stack.enter_context(nc.semaphore(name))
            self.cnt[name] = 0
        self.tn = {}
        self.uid = 0

    def push(self):
        s = contextlib.ExitStack()
        self.scopes.append(s)
        return s

    def pop(self):
        s = self.scopes.pop()
        s.close()

    def name(self, base):
        self.uid += 1
        return f"{base}_{self.uid}"

    def inc(self, inst, ctr, n=1):
        inst.then_inc(self.sem[ctr], n)
        self.cnt[ctr] += n
        return self.cnt[ctr]

    def dma(self, eng, out, in_, ctr="ld"):
        if eng is self.nc.gpsimd and ctr == "ld":
            ctr = "lg"
        return self.inc(eng.dma_start(out=out, in_=in_), ctr, 16)

    def wait(self, eng, ctr, val=None):
        v = self.cnt[ctr] if val is None else val
        if v > 0:
            eng.wait_ge(self.sem[ctr], v)

    def sb(self, base, shape, dtype, persistent=False):
        s = self.stack if persistent else self.scopes[-1]
        t = s.enter_context(self.nc.sbuf_tensor(self.name(base), shape, dtype))
        if persistent:
            self.tn[base] = t
        return t

    def ps(self, base, shape, dtype):
        return self.scopes[-1].enter_context(
            self.nc.psum_tensor(self.name(base), shape, dtype))

    def fence(self):
        nc = self.nc
        for eng in [nc.tensor, nc.vector, nc.scalar, nc.gpsimd, nc.sync]:
            for ctr in ["te", "ve", "se", "po", "ld", "lg", "st", "cc",
                        "wa", "wb", "wc", "wd", "sa", "sb", "wx", "wy", "wz",
                     "ha", "hb", "hc", "hd"]:
                self.wait(eng, ctr)


def ap2(t, col, pat):
    rowlen = int(np.prod(t.shape[1:])) if hasattr(t, "shape") else None
    return AP(t, col, [[rowlen, t.shape[0]]] + pat)



def emit_load(b, io):
    nc = b.nc
    x = b.sb("x_res", [128, NCH * TLOC], F32, persistent=True)
    b.sb("x_ln", [128, NCH * TLOC], BF16, persistent=True)
    for c in range(NCH):
        b.dma(nc.sync, x[:, c * TLOC : (c + 1) * TLOC],
              AP(io["x"], c * 128 * TLOC, [[TLOC, 128], [1, TLOC]]))
    for nm, shape, dt in [("ident_bf", [128, 128], BF16), ("ident_f32", [128, 128], F32),
                          ("ones_col", [128, 1], F32), ("ones_row", [1, 128], F32),
                          ("ones_col_bf", [128, 1], BF16),
                          ("emat", [8, 128], BF16),
                          ("cvals", [128, 8], F32)]:
        t = b.sb(nm, shape, dt, persistent=True)
        b.dma(nc.sync, t[:, :], io[nm][:, :])
    b.load_done = b.cnt["ld"]



def emit_ln(b):
    nc = b.nc
    x = b.tn["x_res"]
    xln = b.tn["x_ln"]
    b.push()
    ps_sx = b.ps("ps_sx", [1, 1024], F32)
    ps_sxx = b.ps("ps_sxx", [1, 1024], F32)
    ps_bc = b.ps("ps_bc", [128, 2048], F32)
    sq = b.sb("sq", [128, 2 * 1024], F32)
    sm = b.sb("lnsm", [1, 8 * 1024], F32)
    mean = sm[:, 0:1024]; ex2 = sm[:, 1024:2048]; m2 = sm[:, 2048:3072]
    var = sm[:, 3072:4096]; std = sm[:, 4096:5120]; rstd = sm[:, 5120:6144]
    bmr = sm[:, 6144:7168]
    ones = b.tn["ones_col"]
    onesr = b.tn["ones_row"]

    b.fence()
    sq_guard = {}
    for c in range(NCH):
        buf = sq[:, (c % 2) * 1024 : ((c % 2) + 1) * 1024]
        if c >= 2:
            b.wait(nc.scalar, "te", sq_guard[c - 2])
        b.inc(nc.scalar.activation(buf, x[:, c * TLOC : (c + 1) * TLOC], AF.Square), "se")
        b.wait(nc.tensor, "se")
        for hf in range(2):
            xs = x[:, c * TLOC + hf * 512 : c * TLOC + (hf + 1) * 512]
            b.inc(nc.tensor.matmul(ps_sx[:, hf * 512 : (hf + 1) * 512], ones[:, :], xs,
                                   start=(c == 0), stop=(c == NCH - 1)), "te")
            qs = buf[:, hf * 512 : (hf + 1) * 512]
            b.inc(nc.tensor.matmul(ps_sxx[:, hf * 512 : (hf + 1) * 512], ones[:, :], qs,
                                   start=(c == 0), stop=(c == NCH - 1)), "te")
        sq_guard[c] = b.cnt["te"]

    b.wait(nc.vector, "te")
    b.inc(nc.vector.tensor_scalar_mul(mean, ps_sx[:, :], 1.0 / D), "ve")
    b.inc(nc.vector.tensor_scalar_mul(ex2, ps_sxx[:, :], 1.0 / D), "ve")
    b.wait(nc.vector, "ve")
    b.inc(nc.vector.tensor_tensor(m2, mean, mean, ALU.mult), "ve")
    b.wait(nc.vector, "ve")
    b.inc(nc.vector.tensor_tensor(var, ex2, m2, ALU.subtract), "ve")
    b.wait(nc.scalar, "ve")
    b.inc(nc.scalar.activation(std, var, AF.Sqrt, bias=b.tn["cvals"][0:1, 0:1]), "se")
    b.wait(nc.vector, "se")
    b.inc(nc.vector.reciprocal(rstd, std), "ve")
    b.wait(nc.vector, "ve")
    b.inc(nc.vector.scalar_tensor_tensor(bmr, mean, -1.0, rstd, ALU.mult, ALU.mult), "ve")

    b.wait(nc.tensor, "ve")
    for hf in range(2):
        b.inc(nc.tensor.matmul(ps_bc[:, hf * 512 : (hf + 1) * 512], onesr[:, :],
                               rstd[:, hf * 512 : (hf + 1) * 512], start=True, stop=True), "te")
        b.inc(nc.tensor.matmul(ps_bc[:, 1024 + hf * 512 : 1024 + (hf + 1) * 512], onesr[:, :],
                               bmr[:, hf * 512 : (hf + 1) * 512], start=True, stop=True), "te")
    te_bc = b.cnt["te"]

    b.wait(nc.vector, "te", te_bc)
    for c in range(NCH):
        buf = sq[:, (c % 2) * 1024 : ((c % 2) + 1) * 1024]
        b.wait(nc.vector, "ve")
        b.inc(nc.vector.tensor_tensor(buf, x[:, c * TLOC : (c + 1) * TLOC],
                                      ps_bc[:, 0:1024], ALU.mult), "ve")
        b.wait(nc.vector, "ve")
        b.inc(nc.vector.tensor_tensor(xln[:, c * TLOC : (c + 1) * TLOC], buf,
                                      ps_bc[:, 1024:2048], ALU.add), "ve")
    b.ln_done = b.cnt["ve"]
    b.pop()



def emit_ffn0(b, io):
    nc = b.nc
    x = b.tn["x_res"]
    xln = b.tn["x_ln"]
    ln_done = b.ln_done
    b.push()
    b.fence()

    rw = b.sb("f0_rw", [128, NCH * 8], BF16)
    acat = b.sb("f0_acat", [128, NCH * 128], BF16)
    bcat = b.sb("f0_bcat", [128, 2048], BF16)
    bb = b.sb("f0_bb", [128, 16], F32)
    bbh = b.sb("f0_bbh", [128, 16], F32)
    rb = b.sb("f0_rb", [8, 1], F32)
    b.dma(nc.gpsimd, rw[:, :], io["f0_rwT"][:, :])
    b.dma(nc.gpsimd, acat[:, :], io["f0_Acat"][:, :])
    b.dma(nc.gpsimd, bcat[:, :], io["f0_Bcat"][:, :])
    b.dma(nc.sync, bb[:, :], io["f0_bb"][:, :])
    b.dma(nc.sync, rb[:, :], io["f0_rb"][:, :])
    ld_small = b.cnt["ld"]
    lg_small = b.cnt["lg"]

    ps_r = b.ps("f0_psr", [8, 1024], F32)
    ps_mid = b.ps("f0_psmid", [128, 1024], F32)
    ps_small = b.ps("f0_pssmall", [128, 512], F32)
    ps_res = b.ps("f0_psres", [128, 2 * 512], F32)

    logits = b.sb("f0_logits", [8, 1024], F32)
    mid_sb = b.sb("f0_mid", [128, 1024], BF16)
    ltm = b.sb("f0_ltm", [128, 64], F32)
    sm = b.sb("f0_sm", [128, 6 * 64], F32)
    m1 = sm[:, 0:8]; m2 = sm[:, 8:16]; dd = sm[:, 16:24]
    w1 = sm[:, 24:32]; w2 = sm[:, 32:40]
    mask1 = sm[:, 64:128]; l2 = sm[:, 128:192]; mask2 = sm[:, 192:256]
    g1 = sm[:, 256:320]; gate_tm = sm[:, 320:384]
    gfm_sb = b.sb("f0_gfm", [8, 1024], BF16)
    gmid = b.sb("f0_gmid", [128, 1024], BF16)
    wbuf = b.sb("f0_wbuf", [128, 2 * 2048], BF16)
    tmp = b.sb("f0_tmp", [128, 2 * 512], F32)

    identf = b.tn["ident_f32"]
    emat = b.tn["emat"]

    b.wait(nc.vector, "ld", ld_small)
    b.inc(nc.vector.tensor_scalar_mul(bbh[:, :], bb[:, :], 0.5), "ve")
    bbh_done = b.cnt["ve"]

    b.wait(nc.tensor, "ld", ld_small)
    b.wait(nc.tensor, "lg", lg_small)
    b.wait(nc.tensor, "ve", ln_done)
    for c in range(NCH):
        for hf in range(2):
            xs = xln[:, c * TLOC + hf * 512 : c * TLOC + (hf + 1) * 512]
            b.inc(nc.tensor.matmul(ps_r[:, hf * 512 : (hf + 1) * 512],
                                   rw[:, c * 8 : (c + 1) * 8], xs,
                                   start=(c == 0), stop=(c == NCH - 1)), "te")
    te_router = b.cnt["te"]
    for c in range(NCH):
        for hf in range(2):
            xs = xln[:, c * TLOC + hf * 512 : c * TLOC + (hf + 1) * 512]
            b.inc(nc.tensor.matmul(ps_mid[:, hf * 512 : (hf + 1) * 512],
                                   acat[:, c * 128 : (c + 1) * 128], xs,
                                   start=(c == 0), stop=(c == NCH - 1)), "te")
    te_mid = b.cnt["te"]

    b.wait(nc.scalar, "te", te_router)
    b.inc(nc.scalar.activation(logits[:, :], ps_r[:, :], AF.Identity, bias=rb[:, 0:1]), "se")
    se_logits = b.cnt["se"]
    b.wait(nc.scalar, "te", te_mid)
    b.inc(nc.scalar.activation(mid_sb[:, :], ps_mid[:, :], AF.Copy), "se")
    se_mid = b.cnt["se"]

    b.wait(nc.tensor, "se", se_logits)
    for tt in range(8):
        b.inc(nc.tensor.transpose(ps_small[:, 8 * tt : 8 * tt + 8],
                                  logits[:, tt * 128 : (tt + 1) * 128], identf[0:8, 0:8]), "te")
    te_tm = b.cnt["te"]
    b.wait(nc.vector, "te", te_tm)
    b.inc(nc.vector.tensor_copy(ltm[:, :], ps_small[:, 0:64]), "ve")
    b.wait(nc.vector, "ve")

    ltm_v = AP(ltm, 0, [[64, 128], [8, 8], [1, 8]])
    SMW = 6 * 64
    m1rep = AP(sm, 0, [[SMW, 128], [1, 8], [0, 8]])
    m2rep = AP(sm, 8, [[SMW, 128], [1, 8], [0, 8]])
    w1rep = AP(sm, 24, [[SMW, 128], [1, 8], [0, 8]])
    w2rep = AP(sm, 32, [[SMW, 128], [1, 8], [0, 8]])
    l2_v = AP(sm, 128, [[SMW, 128], [8, 8], [1, 8]])

    b.inc(nc.vector.tensor_reduce(m1, ltm_v, AX.X, ALU.max), "ve")
    b.wait(nc.vector, "ve")
    b.inc(nc.vector.tensor_tensor(mask1, ltm[:, :], m1rep, ALU.is_equal), "ve")
    b.wait(nc.vector, "ve")
    b.inc(nc.vector.scalar_tensor_tensor(l2, mask1, NEG, ltm[:, :], ALU.mult, ALU.add), "ve")
    b.wait(nc.vector, "ve")
    b.inc(nc.vector.tensor_reduce(m2, l2_v, AX.X, ALU.max), "ve")
    b.wait(nc.vector, "ve")
    b.inc(nc.vector.tensor_tensor(mask2, l2, m2rep, ALU.is_equal), "ve")
    b.inc(nc.vector.tensor_tensor(dd, m2, m1, ALU.subtract), "ve")
    b.wait(nc.scalar, "ve")
    b.inc(nc.scalar.activation(w1, dd, AF.Sigmoid, scale=-1.0), "se")
    b.wait(nc.vector, "se")
    b.inc(nc.vector.tensor_scalar(w2, w1, -2.0, 2.0, ALU.mult, ALU.add), "ve")
    b.wait(nc.vector, "ve")
    b.inc(nc.vector.tensor_scalar_mul(w1, w1, 2.0), "ve")
    b.wait(nc.vector, "ve")
    b.inc(nc.vector.tensor_tensor(g1, mask1, w1rep, ALU.mult), "ve")
    b.wait(nc.vector, "ve")
    b.inc(nc.vector.tensor_tensor(gate_tm, mask2, w2rep, ALU.mult), "ve")
    b.wait(nc.vector, "ve")
    b.inc(nc.vector.tensor_tensor(gate_tm, gate_tm, g1, ALU.add), "ve")
    ve_gate = b.cnt["ve"]

    b.wait(nc.tensor, "ve", ve_gate)
    for tt in range(8):
        b.inc(nc.tensor.transpose(ps_r[:, tt * 128 : (tt + 1) * 128],
                                  gate_tm[:, 8 * tt : 8 * tt + 8], identf[:, :]), "te")
    te_gfm = b.cnt["te"]
    b.wait(nc.scalar, "te", te_gfm)
    b.inc(nc.scalar.activation(gfm_sb[:, :], ps_r[:, :], AF.Copy), "se")
    se_gfm = b.cnt["se"]
    b.wait(nc.tensor, "se", se_gfm)
    b.wait(nc.tensor, "se", se_mid)
    for hf in range(2):
        b.inc(nc.tensor.matmul(ps_mid[:, hf * 512 : (hf + 1) * 512], emat[:, :],
                               gfm_sb[:, hf * 512 : (hf + 1) * 512], start=True, stop=True), "te")
    te_gexp = b.cnt["te"]
    b.wait(nc.vector, "te", te_gexp)
    b.inc(nc.vector.tensor_tensor(gmid[:, :], mid_sb[:, :], ps_mid[:, :], ALU.mult), "ve")
    ve_gmid = b.cnt["ve"]

    ld_w = {}
    w_guard = {}
    ep_guard_se = {}
    ep_guard_ve = {}

    def issue_wload(oc):
        par = "wa" if oc % 2 == 0 else "wb"
        if oc >= 2:
            b.wait(nc.gpsimd, "te", w_guard[oc - 2])
        ld_w[oc] = (par, b.dma(nc.gpsimd,
                         AP(wbuf, (oc % 2) * 2048, [[2 * 2048, 128], [1, 2048]]),
                         AP(io["f0_bwT"], oc * 128 * 2048, [[2048, 128], [1, 2048]]),
                         ctr=par))

    issue_wload(0)
    issue_wload(1)
    for oc in range(NCH):
        wb = wbuf[:, (oc % 2) * 2048 : ((oc % 2) + 1) * 2048]
        b.wait(nc.tensor, ld_w[oc][0], ld_w[oc][1])
        if oc == 0:
            b.wait(nc.tensor, "ve", ve_gmid)
            b.wait(nc.scalar, "ve", bbh_done)
        for hf in range(2):
            pr = ps_res[:, hf * 512 : (hf + 1) * 512]
            if oc >= 1:
                b.wait(nc.tensor, "se", ep_guard_se[(oc - 1, hf)])
            for c in range(NCH):
                xs = xln[:, c * TLOC + hf * 512 : c * TLOC + (hf + 1) * 512]
                b.inc(nc.tensor.matmul(pr, wb[:, c * 128 : (c + 1) * 128], xs,
                                       start=(c == 0), stop=False), "te")
            b.inc(nc.tensor.matmul(pr, bcat[:, oc * 128 : (oc + 1) * 128],
                                   gmid[:, hf * 512 : (hf + 1) * 512],
                                   start=False, stop=True), "te")
        w_guard[oc] = b.cnt["te"]
        if oc + 2 < NCH:
            issue_wload(oc + 2)
        for hf in range(2):
            pr = ps_res[:, hf * 512 : (hf + 1) * 512]
            tb = tmp[:, hf * 512 : (hf + 1) * 512]
            b.wait(nc.scalar, "te", w_guard[oc])
            if oc >= 1:
                b.wait(nc.scalar, "ve", ep_guard_ve[(oc - 1, hf)])
            ep_guard_se[(oc, hf)] = b.inc(
                nc.scalar.activation(tb, pr, AF.Identity, bias=bbh[:, oc : oc + 1], scale=0.5), "se")
            b.wait(nc.vector, "se")
            xs_out = x[:, oc * TLOC + hf * 512 : oc * TLOC + (hf + 1) * 512]
            ep_guard_ve[(oc, hf)] = b.inc(
                nc.vector.tensor_tensor(xs_out, xs_out, tb, ALU.add), "ve")
    b.pop()




def emit_attn0(b, io):
    nc = b.nc
    x = b.tn["x_res"]
    xln = b.tn["x_ln"]
    ln_done = b.ln_done
    b.push()
    b.fence()

    SCALE = 1.0 / float(np.sqrt(128.0))
    ag_in = nc.dram_tensor(b.name("ag1_in"), [4096, 1024], BF16)
    ag_out = nc.dram_tensor(b.name("ag1_out"), [16384, 1024], BF16)

    q_sb = b.sb("a0_q", [128, 16 * 1024], BF16)
    ba = b.sb("a0_ba", [128, 48], F32)
    bp = b.sb("a0_bp", [128, 16], F32)
    mcol = b.sb("a0_mcol", [128, 24], F32)
    dm = b.sb("a0_dm", [128, 4 * 512], F32)
    bro = b.sb("a0_bro", [1, 8], mybir.dt.int32)
    b.dma(nc.sync, ba[:, :], io["b_attn"][:, :])
    b.dma(nc.sync, bp[:, :], io["b_proj"][:, :])
    b.dma(nc.sync, mcol[:, :], io["mcol"][:, :])
    b.dma(nc.sync, dm[:, :], io["attn0_masks"][:, :])
    b.dma(nc.sync, bro[:, :], io["broff"][:, :])
    ld_small = b.cnt["ld"]

    b.push()
    wbuf = b.sb("a0_wbuf", [128, 2 * 2048], BF16)
    kvst = b.sb("a0_kvst", [128, 2 * 1024], BF16)
    ps_qkv = b.ps("a0_psqkv", [128, 2 * 1024], F32)

    ld_w = {}
    w_guard = {}
    ev_guard = {}
    st_guard = {}

    def wload_a0(oc):
        par = "wa" if oc % 2 == 0 else "wb"
        if oc >= 2:
            b.wait(nc.gpsimd, "te", w_guard[oc - 2])
        ld_w[oc] = (par, b.dma(nc.gpsimd,
                    AP(wbuf, (oc % 2) * 2048, [[2 * 2048, 128], [1, 2048]]),
                    AP(io["w_attn"], oc * 128 * 2048, [[2048, 128], [1, 2048]]),
                    ctr=par))

    wload_a0(0)
    wload_a0(1)
    b.wait(nc.tensor, "ve", ln_done)
    b.wait(nc.scalar, "ld", ld_small)
    for oc in range(48):
        wb = wbuf[:, (oc % 2) * 2048 : ((oc % 2) + 1) * 2048]
        pq = ps_qkv[:, (oc % 2) * 1024 : ((oc % 2) + 1) * 1024]
        b.wait(nc.tensor, ld_w[oc][0], ld_w[oc][1])
        if oc >= 2:
            b.wait(nc.tensor, "se", ev_guard[oc - 2])
        for c in range(NCH):
            for hf in range(2):
                xs = xln[:, c * TLOC + hf * 512 : c * TLOC + (hf + 1) * 512]
                b.inc(nc.tensor.matmul(pq[:, hf * 512 : (hf + 1) * 512],
                                       wb[:, c * 128 : (c + 1) * 128], xs,
                                       start=(c == 0), stop=(c == NCH - 1)), "te")
        w_guard[oc] = b.cnt["te"]
        if oc + 2 < 48:
            wload_a0(oc + 2)
        b.wait(nc.scalar, "te", w_guard[oc])
        if oc < 16:
            ev_guard[oc] = b.inc(nc.scalar.activation(
                q_sb[:, oc * 1024 : (oc + 1) * 1024], pq, AF.Identity,
                bias=ba[:, oc : oc + 1]), "se")
        else:
            kv = kvst[:, (oc % 2) * 1024 : ((oc % 2) + 1) * 1024]
            spar = "hc" if oc % 2 == 0 else "hd"
            if oc >= 18:
                b.wait(nc.scalar, spar, st_guard[oc - 2])
            ev_guard[oc] = b.inc(nc.scalar.activation(
                kv, pq, AF.Identity, bias=ba[:, oc : oc + 1]), "se")
            b.wait(nc.sync, "se", ev_guard[oc])
            row = (oc - 16) * 128 if oc < 32 else 2048 + (oc - 32) * 128
            st_guard[oc] = b.dma(nc.sync, AP(ag_in, row * 1024, [[1024, 128], [1, 1024]]),
                                 kv, ctr=spar)
    b.wait(nc.gpsimd, "hc")
    b.wait(nc.gpsimd, "hd")
    b.inc(nc.gpsimd.collective_compute(
        "AllGather", ALU.bypass, replica_groups=[[0, 1, 2, 3], [4, 5, 6, 7]],
        ins=[ag_in.ap().opt()], outs=[ag_out.ap().opt()]), "cc")
    cc_done = b.cnt["cc"]
    b.pop()

    b.push()
    kbuf = b.sb("a0_kbuf", [128, 2 * 3 * 1024], BF16)
    vbuf = b.sb("a0_vbuf", [128, 2 * 3 * 1024], BF16)
    kobuf = b.sb("a0_kobuf", [128, 2 * 1024], BF16)
    vobuf = b.sb("a0_vobuf", [128, 2 * 1024], BF16)
    vt_sb = b.sb("a0_vt", [128, 4 * 1024], BF16)
    pt_sb = b.sb("a0_pt", [128, 2 * 512], BF16)
    negm_sb = b.sb("a0_negm", [1, 2 * 1024], F32)
    negmc = b.sb("a0_negmc", [128, 8], F32)
    zr = b.sb("a0_zr", [1, 4 * 512], F32)
    zbc_sb = b.sb("a0_zbc", [128, 512], F32)

    ps_sA = b.ps("a0_pssA", [128, 512], F32)
    ps_sB = b.ps("a0_pssB", [128, 512], F32)
    ps_o = b.ps("a0_pso", [128, 512], F32)
    ps_z = b.ps("a0_psz", [1, 512], F32)
    ps_m = b.ps("a0_psm", [1, 1024], F32)
    ps_vt = b.ps("a0_psvt", [128, 1024], BF16)
    ps_zbc = b.ps("a0_pszbc", [128, 512], F32)

    onesr = b.tn["ones_row"]
    onescb = b.tn["ones_col_bf"]
    identf = b.tn["ident_f32"]
    identb = b.tn["ident_bf"]
    zcol = b.tn["cvals"][:, 3:4]

    kvn = nc.dram_tensor(b.name("a0_kvn"), [16384, 1024], BF16)
    regs = [b.scopes[-1].enter_context(nc.gpsimd.register(f"a0r{i}")) for i in range(8)]
    b.wait(nc.gpsimd, "ld", ld_small)
    for i in range(8):
        nc.gpsimd.reg_load(regs[i], bro[0:1, i : i + 1])
    b.wait(nc.gpsimd, "cc", cc_done)
    dst_rows = [0, 2048, 4096, 10240, 6144, 12288, 8192, 14336]
    dst_for = [0, 2048, 4096, 10240, 6144, 12288, 8192, 14336]
    for i in range(8):
        b.dma(nc.gpsimd, AP(kvn, dst_for[i] * 1024, [[1024, 2048], [1, 1024]]),
              AP(ag_out, regs[i], [[1024, 2048], [1, 1024]]), ctr="lg")
    kvn_done = b.cnt["lg"]

    kv_ld = {}

    def kvload(h):
        par = h % 2
        ctr = "ha" if par == 0 else "hb"
        if h >= 2:
            b.wait(nc.sync, "te", b.a0_head_guard[h - 2])
        else:
            b.wait(nc.sync, "lg", kvn_done)
        b.dma(nc.sync, kobuf[:, par * 1024 : (par + 1) * 1024],
              AP(kvn, (0 + 128 * h) * 1024, [[1024, 128], [1, 1024]]), ctr=ctr)
        b.dma(nc.sync, vobuf[:, par * 1024 : (par + 1) * 1024],
              AP(kvn, (2048 + 128 * h) * 1024, [[1024, 128], [1, 1024]]), ctr=ctr)
        for j in range(3):
            b.dma(nc.sync, kbuf[:, (par * 3 + j) * 1024 : (par * 3 + j + 1) * 1024],
                  AP(kvn, (4096 + 2048 * j + 128 * h) * 1024, [[1024, 128], [1, 1024]]), ctr=ctr)
            b.dma(nc.sync, vbuf[:, (par * 3 + j) * 1024 : (par * 3 + j + 1) * 1024],
                  AP(kvn, (10240 + 2048 * j + 128 * h) * 1024, [[1024, 128], [1, 1024]]), ctr=ctr)
        kv_ld[h] = (ctr, b.cnt[ctr])

    b.a0_head_guard = {}
    kvload(0)
    kvload(1)
    for h in range(16):
        par = h % 2
        ctr, ldv = kv_ld[h]
        b.wait(nc.tensor, ctr, ldv)
        vt_guard = b.cnt["ve"]
        for grp in range(4):
            if grp < 3:
                vsrc = vbuf[:, (par * 3 + grp) * 1024 : (par * 3 + grp + 1) * 1024]
            else:
                vsrc = vobuf[:, par * 1024 : (par + 1) * 1024]
            b.wait(nc.tensor, "ve", vt_guard)
            for tt in range(8):
                b.inc(nc.tensor.transpose(ps_vt[:, tt * 128 : (tt + 1) * 128],
                                          vsrc[:, tt * 128 : (tt + 1) * 128],
                                          identb[:, :]), "te")
            b.wait(nc.vector, "te")
            vt_guard = b.inc(nc.vector.tensor_copy(
                vt_sb[:, grp * 1024 : (grp + 1) * 1024], ps_vt[:, :]), "ve")
        msub_guard = b.cnt["ve"]
        for qt in range(8):
            ns = max(4, 4 * qt)
            b.wait(nc.tensor, "ve", msub_guard if qt == 0 else b.cnt["ve"])
            b.inc(nc.tensor.matmul(
                ps_sA[:, 0:ns],
                q_sb[:, h * 1024 + qt * 128 : h * 1024 + (qt + 1) * 128],
                AP(kobuf, par * 1024, [[2 * 1024, 128], [32, ns]]),
                start=True, stop=True), "te")
            b.wait(nc.vector, "te")
            b.inc(nc.vector.tensor_reduce(
                negmc[:, qt : qt + 1], ps_sA[:, 0:ns],
                AX.X, ALU.max, negate=True), "ve")
        b.wait(nc.tensor, "ve")
        for qt in range(8):
            b.inc(nc.tensor.transpose(ps_m[0:1, qt * 128 : (qt + 1) * 128],
                                      negmc[:, qt : qt + 1], identf[:, :]), "te")
        b.wait(nc.vector, "te")
        b.inc(nc.vector.tensor_copy(negm_sb[:, par * 1024 : (par + 1) * 1024],
                                    ps_m[:, :]), "ve")
        negm_done = b.cnt["ve"]

        for qb in range(2):
            visits = [("oth", st) for st in range(24)] +                      [("own", j) for j in range(4 * qb + 4)]
            nv = len(visits)
            exp_guard = {}
            zav_guard = {}
            for vi, (kind, idx) in enumerate(visits):
                pp = vi % 2
                ps_s = ps_sA if pp == 0 else ps_sB
                if vi >= 2:
                    b.wait(nc.tensor, "se", exp_guard[vi - 2])
                if vi == 0:
                    b.wait(nc.tensor, "ve")
                b.inc(nc.tensor.matmul(ps_s[:, :], onesr[:, :],
                                       negm_sb[0:1, par * 1024 + qb * 512 : par * 1024 + (qb + 1) * 512],
                                       start=True, stop=False, skip_group_check=True), "te")
                if kind == "oth":
                    kt = kbuf[:, (par * 3) * 1024 + idx * 128 : (par * 3) * 1024 + (idx + 1) * 128]
                    vt = vt_sb[:, idx * 128 : (idx + 1) * 128]
                    biascol = mcol[:, idx : idx + 1]
                    diag = None
                else:
                    kt = kobuf[:, par * 1024 + idx * 128 : par * 1024 + (idx + 1) * 128]
                    vt = vt_sb[:, 3 * 1024 + idx * 128 : 3 * 1024 + (idx + 1) * 128]
                    biascol = zcol
                    di = idx - 4 * qb
                    diag = dm[:, di * 512 : (di + 1) * 512] if 0 <= di < 4 else None
                b.inc(nc.tensor.matmul(ps_s[:, :], kt,
                                       q_sb[:, h * 1024 + qb * 512 : h * 1024 + (qb + 1) * 512],
                                       start=False, stop=True, skip_group_check=True), "te")
                te_s = b.cnt["te"]
                if diag is not None:
                    b.wait(nc.vector, "te", te_s)
                    b.inc(nc.vector.tensor_tensor(ps_s[:, :], ps_s[:, :], diag, ALU.add), "ve")
                    b.wait(nc.scalar, "ve")
                else:
                    b.wait(nc.scalar, "te", te_s)
                if vi >= 2:
                    b.wait(nc.scalar, "te", zav_guard[vi - 2])
                exp_guard[vi] = b.inc(nc.scalar.activation(
                    pt_sb[:, pp * 512 : (pp + 1) * 512], ps_s[:, :], AF.Exp,
                    bias=biascol, scale=SCALE), "se")
                b.wait(nc.tensor, "se", exp_guard[vi])
                b.inc(nc.tensor.matmul(ps_z[:, :], onescb[:, :],
                                       pt_sb[:, pp * 512 : (pp + 1) * 512],
                                       start=(vi == 0), stop=(vi == nv - 1),
                                       skip_group_check=True), "te")
                b.inc(nc.tensor.matmul(ps_o[:, :], vt,
                                       pt_sb[:, pp * 512 : (pp + 1) * 512],
                                       start=(vi == 0), stop=(vi == nv - 1),
                                       skip_group_check=True), "te")
                zav_guard[vi] = b.cnt["te"]
            te_fin = b.cnt["te"]
            b.wait(nc.vector, "te", te_fin)
            b.inc(nc.vector.tensor_copy(zr[:, 0:512], ps_z[:, :]), "ve")
            b.wait(nc.vector, "ve")
            b.inc(nc.vector.tensor_scalar_add(zr[:, 512:1024], zr[:, 0:512], 1e-30), "ve")
            b.wait(nc.vector, "ve")
            b.inc(nc.vector.reciprocal(zr[:, 1024:1536], zr[:, 512:1024]), "ve")
            ve_zi = b.cnt["ve"]
            b.wait(nc.tensor, "ve", ve_zi)
            b.inc(nc.tensor.matmul(ps_zbc[:, :], onesr[:, :], zr[0:1, 1024:1536],
                                   start=True, stop=True, skip_group_check=True), "te")
            b.wait(nc.scalar, "te")
            se_zbc = b.inc(nc.scalar.activation(zbc_sb[:, :], ps_zbc[:, :], AF.Identity,
                                                bias=zcol), "se")
            b.wait(nc.vector, "se", se_zbc)
            b.inc(nc.vector.tensor_tensor(
                xln[:, h * 1024 + qb * 512 : h * 1024 + (qb + 1) * 512],
                ps_o[:, :], zbc_sb[:, :], ALU.mult), "ve")
        b.a0_head_guard[h] = b.cnt["te"]
        if h + 2 < 16:
            kvload(h + 2)
    ao_done = b.cnt["ve"]
    b.pop()

    b.push()
    b.fence()
    wbufp = b.sb("a0_wbufp", [128, 2 * 2048], BF16)
    tmp = b.sb("a0_ptmp", [128, 2 * 512], F32)
    ps_res = b.ps("a0_psres", [128, 2 * 512], F32)
    ld_wp = {}
    wp_guard = {}
    epp_se = {}
    epp_ve = {}

    def wloadp(oc):
        par = "wa" if oc % 2 == 0 else "wb"
        if oc >= 2:
            b.wait(nc.gpsimd, "te", wp_guard[oc - 2])
        ld_wp[oc] = (par, b.dma(nc.gpsimd,
                     AP(wbufp, (oc % 2) * 2048, [[2 * 2048, 128], [1, 2048]]),
                     AP(io["w_proj"], oc * 128 * 2048, [[2048, 128], [1, 2048]]),
                     ctr=par))

    wloadp(0)
    wloadp(1)
    b.wait(nc.tensor, "ve", ao_done)
    for oc in range(NCH):
        wb = wbufp[:, (oc % 2) * 2048 : ((oc % 2) + 1) * 2048]
        b.wait(nc.tensor, ld_wp[oc][0], ld_wp[oc][1])
        for hf in range(2):
            pr = ps_res[:, hf * 512 : (hf + 1) * 512]
            if oc >= 1:
                b.wait(nc.tensor, "se", epp_se[(oc - 1, hf)])
            for c in range(NCH):
                aos = xln[:, c * TLOC + hf * 512 : c * TLOC + (hf + 1) * 512]
                b.inc(nc.tensor.matmul(pr, wb[:, c * 128 : (c + 1) * 128], aos,
                                       start=(c == 0), stop=(c == NCH - 1)), "te")
        wp_guard[oc] = b.cnt["te"]
        if oc + 2 < NCH:
            wloadp(oc + 2)
        for hf in range(2):
            pr = ps_res[:, hf * 512 : (hf + 1) * 512]
            tb = tmp[:, hf * 512 : (hf + 1) * 512]
            b.wait(nc.scalar, "te", wp_guard[oc])
            if oc >= 1:
                b.wait(nc.scalar, "ve", epp_ve[(oc - 1, hf)])
            epp_se[(oc, hf)] = b.inc(nc.scalar.activation(
                tb, pr, AF.Identity, bias=bp[:, oc : oc + 1]), "se")
            b.wait(nc.vector, "se")
            xs_out = x[:, oc * TLOC + hf * 512 : oc * TLOC + (hf + 1) * 512]
            epp_ve[(oc, hf)] = b.inc(
                nc.vector.tensor_tensor(xs_out, xs_out, tb, ALU.add), "ve")
    b.pop()
    b.pop()



def emit_ffn1(b, io):
    nc = b.nc
    x = b.tn["x_res"]
    xln = b.tn["x_ln"]
    ln_done = b.ln_done
    b.push()
    b.fence()

    IC = 256
    NIC = 7168 // IC
    NIC_RUN = FFN1_NIC
    identf = b.tn["ident_f32"]

    confw = b.sb("f1_confw", [128, NCH * 8], BF16)
    cb = b.sb("f1_cb", [8, 1], F32)
    wl = b.sb("f1_wl", [8, 1], F32)
    webc = b.sb("f1_webc", [128, 8 * 1024], BF16)
    mid_g = b.sb("f1_midg", [128, 4 * 1024], BF16)
    mid_u = b.sb("f1_midu", [128, 4 * 1024], BF16)
    md_sb = b.sb("f1_md", [128, 4 * 1024], BF16)
    ps_base = b.ps("f1_psbase", [128, 4 * 512], F32)
    ps_md = b.ps("f1_psmd", [64, 512], F32)
    ps_util = b.ps("f1_psutil", [128, 2 * 512], F32)
    ps_tm = b.ps("f1_pstm", [128, 128], F32)

    b.push()
    erow = b.sb("f1_erow", [8, 8 * 128], F32)
    we8 = b.sb("f1_we8", [8, 1024], F32)
    conf_sb = b.sb("f1_conf", [8, 1024], F32)
    bids = b.sb("f1_bids", [8, 1024], F32)
    ltm = b.sb("f1_ltm", [128, 64], F32)
    sm = b.sb("f1_sm", [128, 6 * 64], F32)
    m1 = sm[:, 0:8]; m2 = sm[:, 8:16]; dd = sm[:, 16:24]
    w1 = sm[:, 24:32]; w2 = sm[:, 32:40]
    mask1 = sm[:, 64:128]; l2 = sm[:, 128:192]; mask2 = sm[:, 192:256]
    g1 = sm[:, 256:320]; gate_tm = sm[:, 320:384]

    b.dma(nc.gpsimd, confw[:, :], io["conf_WT"][:, :])
    b.dma(nc.sync, cb[:, :], io["conf_b"][:, :])
    b.dma(nc.sync, wl[:, :], io["wealth"][:, :])
    b.dma(nc.sync, erow[:, :], io["erow"][:, :])
    ld_small = b.cnt["ld"]
    lg_small = b.cnt["lg"]

    b.wait(nc.tensor, "lg", lg_small)
    b.wait(nc.tensor, "ve", ln_done)
    for hf in range(2):
        for c in range(NCH):
            xs = xln[:, c * TLOC + hf * 512 : c * TLOC + (hf + 1) * 512]
            b.inc(nc.tensor.matmul(ps_util[0:8, hf * 512 : (hf + 1) * 512],
                                   confw[:, c * 8 : (c + 1) * 8], xs,
                                   start=(c == 0), stop=(c == NCH - 1)), "te")
    te_conf = b.cnt["te"]
    b.wait(nc.scalar, "te", te_conf)
    b.wait(nc.scalar, "ld", ld_small)
    b.inc(nc.scalar.activation(conf_sb[:, :], ps_util[0:8, 0:1024], AF.Sigmoid,
                               bias=cb[:, 0:1]), "se")
    b.wait(nc.vector, "se")
    b.inc(nc.vector.tensor_scalar(bids[:, :], conf_sb[:, :], wl[:, 0:1], None, ALU.mult), "ve")
    ve_bids = b.cnt["ve"]
    b.wait(nc.tensor, "ve", ve_bids)
    for tt in range(8):
        b.inc(nc.tensor.transpose(AP(ps_tm, 8 * tt, [[128, 128], [1, 8]]),
                                  bids[:, tt * 128 : (tt + 1) * 128], identf[0:8, 0:8]), "te")
    te_tm = b.cnt["te"]
    b.wait(nc.vector, "te", te_tm)
    b.inc(nc.vector.tensor_copy(ltm[:, :], AP(ps_tm, 0, [[128, 128], [1, 64]])), "ve")
    b.wait(nc.vector, "ve")
    SMW = 6 * 64
    ltm_v = AP(ltm, 0, [[64, 128], [8, 8], [1, 8]])
    m1rep = AP(sm, 0, [[SMW, 128], [1, 8], [0, 8]])
    m2rep = AP(sm, 8, [[SMW, 128], [1, 8], [0, 8]])
    w1rep = AP(sm, 24, [[SMW, 128], [1, 8], [0, 8]])
    w2rep = AP(sm, 32, [[SMW, 128], [1, 8], [0, 8]])
    l2_v = AP(sm, 128, [[SMW, 128], [8, 8], [1, 8]])
    b.inc(nc.vector.tensor_reduce(m1, ltm_v, AX.X, ALU.max), "ve")
    b.wait(nc.vector, "ve")
    b.inc(nc.vector.tensor_tensor(mask1, ltm[:, :], m1rep, ALU.is_equal), "ve")
    b.wait(nc.vector, "ve")
    b.inc(nc.vector.scalar_tensor_tensor(l2, mask1, NEG, ltm[:, :], ALU.mult, ALU.add), "ve")
    b.wait(nc.vector, "ve")
    b.inc(nc.vector.tensor_reduce(m2, l2_v, AX.X, ALU.max), "ve")
    b.wait(nc.vector, "ve")
    b.inc(nc.vector.tensor_tensor(mask2, l2, m2rep, ALU.is_equal), "ve")
    b.inc(nc.vector.tensor_tensor(dd, m2, m1, ALU.subtract), "ve")
    b.wait(nc.scalar, "ve")
    b.inc(nc.scalar.activation(w1, dd, AF.Sigmoid, scale=-1.0), "se")
    b.wait(nc.vector, "se")
    b.inc(nc.vector.tensor_scalar(w2, w1, -0.5, 0.5, ALU.mult, ALU.add), "ve")
    b.wait(nc.vector, "ve")
    b.inc(nc.vector.tensor_scalar_mul(w1, w1, 0.5), "ve")
    b.wait(nc.vector, "ve")
    b.inc(nc.vector.tensor_tensor(g1, mask1, w1rep, ALU.mult), "ve")
    b.wait(nc.vector, "ve")
    b.inc(nc.vector.tensor_tensor(gate_tm, mask2, w2rep, ALU.mult), "ve")
    b.wait(nc.vector, "ve")
    b.inc(nc.vector.tensor_tensor(gate_tm, gate_tm, g1, ALU.add), "ve")
    ve_gate = b.cnt["ve"]
    b.wait(nc.tensor, "ve", ve_gate)
    b.wait(nc.tensor, "se")
    for tt in range(8):
        b.inc(nc.tensor.transpose(ps_util[0:8, tt * 128 : (tt + 1) * 128],
                                  gate_tm[:, 8 * tt : 8 * tt + 8], identf[:, :]), "te")
    te_we = b.cnt["te"]
    b.wait(nc.vector, "te", te_we)
    b.inc(nc.vector.tensor_copy(we8[:, :], ps_util[0:8, 0:1024]), "ve")
    ve_we8 = b.cnt["ve"]
    b.wait(nc.tensor, "ve", ve_we8)
    for e in range(8):
        if e > 0:
            b.wait(nc.tensor, "se")
        for hf in range(2):
            b.inc(nc.tensor.matmul(ps_util[:, hf * 512 : (hf + 1) * 512],
                                   erow[:, e * 128 : (e + 1) * 128],
                                   we8[:, hf * 512 : (hf + 1) * 512],
                                   start=True, stop=True), "te")
        b.wait(nc.scalar, "te")
        b.inc(nc.scalar.activation(webc[:, e * 1024 : (e + 1) * 1024],
                                   ps_util[:, :], AF.Copy), "se")
    se_webc = b.cnt["se"]
    b.fence()
    b.pop()

    b.push()
    wg_b = b.sb("f1_wgb", [128, 4096], BF16)
    wu_b = b.sb("f1_wub", [128, 4096], BF16)
    wd_b = b.sb("f1_wdb", [128, 2 * 4096], BF16)
    gB_sb = b.sb("f1_gB2", [128, 2 * 1024], BF16)
    uB_sb = b.sb("f1_uB2", [128, 2 * 1024], BF16)
    dA_sb = b.sb("f1_dA", [128, 1024], BF16)
    G_sb = b.sb("f1_G", [128, 1024], BF16)
    U_sb = b.sb("f1_U", [128, 1024], BF16)
    s_sb = b.sb("f1_s", [128, 2 * 1024], BF16)
    h_sb = b.sb("f1_h", [128, 2 * 1024], BF16)
    whbf = b.sb("f1_whbf", [128, 2 * 1024], BF16)
    hbarbf = b.sb("f1_hbarbf", [128, 4 * 1024], BF16)

    b.wait(nc.tensor, "se", se_webc)
    for (mid, wsrc, ioA, par) in [(mid_g, wg_b, "gateAcat", "wa"),
                                  (mid_u, wu_b, "upAcat", "wb")]:
        for piece in range(2):
            if piece == 1:
                b.wait(nc.gpsimd, "te")
            ldv = b.dma(nc.gpsimd, AP(wsrc, 0, [[4096, 128], [1, 4096]]),
                        AP(io[ioA], piece * 128 * 4096, [[4096, 128], [1, 4096]]),
                        ctr=par)
            b.wait(nc.tensor, par, ldv)
            for mc2 in range(2):
                mc = piece * 2 + mc2
                c0 = mc2 * 128
                for hf in range(2):
                    pu = ps_util[:, hf * 512 : (hf + 1) * 512]
                    for c in range(NCH):
                        xs = xln[:, c * TLOC + hf * 512 : c * TLOC + (hf + 1) * 512]
                        b.inc(nc.tensor.matmul(pu, wsrc[:, c * 256 + c0 : c * 256 + c0 + 128],
                                               xs, start=(c == 0), stop=(c == NCH - 1)), "te")
                    b.wait(nc.scalar, "te")
                    b.inc(nc.scalar.activation(mid[:, mc * 1024 + hf * 512 : mc * 1024 + (hf + 1) * 512],
                                               pu, AF.Copy, scale=0.25), "se")
                    b.wait(nc.tensor, "se")
    se_mid = b.cnt["se"]
    b.wait(nc.vector, "se", se_mid)
    b.inc(nc.vector.memset(md_sb[:, :], 0.0), "ve")

    wload = {}

    def loadw(tag, ic):
        par = {"g": ("wa", "wa"), "u": ("wb", "wb"), "d": ("sa", "sb"),
               "gB": ("wd", "wx"), "uB": ("wy", "wz"), "dA": ("wc", "wc")}[tag][ic % 2]
        back = 1 if tag in ("g", "u", "dA") else 2
        if (tag, ic - back) in wload and wload[(tag, ic - back)][2] is not None:
            b.wait(nc.gpsimd, "te", wload[(tag, ic - back)][2])
        sl = ic % 2
        if tag == "g":
            v = b.dma(nc.gpsimd, AP(wg_b, 0, [[4096, 128], [1, 4096]]),
                      AP(io["base_gate"], ic * 128 * 4096, [[4096, 128], [1, 4096]]),
                      ctr=par)
        elif tag == "u":
            v = b.dma(nc.gpsimd, AP(wu_b, 0, [[4096, 128], [1, 4096]]),
                      AP(io["base_up"], ic * 128 * 4096, [[4096, 128], [1, 4096]]),
                      ctr=par)
        elif tag == "d":
            v = b.dma(nc.gpsimd, AP(wd_b, sl * 4096, [[2 * 4096, 128], [1, 4096]]),
                      AP(io["base_down"], ic * 128 * 4096, [[4096, 128], [1, 4096]]),
                      ctr=par)
        elif tag in ("gB", "uB"):
            buf = gB_sb if tag == "gB" else uB_sb
            ionm = "gate_B" if tag == "gB" else "up_B"
            for parity in range(2):
                dst = buf[64 * parity : 64 * parity + 64, sl * 1024 : (sl + 1) * 1024]
                v = b.dma(nc.gpsimd,
                          AP(dst.tensor, dst.offset, [[2 * 1024, 64], [1, 1024]]),
                          AP(io[ionm], (ic * 2 + parity) * 64 * 1024, [[1024, 64], [1, 1024]]),
                          ctr=par)
        elif tag == "dA":
            v = b.dma(nc.gpsimd, AP(dA_sb, 0, [[1024, 128], [1, 1024]]),
                      AP(io["down_A"], ic * 128 * 1024, [[1024, 128], [1, 1024]]),
                      ctr=par)
        wload[(tag, ic)] = [par, v, None]

    def setg(tag, ic):
        wload[(tag, ic)][2] = b.cnt["te"]

    b.wait(nc.gpsimd, "te")
    loadw("g", 0)
    loadw("u", 0)
    loadw("dA", 0)
    for ic in [0, 1]:
        for tag in ["d", "gB", "uB"]:
            loadw(tag, ic)

    wload = {}

    def loadw(tag, ic):
        par = {"g": ("wa", "wa"), "u": ("wb", "wb"), "d": ("sa", "sb"),
               "gB": ("wd", "wx"), "uB": ("wy", "wz"), "dA": ("wc", "wc")}[tag][ic % 2]
        back = 1 if tag in ("g", "u", "dA") else 2
        if (tag, ic - back) in wload and wload[(tag, ic - back)][2] is not None:
            b.wait(nc.gpsimd, "te", wload[(tag, ic - back)][2])
        sl = ic % 2
        if tag == "g":
            v = b.dma(nc.gpsimd, AP(wg_b, 0, [[4096, 128], [1, 4096]]),
                      AP(io["base_gate"], ic * 128 * 4096, [[4096, 128], [1, 4096]]),
                      ctr=par)
        elif tag == "u":
            v = b.dma(nc.gpsimd, AP(wu_b, 0, [[4096, 128], [1, 4096]]),
                      AP(io["base_up"], ic * 128 * 4096, [[4096, 128], [1, 4096]]),
                      ctr=par)
        elif tag == "d":
            v = b.dma(nc.gpsimd, AP(wd_b, sl * 4096, [[2 * 4096, 128], [1, 4096]]),
                      AP(io["base_down"], ic * 128 * 4096, [[4096, 128], [1, 4096]]),
                      ctr=par)
        elif tag in ("gB", "uB"):
            buf = gB_sb if tag == "gB" else uB_sb
            ionm = "gate_B" if tag == "gB" else "up_B"
            for parity in range(2):
                dst = buf[64 * parity : 64 * parity + 64, sl * 1024 : (sl + 1) * 1024]
                v = b.dma(nc.gpsimd,
                          AP(dst.tensor, dst.offset, [[2 * 1024, 64], [1, 1024]]),
                          AP(io[ionm], (ic * 2 + parity) * 64 * 1024, [[1024, 64], [1, 1024]]),
                          ctr=par)
        elif tag == "dA":
            v = b.dma(nc.gpsimd, AP(dA_sb, 0, [[1024, 128], [1, 1024]]),
                      AP(io["down_A"], ic * 128 * 1024, [[1024, 128], [1, 1024]]),
                      ctr=par)
        wload[(tag, ic)] = [par, v, None]

    def setg(tag, ic):
        wload[(tag, ic)][2] = b.cnt["te"]

    b.wait(nc.gpsimd, "te")
    loadw("g", 0)
    loadw("u", 0)
    loadw("dA", 0)
    for ic in [0, 1]:
        for tag in ["d", "gB", "uB"]:
            loadw(tag, ic)

    identb = b.tn["ident_bf"]
    ve_h_g = {}
    se_s_g = {}
    md_ve = 0
    for ic in range(min(NIC, FFN1_NIC) if FFN1_SECT >= 2 else 0):
        for tag in ["g", "u", "d", "gB", "uB", "dA"]:
            b.wait(nc.tensor, wload[(tag, ic)][0], wload[(tag, ic)][1])
        sl = (ic % 2)
        for th in range(2):
            if ic > 0 or th > 0:
                b.wait(nc.tensor, "ve")
                b.wait(nc.tensor, "se")
            for oc in range(2):
                pb = ps_base[:, oc * 512 : (oc + 1) * 512]
                for c in range(NCH):
                    xs = xln[:, c * TLOC + th * 512 : c * TLOC + (th + 1) * 512]
                    b.inc(nc.tensor.matmul(pb, wg_b[:, c * 256 + oc * 128 :
                                                    c * 256 + (oc + 1) * 128],
                                           xs, start=(c == 0), stop=(c == NCH - 1)), "te")
            te_G = b.cnt["te"]
            b.wait(nc.scalar, "te", te_G)
            b.inc(nc.scalar.activation(G_sb[:, :], AP(ps_base, 0, [[2048, 128], [1, 1024]]),
                                       AF.Copy), "se")
            se_G = b.cnt["se"]
            for oc in range(2):
                pb = ps_base[:, 1024 + oc * 512 : 1024 + (oc + 1) * 512]
                for c in range(NCH):
                    xs = xln[:, c * TLOC + th * 512 : c * TLOC + (th + 1) * 512]
                    b.inc(nc.tensor.matmul(pb, wu_b[:, c * 256 + oc * 128 :
                                                    c * 256 + (oc + 1) * 128],
                                           xs, start=(c == 0), stop=(c == NCH - 1)), "te")
            te_U = b.cnt["te"]
            b.wait(nc.scalar, "te", te_U)
            b.inc(nc.scalar.activation(U_sb[:, :], AP(ps_base, 1024, [[2048, 128], [1, 1024]]),
                                       AF.Copy), "se")
            se_U = b.cnt["se"]
            for e in range(8):
                pe_ = e % 2
                bank = lambda oc: ps_base[:, pe_ * 1024 + oc * 512 : pe_ * 1024 + (oc + 1) * 512]
                ebase = 64 * (e % 2)
                eg = (e // 2) * 1024
                if e == 0:
                    b.wait(nc.tensor, "se", se_U)
                elif e == 1:
                    b.wait(nc.tensor, "se", se_U)
                else:
                    b.wait(nc.tensor, "ve", ve_h_g[pe_])
                if e >= 2:
                    b.wait(nc.tensor, "se", se_s_g[pe_])
                for oc in range(2):
                    b.inc(nc.tensor.matmul(bank(oc), identb[:, :],
                                           G_sb[:, oc * 512 : (oc + 1) * 512],
                                           start=True, stop=False, skip_group_check=True), "te")
                    b.inc(nc.tensor.matmul(
                        bank(oc), gB_sb[ebase : ebase + 64, sl * 1024 + (e // 2) * 256 + oc * 128 :
                                        sl * 1024 + (e // 2) * 256 + (oc + 1) * 128],
                        mid_g[ebase : ebase + 64, eg + th * 512 : eg + (th + 1) * 512],
                        start=False, stop=True, skip_group_check=True), "te")
                te_g = b.cnt["te"]
                b.wait(nc.scalar, "te", te_g)
                if e >= 2:
                    b.wait(nc.scalar, "ve", ve_h_g[pe_])
                se_s_g[pe_] = b.inc(nc.scalar.activation(
                    s_sb[:, pe_ * 1024 : (pe_ + 1) * 1024],
                    AP(ps_base, pe_ * 1024, [[2048, 128], [1, 1024]]), AF.Silu), "se")
                b.wait(nc.tensor, "se", se_s_g[pe_])
                for oc in range(2):
                    b.inc(nc.tensor.matmul(bank(oc), identb[:, :],
                                           U_sb[:, oc * 512 : (oc + 1) * 512],
                                           start=True, stop=False, skip_group_check=True), "te")
                    b.inc(nc.tensor.matmul(
                        bank(oc), uB_sb[ebase : ebase + 64, sl * 1024 + (e // 2) * 256 + oc * 128 :
                                        sl * 1024 + (e // 2) * 256 + (oc + 1) * 128],
                        mid_u[ebase : ebase + 64, eg + th * 512 : eg + (th + 1) * 512],
                        start=False, stop=True, skip_group_check=True), "te")
                te_u = b.cnt["te"]
                b.wait(nc.vector, "te", te_u)
                b.wait(nc.vector, "se", se_s_g[pe_])
                if e >= 2:
                    b.wait(nc.vector, "po")
                b.inc(nc.vector.tensor_tensor(
                    h_sb[:, pe_ * 1024 : (pe_ + 1) * 1024],
                    s_sb[:, pe_ * 1024 : (pe_ + 1) * 1024],
                    AP(ps_base, pe_ * 1024, [[2048, 128], [1, 1024]]), ALU.mult), "ve")
                ve_h_g[pe_] = b.cnt["ve"]
                b.wait(nc.gpsimd, "ve", ve_h_g[pe_])
                if e >= 2:
                    b.wait(nc.gpsimd, "te")
                wev = AP(webc, e * 1024 + th * 512, [[8 * 1024, 128], [0, 2], [1, 512]])
                b.inc(nc.gpsimd.tensor_tensor(
                    whbf[:, pe_ * 1024 : (pe_ + 1) * 1024],
                    h_sb[:, pe_ * 1024 : (pe_ + 1) * 1024], wev, ALU.mult), "po")
                po_wh = b.cnt["po"]
                b.wait(nc.vector, "po", po_wh)
                hbs = hbarbf[:, (sl * 2 + th) * 1024 : (sl * 2 + th + 1) * 1024]
                if e == 0:
                    b.inc(nc.vector.tensor_copy(hbs, whbf[:, pe_ * 1024 : (pe_ + 1) * 1024]), "ve")
                else:
                    b.wait(nc.vector, "ve")
                    b.inc(nc.vector.tensor_tensor(hbs, hbs,
                                                  whbf[:, pe_ * 1024 : (pe_ + 1) * 1024],
                                                  ALU.add), "ve")
                b.wait(nc.tensor, "po", po_wh)
                if md_ve:
                    b.wait(nc.tensor, "ve", md_ve)
                for kc in range(2):
                    b.inc(nc.tensor.matmul(
                        ps_md[:, :],
                        dA_sb[:, e * 128 + kc * 64 : e * 128 + (kc + 1) * 64],
                        whbf[:, pe_ * 1024 + kc * 512 : pe_ * 1024 + (kc + 1) * 512],
                        start=(kc == 0), stop=(kc == 1)), "te")
                te_md = b.cnt["te"]
                b.wait(nc.vector, "te", te_md)
                mslice = md_sb[ebase : ebase + 64, (e // 2) * 1024 + th * 512 :
                               (e // 2) * 1024 + (th + 1) * 512]
                b.inc(nc.vector.tensor_tensor(mslice, mslice, ps_md[:, :], ALU.add), "ve")
                md_ve = b.cnt["ve"]
            if ic % 2 == 1:
                b.wait(nc.tensor, "ve")
                dp_ve = {}
                for oc in range(NCH):
                    pu = ps_util[:, (oc % 2) * 512 : ((oc % 2) + 1) * 512]
                    if oc >= 2:
                        b.wait(nc.tensor, "ve", dp_ve[oc - 2])
                    for pr in range(2):
                        psl = (ic - 1 + pr) % 2
                        for kc in range(2):
                            b.inc(nc.tensor.matmul(
                                pu, wd_b[:, psl * 4096 + kc * 2048 + oc * 128 :
                                         psl * 4096 + kc * 2048 + (oc + 1) * 128],
                                hbarbf[:, (psl * 2 + th) * 1024 + kc * 512 :
                                       (psl * 2 + th) * 1024 + (kc + 1) * 512],
                                start=(pr == 0 and kc == 0), stop=(pr == 1 and kc == 1)), "te")
                    b.wait(nc.vector, "te")
                    xs_out = x[:, oc * TLOC + th * 512 : oc * TLOC + (th + 1) * 512]
                    dp_ve[oc] = b.inc(nc.vector.tensor_tensor(xs_out, xs_out, pu, ALU.add), "ve")
        for tag in ["g", "u", "gB", "uB", "dA"]:
            setg(tag, ic)
        if ic % 2 == 1:
            setg("d", ic - 1)
            setg("d", ic)
        if ic + 1 < NIC:
            loadw("g", ic + 1)
            loadw("u", ic + 1)
            loadw("dA", ic + 1)
        if ic % 2 == 1:
            if ic + 1 < NIC:
                loadw("d", ic + 1)
            if ic + 2 < NIC:
                loadw("d", ic + 2)
        if ic + 2 < NIC:
            for tag in ["gB", "uB"]:
                loadw(tag, ic + 2)
    b.pop()

    b.push()
    b.fence()
    mdb = b.sb("f1_mdbf", [128, 4 * 1024], BF16)
    dB_sb = b.sb("f1_dB", [128, 4 * 2048], BF16)
    if FFN1_SECT < 3:
        b.pop()
        b.pop()
        return
    b.inc(nc.vector.tensor_scalar_mul(mdb[:, :], md_sb[:, :], 0.25), "ve")
    ve_mdb = b.cnt["ve"]
    for e in range(8):
        dst = dB_sb[64 * (e % 2) : 64 * (e % 2) + 64,
                    (e // 2) * 2048 : (e // 2 + 1) * 2048]
        b.dma(nc.gpsimd, dst, AP(io["down_B"], e * 64 * 2048, [[2048, 64], [1, 2048]]),
              ctr="lg")
    lg_dB = b.cnt["lg"]
    b.wait(nc.tensor, "lg", lg_dB)
    b.wait(nc.tensor, "ve", ve_mdb)
    ol_ve = {}
    ps_ol = ps_util
    for oc in range(NCH):
        for th in range(2):
            puA = ps_ol[:, 0:512]
            puB = ps_ol[:, 512:1024]
            if oc >= 1 or th == 1:
                b.wait(nc.tensor, "ve")
            for parity, pu in [(0, puA), (1, puB)]:
                for i_e, e in enumerate([parity, 2 + parity, 4 + parity, 6 + parity]):
                    ebase = 64 * parity
                    b.inc(nc.tensor.matmul(
                        pu, dB_sb[ebase : ebase + 64, (e // 2) * 2048 + oc * 128 :
                                  (e // 2) * 2048 + (oc + 1) * 128],
                        mdb[ebase : ebase + 64, (e // 2) * 1024 + th * 512 :
                            (e // 2) * 1024 + (th + 1) * 512],
                        start=(i_e == 0), stop=(i_e == 3)), "te")
            b.wait(nc.vector, "te")
            xs_out = x[:, oc * TLOC + th * 512 : oc * TLOC + (th + 1) * 512]
            b.inc(nc.vector.tensor_tensor(xs_out, xs_out, puA, ALU.add), "ve")
            b.wait(nc.vector, "ve")
            ol_ve[(oc, th)] = b.inc(nc.vector.tensor_tensor(xs_out, xs_out, puB, ALU.add), "ve")
    b.pop()
    b.pop()



def emit_attn1(b, io):
    nc = b.nc
    x = b.tn["x_res"]
    xln = b.tn["x_ln"]
    ln_done = b.ln_done
    b.push()
    b.fence()

    SC = 1.0 / float(np.sqrt(128.0))
    identf = b.tn["ident_f32"]
    identb = b.tn["ident_bf"]
    onesr = b.tn["ones_row"]

    ag_in = nc.dram_tensor(b.name("ag2_in"), [1056, 256], BF16)
    ag_out = nc.dram_tensor(b.name("ag2_out"), [4224, 256], BF16)

    qnw = b.sb("a1_qnw", [1, 128], F32)
    knw = b.sb("a1_knw", [1, 128], F32)
    sinkr = b.sb("a1_sink", [1, 16], F32)
    rcos = b.sb("a1_rcos", [64, 1024], F32)
    rsin = b.sb("a1_rsin", [64, 1024], F32)
    a2o = b.sb("a1_a2o", [1, 4], mybir.dt.int32)
    b.dma(nc.sync, qnw[:, :], io["qnw"][:, :])
    b.dma(nc.sync, knw[:, :], io["knw"][:, :])
    b.dma(nc.sync, sinkr[:, :], io["sink"][:, :])
    b.dma(nc.sync, rcos[:, :], io["rcos"][:, :])
    b.dma(nc.sync, rsin[:, :], io["rsin"][:, :])
    b.dma(nc.sync, a2o[:, :], io["a2off"][:, :])
    ld_small = b.cnt["ld"]

    b.push()
    q_sb = b.sb("a1_q", [128, 16 * 1024], BF16)

    b.push()
    hq_sb = b.sb("a1_hq", [128, 4 * 1024], BF16)
    ksw_f = b.sb("a1_kswf", [128, 1024], F32)
    kn_sb = b.sb("a1_kn", [128, 1024], BF16)
    vsw_sb = b.sb("a1_vsw", [128, 1024], BF16)
    vswtm_sb = b.sb("a1_vswtm", [128, 1024], BF16)
    ck_f = b.sb("a1_ckf", [128, 64], F32)
    ckn_sb = b.sb("a1_ckn", [128, 64], BF16)
    wbuf = b.sb("a1_wbuf", [128, 2 * 2048], BF16)
    rsc = b.sb("a1_rsc", [32, 3 * 2048], F32)
    sqs = b.sb("a1_sqs", [128, 1024], F32)
    srow = b.sb("a1_srow", [1, 1024], F32)

    ps_a = b.ps("a1_psa", [128, 2 * 1024], F32)
    ps_st = b.ps("a1_psst", [1, 512], F32)
    ps_bc = b.ps("a1_psbc", [128, 1024], F32)
    ps_vt = b.ps("a1_psvt", [128, 1024], BF16)

    ld_w = {}
    wg_guard = {}

    def wload(tag, ioname, blk, idx, nkc, par):
        if (tag, idx - 2) in ld_w and ld_w[(tag, idx - 2)][2] is not None:
            b.wait(nc.gpsimd, "te", ld_w[(tag, idx - 2)][2])
        w = nkc * 128
        v = b.dma(nc.gpsimd,
                  AP(wbuf, (idx % 2) * 2048, [[2 * 2048, 128], [1, w]]),
                  AP(io[ioname], blk * 128 * w, [[w, 128], [1, w]]),
                  ctr=par)
        ld_w[(tag, idx)] = [par, v, None]

    b.wait(nc.tensor, "ve", ln_done)
    b.wait(nc.tensor, "ld", ld_small)
    wload("qc", "Wq_c", 0, 0, NCH, "wa")
    wload("qc", "Wq_c", 1, 1, NCH, "wb")
    for oc in range(4):
        pq = ps_a[:, (oc % 2) * 1024 : ((oc % 2) + 1) * 1024]
        b.wait(nc.tensor, ld_w[("qc", oc)][0], ld_w[("qc", oc)][1])
        if oc >= 2:
            b.wait(nc.tensor, "se")
        for hf in range(2):
            for c in range(NCH):
                xs = xln[:, c * TLOC + hf * 512 : c * TLOC + (hf + 1) * 512]
                b.inc(nc.tensor.matmul(pq[:, hf * 512 : (hf + 1) * 512],
                                       wbuf[:, (oc % 2) * 2048 + c * 128 : (oc % 2) * 2048 + (c + 1) * 128],
                                       xs, start=(c == 0), stop=(c == NCH - 1)), "te")
        ld_w[("qc", oc)][2] = b.cnt["te"]
        if oc + 2 < 4:
            wload("qc", "Wq_c", oc + 2, oc + 2, NCH, "wa" if oc % 2 == 0 else "wb")
        b.wait(nc.scalar, "te")
        b.inc(nc.scalar.activation(hq_sb[:, oc * 1024 : (oc + 1) * 1024], pq, AF.Copy), "se")
    se_hq = b.cnt["se"]
    for oc in range(16):
        if oc < 2:
            b.wait(nc.gpsimd, "te")
        wload("qu", "Wq_up", oc, 4 + oc, 4, "wa" if oc % 2 == 0 else "wb")
        pq = ps_a[:, (oc % 2) * 1024 : ((oc % 2) + 1) * 1024]
        b.wait(nc.tensor, ld_w[("qu", 4 + oc)][0], ld_w[("qu", 4 + oc)][1])
        if oc == 0:
            b.wait(nc.tensor, "se", se_hq)
        else:
            b.wait(nc.tensor, "se")
        for hf in range(2):
            for c in range(4):
                hs = hq_sb[:, c * 1024 + hf * 512 : c * 1024 + (hf + 1) * 512]
                b.inc(nc.tensor.matmul(pq[:, hf * 512 : (hf + 1) * 512],
                                       wbuf[:, (oc % 2) * 2048 + c * 128 : (oc % 2) * 2048 + (c + 1) * 128],
                                       hs, start=(c == 0), stop=(c == 3)), "te")
        ld_w[("qu", 4 + oc)][2] = b.cnt["te"]
        b.wait(nc.scalar, "te")
        b.inc(nc.scalar.activation(q_sb[:, oc * 1024 : (oc + 1) * 1024], pq, AF.Copy), "se")
    se_qpre = b.cnt["se"]
    b.wait(nc.gpsimd, "te")
    wload("kv", "Wk", 0, 20, NCH, "wa")
    wload("kv", "Wv", 0, 21, NCH, "wb")
    b.wait(nc.tensor, ld_w[("kv", 20)][0], ld_w[("kv", 20)][1])
    b.wait(nc.tensor, "se")
    for hf in range(2):
        for c in range(NCH):
            xs = xln[:, c * TLOC + hf * 512 : c * TLOC + (hf + 1) * 512]
            b.inc(nc.tensor.matmul(ps_a[:, hf * 512 : (hf + 1) * 512],
                                   wbuf[:, c * 128 : (c + 1) * 128], xs,
                                   start=(c == 0), stop=(c == NCH - 1)), "te")
    te_k = b.cnt["te"]
    b.wait(nc.tensor, ld_w[("kv", 21)][0], ld_w[("kv", 21)][1])
    for hf in range(2):
        for c in range(NCH):
            xs = xln[:, c * TLOC + hf * 512 : c * TLOC + (hf + 1) * 512]
            b.inc(nc.tensor.matmul(ps_a[:, 1024 + hf * 512 : 1024 + (hf + 1) * 512],
                                   wbuf[:, 2048 + c * 128 : 2048 + (c + 1) * 128], xs,
                                   start=(c == 0), stop=(c == NCH - 1)), "te")
    te_v = b.cnt["te"]
    b.wait(nc.scalar, "te", te_k)
    b.inc(nc.scalar.activation(ksw_f[:, :], ps_a[:, 0:1024], AF.Copy), "se")
    b.wait(nc.scalar, "te", te_v)
    b.inc(nc.scalar.activation(vsw_sb[:, :], ps_a[:, 1024:2048], AF.Copy), "se")
    se_kv = b.cnt["se"]

    wld_c = {}
    for j in range(16):
        par = "wc" if j % 2 == 0 else "wd"
        if j < 2:
            b.wait(nc.gpsimd, "te")
        else:
            b.wait(nc.gpsimd, "te", wld_c[j - 2])
        b.dma(nc.gpsimd, AP(wbuf, (j % 2) * 2048, [[2 * 2048, 128], [1, 2048]]),
              AP(io["W_comp"], j * 128 * 2048, [[2048, 128], [1, 2048]]),
              ctr=par)
        ldv = b.cnt[par]
        b.wait(nc.tensor, par, ldv)
        for dc in range(NCH):
            rhs = AP(xln, dc * TLOC + j, [[NCH * TLOC, 128], [16, 64]])
            b.inc(nc.tensor.matmul(ps_bc[:, 0:64],
                                   wbuf[:, (j % 2) * 2048 + dc * 128 : (j % 2) * 2048 + (dc + 1) * 128],
                                   rhs, start=(j == 0 and dc == 0), stop=(j == 15 and dc == NCH - 1)),
                  "te")
        wld_c[j] = b.cnt["te"]
    te_ck = b.cnt["te"]
    b.wait(nc.scalar, "te", te_ck)
    b.inc(nc.scalar.activation(ck_f[:, :], ps_bc[:, 0:64], AF.Copy), "se")
    se_ck = b.cnt["se"]

    def rope(buf, nblk, blkstride, cosap, sinap, base_ve):
        x1 = AP(buf, 0, [[blkstride * nblk if False else buf.shape[1], 32], [blkstride, nblk], [1, 1024]])
        return None

    b.wait(nc.vector, "se", se_qpre)
    QW = 16 * 1024
    tA = rsc[:, 0:2048]; tB = rsc[:, 2048:4096]; tC = rsc[:, 4096:6144]
    for hp in range(8):
        off = hp * 2048
        x1 = AP(q_sb, off, [[QW, 32], [1024, 2], [1, 1024]])
        x2 = AP(q_sb, 32 * QW + off, [[QW, 32], [1024, 2], [1, 1024]])
        cosr = AP(rcos, 0, [[1024, 32], [0, 2], [1, 1024]])
        sinr = AP(rsin, 0, [[1024, 32], [0, 2], [1, 1024]])
        cosr32 = AP(rcos, 32 * 1024, [[1024, 32], [0, 2], [1, 1024]])
        sinr32 = AP(rsin, 32 * 1024, [[1024, 32], [0, 2], [1, 1024]])
        b.wait(nc.vector, "ve")
        b.inc(nc.vector.tensor_tensor(tA, x1, cosr, ALU.mult), "ve")
        b.inc(nc.vector.tensor_tensor(tB, x1, sinr, ALU.mult), "ve")
        b.inc(nc.vector.tensor_tensor(tC, x2, sinr32, ALU.mult), "ve")
        b.wait(nc.vector, "ve")
        b.inc(nc.vector.tensor_tensor(x1, tA, tC, ALU.subtract), "ve")
        b.wait(nc.vector, "ve")
        b.inc(nc.vector.tensor_tensor(tC, x2, cosr32, ALU.mult), "ve")
        b.wait(nc.vector, "ve")
        b.inc(nc.vector.tensor_tensor(x2, tB, tC, ALU.add), "ve")
    ve_qrope = b.cnt["ve"]
    b.wait(nc.vector, "se", se_kv)
    b.wait(nc.vector, "ve")
    x1 = AP(ksw_f, 0, [[1024, 32], [1, 1024]])
    x2 = AP(ksw_f, 32 * 1024, [[1024, 32], [1, 1024]])
    cos1 = rcos[0:32, :]
    sin1 = rsin[0:32, :]
    cos32 = rcos[32:64, :]
    sin32 = rsin[32:64, :]
    b.inc(nc.vector.tensor_tensor(tA[:, 0:1024], x1, cos1, ALU.mult), "ve")
    b.inc(nc.vector.tensor_tensor(tB[:, 0:1024], x1, sin1, ALU.mult), "ve")
    b.inc(nc.vector.tensor_tensor(tC[:, 0:1024], x2, sin32, ALU.mult), "ve")
    b.wait(nc.vector, "ve")
    b.inc(nc.vector.tensor_tensor(x1, tA[:, 0:1024], tC[:, 0:1024], ALU.subtract), "ve")
    b.wait(nc.vector, "ve")
    b.inc(nc.vector.tensor_tensor(tC[:, 0:1024], x2, cos32, ALU.mult), "ve")
    b.wait(nc.vector, "ve")
    b.inc(nc.vector.tensor_tensor(x2, tB[:, 0:1024], tC[:, 0:1024], ALU.add), "ve")
    ve_krope = b.cnt["ve"]

    def rmsnorm_fm(src_t, out_t, src_col, w_row, width, ve_ready):
        b.wait(nc.scalar, "ve", ve_ready)
        for hf in range((width + 511) // 512):
            w0 = src_col + hf * 512
            w1 = min(src_col + width, w0 + 512)
            n = w1 - w0
            b.wait(nc.scalar, "te")
            b.inc(nc.scalar.activation(sqs[:, 0:n], src_t[:, w0:w1], AF.Square), "se")
            b.wait(nc.tensor, "se")
            b.inc(nc.tensor.matmul(ps_st[:, 0:n], b.tn["ones_col"][:, :],
                                   sqs[:, 0:n], start=True, stop=True), "te")
            te_ss = b.cnt["te"]
            b.wait(nc.vector, "te", te_ss)
            b.inc(nc.vector.tensor_scalar_mul(srow[:, 0:n], ps_st[:, 0:n], 1.0 / 128.0), "ve")
            b.wait(nc.scalar, "ve")
            b.inc(nc.scalar.activation(srow[:, 0:n], srow[:, 0:n], AF.Sqrt,
                                       bias=b.tn["cvals"][0:1, 1:2]), "se")
            b.wait(nc.vector, "se")
            b.inc(nc.vector.reciprocal(srow[:, 0:n], srow[:, 0:n]), "ve")
            ve_r = b.cnt["ve"]
            b.wait(nc.tensor, "ve", ve_r)
            b.inc(nc.tensor.matmul(ps_bc[:, 0:n], w_row, srow[:, 0:n],
                                   start=True, stop=True), "te")
            te_m = b.cnt["te"]
            b.wait(nc.vector, "te", te_m)
            b.inc(nc.vector.tensor_tensor(out_t[:, w0:w1], src_t[:, w0:w1],
                                          ps_bc[:, 0:n], ALU.mult), "ve")
        return b.cnt["ve"]

    last = ve_qrope
    for h in range(16):
        last = rmsnorm_fm(q_sb, q_sb, h * 1024, qnw[:, :], 1024, last)
    ve_qn = last
    ve_kn = rmsnorm_fm(ksw_f, kn_sb, 0, knw[:, :], 1024, ve_krope)
    b.wait(nc.vector, "se", se_ck)
    ve_ckn = rmsnorm_fm(ck_f, ckn_sb, 0, knw[:, :], 64, b.cnt["ve"])

    b.wait(nc.tensor, "se", se_kv)
    for tt in range(8):
        b.inc(nc.tensor.transpose(ps_vt[:, tt * 128 : (tt + 1) * 128],
                                  vsw_sb[:, tt * 128 : (tt + 1) * 128], identb[:, :]), "te")
    te_vt = b.cnt["te"]
    b.wait(nc.vector, "te", te_vt)
    b.inc(nc.vector.tensor_copy(vswtm_sb[:, :], ps_vt[:, :]), "ve")
    ve_vt = b.cnt["ve"]

    b.wait(nc.sync, "ve", ve_kn)
    b.dma(nc.sync, AP(ag_in, 0, [[1024, 128], [1, 1024]]), kn_sb[:, :], ctr="st")
    b.wait(nc.sync, "ve", ve_vt)
    b.dma(nc.sync, AP(ag_in, 131072, [[1024, 128], [1, 1024]]), vswtm_sb[:, :], ctr="st")
    b.wait(nc.sync, "ve", ve_ckn)
    b.dma(nc.sync, AP(ag_in, 262144, [[64, 128], [1, 64]]), ckn_sb[:, :], ctr="st")
    b.wait(nc.gpsimd, "st")
    b.inc(nc.gpsimd.collective_compute(
        "AllGather", ALU.bypass, replica_groups=[[0, 1, 2, 3], [4, 5, 6, 7]],
        ins=[ag_in.ap().opt()], outs=[ag_out.ap().opt()]), "cc")
    cc2 = b.cnt["cc"]
    b.wait(nc.vector, "ve", ve_qn)
    b.pop()

    b.push()
    b.fence()
    kswp = b.sb("a1_kswp", [128, 2 * 1024], BF16)
    vswp = b.sb("a1_vswp", [128, 2 * 1024], BF16)
    cka = b.sb("a1_cka", [128, 256], BF16)
    cktm = b.sb("a1_cktm", [128, 2 * 128], BF16)
    cmask = b.sb("a1_cmask", [128, 8 * 256], F32)
    swmr = b.sb("a1_swmr", [1, 8 * 640], F32)
    swmd = b.sb("a1_swmd", [128, 128], F32)
    swmo = b.sb("a1_swmo", [128, 128], F32)
    sinkpack = b.sb("a1_sinkpack", [128, 64], F32)
    spr = b.sb("a1_spr", [1, 64], F32)
    tsw = b.sb("a1_tsw", [128, 640], F32)
    ptile = b.sb("a1_ptile", [128, 640 + 256], BF16)
    pts = b.sb("a1_pts", [128, 7 * 128], BF16)
    zcol = b.sb("a1_zcol", [128, 8], F32)

    ps_sw = b.ps("a1_pssw", [128, 640], F32)
    ps_c = b.ps("a1_psc", [128, 256], F32)
    ps_o = b.ps("a1_pso", [128, 128], F32)
    ps_pt = b.ps("a1_pspt", [128, 7 * 128], BF16)
    ps_sk = b.ps("a1_pssk", [128, 64], F32)

    b.dma(nc.sync, cmask[:, :], AP(io["c_masks"], 0, [[256, 128], [128 * 256, 8], [1, 256]]))
    b.dma(nc.sync, swmr[:, :], io["swm_row"][:, :])
    b.dma(nc.sync, swmd[:, :], io["sw_mask_diag"][:, :])
    b.dma(nc.sync, swmo[:, :], io["sw_mask_old"][:, :])
    ld_m = b.cnt["ld"]

    regs = [b.scopes[-1].enter_context(nc.gpsimd.register(f"a1r{i}")) for i in range(4)]
    b.wait(nc.gpsimd, "ld", ld_small)
    for i in range(4):
        nc.gpsimd.reg_load(regs[i], a2o[0:1, i : i + 1])
    b.wait(nc.gpsimd, "cc", cc2)
    dst = [kswp[:, 0:1024], vswp[:, 0:1024], kswp[:, 1024:2048], vswp[:, 1024:2048]]
    for i in range(4):
        b.dma(nc.gpsimd, dst[i], AP(ag_out, regs[i], [[1024, 128], [1, 1024]]), ctr="lg")
    for r2 in range(4):
        b.dma(nc.sync, cka[:, r2 * 64 : (r2 + 1) * 64],
              AP(ag_out, r2 * 1056 * 256 + 262144, [[64, 128], [1, 64]]), ctr="ha")
    lg_kv = b.cnt["lg"]
    wc_ck = b.cnt["ha"]

    b.wait(nc.tensor, "ha", wc_ck)
    for tt in range(2):
        b.inc(nc.tensor.transpose(ps_pt[:, tt * 128 : (tt + 1) * 128],
                                  cka[:, tt * 128 : (tt + 1) * 128], identb[:, :]), "te")
    te_cktm = b.cnt["te"]
    b.wait(nc.vector, "te", te_cktm)
    b.inc(nc.vector.tensor_copy(cktm[:, :], AP(ps_pt, 0, [[896, 128], [1, 256]])), "ve")
    ve_cktm = b.cnt["ve"]

    b.wait(nc.vector, "ld", ld_small)
    b.inc(nc.vector.tensor_scalar_max(spr[:, 48:64], sinkr[:, :], 0.0), "ve")
    b.wait(nc.vector, "ve")
    b.inc(nc.vector.tensor_tensor(spr[:, 16:32], sinkr[:, :], spr[:, 48:64], ALU.subtract), "ve")
    b.inc(nc.vector.tensor_scalar_mul(spr[:, 0:16], spr[:, 48:64], -1.0), "ve")
    b.wait(nc.scalar, "ve")
    b.inc(nc.scalar.activation(spr[:, 16:32], spr[:, 16:32], AF.Exp), "se")
    b.wait(nc.vector, "ve")
    b.inc(nc.vector.tensor_scalar_mul(spr[:, 32:48], sinkr[:, :], -1.0), "ve")
    b.wait(nc.vector, "ve")
    b.inc(nc.vector.tensor_copy(spr[:, 48:64], sinkr[:, :]), "ve")
    ve_spr = b.cnt["ve"]
    b.wait(nc.tensor, "ve", ve_spr)
    b.wait(nc.tensor, "se")
    b.inc(nc.tensor.matmul(ps_sk[:, :], onesr[:, :], spr[:, :], start=True, stop=True), "te")
    b.wait(nc.scalar, "te")
    b.inc(nc.scalar.activation(sinkpack[:, :], ps_sk[:, :], AF.Copy), "se")
    se_sink = b.cnt["se"]

    b.wait(nc.tensor, "lg", lg_kv)
    b.wait(nc.tensor, "ld", ld_m)
    b.wait(nc.tensor, "se", se_sink)
    b.wait(nc.tensor, "ve", ve_cktm)
    for h in range(16):
        for qt in range(8):
            qtile = q_sb[:, h * 1024 + qt * 128 : h * 1024 + (qt + 1) * 128]
            w0 = 512 + 128 * qt
            b.wait(nc.tensor, "se")
            b.inc(nc.tensor.matmul(ps_sw[:, 0:512], onesr[:, :],
                                   swmr[0:1, qt * 640 : qt * 640 + 512],
                                   start=True, stop=False, skip_group_check=True), "te")
            b.inc(nc.tensor.matmul(ps_sw[:, 512:640], onesr[:, :],
                                   swmr[0:1, qt * 640 + 512 : (qt + 1) * 640],
                                   start=True, stop=False, skip_group_check=True), "te")
            b.inc(nc.tensor.matmul(ps_sw[:, 0:512], qtile,
                                   AP(kswp, w0, [[2048, 128], [1, 512]]),
                                   start=False, stop=True, skip_group_check=True), "te")
            b.inc(nc.tensor.matmul(ps_sw[:, 512:640], qtile,
                                   AP(kswp, w0 + 512, [[2048, 128], [1, 128]]),
                                   start=False, stop=True, skip_group_check=True), "te")
            te_sws = b.cnt["te"]
            b.wait(nc.scalar, "te", te_sws)
            b.wait(nc.scalar, "ve")
            b.inc(nc.scalar.activation(tsw[:, :], ps_sw[:, :], AF.Tanh,
                                       scale=SC / 50.0), "se")
            te_tanh = b.cnt["se"]
            b.wait(nc.vector, "se", te_tanh)
            b.inc(nc.vector.tensor_tensor(tsw[:, 0:128], tsw[:, 0:128], swmo[:, :], ALU.add), "ve")
            b.inc(nc.vector.tensor_tensor(tsw[:, 512:640], tsw[:, 512:640], swmd[:, :], ALU.add), "ve")
            ve_msk = b.cnt["ve"]
            b.wait(nc.scalar, "ve", ve_msk)
            b.inc(nc.scalar.activation(ptile[:, 0:640], tsw[:, :], AF.Exp, scale=50.0,
                                       bias=sinkpack[:, h : h + 1],
                                       accum_out=zcol[:, 0:1]), "se")
            se_swexp = b.cnt["se"]
            b.wait(nc.tensor, "ve")
            b.inc(nc.tensor.matmul(ps_c[:, :], qtile, cka[:, :],
                                   start=True, stop=True, skip_group_check=True), "te")
            te_cs = b.cnt["te"]
            b.wait(nc.vector, "te", te_cs)
            b.inc(nc.vector.tensor_tensor(ps_c[:, :], ps_c[:, :],
                                          cmask[:, qt * 256 : (qt + 1) * 256], ALU.add), "ve")
            b.wait(nc.vector, "ve")
            b.inc(nc.vector.tensor_reduce(zcol[:, 2:3], ps_c[:, :], AX.X, ALU.max,
                                          negate=True), "ve")
            b.wait(nc.vector, "ve")
            b.inc(nc.vector.tensor_scalar(zcol[:, 3:4], zcol[:, 2:3], SC,
                                          sinkpack[:, 32 + h : 33 + h], ALU.mult, ALU.min), "ve")
            ve_cbias = b.cnt["ve"]
            b.wait(nc.scalar, "ve", ve_cbias)
            b.inc(nc.scalar.activation(ptile[:, 640:896], ps_c[:, :], AF.Exp, scale=SC,
                                       bias=zcol[:, 3:4], accum_out=zcol[:, 4:5]), "se")
            se_cexp = b.cnt["se"]
            b.wait(nc.scalar, "se", se_cexp)
            b.inc(nc.scalar.activation(zcol[:, 5:6], zcol[:, 3:4], AF.Exp,
                                       bias=sinkpack[:, 48 + h : 49 + h]), "se")
            se_csink = b.cnt["se"]
            b.wait(nc.vector, "se", se_csink)
            b.inc(nc.vector.tensor_tensor(zcol[:, 4:5], zcol[:, 4:5], zcol[:, 5:6], ALU.add), "ve")
            b.inc(nc.vector.tensor_scalar(zcol[:, 1:2], zcol[:, 0:1],
                                          sinkpack[:, 16 + h : 17 + h], None, ALU.add), "ve")
            b.wait(nc.vector, "ve")
            b.inc(nc.vector.reciprocal(zcol[:, 6:7], zcol[:, 1:2]), "ve")
            b.inc(nc.vector.reciprocal(zcol[:, 7:8], zcol[:, 4:5]), "ve")
            b.wait(nc.vector, "ve")
            b.inc(nc.vector.tensor_scalar(ptile[:, 0:640], ptile[:, 0:640],
                                          zcol[:, 6:7], None, ALU.mult), "ve")
            b.inc(nc.vector.tensor_scalar(ptile[:, 640:896], ptile[:, 640:896],
                                          zcol[:, 7:8], None, ALU.mult), "ve")
            ve_pn = b.cnt["ve"]
            b.wait(nc.tensor, "ve", ve_pn)
            for tt in range(7):
                b.inc(nc.tensor.transpose(ps_pt[:, tt * 128 : (tt + 1) * 128],
                                          ptile[:, tt * 128 : (tt + 1) * 128], identb[:, :]),
                      "te")
            te_pt = b.cnt["te"]
            b.wait(nc.vector, "te", te_pt)
            b.inc(nc.vector.tensor_copy(pts[:, :], AP(ps_pt, 0, [[896, 128], [1, 896]])), "ve")
            ve_pts = b.cnt["ve"]
            b.wait(nc.tensor, "ve", ve_pts)
            for k in range(5):
                vt = AP(vswp, (4 + qt + k) * 128, [[2048, 128], [1, 128]])
                b.inc(nc.tensor.matmul(ps_o[:, :], vt, pts[:, k * 128 : (k + 1) * 128],
                                       start=(k == 0), stop=False, skip_group_check=True), "te")
            for tt in range(2):
                b.inc(nc.tensor.matmul(ps_o[:, :], cktm[:, tt * 128 : (tt + 1) * 128],
                                       pts[:, (5 + tt) * 128 : (6 + tt) * 128],
                                       start=False, stop=(tt == 1), skip_group_check=True), "te")
            te_av = b.cnt["te"]
            b.wait(nc.vector, "te", te_av)
            b.inc(nc.vector.tensor_copy(
                xln[:, h * 1024 + qt * 128 : h * 1024 + (qt + 1) * 128], ps_o[:, :]), "ve")
    ao_done = b.cnt["ve"]
    b.pop()
    b.fence()
    b.pop()

    b.push()
    b.fence()
    inter = b.sb("a1_inter", [128, 64 * 512], BF16)
    wbufc = b.sb("a1_wbufc", [128, 2 * 512], BF16)
    wbufo = b.sb("a1_wbufo", [128, 2 * 4096], BF16)
    ps_i = b.ps("a1_psi", [128, 2 * 512], F32)
    ps_r = b.ps("a1_psr", [128, 2 * 512], F32)

    ldc = {}
    ldo = {}
    for th in range(2):
        for ic in range(64):
            g = ic // 16
            par = "wa" if ic % 2 == 0 else "wb"
            key = th * 64 + ic
            if th == 0 and ic < 2:
                pass
            if key >= 2:
                b.wait(nc.gpsimd, "te", ldc[key - 2][1])
            v = b.dma(nc.gpsimd,
                      AP(wbufc, (ic % 2) * 512, [[2 * 512, 128], [1, 512]]),
                      AP(io["Wg"], ic * 128 * 512, [[512, 128], [1, 512]]),
                      ctr=par)
            ldc[key] = [(par, v), None]
            b.wait(nc.tensor, par, v)
            pi = ps_i[:, (ic % 2) * 512 : ((ic % 2) + 1) * 512]
            if key >= 2:
                b.wait(nc.tensor, "se", ldc[key - 2][2] if len(ldc[key - 2]) > 2 else 0)
            for c in range(4):
                aos = xln[:, (g * 4 + c) * TLOC + th * 512 : (g * 4 + c) * TLOC + (th + 1) * 512]
                b.inc(nc.tensor.matmul(pi, wbufc[:, (ic % 2) * 512 + c * 128 :
                                                 (ic % 2) * 512 + (c + 1) * 128],
                                       aos, start=(c == 0), stop=(c == 3)), "te")
            ldc[key][1] = b.cnt["te"]
            b.wait(nc.scalar, "te", ldc[key][1])
            se_i = b.inc(nc.scalar.activation(inter[:, ic * 512 : (ic + 1) * 512], pi,
                                              AF.Copy), "se")
            ldc[key].append(se_i)
        se_inter = b.cnt["se"]
        b.wait(nc.tensor, "se", se_inter)
        for oc in range(16):
            for half in range(2):
                key = (th * 16 + oc) * 2 + half
                par = "wc" if key % 2 == 0 else "wd"
                if th == 0 and oc == 0:
                    prevk = None
                if key >= 2 and (key - 2) in ldo:
                    b.wait(nc.gpsimd, "te", ldo[key - 2][1])
                v = b.dma(nc.gpsimd,
                          AP(wbufo, (key % 2) * 4096, [[2 * 4096, 128], [1, 4096]]),
                          AP(io["Wo"], (oc * 2 + half) * 128 * 4096, [[4096, 128], [1, 4096]]),
                          ctr=par)
                ldo[key] = [(par, v), None]
            pr = ps_r[:, (oc % 2) * 512 : ((oc % 2) + 1) * 512]
            if oc >= 2 or th == 1:
                b.wait(nc.tensor, "ve")
            for half in range(2):
                key = (th * 16 + oc) * 2 + half
                b.wait(nc.tensor, ldo[key][0][0], ldo[key][0][1])
                for c in range(32):
                    iis = inter[:, (half * 32 + c) * 512 : (half * 32 + c + 1) * 512]
                    b.inc(nc.tensor.matmul(pr, wbufo[:, (key % 2) * 4096 + c * 128 :
                                                     (key % 2) * 4096 + (c + 1) * 128],
                                           iis, start=(half == 0 and c == 0),
                                           stop=(half == 1 and c == 31)), "te")
                ldo[key][1] = b.cnt["te"]
            b.wait(nc.vector, "te")
            xs_out = x[:, oc * TLOC + th * 512 : oc * TLOC + (th + 1) * 512]
            b.inc(nc.vector.tensor_tensor(xs_out, xs_out, pr, ALU.add), "ve")
    b.pop()
    b.pop()



def emit_store(b, io_out):
    nc = b.nc
    x = b.tn["x_res"]
    b.fence()
    for c in range(NCH):
        b.dma(nc.sync, AP(io_out, c * 128 * TLOC, [[TLOC, 128], [1, TLOC]]),
              x[:, c * TLOC : (c + 1) * TLOC], ctr="st")
    b.wait(nc.sync, "st")



PHASE_ORDER = ["ffn0", "attn0", "ffn0", "ffn1", "attn1", "ffn1"]
FFN1_NIC = 28
FFN1_SECT = 9
OL_EXPERTS = list(range(8))


def build_program(phases):
    nc = bass.Bass("TRN2", target_bir_lowering=False)
    io = {}
    for name, shape, dt in ALL_INPUTS:
        io[name] = nc.dram_tensor(name, list(shape), dt, kind="ExternalInput")
    y = nc.dram_tensor("y", [2048, 1024], F32, kind="ExternalOutput")
    with contextlib.ExitStack() as stack:
        b = B(nc, stack)
        with nc.Block() as blk:
            @blk.gpsimd
            def _(g):
                emit_load(b, io)
                for ph in phases:
                    emit_ln(b)
                    if ph == "ffn0":
                        emit_ffn0(b, io)
                    elif ph == "attn0":
                        emit_attn0(b, io)
                    elif ph == "ffn1":
                        emit_ffn1(b, io)
                    elif ph == "attn1":
                        emit_attn1(b, io)
                    else:
                        raise ValueError(ph)
                emit_store(b, y)
    return nc


def run_phases(in_maps, phase_groups, trace=False):
    cur = [m["x"] for m in in_maps]
    res = None
    for grp in phase_groups:
        nc = build_program(grp)
        maps = []
        for c in range(NCORES):
            m = dict(in_maps[c])
            m["x"] = cur[c]
            maps.append(m)
        res = run_bass_kernel_spmd(nc, maps, core_ids=list(range(NCORES)), trace=trace)
        cur = [np.asarray(res.results[c]["y"]) for c in range(NCORES)]
    return cur, res


def assemble(cur):
    out = np.zeros((2, 4096, 2048), np.float32)
    for c in range(NCORES):
        bidx = c // 4
        t0 = 1024 * (c % 4)
        out[bidx, t0 : t0 + TLOC, :] = cur[c].T
    return out


def kernel(**inputs):
    in_maps = prep_host(inputs)
    cur, _ = run_phases(in_maps, [PHASE_ORDER])
    return assemble(cur)


# version 04
# speedup vs baseline: 1.0916x; 1.0115x over previous
import contextlib
import numpy as np
import ml_dtypes
import concourse.bass as bass
import concourse.mybir as mybir
from concourse.bass_utils import run_bass_kernel_spmd

F32 = mybir.dt.float32
BF16 = mybir.dt.bfloat16
AF = mybir.ActivationFunctionType
ALU = mybir.AluOpType
AX = mybir.AxisListType
AP = bass.AP

NCORES = 8
TLOC = 1024
D = 2048
NCH = D // 128
NEG = -1e30
BF = ml_dtypes.bfloat16


def _cc(a):
    return np.ascontiguousarray(a)



def prep_host(inputs):
    g = {}
    def colblocks(W, bs=128):
        K_, N_ = W.shape
        nk, nb = K_ // 128, N_ // bs
        return _cc(W.reshape(nk, 128, nb, bs).transpose(2, 1, 0, 3).reshape(nb, 128, nk * bs))

    def rowblocks(W, rs):
        K_, N_ = W.shape
        ns, nk = K_ // rs, rs // 128
        return _cc(W.reshape(ns, nk, 128, N_).transpose(0, 2, 1, 3).reshape(ns, 128, nk * N_))

    g["w_attn"] = colblocks(inputs["L0_attn_w_attn"])
    g["b_attn"] = _cc(inputs["L0_attn_b_attn"].reshape(48, 128).T)
    g["w_proj"] = colblocks(inputs["L0_attn_w_proj"])
    g["b_proj"] = _cc(inputs["L0_attn_b_proj"].reshape(16, 128).T)
    g["f0_bwT"] = colblocks(_cc(inputs["L0_ffn_base_w"].T))
    g["f0_bb"] = _cc(inputs["L0_ffn_base_b"].reshape(16, 128).T)
    g["f0_rwT"] = colblocks(_cc(inputs["L0_ffn_router_w"].T), bs=8)[0]
    g["f0_rb"] = _cc(inputs["L0_ffn_router_b"].reshape(8, 1))
    g["f0_Acat"] = colblocks(inputs["L0_ffn_lora_A"].transpose(1, 0, 2).reshape(2048, 128))[0]
    g["f0_Bcat"] = _cc(inputs["L0_ffn_lora_B"].reshape(128, 2048))
    g["Wq_c"] = colblocks(inputs["L1_attn_Wq_c"])
    g["Wq_up"] = colblocks(inputs["L1_attn_Wq_up"])
    g["Wk"] = colblocks(inputs["L1_attn_Wk"])[0]
    g["Wv"] = colblocks(inputs["L1_attn_Wv"])[0]
    g["W_comp"] = rowblocks(inputs["L1_attn_W_comp"], 2048)
    g["qnw"] = _cc(inputs["L1_attn_q_norm_w"].reshape(1, 128))
    g["knw"] = _cc(inputs["L1_attn_k_norm_w"].reshape(1, 128))
    g["sink"] = _cc(inputs["L1_attn_sink"].reshape(1, 16))
    Wgf = inputs["L1_attn_Wg"].reshape(4 * 512, 2048)
    wg = np.zeros((64, 128, 512), np.float32)
    for g_ in range(4):
        blk = colblocks(_cc(Wgf[g_ * 512 : (g_ + 1) * 512, :]))
        wg[g_ * 16 : (g_ + 1) * 16] = blk
    g["Wg"] = _cc(wg)
    Wo = inputs["L1_attn_Wo"]
    wo = np.zeros((32, 128, 4096), np.float32)
    for oc in range(16):
        for half in range(2):
            sl = _cc(Wo[half * 4096 : (half + 1) * 4096, oc * 128 : (oc + 1) * 128])
            wo[oc * 2 + half] = rowblocks(sl, 4096)[0]
    g["Wo"] = _cc(wo)
    g["wealth"] = _cc(inputs["L1_ffn_expert_wealth"].reshape(8, 1))
    g["conf_WT"] = colblocks(_cc(inputs["L1_ffn_conf_W"].T), bs=8)[0]
    g["conf_b"] = _cc(inputs["L1_ffn_conf_b"].reshape(8, 1))
    g["base_gate"] = colblocks(inputs["L1_ffn_base_gate"], bs=256)
    g["base_up"] = colblocks(inputs["L1_ffn_base_up"], bs=256)
    g["base_down"] = rowblocks(inputs["L1_ffn_base_down"], 256)
    g["gateAcat"] = colblocks(inputs["L1_ffn_gate_A"].transpose(1, 0, 2).reshape(2048, 512), bs=256)
    g["upAcat"] = colblocks(inputs["L1_ffn_up_A"].transpose(1, 0, 2).reshape(2048, 512), bs=256)
    def bpack(Bm):
        a = Bm.reshape(4, 2, 64, 28, 256).transpose(3, 1, 2, 0, 4).reshape(28, 2, 64, 1024)
        return _cc(a)
    g["gate_B"] = bpack(inputs["L1_ffn_gate_B"].reshape(4, 2, 64, 7168))
    g["up_B"] = bpack(inputs["L1_ffn_up_B"].reshape(4, 2, 64, 7168))
    dA = inputs["L1_ffn_down_A"]
    da = dA.reshape(8, 28, 2, 128, 64).transpose(1, 3, 0, 2, 4).reshape(28, 128, 8 * 128)
    g["down_A"] = _cc(da)
    g["down_B"] = _cc(inputs["L1_ffn_down_B"])

    g["ident_bf"] = np.eye(128, dtype=BF)
    g["ident_f32"] = np.eye(128, dtype=np.float32)
    g["ones_col"] = np.ones((128, 1), np.float32)
    g["ones_row"] = np.ones((1, 128), np.float32)
    g["ones_col_bf"] = np.ones((128, 1), BF)
    emat = np.zeros((8, 128), BF)
    for e in range(8):
        emat[e, 16 * e : 16 * e + 16] = 1
    g["emat"] = emat
    emat64 = np.zeros((8, 512), BF)
    for e in range(8):
        emat64[e, 64 * e : 64 * e + 64] = 1
    g["emat64"] = emat64
    a0m = np.zeros((128, 4, 512), np.float32)
    for i, Doff in enumerate([0, -128, -256, -384]):
        ss = np.arange(128)[:, None]
        qq = np.arange(512)[None, :]
        a0m[:, i, :] = np.where(qq + Doff - ss >= 0, 0.0, NEG)
    g["attn0_masks"] = _cc(a0m.reshape(128, 4 * 512))
    qq = np.arange(128)[:, None]
    ss = np.arange(128)[None, :]
    g["sw_mask_diag"] = _cc(np.where(ss <= qq, 0.0, -1e28).astype(np.float32))
    g["sw_mask_old"] = _cc(np.where(ss > qq, 0.0, -1e28).astype(np.float32))
    er = np.zeros((8, 8 * 128), np.float32)
    for e in range(8):
        er[e, e * 128 : (e + 1) * 128] = 1.0
    g["erow"] = er
    cv = np.zeros((128, 8), np.float32)
    cv[:, 0] = 1e-5
    cv[:, 1] = 1e-6
    cv[:, 2] = 1e-30
    g["cvals"] = cv

    rope = 64
    theta = 10000.0
    inv = 1.0 / theta ** (np.arange(0, rope, 2, dtype=np.float32) / rope)
    ang = np.arange(4096, dtype=np.float32)[:, None] * inv[None, :]
    cos_all = np.cos(ang).astype(np.float32)
    sin_all = np.sin(ang).astype(np.float32)

    in_maps = []
    for c in range(NCORES):
        bidx = c // 4
        t0 = 1024 * (c % 4)
        m = dict(g)
        x = inputs["L0_attn_x"][bidx, t0 : t0 + TLOC, :]
        m["x"] = _cc(np.asarray(x, np.float32).T)
        ct = cos_all[t0 : t0 + TLOC].T
        st_ = sin_all[t0 : t0 + TLOC].T
        m["rcos"] = _cc(np.concatenate([ct, ct], axis=0))
        m["rsin"] = _cc(np.concatenate([st_, st_], axis=0))
        cm = np.zeros((8, 128, 256), np.float32)
        for qt in range(8):
            q_global = t0 + qt * 128 + np.arange(128)
            blk_end = np.arange(256) * 16 + 15
            cm[qt] = np.where(blk_end[None, :] < q_global[:, None], 0.0, NEG)
        m["c_masks"] = _cc(cm.reshape(8 * 128, 256))
        r = c % 4
        oth = [(r + 1 + j) % 4 for j in range(3)]
        bo = np.zeros((1, 8), np.int32)
        bo[0, 0] = (4096 * r) * 1024
        bo[0, 1] = (4096 * r + 2048) * 1024
        for j in range(3):
            bo[0, 2 + 2 * j] = (4096 * oth[j]) * 1024
            bo[0, 3 + 2 * j] = (4096 * oth[j] + 2048) * 1024
        m["broff"] = bo
        mc = np.zeros((128, 24), np.float32)
        for j in range(3):
            val = 0.0 if oth[j] < r else NEG
            mc[:, j * 8 : (j + 1) * 8] = val
        m["mcol"] = _cc(mc)
        prev = (r - 1) % 4
        a2 = np.zeros((1, 4), np.int32)
        a2[0, 0] = prev * 1056 * 256
        a2[0, 1] = prev * 1056 * 256 + 131072
        a2[0, 2] = r * 1056 * 256
        a2[0, 3] = r * 1056 * 256 + 131072
        m["a2off"] = a2
        swm = np.zeros((8, 640), np.float32)
        if r == 0:
            for qt in range(8):
                swm[qt, 0 : 128 * (4 - qt) if qt < 4 else 0] = -1e6
        m["swm_row"] = _cc(swm.reshape(1, 8 * 640))
        in_maps.append(m)
    return in_maps


ALL_INPUTS = [
    ("x", (2048, 1024), F32),
    ("w_attn", (48, 128, 2048), F32), ("b_attn", (128, 48), F32),
    ("w_proj", (16, 128, 2048), F32), ("b_proj", (128, 16), F32),
    ("f0_bwT", (16, 128, 2048), F32), ("f0_bb", (128, 16), F32),
    ("f0_rwT", (128, 128), F32), ("f0_rb", (8, 1), F32),
    ("f0_Acat", (128, 2048), F32), ("f0_Bcat", (128, 2048), F32),
    ("Wq_c", (4, 128, 2048), F32), ("Wq_up", (16, 128, 512), F32),
    ("Wk", (128, 2048), F32), ("Wv", (128, 2048), F32),
    ("W_comp", (16, 128, 2048), F32),
    ("qnw", (1, 128), F32), ("knw", (1, 128), F32), ("sink", (1, 16), F32),
    ("Wg", (64, 128, 512), F32), ("Wo", (32, 128, 4096), F32),
    ("wealth", (8, 1), F32), ("conf_WT", (128, 128), F32), ("conf_b", (8, 1), F32),
    ("base_gate", (28, 128, 4096), F32), ("base_up", (28, 128, 4096), F32),
    ("base_down", (28, 128, 4096), F32),
    ("gateAcat", (2, 128, 4096), F32), ("upAcat", (2, 128, 4096), F32),
    ("gate_B", (28, 2, 64, 1024), F32), ("up_B", (28, 2, 64, 1024), F32),
    ("down_A", (28, 128, 1024), F32), ("down_B", (8, 64, 2048), F32),
    ("ident_bf", (128, 128), BF16), ("ident_f32", (128, 128), F32),
    ("ones_col", (128, 1), F32), ("ones_row", (1, 128), F32),
    ("ones_col_bf", (128, 1), BF16),
    ("emat", (8, 128), BF16), ("emat64", (8, 512), BF16),
    ("attn0_masks", (128, 4 * 512), F32),
    ("sw_mask_diag", (128, 128), F32), ("sw_mask_old", (128, 128), F32),
    ("rcos", (64, 1024), F32), ("rsin", (64, 1024), F32),
    ("c_masks", (8 * 128, 256), F32),
    ("cvals", (128, 8), F32),
    ("broff", (1, 8), mybir.dt.int32),
    ("erow", (8, 8 * 128), F32),
    ("a2off", (1, 4), mybir.dt.int32),
    ("swm_row", (1, 8 * 640), F32),
    ("mcol", (128, 24), F32),
]



class B:
    def __init__(self, nc, stack):
        self.nc = nc
        self.stack = stack
        self.scopes = [stack]
        self.sem = {}
        self.cnt = {}
        for name in ["ld", "lg", "st", "te", "ve", "se", "po", "cc",
                     "wa", "wb", "wc", "wd", "sa", "sb", "wx", "wy", "wz",
                     "ha", "hb", "hc", "hd"]:
            self.sem[name] = stack.enter_context(nc.semaphore(name))
            self.cnt[name] = 0
        self.tn = {}
        self.uid = 0

    def push(self):
        s = contextlib.ExitStack()
        self.scopes.append(s)
        return s

    def pop(self):
        s = self.scopes.pop()
        s.close()

    def name(self, base):
        self.uid += 1
        return f"{base}_{self.uid}"

    def inc(self, inst, ctr, n=1):
        inst.then_inc(self.sem[ctr], n)
        self.cnt[ctr] += n
        return self.cnt[ctr]

    def dma(self, eng, out, in_, ctr="ld"):
        if eng is self.nc.gpsimd and ctr == "ld":
            ctr = "lg"
        return self.inc(eng.dma_start(out=out, in_=in_), ctr, 16)

    def wait(self, eng, ctr, val=None):
        v = self.cnt[ctr] if val is None else val
        if v > 0:
            eng.wait_ge(self.sem[ctr], v)

    def sb(self, base, shape, dtype, persistent=False):
        s = self.stack if persistent else self.scopes[-1]
        t = s.enter_context(self.nc.sbuf_tensor(self.name(base), shape, dtype))
        if persistent:
            self.tn[base] = t
        return t

    def ps(self, base, shape, dtype):
        return self.scopes[-1].enter_context(
            self.nc.psum_tensor(self.name(base), shape, dtype))

    def fence(self):
        nc = self.nc
        for eng in [nc.tensor, nc.vector, nc.scalar, nc.gpsimd, nc.sync]:
            for ctr in ["te", "ve", "se", "po", "ld", "lg", "st", "cc",
                        "wa", "wb", "wc", "wd", "sa", "sb", "wx", "wy", "wz",
                     "ha", "hb", "hc", "hd"]:
                self.wait(eng, ctr)


def ap2(t, col, pat):
    rowlen = int(np.prod(t.shape[1:])) if hasattr(t, "shape") else None
    return AP(t, col, [[rowlen, t.shape[0]]] + pat)



def emit_load(b, io):
    nc = b.nc
    x = b.sb("x_res", [128, NCH * TLOC], F32, persistent=True)
    b.sb("x_ln", [128, NCH * TLOC], BF16, persistent=True)
    for c in range(NCH):
        b.dma(nc.sync, x[:, c * TLOC : (c + 1) * TLOC],
              AP(io["x"], c * 128 * TLOC, [[TLOC, 128], [1, TLOC]]))
    for nm, shape, dt in [("ident_bf", [128, 128], BF16), ("ident_f32", [128, 128], F32),
                          ("ones_col", [128, 1], F32), ("ones_row", [1, 128], F32),
                          ("ones_col_bf", [128, 1], BF16),
                          ("emat", [8, 128], BF16),
                          ("cvals", [128, 8], F32)]:
        t = b.sb(nm, shape, dt, persistent=True)
        b.dma(nc.sync, t[:, :], io[nm][:, :])
    b.load_done = b.cnt["ld"]



def emit_ln(b):
    nc = b.nc
    x = b.tn["x_res"]
    xln = b.tn["x_ln"]
    b.push()
    ps_sx = b.ps("ps_sx", [1, 1024], F32)
    ps_sxx = b.ps("ps_sxx", [1, 1024], F32)
    ps_bc = b.ps("ps_bc", [128, 2048], F32)
    sq = b.sb("sq", [128, 2 * 2048], BF16)
    sm = b.sb("lnsm", [1, 8 * 1024], F32)
    mean = sm[:, 0:1024]; ex2 = sm[:, 1024:2048]; m2 = sm[:, 2048:3072]
    var = sm[:, 3072:4096]; std = sm[:, 4096:5120]; rstd = sm[:, 5120:6144]
    bmr = sm[:, 6144:7168]
    ones = b.tn["ones_col"]
    onesr = b.tn["ones_row"]

    b.fence()
    sq_guard = {}
    onesb = b.tn["ones_col_bf"]
    for c in range(NCH):
        xbf = sq[:, (c % 2) * 1024 : (c % 2) * 1024 + 512]
        sqb = sq[:, (c % 2) * 1024 + 512 : ((c % 2) + 1) * 1024]
        xbf = AP(sq, (c % 2) * 2048, [[2 * 2048, 128], [1, 1024]])
        sqb = AP(sq, (c % 2) * 2048 + 1024, [[2 * 2048, 128], [1, 1024]])
        if c >= 2:
            b.wait(nc.scalar, "te", sq_guard[c - 2])
            b.wait(nc.vector, "te", sq_guard[c - 2])
        b.inc(nc.scalar.activation(xbf, x[:, c * TLOC : (c + 1) * TLOC], AF.Copy), "se")
        b.wait(nc.vector, "se")
        b.inc(nc.vector.tensor_tensor(sqb, xbf, xbf, ALU.mult), "ve")
        b.wait(nc.tensor, "se")
        b.wait(nc.tensor, "ve")
        for hf in range(2):
            xs = AP(sq, (c % 2) * 2048 + hf * 512, [[2 * 2048, 128], [1, 512]])
            b.inc(nc.tensor.matmul(ps_sx[:, hf * 512 : (hf + 1) * 512], onesb[:, :], xs,
                                   start=(c == 0), stop=(c == NCH - 1)), "te")
            qs = AP(sq, (c % 2) * 2048 + 1024 + hf * 512, [[2 * 2048, 128], [1, 512]])
            b.inc(nc.tensor.matmul(ps_sxx[:, hf * 512 : (hf + 1) * 512], onesb[:, :], qs,
                                   start=(c == 0), stop=(c == NCH - 1)), "te")
        sq_guard[c] = b.cnt["te"]

    b.wait(nc.vector, "te")
    b.inc(nc.vector.tensor_scalar_mul(mean, ps_sx[:, :], 1.0 / D), "ve")
    b.inc(nc.vector.tensor_scalar_mul(ex2, ps_sxx[:, :], 1.0 / D), "ve")
    b.wait(nc.vector, "ve")
    b.inc(nc.vector.tensor_tensor(m2, mean, mean, ALU.mult), "ve")
    b.wait(nc.vector, "ve")
    b.inc(nc.vector.tensor_tensor(var, ex2, m2, ALU.subtract), "ve")
    b.wait(nc.scalar, "ve")
    b.inc(nc.scalar.activation(std, var, AF.Sqrt, bias=b.tn["cvals"][0:1, 0:1]), "se")
    b.wait(nc.vector, "se")
    b.inc(nc.vector.reciprocal(rstd, std), "ve")
    b.wait(nc.vector, "ve")
    b.inc(nc.vector.scalar_tensor_tensor(bmr, mean, -1.0, rstd, ALU.mult, ALU.mult), "ve")

    b.wait(nc.tensor, "ve")
    for hf in range(2):
        b.inc(nc.tensor.matmul(ps_bc[:, hf * 512 : (hf + 1) * 512], onesr[:, :],
                               rstd[:, hf * 512 : (hf + 1) * 512], start=True, stop=True), "te")
        b.inc(nc.tensor.matmul(ps_bc[:, 1024 + hf * 512 : 1024 + (hf + 1) * 512], onesr[:, :],
                               bmr[:, hf * 512 : (hf + 1) * 512], start=True, stop=True), "te")
    te_bc = b.cnt["te"]

    b.wait(nc.vector, "te", te_bc)
    for c in range(NCH):
        buf = sq[:, (c % 2) * 1024 : ((c % 2) + 1) * 1024]
        b.wait(nc.vector, "ve")
        b.inc(nc.vector.tensor_tensor(buf, x[:, c * TLOC : (c + 1) * TLOC],
                                      ps_bc[:, 0:1024], ALU.mult), "ve")
        b.wait(nc.vector, "ve")
        b.inc(nc.vector.tensor_tensor(xln[:, c * TLOC : (c + 1) * TLOC], buf,
                                      ps_bc[:, 1024:2048], ALU.add), "ve")
    b.ln_done = b.cnt["ve"]
    b.pop()



def emit_ffn0(b, io):
    nc = b.nc
    x = b.tn["x_res"]
    xln = b.tn["x_ln"]
    ln_done = b.ln_done
    b.push()
    b.fence()

    rw = b.sb("f0_rw", [128, NCH * 8], BF16)
    acat = b.sb("f0_acat", [128, NCH * 128], BF16)
    bcat = b.sb("f0_bcat", [128, 2048], BF16)
    bb = b.sb("f0_bb", [128, 16], F32)
    bbh = b.sb("f0_bbh", [128, 16], F32)
    rb = b.sb("f0_rb", [8, 1], F32)
    b.dma(nc.gpsimd, rw[:, :], io["f0_rwT"][:, :])
    b.dma(nc.gpsimd, acat[:, :], io["f0_Acat"][:, :])
    b.dma(nc.gpsimd, bcat[:, :], io["f0_Bcat"][:, :])
    b.dma(nc.sync, bb[:, :], io["f0_bb"][:, :])
    b.dma(nc.sync, rb[:, :], io["f0_rb"][:, :])
    ld_small = b.cnt["ld"]
    lg_small = b.cnt["lg"]

    ps_r = b.ps("f0_psr", [8, 1024], F32)
    ps_mid = b.ps("f0_psmid", [128, 1024], F32)
    ps_small = b.ps("f0_pssmall", [128, 512], F32)
    ps_res = b.ps("f0_psres", [128, 2 * 512], F32)

    logits = b.sb("f0_logits", [8, 1024], F32)
    mid_sb = b.sb("f0_mid", [128, 1024], BF16)
    ltm = b.sb("f0_ltm", [128, 64], F32)
    sm = b.sb("f0_sm", [128, 6 * 64], F32)
    m1 = sm[:, 0:8]; m2 = sm[:, 8:16]; dd = sm[:, 16:24]
    w1 = sm[:, 24:32]; w2 = sm[:, 32:40]
    mask1 = sm[:, 64:128]; l2 = sm[:, 128:192]; mask2 = sm[:, 192:256]
    g1 = sm[:, 256:320]; gate_tm = sm[:, 320:384]
    gfm_sb = b.sb("f0_gfm", [8, 1024], BF16)
    gmid = b.sb("f0_gmid", [128, 1024], BF16)
    wbuf = b.sb("f0_wbuf", [128, 2 * 2048], BF16)
    tmp = b.sb("f0_tmp", [128, 2 * 512], F32)

    identf = b.tn["ident_f32"]
    emat = b.tn["emat"]

    b.wait(nc.vector, "ld", ld_small)
    b.inc(nc.vector.tensor_scalar_mul(bbh[:, :], bb[:, :], 0.5), "ve")
    bbh_done = b.cnt["ve"]

    b.wait(nc.tensor, "ld", ld_small)
    b.wait(nc.tensor, "lg", lg_small)
    b.wait(nc.tensor, "ve", ln_done)
    for c in range(NCH):
        for hf in range(2):
            xs = xln[:, c * TLOC + hf * 512 : c * TLOC + (hf + 1) * 512]
            b.inc(nc.tensor.matmul(ps_r[:, hf * 512 : (hf + 1) * 512],
                                   rw[:, c * 8 : (c + 1) * 8], xs,
                                   start=(c == 0), stop=(c == NCH - 1)), "te")
    te_router = b.cnt["te"]
    for c in range(NCH):
        for hf in range(2):
            xs = xln[:, c * TLOC + hf * 512 : c * TLOC + (hf + 1) * 512]
            b.inc(nc.tensor.matmul(ps_mid[:, hf * 512 : (hf + 1) * 512],
                                   acat[:, c * 128 : (c + 1) * 128], xs,
                                   start=(c == 0), stop=(c == NCH - 1)), "te")
    te_mid = b.cnt["te"]

    b.wait(nc.scalar, "te", te_router)
    b.inc(nc.scalar.activation(logits[:, :], ps_r[:, :], AF.Identity, bias=rb[:, 0:1]), "se")
    se_logits = b.cnt["se"]
    b.wait(nc.scalar, "te", te_mid)
    b.inc(nc.scalar.activation(mid_sb[:, :], ps_mid[:, :], AF.Copy), "se")
    se_mid = b.cnt["se"]

    b.wait(nc.tensor, "se", se_logits)
    for tt in range(8):
        b.inc(nc.tensor.transpose(ps_small[:, 8 * tt : 8 * tt + 8],
                                  logits[:, tt * 128 : (tt + 1) * 128], identf[0:8, 0:8]), "te")
    te_tm = b.cnt["te"]
    b.wait(nc.vector, "te", te_tm)
    b.inc(nc.vector.tensor_copy(ltm[:, :], ps_small[:, 0:64]), "ve")
    b.wait(nc.vector, "ve")

    ltm_v = AP(ltm, 0, [[64, 128], [8, 8], [1, 8]])
    SMW = 6 * 64
    m1rep = AP(sm, 0, [[SMW, 128], [1, 8], [0, 8]])
    m2rep = AP(sm, 8, [[SMW, 128], [1, 8], [0, 8]])
    w1rep = AP(sm, 24, [[SMW, 128], [1, 8], [0, 8]])
    w2rep = AP(sm, 32, [[SMW, 128], [1, 8], [0, 8]])
    l2_v = AP(sm, 128, [[SMW, 128], [8, 8], [1, 8]])

    b.inc(nc.vector.tensor_reduce(m1, ltm_v, AX.X, ALU.max), "ve")
    b.wait(nc.vector, "ve")
    b.inc(nc.vector.tensor_tensor(mask1, ltm[:, :], m1rep, ALU.is_equal), "ve")
    b.wait(nc.vector, "ve")
    b.inc(nc.vector.scalar_tensor_tensor(l2, mask1, NEG, ltm[:, :], ALU.mult, ALU.add), "ve")
    b.wait(nc.vector, "ve")
    b.inc(nc.vector.tensor_reduce(m2, l2_v, AX.X, ALU.max), "ve")
    b.wait(nc.vector, "ve")
    b.inc(nc.vector.tensor_tensor(mask2, l2, m2rep, ALU.is_equal), "ve")
    b.inc(nc.vector.tensor_tensor(dd, m2, m1, ALU.subtract), "ve")
    b.wait(nc.scalar, "ve")
    b.inc(nc.scalar.activation(w1, dd, AF.Sigmoid, scale=-1.0), "se")
    b.wait(nc.vector, "se")
    b.inc(nc.vector.tensor_scalar(w2, w1, -2.0, 2.0, ALU.mult, ALU.add), "ve")
    b.wait(nc.vector, "ve")
    b.inc(nc.vector.tensor_scalar_mul(w1, w1, 2.0), "ve")
    b.wait(nc.vector, "ve")
    b.inc(nc.vector.tensor_tensor(g1, mask1, w1rep, ALU.mult), "ve")
    b.wait(nc.vector, "ve")
    b.inc(nc.vector.tensor_tensor(gate_tm, mask2, w2rep, ALU.mult), "ve")
    b.wait(nc.vector, "ve")
    b.inc(nc.vector.tensor_tensor(gate_tm, gate_tm, g1, ALU.add), "ve")
    ve_gate = b.cnt["ve"]

    b.wait(nc.tensor, "ve", ve_gate)
    for tt in range(8):
        b.inc(nc.tensor.transpose(ps_r[:, tt * 128 : (tt + 1) * 128],
                                  gate_tm[:, 8 * tt : 8 * tt + 8], identf[:, :]), "te")
    te_gfm = b.cnt["te"]
    b.wait(nc.scalar, "te", te_gfm)
    b.inc(nc.scalar.activation(gfm_sb[:, :], ps_r[:, :], AF.Copy), "se")
    se_gfm = b.cnt["se"]
    b.wait(nc.tensor, "se", se_gfm)
    b.wait(nc.tensor, "se", se_mid)
    for hf in range(2):
        b.inc(nc.tensor.matmul(ps_mid[:, hf * 512 : (hf + 1) * 512], emat[:, :],
                               gfm_sb[:, hf * 512 : (hf + 1) * 512], start=True, stop=True), "te")
    te_gexp = b.cnt["te"]
    b.wait(nc.vector, "te", te_gexp)
    b.inc(nc.vector.tensor_tensor(gmid[:, :], mid_sb[:, :], ps_mid[:, :], ALU.mult), "ve")
    ve_gmid = b.cnt["ve"]

    ld_w = {}
    w_guard = {}
    ep_guard_se = {}
    ep_guard_ve = {}

    def issue_wload(oc):
        par = "wa" if oc % 2 == 0 else "wb"
        if oc >= 2:
            b.wait(nc.gpsimd, "te", w_guard[oc - 2])
        ld_w[oc] = (par, b.dma(nc.gpsimd,
                         AP(wbuf, (oc % 2) * 2048, [[2 * 2048, 128], [1, 2048]]),
                         AP(io["f0_bwT"], oc * 128 * 2048, [[2048, 128], [1, 2048]]),
                         ctr=par))

    issue_wload(0)
    issue_wload(1)
    for oc in range(NCH):
        wb = wbuf[:, (oc % 2) * 2048 : ((oc % 2) + 1) * 2048]
        b.wait(nc.tensor, ld_w[oc][0], ld_w[oc][1])
        if oc == 0:
            b.wait(nc.tensor, "ve", ve_gmid)
            b.wait(nc.scalar, "ve", bbh_done)
        for hf in range(2):
            pr = ps_res[:, hf * 512 : (hf + 1) * 512]
            if oc >= 1:
                b.wait(nc.tensor, "se", ep_guard_se[(oc - 1, hf)])
            for c in range(NCH):
                xs = xln[:, c * TLOC + hf * 512 : c * TLOC + (hf + 1) * 512]
                b.inc(nc.tensor.matmul(pr, wb[:, c * 128 : (c + 1) * 128], xs,
                                       start=(c == 0), stop=False), "te")
            b.inc(nc.tensor.matmul(pr, bcat[:, oc * 128 : (oc + 1) * 128],
                                   gmid[:, hf * 512 : (hf + 1) * 512],
                                   start=False, stop=True), "te")
        w_guard[oc] = b.cnt["te"]
        if oc + 2 < NCH:
            issue_wload(oc + 2)
        for hf in range(2):
            pr = ps_res[:, hf * 512 : (hf + 1) * 512]
            tb = tmp[:, hf * 512 : (hf + 1) * 512]
            b.wait(nc.scalar, "te", w_guard[oc])
            if oc >= 1:
                b.wait(nc.scalar, "ve", ep_guard_ve[(oc - 1, hf)])
            ep_guard_se[(oc, hf)] = b.inc(
                nc.scalar.activation(tb, pr, AF.Identity, bias=bbh[:, oc : oc + 1], scale=0.5), "se")
            b.wait(nc.vector, "se")
            xs_out = x[:, oc * TLOC + hf * 512 : oc * TLOC + (hf + 1) * 512]
            ep_guard_ve[(oc, hf)] = b.inc(
                nc.vector.tensor_tensor(xs_out, xs_out, tb, ALU.add), "ve")
    b.pop()




def emit_attn0(b, io):
    nc = b.nc
    x = b.tn["x_res"]
    xln = b.tn["x_ln"]
    ln_done = b.ln_done
    b.push()
    b.fence()

    SCALE = 1.0 / float(np.sqrt(128.0))
    ag_in = nc.dram_tensor(b.name("ag1_in"), [4096, 1024], BF16)
    ag_out = nc.dram_tensor(b.name("ag1_out"), [16384, 1024], BF16)

    q_sb = b.sb("a0_q", [128, 16 * 1024], BF16)
    ba = b.sb("a0_ba", [128, 48], F32)
    bp = b.sb("a0_bp", [128, 16], F32)
    mcol = b.sb("a0_mcol", [128, 24], F32)
    dm = b.sb("a0_dm", [128, 4 * 512], F32)
    bro = b.sb("a0_bro", [1, 8], mybir.dt.int32)
    b.dma(nc.sync, ba[:, :], io["b_attn"][:, :])
    b.dma(nc.sync, bp[:, :], io["b_proj"][:, :])
    b.dma(nc.sync, mcol[:, :], io["mcol"][:, :])
    b.dma(nc.sync, dm[:, :], io["attn0_masks"][:, :])
    b.dma(nc.sync, bro[:, :], io["broff"][:, :])
    ld_small = b.cnt["ld"]

    b.push()
    wbuf = b.sb("a0_wbuf", [128, 2 * 2048], BF16)
    kvst = b.sb("a0_kvst", [128, 2 * 1024], BF16)
    ps_qkv = b.ps("a0_psqkv", [128, 2 * 1024], F32)

    ld_w = {}
    w_guard = {}
    ev_guard = {}
    st_guard = {}

    a0_issue = [0]

    def wload_a0(oc):
        oi = a0_issue[0]
        a0_issue[0] += 1
        par = "wa" if oi % 2 == 0 else "wb"
        if oi >= 2:
            b.wait(nc.gpsimd, "te", w_guard[oi - 2])
        ld_w[oc] = (par, b.dma(nc.gpsimd,
                    AP(wbuf, (oi % 2) * 2048, [[2 * 2048, 128], [1, 2048]]),
                    AP(io["w_attn"], oc * 128 * 2048, [[2048, 128], [1, 2048]]),
                    ctr=par))

    OC_ORDER = list(range(16, 48)) + list(range(16))
    wload_a0(OC_ORDER[0])
    wload_a0(OC_ORDER[1])
    b.wait(nc.tensor, "ve", ln_done)
    b.wait(nc.scalar, "ld", ld_small)
    for oi, oc in enumerate(OC_ORDER):
        wb = wbuf[:, (oi % 2) * 2048 : ((oi % 2) + 1) * 2048]
        pq = ps_qkv[:, (oi % 2) * 1024 : ((oi % 2) + 1) * 1024]
        b.wait(nc.tensor, ld_w[oc][0], ld_w[oc][1])
        if oi >= 2:
            b.wait(nc.tensor, "se", ev_guard[OC_ORDER[oi - 2]])
        for c in range(NCH):
            for hf in range(2):
                xs = xln[:, c * TLOC + hf * 512 : c * TLOC + (hf + 1) * 512]
                b.inc(nc.tensor.matmul(pq[:, hf * 512 : (hf + 1) * 512],
                                       wb[:, c * 128 : (c + 1) * 128], xs,
                                       start=(c == 0), stop=(c == NCH - 1)), "te")
        w_guard[oi] = b.cnt["te"]
        if oi + 2 < 48:
            wload_a0(OC_ORDER[oi + 2])
        b.wait(nc.scalar, "te", w_guard[oi])
        if oc < 16:
            ev_guard[oc] = b.inc(nc.scalar.activation(
                q_sb[:, oc * 1024 : (oc + 1) * 1024], pq, AF.Identity,
                bias=ba[:, oc : oc + 1]), "se")
        else:
            kv = kvst[:, (oi % 2) * 1024 : ((oi % 2) + 1) * 1024]
            spar = "hc" if oi % 2 == 0 else "hd"
            if oi >= 2:
                b.wait(nc.scalar, spar, st_guard.get(OC_ORDER[oi - 2], 0))
            ev_guard[oc] = b.inc(nc.scalar.activation(
                kv, pq, AF.Identity, bias=ba[:, oc : oc + 1]), "se")
            b.wait(nc.sync, "se", ev_guard[oc])
            row = (oc - 16) * 128 if oc < 32 else 2048 + (oc - 32) * 128
            st_guard[oc] = b.dma(nc.sync, AP(ag_in, row * 1024, [[1024, 128], [1, 1024]]),
                                 kv, ctr=spar)
        if oi == 31:
            b.wait(nc.gpsimd, "hc")
            b.wait(nc.gpsimd, "hd")
            b.inc(nc.gpsimd.collective_compute(
                "AllGather", ALU.bypass, replica_groups=[[0, 1, 2, 3], [4, 5, 6, 7]],
                ins=[ag_in.ap().opt()], outs=[ag_out.ap().opt()]), "cc")
    cc_done = b.cnt["cc"]
    b.pop()

    b.push()
    kbuf = b.sb("a0_kbuf", [128, 2 * 3 * 1024], BF16)
    vbuf = b.sb("a0_vbuf", [128, 2 * 3 * 1024], BF16)
    kobuf = b.sb("a0_kobuf", [128, 2 * 1024], BF16)
    vobuf = b.sb("a0_vobuf", [128, 2 * 1024], BF16)
    vt_sb = b.sb("a0_vt", [128, 4 * 1024], BF16)
    pt_sb = b.sb("a0_pt", [128, 2 * 512], BF16)
    negm_sb = b.sb("a0_negm", [1, 2 * 1024], F32)
    negmc = b.sb("a0_negmc", [128, 8], F32)
    zr = b.sb("a0_zr", [1, 4 * 512], F32)
    zbc_sb = b.sb("a0_zbc", [128, 512], F32)

    ps_sA = b.ps("a0_pssA", [128, 512], F32)
    ps_sB = b.ps("a0_pssB", [128, 512], F32)
    ps_o = b.ps("a0_pso", [128, 512], F32)
    ps_z = b.ps("a0_psz", [1, 512], F32)
    ps_m = b.ps("a0_psm", [1, 1024], F32)
    ps_vt = b.ps("a0_psvt", [128, 1024], BF16)
    ps_zbc = b.ps("a0_pszbc", [128, 512], F32)

    onesr = b.tn["ones_row"]
    onescb = b.tn["ones_col_bf"]
    identf = b.tn["ident_f32"]
    identb = b.tn["ident_bf"]
    zcol = b.tn["cvals"][:, 3:4]

    kvn = nc.dram_tensor(b.name("a0_kvn"), [16384, 1024], BF16)
    regs = [b.scopes[-1].enter_context(nc.gpsimd.register(f"a0r{i}")) for i in range(8)]
    b.wait(nc.gpsimd, "ld", ld_small)
    for i in range(8):
        nc.gpsimd.reg_load(regs[i], bro[0:1, i : i + 1])
    b.wait(nc.gpsimd, "cc", cc_done)
    dst_rows = [0, 2048, 4096, 10240, 6144, 12288, 8192, 14336]
    dst_for = [0, 2048, 4096, 10240, 6144, 12288, 8192, 14336]
    for i in range(8):
        b.dma(nc.gpsimd, AP(kvn, dst_for[i] * 1024, [[1024, 2048], [1, 1024]]),
              AP(ag_out, regs[i], [[1024, 2048], [1, 1024]]), ctr="lg")
    kvn_done = b.cnt["lg"]

    kv_ld = {}

    def kvload(h):
        par = h % 2
        ctr = "ha" if par == 0 else "hb"
        if h >= 2:
            b.wait(nc.sync, "te", b.a0_head_guard[h - 2])
        else:
            b.wait(nc.sync, "lg", kvn_done)
        b.dma(nc.sync, kobuf[:, par * 1024 : (par + 1) * 1024],
              AP(kvn, (0 + 128 * h) * 1024, [[1024, 128], [1, 1024]]), ctr=ctr)
        b.dma(nc.sync, vobuf[:, par * 1024 : (par + 1) * 1024],
              AP(kvn, (2048 + 128 * h) * 1024, [[1024, 128], [1, 1024]]), ctr=ctr)
        for j in range(3):
            b.dma(nc.sync, kbuf[:, (par * 3 + j) * 1024 : (par * 3 + j + 1) * 1024],
                  AP(kvn, (4096 + 2048 * j + 128 * h) * 1024, [[1024, 128], [1, 1024]]), ctr=ctr)
            b.dma(nc.sync, vbuf[:, (par * 3 + j) * 1024 : (par * 3 + j + 1) * 1024],
                  AP(kvn, (10240 + 2048 * j + 128 * h) * 1024, [[1024, 128], [1, 1024]]), ctr=ctr)
        kv_ld[h] = (ctr, b.cnt[ctr])

    b.a0_head_guard = {}
    kvload(0)
    kvload(1)
    for h in range(16):
        par = h % 2
        ctr, ldv = kv_ld[h]
        b.wait(nc.tensor, ctr, ldv)
        vt_guard = b.cnt["ve"]
        for grp in range(4):
            if grp < 3:
                vsrc = vbuf[:, (par * 3 + grp) * 1024 : (par * 3 + grp + 1) * 1024]
            else:
                vsrc = vobuf[:, par * 1024 : (par + 1) * 1024]
            b.wait(nc.tensor, "ve", vt_guard)
            for tt in range(8):
                b.inc(nc.tensor.transpose(ps_vt[:, tt * 128 : (tt + 1) * 128],
                                          vsrc[:, tt * 128 : (tt + 1) * 128],
                                          identb[:, :]), "te")
            b.wait(nc.vector, "te")
            vt_guard = b.inc(nc.vector.tensor_copy(
                vt_sb[:, grp * 1024 : (grp + 1) * 1024], ps_vt[:, :]), "ve")
        msub_guard = b.cnt["ve"]
        for qt in range(8):
            ns = max(4, 4 * qt)
            b.wait(nc.tensor, "ve", msub_guard if qt == 0 else b.cnt["ve"])
            b.inc(nc.tensor.matmul(
                ps_sA[:, 0:ns],
                q_sb[:, h * 1024 + qt * 128 : h * 1024 + (qt + 1) * 128],
                AP(kobuf, par * 1024, [[2 * 1024, 128], [32, ns]]),
                start=True, stop=True), "te")
            b.wait(nc.vector, "te")
            b.inc(nc.vector.tensor_reduce(
                negmc[:, qt : qt + 1], ps_sA[:, 0:ns],
                AX.X, ALU.max, negate=True), "ve")
        b.wait(nc.tensor, "ve")
        for qt in range(8):
            b.inc(nc.tensor.transpose(ps_m[0:1, qt * 128 : (qt + 1) * 128],
                                      negmc[:, qt : qt + 1], identf[:, :]), "te")
        b.wait(nc.vector, "te")
        b.inc(nc.vector.tensor_copy(negm_sb[:, par * 1024 : (par + 1) * 1024],
                                    ps_m[:, :]), "ve")
        negm_done = b.cnt["ve"]

        for qb in range(2):
            visits = [("oth", st) for st in range(24)] +                      [("own", j) for j in range(4 * qb + 4)]
            nv = len(visits)
            exp_guard = {}
            zav_guard = {}
            for vi, (kind, idx) in enumerate(visits):
                pp = vi % 2
                ps_s = ps_sA if pp == 0 else ps_sB
                if vi >= 2:
                    b.wait(nc.tensor, "se", exp_guard[vi - 2])
                if vi == 0:
                    b.wait(nc.tensor, "ve")
                b.inc(nc.tensor.matmul(ps_s[:, :], onesr[:, :],
                                       negm_sb[0:1, par * 1024 + qb * 512 : par * 1024 + (qb + 1) * 512],
                                       start=True, stop=False, skip_group_check=True), "te")
                if kind == "oth":
                    kt = kbuf[:, (par * 3) * 1024 + idx * 128 : (par * 3) * 1024 + (idx + 1) * 128]
                    vt = vt_sb[:, idx * 128 : (idx + 1) * 128]
                    biascol = mcol[:, idx : idx + 1]
                    diag = None
                else:
                    kt = kobuf[:, par * 1024 + idx * 128 : par * 1024 + (idx + 1) * 128]
                    vt = vt_sb[:, 3 * 1024 + idx * 128 : 3 * 1024 + (idx + 1) * 128]
                    biascol = zcol
                    di = idx - 4 * qb
                    diag = dm[:, di * 512 : (di + 1) * 512] if 0 <= di < 4 else None
                b.inc(nc.tensor.matmul(ps_s[:, :], kt,
                                       q_sb[:, h * 1024 + qb * 512 : h * 1024 + (qb + 1) * 512],
                                       start=False, stop=True, skip_group_check=True), "te")
                te_s = b.cnt["te"]
                if diag is not None:
                    b.wait(nc.vector, "te", te_s)
                    b.inc(nc.vector.tensor_tensor(ps_s[:, :], ps_s[:, :], diag, ALU.add), "ve")
                    b.wait(nc.scalar, "ve")
                else:
                    b.wait(nc.scalar, "te", te_s)
                if vi >= 2:
                    b.wait(nc.scalar, "te", zav_guard[vi - 2])
                exp_guard[vi] = b.inc(nc.scalar.activation(
                    pt_sb[:, pp * 512 : (pp + 1) * 512], ps_s[:, :], AF.Exp,
                    bias=biascol, scale=SCALE), "se")
                b.wait(nc.tensor, "se", exp_guard[vi])
                b.inc(nc.tensor.matmul(ps_z[:, :], onescb[:, :],
                                       pt_sb[:, pp * 512 : (pp + 1) * 512],
                                       start=(vi == 0), stop=(vi == nv - 1),
                                       skip_group_check=True), "te")
                b.inc(nc.tensor.matmul(ps_o[:, :], vt,
                                       pt_sb[:, pp * 512 : (pp + 1) * 512],
                                       start=(vi == 0), stop=(vi == nv - 1),
                                       skip_group_check=True), "te")
                zav_guard[vi] = b.cnt["te"]
            te_fin = b.cnt["te"]
            b.wait(nc.vector, "te", te_fin)
            b.inc(nc.vector.tensor_copy(zr[:, 0:512], ps_z[:, :]), "ve")
            b.wait(nc.vector, "ve")
            b.inc(nc.vector.tensor_scalar_add(zr[:, 512:1024], zr[:, 0:512], 1e-30), "ve")
            b.wait(nc.vector, "ve")
            b.inc(nc.vector.reciprocal(zr[:, 1024:1536], zr[:, 512:1024]), "ve")
            ve_zi = b.cnt["ve"]
            b.wait(nc.tensor, "ve", ve_zi)
            b.inc(nc.tensor.matmul(ps_zbc[:, :], onesr[:, :], zr[0:1, 1024:1536],
                                   start=True, stop=True, skip_group_check=True), "te")
            b.wait(nc.scalar, "te")
            se_zbc = b.inc(nc.scalar.activation(zbc_sb[:, :], ps_zbc[:, :], AF.Identity,
                                                bias=zcol), "se")
            b.wait(nc.vector, "se", se_zbc)
            b.inc(nc.vector.tensor_tensor(
                xln[:, h * 1024 + qb * 512 : h * 1024 + (qb + 1) * 512],
                ps_o[:, :], zbc_sb[:, :], ALU.mult), "ve")
        b.a0_head_guard[h] = b.cnt["te"]
        if h + 2 < 16:
            kvload(h + 2)
    ao_done = b.cnt["ve"]
    b.pop()

    b.push()
    b.fence()
    wbufp = b.sb("a0_wbufp", [128, 2 * 2048], BF16)
    tmp = b.sb("a0_ptmp", [128, 2 * 512], F32)
    ps_res = b.ps("a0_psres", [128, 2 * 512], F32)
    ld_wp = {}
    wp_guard = {}
    epp_se = {}
    epp_ve = {}

    def wloadp(oc):
        par = "wa" if oc % 2 == 0 else "wb"
        if oc >= 2:
            b.wait(nc.gpsimd, "te", wp_guard[oc - 2])
        ld_wp[oc] = (par, b.dma(nc.gpsimd,
                     AP(wbufp, (oc % 2) * 2048, [[2 * 2048, 128], [1, 2048]]),
                     AP(io["w_proj"], oc * 128 * 2048, [[2048, 128], [1, 2048]]),
                     ctr=par))

    wloadp(0)
    wloadp(1)
    b.wait(nc.tensor, "ve", ao_done)
    for oc in range(NCH):
        wb = wbufp[:, (oc % 2) * 2048 : ((oc % 2) + 1) * 2048]
        b.wait(nc.tensor, ld_wp[oc][0], ld_wp[oc][1])
        for hf in range(2):
            pr = ps_res[:, hf * 512 : (hf + 1) * 512]
            if oc >= 1:
                b.wait(nc.tensor, "se", epp_se[(oc - 1, hf)])
            for c in range(NCH):
                aos = xln[:, c * TLOC + hf * 512 : c * TLOC + (hf + 1) * 512]
                b.inc(nc.tensor.matmul(pr, wb[:, c * 128 : (c + 1) * 128], aos,
                                       start=(c == 0), stop=(c == NCH - 1)), "te")
        wp_guard[oc] = b.cnt["te"]
        if oc + 2 < NCH:
            wloadp(oc + 2)
        for hf in range(2):
            pr = ps_res[:, hf * 512 : (hf + 1) * 512]
            tb = tmp[:, hf * 512 : (hf + 1) * 512]
            b.wait(nc.scalar, "te", wp_guard[oc])
            if oc >= 1:
                b.wait(nc.scalar, "ve", epp_ve[(oc - 1, hf)])
            epp_se[(oc, hf)] = b.inc(nc.scalar.activation(
                tb, pr, AF.Identity, bias=bp[:, oc : oc + 1]), "se")
            b.wait(nc.vector, "se")
            xs_out = x[:, oc * TLOC + hf * 512 : oc * TLOC + (hf + 1) * 512]
            epp_ve[(oc, hf)] = b.inc(
                nc.vector.tensor_tensor(xs_out, xs_out, tb, ALU.add), "ve")
    b.pop()
    b.pop()



def emit_ffn1(b, io):
    nc = b.nc
    x = b.tn["x_res"]
    xln = b.tn["x_ln"]
    ln_done = b.ln_done
    b.push()
    b.fence()

    IC = 256
    NIC = 7168 // IC
    NIC_RUN = FFN1_NIC
    identf = b.tn["ident_f32"]

    confw = b.sb("f1_confw", [128, NCH * 8], BF16)
    cb = b.sb("f1_cb", [8, 1], F32)
    wl = b.sb("f1_wl", [8, 1], F32)
    webc = b.sb("f1_webc", [128, 8 * 1024], BF16)
    mid_g = b.sb("f1_midg", [128, 4 * 1024], BF16)
    mid_u = b.sb("f1_midu", [128, 4 * 1024], BF16)
    md_sb = b.sb("f1_md", [128, 4 * 1024], BF16)
    ps_base = b.ps("f1_psbase", [128, 4 * 512], F32)
    ps_md = b.ps("f1_psmd", [64, 512], F32)
    ps_util = b.ps("f1_psutil", [128, 2 * 512], F32)
    ps_tm = b.ps("f1_pstm", [128, 128], F32)

    b.push()
    erow = b.sb("f1_erow", [8, 8 * 128], F32)
    we8 = b.sb("f1_we8", [8, 1024], F32)
    conf_sb = b.sb("f1_conf", [8, 1024], F32)
    bids = b.sb("f1_bids", [8, 1024], F32)
    ltm = b.sb("f1_ltm", [128, 64], F32)
    sm = b.sb("f1_sm", [128, 6 * 64], F32)
    m1 = sm[:, 0:8]; m2 = sm[:, 8:16]; dd = sm[:, 16:24]
    w1 = sm[:, 24:32]; w2 = sm[:, 32:40]
    mask1 = sm[:, 64:128]; l2 = sm[:, 128:192]; mask2 = sm[:, 192:256]
    g1 = sm[:, 256:320]; gate_tm = sm[:, 320:384]

    b.dma(nc.gpsimd, confw[:, :], io["conf_WT"][:, :])
    b.dma(nc.sync, cb[:, :], io["conf_b"][:, :])
    b.dma(nc.sync, wl[:, :], io["wealth"][:, :])
    b.dma(nc.sync, erow[:, :], io["erow"][:, :])
    ld_small = b.cnt["ld"]
    lg_small = b.cnt["lg"]

    b.wait(nc.tensor, "lg", lg_small)
    b.wait(nc.tensor, "ve", ln_done)
    for hf in range(2):
        for c in range(NCH):
            xs = xln[:, c * TLOC + hf * 512 : c * TLOC + (hf + 1) * 512]
            b.inc(nc.tensor.matmul(ps_util[0:8, hf * 512 : (hf + 1) * 512],
                                   confw[:, c * 8 : (c + 1) * 8], xs,
                                   start=(c == 0), stop=(c == NCH - 1)), "te")
    te_conf = b.cnt["te"]
    b.wait(nc.scalar, "te", te_conf)
    b.wait(nc.scalar, "ld", ld_small)
    b.inc(nc.scalar.activation(conf_sb[:, :], ps_util[0:8, 0:1024], AF.Sigmoid,
                               bias=cb[:, 0:1]), "se")
    b.wait(nc.vector, "se")
    b.inc(nc.vector.tensor_scalar(bids[:, :], conf_sb[:, :], wl[:, 0:1], None, ALU.mult), "ve")
    ve_bids = b.cnt["ve"]
    b.wait(nc.tensor, "ve", ve_bids)
    for tt in range(8):
        b.inc(nc.tensor.transpose(AP(ps_tm, 8 * tt, [[128, 128], [1, 8]]),
                                  bids[:, tt * 128 : (tt + 1) * 128], identf[0:8, 0:8]), "te")
    te_tm = b.cnt["te"]
    b.wait(nc.vector, "te", te_tm)
    b.inc(nc.vector.tensor_copy(ltm[:, :], AP(ps_tm, 0, [[128, 128], [1, 64]])), "ve")
    b.wait(nc.vector, "ve")
    SMW = 6 * 64
    ltm_v = AP(ltm, 0, [[64, 128], [8, 8], [1, 8]])
    m1rep = AP(sm, 0, [[SMW, 128], [1, 8], [0, 8]])
    m2rep = AP(sm, 8, [[SMW, 128], [1, 8], [0, 8]])
    w1rep = AP(sm, 24, [[SMW, 128], [1, 8], [0, 8]])
    w2rep = AP(sm, 32, [[SMW, 128], [1, 8], [0, 8]])
    l2_v = AP(sm, 128, [[SMW, 128], [8, 8], [1, 8]])
    b.inc(nc.vector.tensor_reduce(m1, ltm_v, AX.X, ALU.max), "ve")
    b.wait(nc.vector, "ve")
    b.inc(nc.vector.tensor_tensor(mask1, ltm[:, :], m1rep, ALU.is_equal), "ve")
    b.wait(nc.vector, "ve")
    b.inc(nc.vector.scalar_tensor_tensor(l2, mask1, NEG, ltm[:, :], ALU.mult, ALU.add), "ve")
    b.wait(nc.vector, "ve")
    b.inc(nc.vector.tensor_reduce(m2, l2_v, AX.X, ALU.max), "ve")
    b.wait(nc.vector, "ve")
    b.inc(nc.vector.tensor_tensor(mask2, l2, m2rep, ALU.is_equal), "ve")
    b.inc(nc.vector.tensor_tensor(dd, m2, m1, ALU.subtract), "ve")
    b.wait(nc.scalar, "ve")
    b.inc(nc.scalar.activation(w1, dd, AF.Sigmoid, scale=-1.0), "se")
    b.wait(nc.vector, "se")
    b.inc(nc.vector.tensor_scalar(w2, w1, -0.5, 0.5, ALU.mult, ALU.add), "ve")
    b.wait(nc.vector, "ve")
    b.inc(nc.vector.tensor_scalar_mul(w1, w1, 0.5), "ve")
    b.wait(nc.vector, "ve")
    b.inc(nc.vector.tensor_tensor(g1, mask1, w1rep, ALU.mult), "ve")
    b.wait(nc.vector, "ve")
    b.inc(nc.vector.tensor_tensor(gate_tm, mask2, w2rep, ALU.mult), "ve")
    b.wait(nc.vector, "ve")
    b.inc(nc.vector.tensor_tensor(gate_tm, gate_tm, g1, ALU.add), "ve")
    ve_gate = b.cnt["ve"]
    b.wait(nc.tensor, "ve", ve_gate)
    b.wait(nc.tensor, "se")
    for tt in range(8):
        b.inc(nc.tensor.transpose(ps_util[0:8, tt * 128 : (tt + 1) * 128],
                                  gate_tm[:, 8 * tt : 8 * tt + 8], identf[:, :]), "te")
    te_we = b.cnt["te"]
    b.wait(nc.vector, "te", te_we)
    b.inc(nc.vector.tensor_copy(we8[:, :], ps_util[0:8, 0:1024]), "ve")
    ve_we8 = b.cnt["ve"]
    b.wait(nc.tensor, "ve", ve_we8)
    for e in range(8):
        if e > 0:
            b.wait(nc.tensor, "se")
        for hf in range(2):
            b.inc(nc.tensor.matmul(ps_util[:, hf * 512 : (hf + 1) * 512],
                                   erow[:, e * 128 : (e + 1) * 128],
                                   we8[:, hf * 512 : (hf + 1) * 512],
                                   start=True, stop=True), "te")
        b.wait(nc.scalar, "te")
        b.inc(nc.scalar.activation(webc[:, e * 1024 : (e + 1) * 1024],
                                   ps_util[:, :], AF.Copy), "se")
    se_webc = b.cnt["se"]
    b.fence()
    b.pop()

    b.push()
    wg_b = b.sb("f1_wgb", [128, 4096], BF16)
    wu_b = b.sb("f1_wub", [128, 4096], BF16)
    wd_b = b.sb("f1_wdb", [128, 2 * 4096], BF16)
    gB_sb = b.sb("f1_gB2", [128, 2 * 1024], BF16)
    uB_sb = b.sb("f1_uB2", [128, 2 * 1024], BF16)
    dA_sb = b.sb("f1_dA", [128, 1024], BF16)
    G_sb = b.sb("f1_G", [128, 1024], BF16)
    U_sb = b.sb("f1_U", [128, 1024], BF16)
    s_sb = b.sb("f1_s", [128, 2 * 1024], BF16)
    h_sb = b.sb("f1_h", [128, 2 * 1024], BF16)
    whbf = b.sb("f1_whbf", [128, 2 * 1024], BF16)
    hbarbf = b.sb("f1_hbarbf", [128, 4 * 1024], BF16)

    b.wait(nc.tensor, "se", se_webc)
    for (mid, wsrc, ioA, par) in [(mid_g, wg_b, "gateAcat", "wa"),
                                  (mid_u, wu_b, "upAcat", "wb")]:
        for piece in range(2):
            if piece == 1:
                b.wait(nc.gpsimd, "te")
            ldv = b.dma(nc.gpsimd, AP(wsrc, 0, [[4096, 128], [1, 4096]]),
                        AP(io[ioA], piece * 128 * 4096, [[4096, 128], [1, 4096]]),
                        ctr=par)
            b.wait(nc.tensor, par, ldv)
            for mc2 in range(2):
                mc = piece * 2 + mc2
                c0 = mc2 * 128
                for hf in range(2):
                    pu = ps_util[:, hf * 512 : (hf + 1) * 512]
                    for c in range(NCH):
                        xs = xln[:, c * TLOC + hf * 512 : c * TLOC + (hf + 1) * 512]
                        b.inc(nc.tensor.matmul(pu, wsrc[:, c * 256 + c0 : c * 256 + c0 + 128],
                                               xs, start=(c == 0), stop=(c == NCH - 1)), "te")
                    b.wait(nc.scalar, "te")
                    b.inc(nc.scalar.activation(mid[:, mc * 1024 + hf * 512 : mc * 1024 + (hf + 1) * 512],
                                               pu, AF.Copy, scale=0.25), "se")
                    b.wait(nc.tensor, "se")
    se_mid = b.cnt["se"]
    b.wait(nc.vector, "se", se_mid)
    b.inc(nc.vector.memset(md_sb[:, :], 0.0), "ve")

    wload = {}

    def loadw(tag, ic):
        par = {"g": ("wa", "wa"), "u": ("wb", "wb"), "d": ("sa", "sb"),
               "gB": ("wd", "wx"), "uB": ("wy", "wz"), "dA": ("wc", "wc")}[tag][ic % 2]
        back = 1 if tag in ("g", "u", "dA") else 2
        if (tag, ic - back) in wload and wload[(tag, ic - back)][2] is not None:
            b.wait(nc.gpsimd, "te", wload[(tag, ic - back)][2])
        sl = ic % 2
        if tag == "g":
            v = b.dma(nc.gpsimd, AP(wg_b, 0, [[4096, 128], [1, 4096]]),
                      AP(io["base_gate"], ic * 128 * 4096, [[4096, 128], [1, 4096]]),
                      ctr=par)
        elif tag == "u":
            v = b.dma(nc.gpsimd, AP(wu_b, 0, [[4096, 128], [1, 4096]]),
                      AP(io["base_up"], ic * 128 * 4096, [[4096, 128], [1, 4096]]),
                      ctr=par)
        elif tag == "d":
            v = b.dma(nc.gpsimd, AP(wd_b, sl * 4096, [[2 * 4096, 128], [1, 4096]]),
                      AP(io["base_down"], ic * 128 * 4096, [[4096, 128], [1, 4096]]),
                      ctr=par)
        elif tag in ("gB", "uB"):
            buf = gB_sb if tag == "gB" else uB_sb
            ionm = "gate_B" if tag == "gB" else "up_B"
            for parity in range(2):
                dst = buf[64 * parity : 64 * parity + 64, sl * 1024 : (sl + 1) * 1024]
                v = b.dma(nc.gpsimd,
                          AP(dst.tensor, dst.offset, [[2 * 1024, 64], [1, 1024]]),
                          AP(io[ionm], (ic * 2 + parity) * 64 * 1024, [[1024, 64], [1, 1024]]),
                          ctr=par)
        elif tag == "dA":
            v = b.dma(nc.gpsimd, AP(dA_sb, 0, [[1024, 128], [1, 1024]]),
                      AP(io["down_A"], ic * 128 * 1024, [[1024, 128], [1, 1024]]),
                      ctr=par)
        wload[(tag, ic)] = [par, v, None]

    def setg(tag, ic):
        wload[(tag, ic)][2] = b.cnt["te"]

    b.wait(nc.gpsimd, "te")
    loadw("g", 0)
    loadw("u", 0)
    loadw("dA", 0)
    for ic in [0, 1]:
        for tag in ["d", "gB", "uB"]:
            loadw(tag, ic)

    wload = {}

    def loadw(tag, ic):
        par = {"g": ("wa", "wa"), "u": ("wb", "wb"), "d": ("sa", "sb"),
               "gB": ("wd", "wx"), "uB": ("wy", "wz"), "dA": ("wc", "wc")}[tag][ic % 2]
        back = 1 if tag in ("g", "u", "dA") else 2
        if (tag, ic - back) in wload and wload[(tag, ic - back)][2] is not None:
            b.wait(nc.gpsimd, "te", wload[(tag, ic - back)][2])
        sl = ic % 2
        if tag == "g":
            v = b.dma(nc.gpsimd, AP(wg_b, 0, [[4096, 128], [1, 4096]]),
                      AP(io["base_gate"], ic * 128 * 4096, [[4096, 128], [1, 4096]]),
                      ctr=par)
        elif tag == "u":
            v = b.dma(nc.gpsimd, AP(wu_b, 0, [[4096, 128], [1, 4096]]),
                      AP(io["base_up"], ic * 128 * 4096, [[4096, 128], [1, 4096]]),
                      ctr=par)
        elif tag == "d":
            v = b.dma(nc.gpsimd, AP(wd_b, sl * 4096, [[2 * 4096, 128], [1, 4096]]),
                      AP(io["base_down"], ic * 128 * 4096, [[4096, 128], [1, 4096]]),
                      ctr=par)
        elif tag in ("gB", "uB"):
            buf = gB_sb if tag == "gB" else uB_sb
            ionm = "gate_B" if tag == "gB" else "up_B"
            for parity in range(2):
                dst = buf[64 * parity : 64 * parity + 64, sl * 1024 : (sl + 1) * 1024]
                v = b.dma(nc.gpsimd,
                          AP(dst.tensor, dst.offset, [[2 * 1024, 64], [1, 1024]]),
                          AP(io[ionm], (ic * 2 + parity) * 64 * 1024, [[1024, 64], [1, 1024]]),
                          ctr=par)
        elif tag == "dA":
            v = b.dma(nc.gpsimd, AP(dA_sb, 0, [[1024, 128], [1, 1024]]),
                      AP(io["down_A"], ic * 128 * 1024, [[1024, 128], [1, 1024]]),
                      ctr=par)
        wload[(tag, ic)] = [par, v, None]

    def setg(tag, ic):
        wload[(tag, ic)][2] = b.cnt["te"]

    b.wait(nc.gpsimd, "te")
    loadw("g", 0)
    loadw("u", 0)
    loadw("dA", 0)
    for ic in [0, 1]:
        for tag in ["d", "gB", "uB"]:
            loadw(tag, ic)

    identb = b.tn["ident_bf"]
    ve_h_g = {}
    se_s_g = {}
    md_ve = 0
    for ic in range(min(NIC, FFN1_NIC) if FFN1_SECT >= 2 else 0):
        for tag in ["g", "u", "d", "gB", "uB", "dA"]:
            b.wait(nc.tensor, wload[(tag, ic)][0], wload[(tag, ic)][1])
        sl = (ic % 2)
        for th in range(2):
            if ic > 0 or th > 0:
                b.wait(nc.tensor, "ve")
                b.wait(nc.tensor, "se")
            for oc in range(2):
                pb = ps_base[:, oc * 512 : (oc + 1) * 512]
                for c in range(NCH):
                    xs = xln[:, c * TLOC + th * 512 : c * TLOC + (th + 1) * 512]
                    b.inc(nc.tensor.matmul(pb, wg_b[:, c * 256 + oc * 128 :
                                                    c * 256 + (oc + 1) * 128],
                                           xs, start=(c == 0), stop=(c == NCH - 1)), "te")
            te_G = b.cnt["te"]
            b.wait(nc.scalar, "te", te_G)
            b.inc(nc.scalar.activation(G_sb[:, :], AP(ps_base, 0, [[2048, 128], [1, 1024]]),
                                       AF.Copy), "se")
            se_G = b.cnt["se"]
            for oc in range(2):
                pb = ps_base[:, 1024 + oc * 512 : 1024 + (oc + 1) * 512]
                for c in range(NCH):
                    xs = xln[:, c * TLOC + th * 512 : c * TLOC + (th + 1) * 512]
                    b.inc(nc.tensor.matmul(pb, wu_b[:, c * 256 + oc * 128 :
                                                    c * 256 + (oc + 1) * 128],
                                           xs, start=(c == 0), stop=(c == NCH - 1)), "te")
            te_U = b.cnt["te"]
            b.wait(nc.scalar, "te", te_U)
            b.inc(nc.scalar.activation(U_sb[:, :], AP(ps_base, 1024, [[2048, 128], [1, 1024]]),
                                       AF.Copy), "se")
            se_U = b.cnt["se"]
            for e in range(8):
                pe_ = e % 2
                bank = lambda oc: ps_base[:, pe_ * 1024 + oc * 512 : pe_ * 1024 + (oc + 1) * 512]
                ebase = 64 * (e % 2)
                eg = (e // 2) * 1024
                if e == 0:
                    b.wait(nc.tensor, "se", se_U)
                elif e == 1:
                    b.wait(nc.tensor, "se", se_U)
                else:
                    b.wait(nc.tensor, "ve", ve_h_g[pe_])
                if e >= 2:
                    b.wait(nc.tensor, "se", se_s_g[pe_])
                for oc in range(2):
                    b.inc(nc.tensor.matmul(bank(oc), identb[:, :],
                                           G_sb[:, oc * 512 : (oc + 1) * 512],
                                           start=True, stop=False, skip_group_check=True), "te")
                    b.inc(nc.tensor.matmul(
                        bank(oc), gB_sb[ebase : ebase + 64, sl * 1024 + (e // 2) * 256 + oc * 128 :
                                        sl * 1024 + (e // 2) * 256 + (oc + 1) * 128],
                        mid_g[ebase : ebase + 64, eg + th * 512 : eg + (th + 1) * 512],
                        start=False, stop=True, skip_group_check=True), "te")
                te_g = b.cnt["te"]
                b.wait(nc.scalar, "te", te_g)
                if e >= 2:
                    b.wait(nc.scalar, "ve", ve_h_g[pe_])
                se_s_g[pe_] = b.inc(nc.scalar.activation(
                    s_sb[:, pe_ * 1024 : (pe_ + 1) * 1024],
                    AP(ps_base, pe_ * 1024, [[2048, 128], [1, 1024]]), AF.Silu), "se")
                b.wait(nc.tensor, "se", se_s_g[pe_])
                for oc in range(2):
                    b.inc(nc.tensor.matmul(bank(oc), identb[:, :],
                                           U_sb[:, oc * 512 : (oc + 1) * 512],
                                           start=True, stop=False, skip_group_check=True), "te")
                    b.inc(nc.tensor.matmul(
                        bank(oc), uB_sb[ebase : ebase + 64, sl * 1024 + (e // 2) * 256 + oc * 128 :
                                        sl * 1024 + (e // 2) * 256 + (oc + 1) * 128],
                        mid_u[ebase : ebase + 64, eg + th * 512 : eg + (th + 1) * 512],
                        start=False, stop=True, skip_group_check=True), "te")
                te_u = b.cnt["te"]
                b.wait(nc.vector, "te", te_u)
                b.wait(nc.vector, "se", se_s_g[pe_])
                if e >= 2:
                    b.wait(nc.vector, "po")
                b.inc(nc.vector.tensor_tensor(
                    h_sb[:, pe_ * 1024 : (pe_ + 1) * 1024],
                    s_sb[:, pe_ * 1024 : (pe_ + 1) * 1024],
                    AP(ps_base, pe_ * 1024, [[2048, 128], [1, 1024]]), ALU.mult), "ve")
                ve_h_g[pe_] = b.cnt["ve"]
                b.wait(nc.gpsimd, "ve", ve_h_g[pe_])
                if e >= 2:
                    b.wait(nc.gpsimd, "te")
                wev = AP(webc, e * 1024 + th * 512, [[8 * 1024, 128], [0, 2], [1, 512]])
                b.inc(nc.gpsimd.tensor_tensor(
                    whbf[:, pe_ * 1024 : (pe_ + 1) * 1024],
                    h_sb[:, pe_ * 1024 : (pe_ + 1) * 1024], wev, ALU.mult), "po")
                po_wh = b.cnt["po"]
                b.wait(nc.vector, "po", po_wh)
                hbs = hbarbf[:, (sl * 2 + th) * 1024 : (sl * 2 + th + 1) * 1024]
                if e == 0:
                    b.inc(nc.vector.tensor_copy(hbs, whbf[:, pe_ * 1024 : (pe_ + 1) * 1024]), "ve")
                else:
                    b.wait(nc.vector, "ve")
                    b.inc(nc.vector.tensor_tensor(hbs, hbs,
                                                  whbf[:, pe_ * 1024 : (pe_ + 1) * 1024],
                                                  ALU.add), "ve")
                b.wait(nc.tensor, "po", po_wh)
                if md_ve:
                    b.wait(nc.tensor, "ve", md_ve)
                for kc in range(2):
                    b.inc(nc.tensor.matmul(
                        ps_md[:, :],
                        dA_sb[:, e * 128 + kc * 64 : e * 128 + (kc + 1) * 64],
                        whbf[:, pe_ * 1024 + kc * 512 : pe_ * 1024 + (kc + 1) * 512],
                        start=(kc == 0), stop=(kc == 1)), "te")
                te_md = b.cnt["te"]
                b.wait(nc.vector, "te", te_md)
                mslice = md_sb[ebase : ebase + 64, (e // 2) * 1024 + th * 512 :
                               (e // 2) * 1024 + (th + 1) * 512]
                b.inc(nc.vector.tensor_tensor(mslice, mslice, ps_md[:, :], ALU.add), "ve")
                md_ve = b.cnt["ve"]
            if ic % 2 == 1:
                b.wait(nc.tensor, "ve")
                dp_ve = {}
                for oc in range(NCH):
                    pu = ps_util[:, (oc % 2) * 512 : ((oc % 2) + 1) * 512]
                    if oc >= 2:
                        b.wait(nc.tensor, "ve", dp_ve[oc - 2])
                    for pr in range(2):
                        psl = (ic - 1 + pr) % 2
                        for kc in range(2):
                            b.inc(nc.tensor.matmul(
                                pu, wd_b[:, psl * 4096 + kc * 2048 + oc * 128 :
                                         psl * 4096 + kc * 2048 + (oc + 1) * 128],
                                hbarbf[:, (psl * 2 + th) * 1024 + kc * 512 :
                                       (psl * 2 + th) * 1024 + (kc + 1) * 512],
                                start=(pr == 0 and kc == 0), stop=(pr == 1 and kc == 1)), "te")
                    b.wait(nc.vector, "te")
                    xs_out = x[:, oc * TLOC + th * 512 : oc * TLOC + (th + 1) * 512]
                    dp_ve[oc] = b.inc(nc.vector.tensor_tensor(xs_out, xs_out, pu, ALU.add), "ve")
        for tag in ["g", "u", "gB", "uB", "dA"]:
            setg(tag, ic)
        if ic % 2 == 1:
            setg("d", ic - 1)
            setg("d", ic)
        if ic + 1 < NIC:
            loadw("g", ic + 1)
            loadw("u", ic + 1)
            loadw("dA", ic + 1)
        if ic % 2 == 1:
            if ic + 1 < NIC:
                loadw("d", ic + 1)
            if ic + 2 < NIC:
                loadw("d", ic + 2)
        if ic + 2 < NIC:
            for tag in ["gB", "uB"]:
                loadw(tag, ic + 2)
    b.pop()

    b.push()
    b.fence()
    mdb = b.sb("f1_mdbf", [128, 4 * 1024], BF16)
    dB_sb = b.sb("f1_dB", [128, 4 * 2048], BF16)
    if FFN1_SECT < 3:
        b.pop()
        b.pop()
        return
    b.inc(nc.vector.tensor_scalar_mul(mdb[:, :], md_sb[:, :], 0.25), "ve")
    ve_mdb = b.cnt["ve"]
    for e in range(8):
        dst = dB_sb[64 * (e % 2) : 64 * (e % 2) + 64,
                    (e // 2) * 2048 : (e // 2 + 1) * 2048]
        b.dma(nc.gpsimd, dst, AP(io["down_B"], e * 64 * 2048, [[2048, 64], [1, 2048]]),
              ctr="lg")
    lg_dB = b.cnt["lg"]
    b.wait(nc.tensor, "lg", lg_dB)
    b.wait(nc.tensor, "ve", ve_mdb)
    ol_ve = {}
    ps_ol = ps_util
    for oc in range(NCH):
        for th in range(2):
            puA = ps_ol[:, 0:512]
            puB = ps_ol[:, 512:1024]
            if oc >= 1 or th == 1:
                b.wait(nc.tensor, "ve")
            for parity, pu in [(0, puA), (1, puB)]:
                for i_e, e in enumerate([parity, 2 + parity, 4 + parity, 6 + parity]):
                    ebase = 64 * parity
                    b.inc(nc.tensor.matmul(
                        pu, dB_sb[ebase : ebase + 64, (e // 2) * 2048 + oc * 128 :
                                  (e // 2) * 2048 + (oc + 1) * 128],
                        mdb[ebase : ebase + 64, (e // 2) * 1024 + th * 512 :
                            (e // 2) * 1024 + (th + 1) * 512],
                        start=(i_e == 0), stop=(i_e == 3)), "te")
            b.wait(nc.vector, "te")
            xs_out = x[:, oc * TLOC + th * 512 : oc * TLOC + (th + 1) * 512]
            b.inc(nc.vector.tensor_tensor(xs_out, xs_out, puA, ALU.add), "ve")
            b.wait(nc.vector, "ve")
            ol_ve[(oc, th)] = b.inc(nc.vector.tensor_tensor(xs_out, xs_out, puB, ALU.add), "ve")
    b.pop()
    b.pop()



def emit_attn1(b, io):
    nc = b.nc
    x = b.tn["x_res"]
    xln = b.tn["x_ln"]
    ln_done = b.ln_done
    b.push()
    b.fence()

    SC = 1.0 / float(np.sqrt(128.0))
    identf = b.tn["ident_f32"]
    identb = b.tn["ident_bf"]
    onesr = b.tn["ones_row"]

    ag_in = nc.dram_tensor(b.name("ag2_in"), [1056, 256], BF16)
    ag_out = nc.dram_tensor(b.name("ag2_out"), [4224, 256], BF16)

    qnw = b.sb("a1_qnw", [1, 128], F32)
    knw = b.sb("a1_knw", [1, 128], F32)
    sinkr = b.sb("a1_sink", [1, 16], F32)
    rcos = b.sb("a1_rcos", [64, 1024], F32)
    rsin = b.sb("a1_rsin", [64, 1024], F32)
    a2o = b.sb("a1_a2o", [1, 4], mybir.dt.int32)
    b.dma(nc.sync, qnw[:, :], io["qnw"][:, :])
    b.dma(nc.sync, knw[:, :], io["knw"][:, :])
    b.dma(nc.sync, sinkr[:, :], io["sink"][:, :])
    b.dma(nc.sync, rcos[:, :], io["rcos"][:, :])
    b.dma(nc.sync, rsin[:, :], io["rsin"][:, :])
    b.dma(nc.sync, a2o[:, :], io["a2off"][:, :])
    ld_small = b.cnt["ld"]

    b.push()
    q_sb = b.sb("a1_q", [128, 16 * 1024], BF16)

    b.push()
    hq_sb = b.sb("a1_hq", [128, 4 * 1024], BF16)
    ksw_f = b.sb("a1_kswf", [128, 1024], F32)
    kn_sb = b.sb("a1_kn", [128, 1024], BF16)
    vsw_sb = b.sb("a1_vsw", [128, 1024], BF16)
    vswtm_sb = b.sb("a1_vswtm", [128, 1024], BF16)
    ck_f = b.sb("a1_ckf", [128, 64], F32)
    ckn_sb = b.sb("a1_ckn", [128, 64], BF16)
    wbuf = b.sb("a1_wbuf", [128, 2 * 2048], BF16)
    rsc = b.sb("a1_rsc", [32, 3 * 2048], F32)
    sqs = b.sb("a1_sqs", [128, 1024], F32)
    srow = b.sb("a1_srow", [1, 1024], F32)

    ps_a = b.ps("a1_psa", [128, 2 * 1024], F32)
    ps_st = b.ps("a1_psst", [1, 512], F32)
    ps_bc = b.ps("a1_psbc", [128, 1024], F32)
    ps_vt = b.ps("a1_psvt", [128, 1024], BF16)

    ld_w = {}
    wg_guard = {}

    def wload(tag, ioname, blk, idx, nkc, par):
        if (tag, idx - 2) in ld_w and ld_w[(tag, idx - 2)][2] is not None:
            b.wait(nc.gpsimd, "te", ld_w[(tag, idx - 2)][2])
        w = nkc * 128
        v = b.dma(nc.gpsimd,
                  AP(wbuf, (idx % 2) * 2048, [[2 * 2048, 128], [1, w]]),
                  AP(io[ioname], blk * 128 * w, [[w, 128], [1, w]]),
                  ctr=par)
        ld_w[(tag, idx)] = [par, v, None]

    b.wait(nc.tensor, "ve", ln_done)
    b.wait(nc.tensor, "ld", ld_small)
    wload("qc", "Wq_c", 0, 0, NCH, "wa")
    wload("qc", "Wq_c", 1, 1, NCH, "wb")
    for oc in range(4):
        pq = ps_a[:, (oc % 2) * 1024 : ((oc % 2) + 1) * 1024]
        b.wait(nc.tensor, ld_w[("qc", oc)][0], ld_w[("qc", oc)][1])
        if oc >= 2:
            b.wait(nc.tensor, "se")
        for hf in range(2):
            for c in range(NCH):
                xs = xln[:, c * TLOC + hf * 512 : c * TLOC + (hf + 1) * 512]
                b.inc(nc.tensor.matmul(pq[:, hf * 512 : (hf + 1) * 512],
                                       wbuf[:, (oc % 2) * 2048 + c * 128 : (oc % 2) * 2048 + (c + 1) * 128],
                                       xs, start=(c == 0), stop=(c == NCH - 1)), "te")
        ld_w[("qc", oc)][2] = b.cnt["te"]
        if oc + 2 < 4:
            wload("qc", "Wq_c", oc + 2, oc + 2, NCH, "wa" if oc % 2 == 0 else "wb")
        b.wait(nc.scalar, "te")
        b.inc(nc.scalar.activation(hq_sb[:, oc * 1024 : (oc + 1) * 1024], pq, AF.Copy), "se")
    se_hq = b.cnt["se"]
    for oc in range(16):
        if oc < 2:
            b.wait(nc.gpsimd, "te")
        wload("qu", "Wq_up", oc, 4 + oc, 4, "wa" if oc % 2 == 0 else "wb")
        pq = ps_a[:, (oc % 2) * 1024 : ((oc % 2) + 1) * 1024]
        b.wait(nc.tensor, ld_w[("qu", 4 + oc)][0], ld_w[("qu", 4 + oc)][1])
        if oc == 0:
            b.wait(nc.tensor, "se", se_hq)
        else:
            b.wait(nc.tensor, "se")
        for hf in range(2):
            for c in range(4):
                hs = hq_sb[:, c * 1024 + hf * 512 : c * 1024 + (hf + 1) * 512]
                b.inc(nc.tensor.matmul(pq[:, hf * 512 : (hf + 1) * 512],
                                       wbuf[:, (oc % 2) * 2048 + c * 128 : (oc % 2) * 2048 + (c + 1) * 128],
                                       hs, start=(c == 0), stop=(c == 3)), "te")
        ld_w[("qu", 4 + oc)][2] = b.cnt["te"]
        b.wait(nc.scalar, "te")
        b.inc(nc.scalar.activation(q_sb[:, oc * 1024 : (oc + 1) * 1024], pq, AF.Copy), "se")
    se_qpre = b.cnt["se"]
    b.wait(nc.gpsimd, "te")
    wload("kv", "Wk", 0, 20, NCH, "wa")
    wload("kv", "Wv", 0, 21, NCH, "wb")
    b.wait(nc.tensor, ld_w[("kv", 20)][0], ld_w[("kv", 20)][1])
    b.wait(nc.tensor, "se")
    for hf in range(2):
        for c in range(NCH):
            xs = xln[:, c * TLOC + hf * 512 : c * TLOC + (hf + 1) * 512]
            b.inc(nc.tensor.matmul(ps_a[:, hf * 512 : (hf + 1) * 512],
                                   wbuf[:, c * 128 : (c + 1) * 128], xs,
                                   start=(c == 0), stop=(c == NCH - 1)), "te")
    te_k = b.cnt["te"]
    b.wait(nc.tensor, ld_w[("kv", 21)][0], ld_w[("kv", 21)][1])
    for hf in range(2):
        for c in range(NCH):
            xs = xln[:, c * TLOC + hf * 512 : c * TLOC + (hf + 1) * 512]
            b.inc(nc.tensor.matmul(ps_a[:, 1024 + hf * 512 : 1024 + (hf + 1) * 512],
                                   wbuf[:, 2048 + c * 128 : 2048 + (c + 1) * 128], xs,
                                   start=(c == 0), stop=(c == NCH - 1)), "te")
    te_v = b.cnt["te"]
    b.wait(nc.scalar, "te", te_k)
    b.inc(nc.scalar.activation(ksw_f[:, :], ps_a[:, 0:1024], AF.Copy), "se")
    b.wait(nc.scalar, "te", te_v)
    b.inc(nc.scalar.activation(vsw_sb[:, :], ps_a[:, 1024:2048], AF.Copy), "se")
    se_kv = b.cnt["se"]

    wld_c = {}
    for j in range(16):
        par = "wc" if j % 2 == 0 else "wd"
        if j < 2:
            b.wait(nc.gpsimd, "te")
        else:
            b.wait(nc.gpsimd, "te", wld_c[j - 2])
        b.dma(nc.gpsimd, AP(wbuf, (j % 2) * 2048, [[2 * 2048, 128], [1, 2048]]),
              AP(io["W_comp"], j * 128 * 2048, [[2048, 128], [1, 2048]]),
              ctr=par)
        ldv = b.cnt[par]
        b.wait(nc.tensor, par, ldv)
        for dc in range(NCH):
            rhs = AP(xln, dc * TLOC + j, [[NCH * TLOC, 128], [16, 64]])
            b.inc(nc.tensor.matmul(ps_bc[:, 0:64],
                                   wbuf[:, (j % 2) * 2048 + dc * 128 : (j % 2) * 2048 + (dc + 1) * 128],
                                   rhs, start=(j == 0 and dc == 0), stop=(j == 15 and dc == NCH - 1)),
                  "te")
        wld_c[j] = b.cnt["te"]
    te_ck = b.cnt["te"]
    b.wait(nc.scalar, "te", te_ck)
    b.inc(nc.scalar.activation(ck_f[:, :], ps_bc[:, 0:64], AF.Copy), "se")
    se_ck = b.cnt["se"]

    def rope(buf, nblk, blkstride, cosap, sinap, base_ve):
        x1 = AP(buf, 0, [[blkstride * nblk if False else buf.shape[1], 32], [blkstride, nblk], [1, 1024]])
        return None

    b.wait(nc.vector, "se", se_qpre)
    QW = 16 * 1024
    tA = rsc[:, 0:2048]; tB = rsc[:, 2048:4096]; tC = rsc[:, 4096:6144]
    for hp in range(8):
        off = hp * 2048
        x1 = AP(q_sb, off, [[QW, 32], [1024, 2], [1, 1024]])
        x2 = AP(q_sb, 32 * QW + off, [[QW, 32], [1024, 2], [1, 1024]])
        cosr = AP(rcos, 0, [[1024, 32], [0, 2], [1, 1024]])
        sinr = AP(rsin, 0, [[1024, 32], [0, 2], [1, 1024]])
        cosr32 = AP(rcos, 32 * 1024, [[1024, 32], [0, 2], [1, 1024]])
        sinr32 = AP(rsin, 32 * 1024, [[1024, 32], [0, 2], [1, 1024]])
        b.wait(nc.vector, "ve")
        b.inc(nc.vector.tensor_tensor(tA, x1, cosr, ALU.mult), "ve")
        b.inc(nc.vector.tensor_tensor(tB, x1, sinr, ALU.mult), "ve")
        b.inc(nc.vector.tensor_tensor(tC, x2, sinr32, ALU.mult), "ve")
        b.wait(nc.vector, "ve")
        b.inc(nc.vector.tensor_tensor(x1, tA, tC, ALU.subtract), "ve")
        b.wait(nc.vector, "ve")
        b.inc(nc.vector.tensor_tensor(tC, x2, cosr32, ALU.mult), "ve")
        b.wait(nc.vector, "ve")
        b.inc(nc.vector.tensor_tensor(x2, tB, tC, ALU.add), "ve")
    ve_qrope = b.cnt["ve"]
    b.wait(nc.vector, "se", se_kv)
    b.wait(nc.vector, "ve")
    x1 = AP(ksw_f, 0, [[1024, 32], [1, 1024]])
    x2 = AP(ksw_f, 32 * 1024, [[1024, 32], [1, 1024]])
    cos1 = rcos[0:32, :]
    sin1 = rsin[0:32, :]
    cos32 = rcos[32:64, :]
    sin32 = rsin[32:64, :]
    b.inc(nc.vector.tensor_tensor(tA[:, 0:1024], x1, cos1, ALU.mult), "ve")
    b.inc(nc.vector.tensor_tensor(tB[:, 0:1024], x1, sin1, ALU.mult), "ve")
    b.inc(nc.vector.tensor_tensor(tC[:, 0:1024], x2, sin32, ALU.mult), "ve")
    b.wait(nc.vector, "ve")
    b.inc(nc.vector.tensor_tensor(x1, tA[:, 0:1024], tC[:, 0:1024], ALU.subtract), "ve")
    b.wait(nc.vector, "ve")
    b.inc(nc.vector.tensor_tensor(tC[:, 0:1024], x2, cos32, ALU.mult), "ve")
    b.wait(nc.vector, "ve")
    b.inc(nc.vector.tensor_tensor(x2, tB[:, 0:1024], tC[:, 0:1024], ALU.add), "ve")
    ve_krope = b.cnt["ve"]

    def rmsnorm_fm(src_t, out_t, src_col, w_row, width, ve_ready):
        b.wait(nc.scalar, "ve", ve_ready)
        for hf in range((width + 511) // 512):
            w0 = src_col + hf * 512
            w1 = min(src_col + width, w0 + 512)
            n = w1 - w0
            b.wait(nc.scalar, "te")
            b.inc(nc.scalar.activation(sqs[:, 0:n], src_t[:, w0:w1], AF.Square), "se")
            b.wait(nc.tensor, "se")
            b.inc(nc.tensor.matmul(ps_st[:, 0:n], b.tn["ones_col"][:, :],
                                   sqs[:, 0:n], start=True, stop=True), "te")
            te_ss = b.cnt["te"]
            b.wait(nc.vector, "te", te_ss)
            b.inc(nc.vector.tensor_scalar_mul(srow[:, 0:n], ps_st[:, 0:n], 1.0 / 128.0), "ve")
            b.wait(nc.scalar, "ve")
            b.inc(nc.scalar.activation(srow[:, 0:n], srow[:, 0:n], AF.Sqrt,
                                       bias=b.tn["cvals"][0:1, 1:2]), "se")
            b.wait(nc.vector, "se")
            b.inc(nc.vector.reciprocal(srow[:, 0:n], srow[:, 0:n]), "ve")
            ve_r = b.cnt["ve"]
            b.wait(nc.tensor, "ve", ve_r)
            b.inc(nc.tensor.matmul(ps_bc[:, 0:n], w_row, srow[:, 0:n],
                                   start=True, stop=True), "te")
            te_m = b.cnt["te"]
            b.wait(nc.vector, "te", te_m)
            b.inc(nc.vector.tensor_tensor(out_t[:, w0:w1], src_t[:, w0:w1],
                                          ps_bc[:, 0:n], ALU.mult), "ve")
        return b.cnt["ve"]

    last = ve_qrope
    for h in range(16):
        last = rmsnorm_fm(q_sb, q_sb, h * 1024, qnw[:, :], 1024, last)
    ve_qn = last
    ve_kn = rmsnorm_fm(ksw_f, kn_sb, 0, knw[:, :], 1024, ve_krope)
    b.wait(nc.vector, "se", se_ck)
    ve_ckn = rmsnorm_fm(ck_f, ckn_sb, 0, knw[:, :], 64, b.cnt["ve"])

    b.wait(nc.tensor, "se", se_kv)
    for tt in range(8):
        b.inc(nc.tensor.transpose(ps_vt[:, tt * 128 : (tt + 1) * 128],
                                  vsw_sb[:, tt * 128 : (tt + 1) * 128], identb[:, :]), "te")
    te_vt = b.cnt["te"]
    b.wait(nc.vector, "te", te_vt)
    b.inc(nc.vector.tensor_copy(vswtm_sb[:, :], ps_vt[:, :]), "ve")
    ve_vt = b.cnt["ve"]

    b.wait(nc.sync, "ve", ve_kn)
    b.dma(nc.sync, AP(ag_in, 0, [[1024, 128], [1, 1024]]), kn_sb[:, :], ctr="st")
    b.wait(nc.sync, "ve", ve_vt)
    b.dma(nc.sync, AP(ag_in, 131072, [[1024, 128], [1, 1024]]), vswtm_sb[:, :], ctr="st")
    b.wait(nc.sync, "ve", ve_ckn)
    b.dma(nc.sync, AP(ag_in, 262144, [[64, 128], [1, 64]]), ckn_sb[:, :], ctr="st")
    b.wait(nc.gpsimd, "st")
    b.inc(nc.gpsimd.collective_compute(
        "AllGather", ALU.bypass, replica_groups=[[0, 1, 2, 3], [4, 5, 6, 7]],
        ins=[ag_in.ap().opt()], outs=[ag_out.ap().opt()]), "cc")
    cc2 = b.cnt["cc"]
    b.wait(nc.vector, "ve", ve_qn)
    b.pop()

    b.push()
    b.fence()
    kswp = b.sb("a1_kswp", [128, 2 * 1024], BF16)
    vswp = b.sb("a1_vswp", [128, 2 * 1024], BF16)
    cka = b.sb("a1_cka", [128, 256], BF16)
    cktm = b.sb("a1_cktm", [128, 2 * 128], BF16)
    cmask = b.sb("a1_cmask", [128, 8 * 256], F32)
    swmr = b.sb("a1_swmr", [1, 8 * 640], F32)
    swmd = b.sb("a1_swmd", [128, 128], F32)
    swmo = b.sb("a1_swmo", [128, 128], F32)
    sinkpack = b.sb("a1_sinkpack", [128, 64], F32)
    spr = b.sb("a1_spr", [1, 64], F32)
    tsw = b.sb("a1_tsw", [128, 640], F32)
    ptile = b.sb("a1_ptile", [128, 640 + 256], BF16)
    pts = b.sb("a1_pts", [128, 7 * 128], BF16)
    zcol = b.sb("a1_zcol", [128, 8], F32)

    ps_sw = b.ps("a1_pssw", [128, 640], F32)
    ps_c = b.ps("a1_psc", [128, 256], F32)
    ps_o = b.ps("a1_pso", [128, 128], F32)
    ps_pt = b.ps("a1_pspt", [128, 7 * 128], BF16)
    ps_sk = b.ps("a1_pssk", [128, 64], F32)

    b.dma(nc.sync, cmask[:, :], AP(io["c_masks"], 0, [[256, 128], [128 * 256, 8], [1, 256]]))
    b.dma(nc.sync, swmr[:, :], io["swm_row"][:, :])
    b.dma(nc.sync, swmd[:, :], io["sw_mask_diag"][:, :])
    b.dma(nc.sync, swmo[:, :], io["sw_mask_old"][:, :])
    ld_m = b.cnt["ld"]

    regs = [b.scopes[-1].enter_context(nc.gpsimd.register(f"a1r{i}")) for i in range(4)]
    b.wait(nc.gpsimd, "ld", ld_small)
    for i in range(4):
        nc.gpsimd.reg_load(regs[i], a2o[0:1, i : i + 1])
    b.wait(nc.gpsimd, "cc", cc2)
    dst = [kswp[:, 0:1024], vswp[:, 0:1024], kswp[:, 1024:2048], vswp[:, 1024:2048]]
    for i in range(4):
        b.dma(nc.gpsimd, dst[i], AP(ag_out, regs[i], [[1024, 128], [1, 1024]]), ctr="lg")
    for r2 in range(4):
        b.dma(nc.sync, cka[:, r2 * 64 : (r2 + 1) * 64],
              AP(ag_out, r2 * 1056 * 256 + 262144, [[64, 128], [1, 64]]), ctr="ha")
    lg_kv = b.cnt["lg"]
    wc_ck = b.cnt["ha"]

    b.wait(nc.tensor, "ha", wc_ck)
    for tt in range(2):
        b.inc(nc.tensor.transpose(ps_pt[:, tt * 128 : (tt + 1) * 128],
                                  cka[:, tt * 128 : (tt + 1) * 128], identb[:, :]), "te")
    te_cktm = b.cnt["te"]
    b.wait(nc.vector, "te", te_cktm)
    b.inc(nc.vector.tensor_copy(cktm[:, :], AP(ps_pt, 0, [[896, 128], [1, 256]])), "ve")
    ve_cktm = b.cnt["ve"]

    b.wait(nc.vector, "ld", ld_small)
    b.inc(nc.vector.tensor_scalar_max(spr[:, 48:64], sinkr[:, :], 0.0), "ve")
    b.wait(nc.vector, "ve")
    b.inc(nc.vector.tensor_tensor(spr[:, 16:32], sinkr[:, :], spr[:, 48:64], ALU.subtract), "ve")
    b.inc(nc.vector.tensor_scalar_mul(spr[:, 0:16], spr[:, 48:64], -1.0), "ve")
    b.wait(nc.scalar, "ve")
    b.inc(nc.scalar.activation(spr[:, 16:32], spr[:, 16:32], AF.Exp), "se")
    b.wait(nc.vector, "ve")
    b.inc(nc.vector.tensor_scalar_mul(spr[:, 32:48], sinkr[:, :], -1.0), "ve")
    b.wait(nc.vector, "ve")
    b.inc(nc.vector.tensor_copy(spr[:, 48:64], sinkr[:, :]), "ve")
    ve_spr = b.cnt["ve"]
    b.wait(nc.tensor, "ve", ve_spr)
    b.wait(nc.tensor, "se")
    b.inc(nc.tensor.matmul(ps_sk[:, :], onesr[:, :], spr[:, :], start=True, stop=True), "te")
    b.wait(nc.scalar, "te")
    b.inc(nc.scalar.activation(sinkpack[:, :], ps_sk[:, :], AF.Copy), "se")
    se_sink = b.cnt["se"]

    b.wait(nc.tensor, "lg", lg_kv)
    b.wait(nc.tensor, "ld", ld_m)
    b.wait(nc.tensor, "se", se_sink)
    b.wait(nc.tensor, "ve", ve_cktm)
    for h in range(16):
        for qt in range(8):
            qtile = q_sb[:, h * 1024 + qt * 128 : h * 1024 + (qt + 1) * 128]
            w0 = 512 + 128 * qt
            b.wait(nc.tensor, "se")
            b.inc(nc.tensor.matmul(ps_sw[:, 0:512], onesr[:, :],
                                   swmr[0:1, qt * 640 : qt * 640 + 512],
                                   start=True, stop=False, skip_group_check=True), "te")
            b.inc(nc.tensor.matmul(ps_sw[:, 512:640], onesr[:, :],
                                   swmr[0:1, qt * 640 + 512 : (qt + 1) * 640],
                                   start=True, stop=False, skip_group_check=True), "te")
            b.inc(nc.tensor.matmul(ps_sw[:, 0:512], qtile,
                                   AP(kswp, w0, [[2048, 128], [1, 512]]),
                                   start=False, stop=True, skip_group_check=True), "te")
            b.inc(nc.tensor.matmul(ps_sw[:, 512:640], qtile,
                                   AP(kswp, w0 + 512, [[2048, 128], [1, 128]]),
                                   start=False, stop=True, skip_group_check=True), "te")
            te_sws = b.cnt["te"]
            b.wait(nc.scalar, "te", te_sws)
            b.wait(nc.scalar, "ve")
            b.inc(nc.scalar.activation(tsw[:, :], ps_sw[:, :], AF.Tanh,
                                       scale=SC / 50.0), "se")
            te_tanh = b.cnt["se"]
            b.wait(nc.vector, "se", te_tanh)
            b.inc(nc.vector.tensor_tensor(tsw[:, 0:128], tsw[:, 0:128], swmo[:, :], ALU.add), "ve")
            b.inc(nc.vector.tensor_tensor(tsw[:, 512:640], tsw[:, 512:640], swmd[:, :], ALU.add), "ve")
            ve_msk = b.cnt["ve"]
            b.wait(nc.scalar, "ve", ve_msk)
            b.inc(nc.scalar.activation(ptile[:, 0:640], tsw[:, :], AF.Exp, scale=50.0,
                                       bias=sinkpack[:, h : h + 1],
                                       accum_out=zcol[:, 0:1]), "se")
            se_swexp = b.cnt["se"]
            b.wait(nc.tensor, "ve")
            b.inc(nc.tensor.matmul(ps_c[:, :], qtile, cka[:, :],
                                   start=True, stop=True, skip_group_check=True), "te")
            te_cs = b.cnt["te"]
            b.wait(nc.vector, "te", te_cs)
            b.inc(nc.vector.tensor_tensor(ps_c[:, :], ps_c[:, :],
                                          cmask[:, qt * 256 : (qt + 1) * 256], ALU.add), "ve")
            b.wait(nc.vector, "ve")
            b.inc(nc.vector.tensor_reduce(zcol[:, 2:3], ps_c[:, :], AX.X, ALU.max,
                                          negate=True), "ve")
            b.wait(nc.vector, "ve")
            b.inc(nc.vector.tensor_scalar(zcol[:, 3:4], zcol[:, 2:3], SC,
                                          sinkpack[:, 32 + h : 33 + h], ALU.mult, ALU.min), "ve")
            ve_cbias = b.cnt["ve"]
            b.wait(nc.scalar, "ve", ve_cbias)
            b.inc(nc.scalar.activation(ptile[:, 640:896], ps_c[:, :], AF.Exp, scale=SC,
                                       bias=zcol[:, 3:4], accum_out=zcol[:, 4:5]), "se")
            se_cexp = b.cnt["se"]
            b.wait(nc.scalar, "se", se_cexp)
            b.inc(nc.scalar.activation(zcol[:, 5:6], zcol[:, 3:4], AF.Exp,
                                       bias=sinkpack[:, 48 + h : 49 + h]), "se")
            se_csink = b.cnt["se"]
            b.wait(nc.vector, "se", se_csink)
            b.inc(nc.vector.tensor_tensor(zcol[:, 4:5], zcol[:, 4:5], zcol[:, 5:6], ALU.add), "ve")
            b.inc(nc.vector.tensor_scalar(zcol[:, 1:2], zcol[:, 0:1],
                                          sinkpack[:, 16 + h : 17 + h], None, ALU.add), "ve")
            b.wait(nc.vector, "ve")
            b.inc(nc.vector.reciprocal(zcol[:, 6:7], zcol[:, 1:2]), "ve")
            b.inc(nc.vector.reciprocal(zcol[:, 7:8], zcol[:, 4:5]), "ve")
            b.wait(nc.vector, "ve")
            b.inc(nc.vector.tensor_scalar(ptile[:, 0:640], ptile[:, 0:640],
                                          zcol[:, 6:7], None, ALU.mult), "ve")
            b.inc(nc.vector.tensor_scalar(ptile[:, 640:896], ptile[:, 640:896],
                                          zcol[:, 7:8], None, ALU.mult), "ve")
            ve_pn = b.cnt["ve"]
            b.wait(nc.tensor, "ve", ve_pn)
            for tt in range(7):
                b.inc(nc.tensor.transpose(ps_pt[:, tt * 128 : (tt + 1) * 128],
                                          ptile[:, tt * 128 : (tt + 1) * 128], identb[:, :]),
                      "te")
            te_pt = b.cnt["te"]
            b.wait(nc.vector, "te", te_pt)
            b.inc(nc.vector.tensor_copy(pts[:, :], AP(ps_pt, 0, [[896, 128], [1, 896]])), "ve")
            ve_pts = b.cnt["ve"]
            b.wait(nc.tensor, "ve", ve_pts)
            for k in range(5):
                vt = AP(vswp, (4 + qt + k) * 128, [[2048, 128], [1, 128]])
                b.inc(nc.tensor.matmul(ps_o[:, :], vt, pts[:, k * 128 : (k + 1) * 128],
                                       start=(k == 0), stop=False, skip_group_check=True), "te")
            for tt in range(2):
                b.inc(nc.tensor.matmul(ps_o[:, :], cktm[:, tt * 128 : (tt + 1) * 128],
                                       pts[:, (5 + tt) * 128 : (6 + tt) * 128],
                                       start=False, stop=(tt == 1), skip_group_check=True), "te")
            te_av = b.cnt["te"]
            b.wait(nc.vector, "te", te_av)
            b.inc(nc.vector.tensor_copy(
                xln[:, h * 1024 + qt * 128 : h * 1024 + (qt + 1) * 128], ps_o[:, :]), "ve")
    ao_done = b.cnt["ve"]
    b.pop()
    b.fence()
    b.pop()

    b.push()
    b.fence()
    inter = b.sb("a1_inter", [128, 64 * 512], BF16)
    wbufc = b.sb("a1_wbufc", [128, 2 * 512], BF16)
    wbufo = b.sb("a1_wbufo", [128, 2 * 4096], BF16)
    ps_i = b.ps("a1_psi", [128, 2 * 512], F32)
    ps_r = b.ps("a1_psr", [128, 2 * 512], F32)

    ldc = {}
    ldo = {}
    for th in range(2):
        for ic in range(64):
            g = ic // 16
            par = "wa" if ic % 2 == 0 else "wb"
            key = th * 64 + ic
            if th == 0 and ic < 2:
                pass
            if key >= 2:
                b.wait(nc.gpsimd, "te", ldc[key - 2][1])
            v = b.dma(nc.gpsimd,
                      AP(wbufc, (ic % 2) * 512, [[2 * 512, 128], [1, 512]]),
                      AP(io["Wg"], ic * 128 * 512, [[512, 128], [1, 512]]),
                      ctr=par)
            ldc[key] = [(par, v), None]
            b.wait(nc.tensor, par, v)
            pi = ps_i[:, (ic % 2) * 512 : ((ic % 2) + 1) * 512]
            if key >= 2:
                b.wait(nc.tensor, "se", ldc[key - 2][2] if len(ldc[key - 2]) > 2 else 0)
            for c in range(4):
                aos = xln[:, (g * 4 + c) * TLOC + th * 512 : (g * 4 + c) * TLOC + (th + 1) * 512]
                b.inc(nc.tensor.matmul(pi, wbufc[:, (ic % 2) * 512 + c * 128 :
                                                 (ic % 2) * 512 + (c + 1) * 128],
                                       aos, start=(c == 0), stop=(c == 3)), "te")
            ldc[key][1] = b.cnt["te"]
            b.wait(nc.scalar, "te", ldc[key][1])
            se_i = b.inc(nc.scalar.activation(inter[:, ic * 512 : (ic + 1) * 512], pi,
                                              AF.Copy), "se")
            ldc[key].append(se_i)
        se_inter = b.cnt["se"]
        b.wait(nc.tensor, "se", se_inter)
        for oc in range(16):
            for half in range(2):
                key = (th * 16 + oc) * 2 + half
                par = "wc" if key % 2 == 0 else "wd"
                if th == 0 and oc == 0:
                    prevk = None
                if key >= 2 and (key - 2) in ldo:
                    b.wait(nc.gpsimd, "te", ldo[key - 2][1])
                v = b.dma(nc.gpsimd,
                          AP(wbufo, (key % 2) * 4096, [[2 * 4096, 128], [1, 4096]]),
                          AP(io["Wo"], (oc * 2 + half) * 128 * 4096, [[4096, 128], [1, 4096]]),
                          ctr=par)
                ldo[key] = [(par, v), None]
            pr = ps_r[:, (oc % 2) * 512 : ((oc % 2) + 1) * 512]
            if oc >= 2 or th == 1:
                b.wait(nc.tensor, "ve")
            for half in range(2):
                key = (th * 16 + oc) * 2 + half
                b.wait(nc.tensor, ldo[key][0][0], ldo[key][0][1])
                for c in range(32):
                    iis = inter[:, (half * 32 + c) * 512 : (half * 32 + c + 1) * 512]
                    b.inc(nc.tensor.matmul(pr, wbufo[:, (key % 2) * 4096 + c * 128 :
                                                     (key % 2) * 4096 + (c + 1) * 128],
                                           iis, start=(half == 0 and c == 0),
                                           stop=(half == 1 and c == 31)), "te")
                ldo[key][1] = b.cnt["te"]
            b.wait(nc.vector, "te")
            xs_out = x[:, oc * TLOC + th * 512 : oc * TLOC + (th + 1) * 512]
            b.inc(nc.vector.tensor_tensor(xs_out, xs_out, pr, ALU.add), "ve")
    b.pop()
    b.pop()



def emit_store(b, io_out):
    nc = b.nc
    x = b.tn["x_res"]
    b.fence()
    for c in range(NCH):
        b.dma(nc.sync, AP(io_out, c * 128 * TLOC, [[TLOC, 128], [1, TLOC]]),
              x[:, c * TLOC : (c + 1) * TLOC], ctr="st")
    b.wait(nc.sync, "st")



PHASE_ORDER = ["ffn0", "attn0", "ffn0", "ffn1", "attn1", "ffn1"]
FFN1_NIC = 28
FFN1_SECT = 9
OL_EXPERTS = list(range(8))


def build_program(phases):
    nc = bass.Bass("TRN2", target_bir_lowering=False)
    io = {}
    for name, shape, dt in ALL_INPUTS:
        io[name] = nc.dram_tensor(name, list(shape), dt, kind="ExternalInput")
    y = nc.dram_tensor("y", [2048, 1024], F32, kind="ExternalOutput")
    with contextlib.ExitStack() as stack:
        b = B(nc, stack)
        with nc.Block() as blk:
            @blk.gpsimd
            def _(g):
                emit_load(b, io)
                for ph in phases:
                    emit_ln(b)
                    if ph == "ffn0":
                        emit_ffn0(b, io)
                    elif ph == "attn0":
                        emit_attn0(b, io)
                    elif ph == "ffn1":
                        emit_ffn1(b, io)
                    elif ph == "attn1":
                        emit_attn1(b, io)
                    else:
                        raise ValueError(ph)
                emit_store(b, y)
    return nc


def run_phases(in_maps, phase_groups, trace=False):
    cur = [m["x"] for m in in_maps]
    res = None
    for grp in phase_groups:
        nc = build_program(grp)
        maps = []
        for c in range(NCORES):
            m = dict(in_maps[c])
            m["x"] = cur[c]
            maps.append(m)
        res = run_bass_kernel_spmd(nc, maps, core_ids=list(range(NCORES)), trace=trace)
        cur = [np.asarray(res.results[c]["y"]) for c in range(NCORES)]
    return cur, res


def assemble(cur):
    out = np.zeros((2, 4096, 2048), np.float32)
    for c in range(NCORES):
        bidx = c // 4
        t0 = 1024 * (c % 4)
        out[bidx, t0 : t0 + TLOC, :] = cur[c].T
    return out


def kernel(**inputs):
    in_maps = prep_host(inputs)
    cur, _ = run_phases(in_maps, [PHASE_ORDER])
    return assemble(cur)
